# version 04
# baseline (speedup 1.0000x reference)
import contextlib
import numpy as np
import concourse.bass as bass
import concourse.mybir as mybir
from concourse.bass_utils import run_bass_kernel_spmd

F32 = mybir.dt.float32
BF16 = mybir.dt.bfloat16
AF = mybir.ActivationFunctionType
ALU = mybir.AluOpType

NCORES = 8
DM = 1024
C = DM // 128
B = 32
S = 2048
TOK = S // NCORES
BLOC = B // NCORES
H = 16
DK = 64
LNEPS = 1e-5


class Sem:
    def __init__(self, name):
        self.name = name
        self.n = 0
        self.h = None

    def inc(self, amt=1):
        self.n += amt
        return self.n


class Prog:

    def __init__(self):
        self.ops = {e: [] for e in ("tensor", "scalar", "vector", "gpsimd", "sync")}
        self.sems = {}

    def sem(self, name):
        if name not in self.sems:
            self.sems[name] = Sem(name)
        return self.sems[name]

    def emit(self, engine, fn, waits=(), inc=None):
        n = None
        if inc is not None:
            n = inc[0].inc(inc[1])
        self.ops[engine].append((tuple(waits), fn, inc))
        return n

    def build_engine(self, eng, name):
        for waits, fn, inc in self.ops[name]:
            for s, v in waits:
                if v > 0:
                    eng.wait_ge(s.h, v)
            inst = fn(eng)
            if inc is not None:
                assert inst is not None, f"inc on None inst in {name}"
                inst.then_inc(inc[0].h, inc[1])



def prechunk(w2):
    return np.ascontiguousarray(
        w2.reshape(C, 128, C, 128).transpose(1, 0, 2, 3).reshape(128, C * C * 128)
    )


def pack_bias(b):
    return np.ascontiguousarray(np.asarray(b, np.float32).reshape(C, 128).T)


def wslice(wsb, kc, mc):
    o = (kc * C + mc) * 128
    return wsb[:, o:o + 128]



def build_program(mode="full"):
    nc = bass.Bass("TRN2", target_bir_lowering=False)
    p = Prog()

    gath_kind = "ExternalInput" if mode == "bc" else "Internal"
    gath_kw = {} if mode == "bc" else {"addr_space": "Shared"}
    gath = [nc.dram_tensor(f"gath{j}", [NCORES * DM, S], F32, kind=gath_kind,
                           **gath_kw).ap()
            for j in range(BLOC)]
    out_h = nc.dram_tensor("out", [DM, B * TOK], F32, kind="ExternalOutput").ap()
    vsp = nc.dram_tensor("vsp", [DM, B * TOK], F32).ap()
    stats_d = nc.dram_tensor("stats_d", [B, 2 * TOK], F32).ap()
    stats_o = nc.dram_tensor("stats_o", [B, 2 * TOK], F32).ap()

    wnames_bc = ["wfT", "ufT", "wiT", "uiT", "wcT", "ucT", "woT", "uoT"]
    whbm = {n: nc.dram_tensor(n, [128, C * C * 128], F32, kind="ExternalInput").ap()
            for n in wnames_bc}
    bnames = ["bf", "bi", "bc", "bo", "inc0"]
    bhbm = {n: nc.dram_tensor(n, [128, C], F32, kind="ExternalInput").ap()
            for n in bnames}

    if mode == "full":
        xT = nc.dram_tensor("xT", [DM, BLOC * S], F32, kind="ExternalInput").ap()
        for n in ["Wq", "Wk", "Wv", "Wpost"]:
            whbm[n] = nc.dram_tensor(n, [128, C * C * 128], F32,
                                     kind="ExternalInput").ap()
        slab = [nc.dram_tensor(f"slab{j}", [DM, S], F32).ap() for j in range(BLOC)]
    else:
        slab = None
    wbf16_d = {n: nc.dram_tensor(n + "_bf", [128, C * C * 128], BF16).ap()
               for n in whbm}
    xbf_d = (nc.dram_tensor("xT_bf", [DM, BLOC * S], BF16).ap()
             if mode == "full" else None)

    es = contextlib.ExitStack()
    with es:
        T = lambda *a, **k: es.enter_context(nc.sbuf_tensor(*a, **k))
        PS = lambda *a, **k: es.enter_context(nc.psum_tensor(*a, **k))

        ones1 = T("ones1", [1, 128], BF16)
        ones1F = T("ones1F", [1, 128], F32)
        ones128 = T("ones128", [128, 1], BF16)
        epst = T("epst", [128, 1], F32)
        bias_sb = {n: T(f"sb_{n}", [128, C], F32) for n in bnames}

        s_ld = p.sem("s_ld")
        s_lf = p.sem("s_lf")
        s_st = p.sem("s_st")
        s_pe = p.sem("s_pe")
        s_act = p.sem("s_act")
        s_dve = p.sem("s_dve")
        s_cc = p.sem("s_cc")

        dyn = {}

        def ginit(g):
            dyn["g_off"] = g.snap(g.partition_id() * TOK)
            return None
        p.emit("gpsimd", ginit)

        def syinit(sy):
            dyn["s_off"] = sy.snap(sy.partition_id() * TOK)
            return None
        p.emit("sync", syinit)

        p.emit("gpsimd", lambda g: g.memset(ones1[:, :], 1.0), inc=(s_ld, 1))
        p.emit("gpsimd", lambda g: g.memset(ones1F[:, :], 1.0), inc=(s_ld, 1))
        p.emit("gpsimd", lambda g: g.memset(ones128[:, :], 1.0), inc=(s_ld, 1))
        p.emit("gpsimd", lambda g: g.memset(epst[:, :], LNEPS), inc=(s_ld, 1))
        consts_ready = (s_ld, s_ld.n)

        for n in bnames:
            p.emit("sync", lambda sy, n=n: sy.dma_start(
                out=bias_sb[n][:, :], in_=bhbm[n]), inc=(s_lf, 16))
        bias_ready = (s_lf, s_lf.n)

        wcast = {}
        for n in whbm:
            wcast[n] = p.emit("gpsimd", lambda g, n=n: g.dma_start(
                out=wbf16_d[n], in_=whbm[n]), inc=(s_ld, 16))
        if mode == "full":
            xcast = p.emit("gpsimd", lambda g: g.dma_start(
                out=xbf_d, in_=xT), inc=(s_ld, 16))


        a_store_end = 0
        if mode == "full":
            actx = dict(nc=nc, p=p, es=es, T=T, PS=PS, xT=xT, whbm=whbm,
                        wbf16_d=wbf16_d, xbf_d=xbf_d, wcast=wcast, xcast=xcast,
                        slab=slab, gath=gath, ones1=ones1, ones128=ones128,
                        ones1F=ones1F,
                        epst=epst, s_ld=s_ld, s_lf=s_lf, s_st=s_st, s_pe=s_pe,
                        s_act=s_act, s_dve=s_dve, s_cc=s_cc)
            a_store_end = build_phase_a(actx)


        w_sb = {n: T(f"w_{n}", [128, C * C * 128], BF16)
                for n in ["wfT", "ufT", "wiT", "uiT", "wcT", "ucT"]}
        w_sb["woT"] = w_sb["wfT"]
        w_sb["uoT"] = w_sb["ufT"]
        c_sb = T("c_sb", [128, C * TOK], F32)
        cbf = T("cbf", [128, C * TOK], BF16)
        xt_sb = [T(f"xt{i}", [128, C * TOK], BF16) for i in range(2)]
        gts = [T(f"gts{i}", [128, 3 * TOK], F32) for i in range(2)]
        tmp1 = T("tmp1", [128, TOK], F32)
        tmp2 = T("tmp2", [128, TOK], F32)
        uoc = T("uoc", [128, C * TOK], F32)
        x1f = [T(f"x1f{i}", [128, C * TOK], F32) for i in range(2)]
        x1b = [T(f"x1b{i}", [128, C * TOK], BF16) for i in range(2)]
        hv = T("hv", [128, C * TOK], F32)
        vb = T("vb", [128, C * TOK], BF16)
        sqb = T("sqb", [128, C * TOK], BF16)
        sst = [T(f"sst{i}", [1, 2 * TOK], F32) for i in range(2)]
        stT = T("stT", [B, 2 * TOK], F32)
        stX = T("stX", [B, TOK], F32)
        stY = T("stY", [B, TOK], F32)
        stO = T("stO", [B, 2 * TOK], F32)
        stin = [T(f"stin{i}", [1, 2 * TOK], F32) for i in range(2)]
        outsb = [T(f"outsb{i}", [128, C * TOK], F32) for i in range(2)]

        psG = [PS(f"psG{i}", [128, 512], F32) for i in range(4)]
        psS = PS("psS", [1, 512], F32)
        psB = PS("psB", [128, 512], F32)

        wA = [(s_st, a_store_end)] if mode == "full" else []
        for n in ["wfT", "ufT", "wiT", "uiT", "wcT", "ucT"]:
            p.emit("sync", lambda sy, n=n: sy.dma_start(
                out=w_sb[n][:, :], in_=wbf16_d[n]),
                waits=wA + [(s_ld, wcast[n])], inc=(s_lf, 16))
        wB_ready = (s_lf, s_lf.n)

        wc0 = [bias_ready] + ([(s_st, a_store_end)] if mode == "full" else [])
        p.emit("vector", lambda v: v.memset(c_sb[:, :], 0.0),
               waits=wc0, inc=(s_dve, 1))
        for c in range(C):
            p.emit("vector", lambda v, c=c: v.tensor_scalar_add(
                c_sb[:, c * TOK:(c + 1) * TOK], c_sb[:, c * TOK:(c + 1) * TOK],
                bias_sb["inc0"][:, c:c + 1]), inc=(s_dve, 1))

        GATES = [("wfT", "ufT", "bf", AF.Sigmoid), ("wiT", "uiT", "bi", AF.Sigmoid),
                 ("wcT", "ucT", "bc", AF.Tanh)]
        pe_group = {}
        act_of_group = {}
        xt_used = {}
        cupd = {}
        cupd[(-1, C - 1)] = s_dve.n
        cupd[(-1, C - 2)] = s_dve.n
        xt_loaded = {}
        ggrp = 0

        for t in range(B):
            j, r = t // NCORES, t % NCORES
            w = [(s_pe, xt_used[t - 2])] if t - 2 in xt_used else []
            if mode == "full":
                w = w + [(s_cc, j + 1), (s_st, a_store_end)]

            def ldxt(g, j=j, r=r, t=t):
                src = gath[j][r * DM:(r + 1) * DM, bass.ds(dyn["g_off"], TOK)]
                src = src.rearrange("(c p) n -> p c n", p=128)
                return g.dma_start(out=xt_sb[t % 2][:, :].rearrange(
                    "p (c n) -> p c n", c=C), in_=src)
            xt_loaded[t] = p.emit("gpsimd", ldxt, waits=w, inc=(s_ld, 16))

            wv = [(s_dve, cupd[(t - 1, C - 1)])]
            if t > 0:
                wv.append((s_pe, pe_group[(t - 1, C - 1, 2)]))
            cbf_n = p.emit("vector", lambda v: v.tensor_copy(cbf[:, :], c_sb[:, :]),
                           waits=wv, inc=(s_dve, 1))

            for m in range(C):
                for gi, (wn, un, bn, af) in enumerate(GATES):
                    bank = psG[ggrp % 4]
                    wmm = []
                    if m == 0 and gi == 0:
                        wmm = [(s_ld, xt_loaded[t]), (s_lf, wB_ready[1]),
                               (s_dve, cbf_n)]
                    if ggrp - 4 >= 0 and (ggrp - 4) in act_of_group:
                        wmm.append((s_act, act_of_group[ggrp - 4]))

                    def mk_mms(te, wn=wn, un=un, m=m, t=t, bank=bank):
                        inst = None
                        for k in range(C):
                            inst = te.matmul(
                                bank[:, 0:TOK], wslice(w_sb[wn], k, m),
                                xt_sb[t % 2][:, k * TOK:(k + 1) * TOK],
                                start=(k == 0), stop=False)
                        for k in range(C):
                            inst = te.matmul(
                                bank[:, 0:TOK], wslice(w_sb[un], k, m),
                                cbf[:, k * TOK:(k + 1) * TOK],
                                start=False, stop=(k == C - 1))
                        return inst
                    pe_group[(t, m, gi)] = p.emit("tensor", mk_mms, waits=wmm,
                                                  inc=(s_pe, 1))

                    wact = [(s_pe, pe_group[(t, m, gi)])]
                    key = (t, m - 2) if m >= 2 else (t - 1, C - 2 + m)
                    if key in cupd:
                        wact.append((s_dve, cupd[key]))

                    def mk_act(sc, af=af, bn=bn, m=m, gi=gi, bank=bank):
                        return sc.activation(
                            gts[m % 2][:, gi * TOK:(gi + 1) * TOK], bank[:, 0:TOK],
                            af, bias=bias_sb[bn][:, m:m + 1], scale=1.0)
                    act_of_group[ggrp] = p.emit("scalar", mk_act, waits=wact,
                                                inc=(s_act, 1))
                    ggrp += 1

                wcu = [(s_act, s_act.n)]
                cs_lo, cs_hi = m * TOK, (m + 1) * TOK
                g0 = gts[m % 2]
                p.emit("vector", lambda v, g0=g0, lo=cs_lo, hi=cs_hi:
                       v.tensor_tensor(tmp1[:, :], g0[:, 0:TOK], c_sb[:, lo:hi],
                                       op=ALU.mult), waits=wcu, inc=(s_dve, 1))
                p.emit("vector", lambda v, g0=g0: v.tensor_tensor(
                    tmp2[:, :], g0[:, TOK:2 * TOK], g0[:, 2 * TOK:3 * TOK],
                    op=ALU.mult), inc=(s_dve, 1))
                cupd[(t, m)] = p.emit("vector", lambda v, lo=cs_lo, hi=cs_hi:
                                      v.tensor_tensor(c_sb[:, lo:hi], tmp1[:, :],
                                                      tmp2[:, :], op=ALU.add),
                                      inc=(s_dve, 1))
            xt_used[t] = pe_group[(t, C - 1, 2)]

        for n in ["woT", "uoT"]:
            p.emit("sync", lambda sy, n=n: sy.dma_start(
                out=w_sb[n][:, :], in_=wbf16_d[n]),
                waits=[(s_pe, pe_group[(B - 1, C - 1, 2)]),
                       (s_ld, wcast[n])], inc=(s_lf, 16))
        wC_ready = (s_lf, s_lf.n)

        cbf_fin = p.emit("vector", lambda v: v.tensor_copy(cbf[:, :], c_sb[:, :]),
                         waits=[(s_dve, cupd[(B - 1, C - 1)]),
                                (s_pe, pe_group[(B - 1, C - 1, 2)])],
                         inc=(s_dve, 1))

        uoc_done = {}
        for m in range(C):
            bank = psG[m % 2]

            def mk_uoc(te, m=m, bank=bank):
                inst = None
                for k in range(C):
                    inst = te.matmul(bank[:, 0:TOK], wslice(w_sb["uoT"], k, m),
                                     cbf[:, k * TOK:(k + 1) * TOK],
                                     start=(k == 0), stop=(k == C - 1))
                return inst
            wmm = [(s_dve, cbf_fin), (s_lf, wC_ready[1])]
            if m >= 2:
                wmm.append((s_dve, uoc_done[m - 2]))
            gn = p.emit("tensor", mk_uoc, waits=wmm, inc=(s_pe, 1))
            uoc_done[m] = p.emit("vector", lambda v, m=m, bank=bank:
                                 v.tensor_scalar_mul(
                                     uoc[:, m * TOK:(m + 1) * TOK],
                                     bank[:, 0:TOK], 1.0),
                                 waits=[(s_pe, gn)], inc=(s_dve, 1))

        x1_loaded = {}
        x1b_loaded = {}
        wo_done = {}
        a1_hist = {}
        a2_hist = {}
        vadds_last = {}
        spill_done = {}
        e2_hist = {}
        st_stored = {}
        for t in range(B):
            j, r = t // NCORES, t % NCORES
            wld = [(s_dve, vadds_last[t - 2])] if t - 2 in vadds_last else []

            def ldf(sy, j=j, r=r, t=t):
                src = gath[j][r * DM:(r + 1) * DM, bass.ds(dyn["s_off"], TOK)]
                src = src.rearrange("(c p) n -> p c n", p=128)
                return sy.dma_start(out=x1f[t % 2][:, :].rearrange(
                    "p (c n) -> p c n", c=C), in_=src)
            x1_loaded[t] = p.emit("sync", ldf, waits=wld, inc=(s_lf, 16))

            wldb = [(s_pe, wo_done[t - 2])] if t - 2 in wo_done else []

            def ldb(g, j=j, r=r, t=t):
                src = gath[j][r * DM:(r + 1) * DM, bass.ds(dyn["g_off"], TOK)]
                src = src.rearrange("(c p) n -> p c n", p=128)
                return g.dma_start(out=x1b[t % 2][:, :].rearrange(
                    "p (c n) -> p c n", c=C), in_=src)
            x1b_loaded[t] = p.emit("gpsimd", ldb, waits=wldb, inc=(s_ld, 16))

            for m in range(C):
                bank = psG[2 + (m % 2)]
                wmm = [(s_ld, x1b_loaded[t])]
                key = (t, m - 2) if m >= 2 else (t - 1, C - 2 + m)
                if key in a1_hist:
                    wmm.append((s_dve, a1_hist[key]))
                if t == 0 and m < 2:
                    wmm.append((s_dve, cbf_fin))

                def mk_wo(te, m=m, t=t, bank=bank):
                    inst = None
                    for k in range(C):
                        inst = te.matmul(bank[:, 0:TOK], wslice(w_sb["woT"], k, m),
                                         x1b[t % 2][:, k * TOK:(k + 1) * TOK],
                                         start=(k == 0), stop=(k == C - 1))
                    return inst
                gn = p.emit("tensor", mk_wo, waits=wmm, inc=(s_pe, 1))
                wo_done[t] = gn

                tmpX = [tmp1, tmp2][m % 2]
                wa1 = [(s_pe, gn)]
                keyp = (t, m - 2) if m >= 2 else (t - 1, C - 2 + m)
                if keyp in a2_hist:
                    wa1.append((s_act, a2_hist[keyp]))
                a1_hist[(t, m)] = p.emit(
                    "vector", lambda v, m=m, bank=bank, tmpX=tmpX:
                    v.tensor_tensor(tmpX[:, :], bank[:, 0:TOK],
                                    uoc[:, m * TOK:(m + 1) * TOK], op=ALU.add),
                    waits=wa1, inc=(s_dve, 1))

                wa2 = [(s_dve, a1_hist[(t, m)])]
                if m == 0 and (t - 1) in spill_done:
                    wa2.append((s_st, spill_done[t - 1]))
                a2_hist[(t, m)] = p.emit(
                    "scalar", lambda sc, m=m, tmpX=tmpX: sc.activation(
                        hv[:, m * TOK:(m + 1) * TOK], tmpX[:, :], AF.Tanh,
                        bias=bias_sb["bo"][:, m:m + 1], scale=1.0),
                    waits=wa2, inc=(s_act, 1))

            for m in range(C):
                wv = [(s_act, a2_hist[(t, m)])]
                if m == 0:
                    wv.append((s_lf, x1_loaded[t]))
                n = p.emit("vector", lambda v, m=m, t=t: v.tensor_tensor(
                    hv[:, m * TOK:(m + 1) * TOK], hv[:, m * TOK:(m + 1) * TOK],
                    x1f[t % 2][:, m * TOK:(m + 1) * TOK], op=ALU.add),
                    waits=wv, inc=(s_dve, 1))
            vadds_last[t] = n

            cp1 = p.emit("vector", lambda v: v.tensor_copy(vb[:, :], hv[:, :]),
                         inc=(s_dve, 1))
            cp2 = p.emit("vector", lambda v: v.tensor_tensor(
                sqb[:, :], vb[:, :], vb[:, :], op=ALU.mult), inc=(s_dve, 1))

            def mk_sums(te):
                inst = None
                for c in range(C):
                    inst = te.matmul(psS[:, 0:TOK], ones128[:, :],
                                     vb[:, c * TOK:(c + 1) * TOK],
                                     start=(c == 0), stop=(c == C - 1))
                for c in range(C):
                    inst = te.matmul(psS[:, TOK:2 * TOK], ones128[:, :],
                                     sqb[:, c * TOK:(c + 1) * TOK],
                                     start=(c == 0), stop=(c == C - 1))
                return inst
            wsum = [(s_dve, cp2), (s_ld, consts_ready[1])]
            if t - 1 in e2_hist:
                wsum.append((s_dve, e2_hist[t - 1]))
            sums_n = p.emit("tensor", mk_sums, waits=wsum, inc=(s_pe, 1))

            we1 = [(s_pe, sums_n)]
            if t - 2 in st_stored:
                we1.append((s_st, st_stored[t - 2]))
            p.emit("vector", lambda v, t=t: v.tensor_scalar_mul(
                sst[t % 2][:, 0:TOK], psS[:, 0:TOK], 1.0 / DM),
                waits=we1, inc=(s_dve, 1))
            e2_hist[t] = p.emit("vector", lambda v, t=t: v.tensor_scalar_mul(
                sst[t % 2][:, TOK:2 * TOK], psS[:, TOK:2 * TOK], 1.0 / DM),
                inc=(s_dve, 1))
            st_stored[t] = p.emit("sync", lambda sy, t=t: sy.dma_start(
                out=stats_d[t:t + 1, :], in_=sst[t % 2][:, :]),
                waits=[(s_dve, e2_hist[t])], inc=(s_st, 16))

            def spv(sy, t=t):
                dst = vsp[:, t * TOK:(t + 1) * TOK].rearrange(
                    "(c p) n -> p c n", p=128)
                return sy.dma_start(out=dst, in_=hv[:, :].rearrange(
                    "p (c n) -> p c n", c=C))
            spill_done[t] = p.emit("sync", spv, waits=[(s_dve, vadds_last[t])],
                                   inc=(s_st, 16))

        st_ld = p.emit("sync", lambda sy: sy.dma_start(
            out=stT[:, :], in_=stats_d),
            waits=[(s_st, st_stored[B - 1])], inc=(s_lf, 16))
        rb1 = p.emit("vector", lambda v: v.tensor_tensor(
            stX[:, :], stT[:, 0:TOK], stT[:, 0:TOK], op=ALU.mult),
            waits=[(s_lf, st_ld)], inc=(s_dve, 1))
        rb2 = p.emit("vector", lambda v: v.tensor_tensor(
            stX[:, :], stT[:, TOK:2 * TOK], stX[:, :], op=ALU.subtract),
            inc=(s_dve, 1))
        rb3 = p.emit("scalar", lambda sc: sc.activation(
            stY[:, :], stX[:, :], AF.Ln, bias=epst[0:B, :], scale=1.0),
            waits=[(s_dve, rb2)], inc=(s_act, 1))
        rb4 = p.emit("scalar", lambda sc: sc.activation(
            stX[:, :], stY[:, :], AF.Exp, bias=0.0, scale=-0.5), inc=(s_act, 1))
        rb5 = p.emit("vector", lambda v: v.tensor_copy(
            stO[:, 0:TOK], stX[:, :]), waits=[(s_act, rb4)], inc=(s_dve, 1))
        rb6 = p.emit("vector", lambda v: v.tensor_tensor(
            stY[:, :], stT[:, 0:TOK], stX[:, :], op=ALU.mult), inc=(s_dve, 1))
        rb7 = p.emit("vector", lambda v: v.tensor_scalar(
            stO[:, TOK:2 * TOK], stY[:, :], -1.0, None, op0=ALU.mult),
            inc=(s_dve, 1))
        stO_st = p.emit("sync", lambda sy: sy.dma_start(
            out=stats_o, in_=stO[:, :]), waits=[(s_dve, rb7)], inc=(s_st, 16))

        out_stored = {}
        appl_done = {}
        bc_hist = {}
        vld = {}
        for t in range(B):
            wldv = ([(s_dve, appl_done[t - 2])] if t - 2 in appl_done
                    else [(s_dve, vadds_last[B - 2 + t])])

            def ldv(g, t=t):
                src = vsp[:, t * TOK:(t + 1) * TOK].rearrange(
                    "(c p) n -> p c n", p=128)
                return g.dma_start(out=x1f[t % 2][:, :].rearrange(
                    "p (c n) -> p c n", c=C), in_=src)
            wldv.append((s_st, spill_done[t]))
            vld[t] = p.emit("gpsimd", ldv, waits=wldv, inc=(s_ld, 16))

            wsi = [(s_st, stO_st)]
            if t - 2 in bc_hist:
                wsi.append((s_pe, bc_hist[t - 2]))
            sld = p.emit("sync", lambda sy, t=t: sy.dma_start(
                out=stin[t % 2][:, :], in_=stats_o[t:t + 1, :]),
                waits=wsi, inc=(s_lf, 16))

            def mk_bc(te, t=t):
                te.matmul(psB[:, 0:TOK], ones1F[:, :],
                          stin[t % 2][:, 0:TOK], start=True, stop=True)
                return te.matmul(psB[:, TOK:2 * TOK], ones1F[:, :],
                                 stin[t % 2][:, TOK:2 * TOK],
                                 start=True, stop=True)
            wbc = [(s_lf, sld)]
            if t - 1 in appl_done:
                wbc.append((s_dve, appl_done[t - 1]))
            bc_n = p.emit("tensor", mk_bc, waits=wbc, inc=(s_pe, 1))
            bc_hist[t] = bc_n

            for m in range(C):
                w1 = [(s_pe, bc_n), (s_ld, vld[t])] if m == 0 else []
                if m == 0 and t - 2 in out_stored:
                    w1.append((s_st, out_stored[t - 2]))
                p.emit("vector", lambda v, m=m, t=t: v.tensor_tensor(
                    outsb[t % 2][:, m * TOK:(m + 1) * TOK],
                    x1f[t % 2][:, m * TOK:(m + 1) * TOK], psB[:, 0:TOK],
                    op=ALU.mult), waits=w1, inc=(s_dve, 1))
                appl_done[t] = p.emit("vector", lambda v, m=m, t=t:
                                      v.tensor_tensor(
                    outsb[t % 2][:, m * TOK:(m + 1) * TOK],
                    outsb[t % 2][:, m * TOK:(m + 1) * TOK], psB[:, TOK:2 * TOK],
                    op=ALU.add), inc=(s_dve, 1))

            def stout(sy, t=t):
                dst = out_h[:, t * TOK:(t + 1) * TOK].rearrange(
                    "(c p) n -> p c n", p=128)
                return sy.dma_start(out=dst, in_=outsb[t % 2][:, :].rearrange(
                    "p (c n) -> p c n", c=C))
            out_stored[t] = p.emit("sync", stout, waits=[(s_dve, appl_done[t])],
                                   inc=(s_st, 16))

        p.emit("sync", lambda sy: None, waits=[(s_st, s_st.n)])

        for s in p.sems.values():
            s.h = es.enter_context(nc.semaphore(s.name))
        with nc.Block() as block:
            @block.tensor
            def _(te):
                p.build_engine(te, "tensor")

            @block.scalar
            def _(sc):
                p.build_engine(sc, "scalar")

            @block.vector
            def _(v):
                p.build_engine(v, "vector")

            @block.gpsimd
            def _(g):
                p.build_engine(g, "gpsimd")

            @block.sync
            def _(sy):
                p.build_engine(sy, "sync")

    return nc


def build_phase_a(ctx):
    nc, p, es = ctx["nc"], ctx["p"], ctx["es"]
    T, PS = ctx["T"], ctx["PS"]
    xT, whbm, slab, gath = ctx["xT"], ctx["whbm"], ctx["slab"], ctx["gath"]
    ones1, ones128, epst = ctx["ones1"], ctx["ones128"], ctx["epst"]
    ones1F = ctx["ones1F"]
    s_ld, s_lf, s_st = ctx["s_ld"], ctx["s_lf"], ctx["s_st"]
    s_pe, s_act, s_dve, s_cc = ctx["s_pe"], ctx["s_act"], ctx["s_dve"], ctx["s_cc"]

    aes = contextlib.ExitStack()
    ctx["aes"] = aes
    TA = lambda *a, **k: aes.enter_context(nc.sbuf_tensor(*a, **k))
    PA = lambda *a, **k: aes.enter_context(nc.psum_tensor(*a, **k))

    arena1 = TA("arena1", [128, C * S], BF16)
    xbf = arena1[:, :]
    aT = arena1
    _war = TA("warena", [128, C * C * 128], BF16)
    warena = [_war, _war]
    qT = TA("qT", [128, C * S], BF16)
    kT = TA("kT", [128, C * S], BF16)
    vP = TA("vP", [128, 16 * (H * 65)], BF16)
    exp_sb = [TA(f"exp{i}", [128, 1024], BF16) for i in range(3)]
    vtmpf = TA("vtmpf", [128, 512], F32)
    rb_row = TA("rb_row", [1, 1024], BF16)
    lnzrow = TA("lnzrow", [1, 1024], F32)
    rbB = TA("rbB", [64, 1024], BF16)
    x1sb = TA("x1sb", [128, C * 512], F32)
    xres = TA("xres", [128, C * 512], F32)
    xpb = arena1[:, 8192:8192 + 4096]
    sqA = arena1[:, 8192 + 4096:16384]
    lnS = TA("lnS", [1, 512], F32)
    lnS2 = TA("lnS2", [1, 512], F32)
    lnT = TA("lnT", [1, 512], F32)
    lnU = TA("lnU", [1, 512], F32)
    lnB = TA("lnB", [1, 1024], F32)
    neg3 = TA("neg3", [128, 1], F32)
    identF = TA("identF", [128, 128], F32)

    psSC = [PA(f"psSC{i}", [128, 1024], F32) for i in range(2)]
    psU = [PA(f"psU{i}", [128, 1024], F32) for i in range(2)]

    SLOT = [(psSC[0], 0), (psSC[0], 512), (psSC[1], 0), (psSC[1], 512)]
    slot_evac = [0, 0, 0, 0]
    slot_mm = [0, 0, 0, 0]

    def slot_ap(u, p0=0, p1=128, w=512):
        t_, off = SLOT[u % 4]
        return t_[p0:p1, off:off + w]

    p.emit("gpsimd", lambda g: g.memset(neg3[:, :], -3.0), inc=(s_ld, 1))
    p.emit("gpsimd", lambda g: g.memset(vP[:, :], 1.0), inc=(s_ld, 1))
    p.emit("gpsimd", lambda g: g.memset(identF[:, :], 0.0))
    p.emit("gpsimd", lambda g: g.affine_select(
        out=identF[:, :], in_=identF[:, :], compare_op=ALU.not_equal,
        fill=1.0, base=0, pattern=[[-1, 128]], channel_multiplier=1),
        inc=(s_ld, 1))
    aconsts = (s_ld, s_ld.n)

    PW = [("Wq", qT), ("Wk", kT), ("Wv", None), ("Wpost", None)]
    su = 0
    xbf_loaded = {}
    wload = {}
    wload_mm = {}
    aT_free = {-1: 0}
    smn_last = {}
    store_cnt = 0
    exp_hist = {}
    av_hist = {}
    tt_hist = {}
    gelu_hist = {}

    for b in range(BLOC):
        wx = [(s_pe, smn_last[b - 1])] if b - 1 in smn_last else []

        def ldx(sy, b=b):
            src = ctx["xbf_d"][:, b * S:(b + 1) * S].rearrange(
                "(c p) n -> p c n", p=128)
            return sy.dma_start(out=arena1[:, :].rearrange(
                "p (c n) -> p c n", c=C), in_=src)
        wx = wx + [(s_ld, ctx["xcast"])]
        xbf_loaded[b] = p.emit("sync", ldx, waits=wx, inc=(s_lf, 16))

        def emit_wload(pi, b=b):
            wwl = []
            prev = (b, pi - 1) if pi >= 1 else (b - 1, 3)
            if prev in wload_mm:
                wwl = [(s_pe, wload_mm[prev])]
            wn = PW[pi][0]
            wload[(b, pi)] = p.emit("sync", lambda sy, wn=wn, pi=pi: sy.dma_start(
                out=warena[pi % 2][:, :], in_=ctx["wbf16_d"][wn]),
                waits=wwl + [(s_ld, ctx["wcast"][wn])], inc=(s_lf, 16))
        emit_wload(0)

        for pi in range(3):
            wn, dstT = PW[pi][0], PW[pi][1]
            for m in range(C):
                wmm = [(s_lf, max(wload[(b, pi)], xbf_loaded[b]))]
                for u in range(4):
                    if slot_evac[u] > 0:
                        wmm.append((s_dve, slot_evac[u]))

                def mk_p(te, pi=pi, m=m, b=b):
                    inst = None
                    for k in range(C):
                        lhs = wslice(warena[pi % 2], k, m)
                        for sblk in range(4):
                            inst = te.matmul(
                                slot_ap(sblk), lhs,
                                xbf[:, k * S + sblk * 512:k * S + sblk * 512 + 512],
                                start=(k == 0), stop=(k == C - 1))
                    return inst
                gn = p.emit("tensor", mk_p, waits=wmm, inc=(s_pe, 1))
                wload_mm[(b, pi)] = gn

                if pi < 2:
                    for sblk in range(4):
                        def ev(v, dstT=dstT, m=m, sblk=sblk):
                            return v.tensor_scalar_mul(
                                dstT[:, m * S + sblk * 512:m * S + sblk * 512 + 512],
                                slot_ap(sblk), 1.0)
                        slot_evac[sblk] = p.emit("vector", ev, waits=[(s_pe, gn)],
                                                 inc=(s_dve, 1))
                else:
                    for sblk in range(4):
                        def evv(v, sblk=sblk):
                            return v.tensor_scalar_mul(vtmpf[:, :],
                                                       slot_ap(sblk), 1.0)
                        wev = [(s_pe, gn)]
                        if (b, m, sblk) != (0, 0, 0):
                            wev.append((s_pe, trn))
                        evn = p.emit("vector", evv, waits=wev, inc=(s_dve, 1))
                        slot_evac[sblk] = evn
                        for ti in range(4):
                            tc = sblk * 4 + ti

                            def mktr(te, ti=ti):
                                return te.transpose(
                                    psU[0][0:128, 0:128],
                                    vtmpf[:, ti * 128:(ti + 1) * 128],
                                    identF[:, :])
                            wtr = [(s_dve, evn), (s_ld, aconsts[1])]
                            if (b, m, sblk, ti) != (0, 0, 0, 0):
                                wtr.append((s_dve, vp_ev))
                            trn = p.emit("tensor", mktr, waits=wtr, inc=(s_pe, 1))

                            def evp(v, tc=tc, m=m):
                                base = tc * H * 65 + (2 * m) * 65
                                dst = vP[:, base:base + 130].rearrange(
                                    "p (h e) -> p h e", h=2)[:, :, 0:64]
                                srcv = psU[0][0:128, 0:128].rearrange(
                                    "p (h e) -> p h e", h=2)
                                return v.tensor_copy(dst, srcv)
                            vp_ev = p.emit("vector", evp, waits=[(s_pe, trn)],
                                           inc=(s_dve, 1))
            emit_wload(pi + 1)
        vp_done = vp_ev

        for sh in range(2):
            expz_hist = {}
            bc_hist = {}

            def emit_scores(h, tc):
                bank = psSC[(tc + h) % 2]
                wsc = []
                kexp = (b, sh, h, tc - 2) if tc >= 2 else \
                       (b, sh, h - 1, 15 - tc) if h >= 1 else \
                       (b, sh - 1, 15, 15 - tc) if sh >= 1 else \
                       (b - 1, 1, 15, 15 - tc)
                if kexp in exp_hist:
                    wsc.append((s_act, exp_hist[kexp]))
                if h == 0 and tc == 0:
                    wsc.append((s_dve, vp_done))
                    for u in range(4):
                        wsc.append((s_dve, slot_evac[u]))

                def mksc(te, h=h, tc=tc, bank=bank):
                    inst = None
                    lhs = kT[(h % 2) * 64:(h % 2) * 64 + 64,
                             (h // 2) * S + tc * 128:(h // 2) * S + tc * 128 + 128]
                    for si in range(2):
                        rhs = qT[(h % 2) * 64:(h % 2) * 64 + 64,
                                 (h // 2) * S + sh * 1024 + si * 512:
                                 (h // 2) * S + sh * 1024 + si * 512 + 512]
                        inst = te.matmul(bank[:, si * 512:si * 512 + 512],
                                         lhs, rhs, start=True, stop=True)
                    return inst
                return p.emit("tensor", mksc, waits=wsc, inc=(s_pe, 1))

            def emit_exp(h, tc, scn):
                bank = psSC[(tc + h) % 2]
                wex = [(s_pe, scn)]
                tcp = 15 - ((15 - tc) % 3)
                kav = (b, sh, h, tc - 3) if tc >= 3 else \
                      (b, sh, h - 1, tcp) if h >= 1 else \
                      (b, sh - 1, 15, tcp) if sh >= 1 else \
                      (b - 1, 1, 15, tcp)
                if kav in av_hist:
                    wex.append((s_pe, av_hist[kav]))

                def mkex(sc, tc=tc, bank=bank):
                    return sc.activation(exp_sb[tc % 3][:, :], bank[:, :],
                                         AF.Exp, bias=neg3[:, :], scale=0.125)
                exp_hist[(b, sh, h, tc)] = p.emit("scalar", mkex, waits=wex,
                                                  inc=(s_act, 1))

            def emit_av(h, tc):
                wav = [(s_act, exp_hist[(b, sh, h, tc)])]
                if tc == 0:
                    ktt = (b, sh, h - 2) if h >= 2 else \
                          (b, sh - 1, 14 + h) if sh >= 1 else \
                          (b - 1, 1, 14 + h)
                    if ktt in tt_hist:
                        wav.append((s_dve, tt_hist[ktt]))

                def mkav(te, h=h, tc=tc):
                    inst = None
                    lhs = vP[:, tc * H * 65 + h * 65:tc * H * 65 + h * 65 + 65]
                    for si in range(2):
                        inst = te.matmul(
                            psU[h % 2][0:65, si * 512:si * 512 + 512],
                            lhs, exp_sb[tc % 3][:, si * 512:si * 512 + 512],
                            start=(tc == 0), stop=(tc == 15))
                    return inst
                av_hist[(b, sh, h, tc)] = p.emit("tensor", mkav, waits=wav,
                                                 inc=(s_pe, 1))

            def emit_tail_a(h):
                ln_n = p.emit("scalar", lambda sc, h=h: sc.activation(
                    lnzrow[:, :], psU[h % 2][64:65, :], AF.Ln),
                    waits=[(s_pe, av_hist[(b, sh, h, 15)])], inc=(s_act, 1))

                def mkexpz(sc, h=h):
                    with nc.allow_low_precision(
                            reason="softmax 1/Z stored bf16; product is "
                                   "rounded to bf16 anyway; diluted by residual"):
                        return sc.activation(rb_row[:, :], lnzrow[:, :],
                                             AF.Exp, scale=-1.0)
                wz = []
                if h - 2 in bc_hist:
                    wz.append((s_pe, bc_hist[h - 2]))
                expz_hist[h] = p.emit("scalar", mkexpz, waits=wz, inc=(s_act, 1))

            def emit_tail_b(h):
                def mkbc(te, h=h):
                    inst = None
                    for si in range(2):
                        inst = te.matmul(
                            psU[h % 2][64:128, si * 512:si * 512 + 512],
                            ones1[0:1, 0:64],
                            rb_row[:, si * 512:si * 512 + 512],
                            start=True, stop=True)
                    return inst
                bc_hist[h] = p.emit("tensor", mkbc, waits=[(s_act, expz_hist[h])],
                                    inc=(s_pe, 1))
                ebn = p.emit("vector", lambda v, h=h: v.tensor_scalar_mul(
                    rbB[:, :], psU[h % 2][64:128, :], 1.0),
                    waits=[(s_pe, bc_hist[h])], inc=(s_dve, 1))

                def mktt(v, h=h):
                    dst = arena1[(h % 2) * 64:(h % 2) * 64 + 64,
                                 (h // 2) * 1024:(h // 2) * 1024 + 1024]
                    return v.tensor_tensor(dst, psU[h % 2][0:64, :], rbB[:, :],
                                           op=ALU.mult)
                tt_hist[(b, sh, h)] = p.emit("vector", mktt, inc=(s_dve, 1))

            for h in range(16):
                for tc in range(16):
                    scn = emit_scores(h, tc)
                    emit_exp(h, tc, scn)
                    if tc >= 1:
                        emit_av(h, tc - 1)
                    if h >= 1 and tc == 2:
                        emit_tail_a(h - 1)
                    if h >= 1 and tc == 8:
                        emit_tail_b(h - 1)
                emit_av(h, 15)
            emit_tail_a(15)
            emit_tail_b(15)

            def mkgl(sc):
                return sc.activation(arena1[:, 0:C * 1024], arena1[:, 0:C * 1024],
                                     AF.Gelu)
            gelu_hist[(b, sh)] = p.emit("scalar", mkgl,
                                        waits=[(s_dve, tt_hist[(b, sh, 15)])],
                                        inc=(s_act, 1))

            for sb2 in range(2):
                sblk = sh * 2 + sb2

                def ldxr(sy, b=b, sblk=sblk):
                    src = xT[:, b * S + sblk * 512:b * S + sblk * 512 + 512]
                    src = src.rearrange("(c p) n -> p c n", p=128)
                    return sy.dma_start(out=xres[:, :].rearrange(
                        "p (c n) -> p c n", c=C), in_=src)
                wxr = [(s_dve, xres_used)] if b + sblk > 0 else []
                xrn = p.emit("sync", ldxr, waits=wxr, inc=(s_lf, 16))

                stt_hist = {}
                for m in range(C):
                    u = su % 4
                    wmm = [(s_act, gelu_hist[(b, sh)])]
                    if slot_evac[u] > 0:
                        wmm.append((s_dve, slot_evac[u]))

                    def mkwp(te, m=m, sb2=sb2, u=u):
                        inst = None
                        for k in range(C):
                            inst = te.matmul(
                                slot_ap(u), wslice(warena[1], k, m),
                                arena1[:, k * 1024 + sb2 * 512:
                                       k * 1024 + sb2 * 512 + 512],
                                start=(k == 0), stop=(k == C - 1))
                        return inst
                    gn = p.emit("tensor", mkwp, waits=wmm, inc=(s_pe, 1))
                    slot_mm[u] = gn
                    aT_free[b] = gn
                    wload_mm[(b, 3)] = gn

                    def mkstt(v, m=m, u=u):
                        return v.scalar_tensor_tensor(
                            x1sb[:, m * 512:(m + 1) * 512], xres[:, m * 512:(m + 1) * 512],
                            2.0, slot_ap(u), op0=ALU.mult, op1=ALU.add)
                    ws = [(s_pe, gn)]
                    if m == 0:
                        ws.append((s_lf, xrn))
                    slot_evac[u] = p.emit("vector", mkstt, waits=ws, inc=(s_dve, 1))
                    stt_hist[m] = slot_evac[u]
                    su += 1

                cpx = p.emit("vector", lambda v: v.tensor_copy(
                    xpb, x1sb[:, :]), waits=[(s_dve, stt_hist[C - 1])],
                    inc=(s_dve, 1))
                cps = p.emit("vector", lambda v: v.tensor_tensor(
                    sqA, xpb, xpb, op=ALU.mult), inc=(s_dve, 1))

                us_, uq = su % 4, (su + 1) % 4
                su += 2

                def mksums(te, us_=us_, uq=uq):
                    inst = None
                    for k in range(C):
                        inst = te.matmul(slot_ap(us_, 0, 1), ones128[:, :],
                                         xpb[:, k * 512:(k + 1) * 512],
                                         start=(k == 0), stop=(k == C - 1))
                    for k in range(C):
                        inst = te.matmul(slot_ap(uq, 0, 1), ones128[:, :],
                                         sqA[:, k * 512:(k + 1) * 512],
                                         start=(k == 0), stop=(k == C - 1))
                    return inst
                wsm = [(s_dve, cps)]
                if slot_evac[us_] > 0:
                    wsm.append((s_dve, slot_evac[us_]))
                if slot_evac[uq] > 0:
                    wsm.append((s_dve, slot_evac[uq]))
                smn = p.emit("tensor", mksums, waits=wsm, inc=(s_pe, 1))
                smn_last[b] = smn

                e1 = p.emit("vector", lambda v, us_=us_: v.tensor_scalar_mul(
                    lnS[:, :], slot_ap(us_, 0, 1), 1.0 / DM),
                    waits=[(s_pe, smn)], inc=(s_dve, 1))
                e2 = p.emit("vector", lambda v, uq=uq: v.tensor_scalar_mul(
                    lnS2[:, :], slot_ap(uq, 0, 1), 1.0 / DM), inc=(s_dve, 1))
                slot_evac[us_] = e1
                slot_evac[uq] = e2
                l1 = p.emit("vector", lambda v: v.tensor_tensor(
                    lnT[:, :], lnS[:, :], lnS[:, :], op=ALU.mult), inc=(s_dve, 1))
                l2 = p.emit("vector", lambda v: v.tensor_tensor(
                    lnT[:, :], lnS2[:, :], lnT[:, :], op=ALU.subtract),
                    inc=(s_dve, 1))
                l3 = p.emit("scalar", lambda sc: sc.activation(
                    lnU[:, :], lnT[:, :], AF.Ln, bias=epst[0:1, :], scale=1.0),
                    waits=[(s_dve, l2)], inc=(s_act, 1))
                l4 = p.emit("scalar", lambda sc: sc.activation(
                    lnT[:, :], lnU[:, :], AF.Exp, bias=0.0, scale=-0.5),
                    inc=(s_act, 1))
                l5 = p.emit("vector", lambda v: v.tensor_copy(
                    lnB[:, 0:512], lnT[:, :]), waits=[(s_act, l4)], inc=(s_dve, 1))
                l6 = p.emit("vector", lambda v: v.tensor_tensor(
                    lnU[:, :], lnS[:, :], lnT[:, :], op=ALU.mult), inc=(s_dve, 1))
                l7 = p.emit("vector", lambda v: v.tensor_scalar(
                    lnB[:, 512:1024], lnU[:, :], -1.0, None, op0=ALU.mult),
                    inc=(s_dve, 1))

                ur, un = su % 4, (su + 1) % 4
                su += 2

                def mkbc2(te, ur=ur, un=un):
                    te.matmul(slot_ap(ur), ones1F[0:1, :], lnB[0:1, 0:512],
                              start=True, stop=True)
                    return te.matmul(slot_ap(un), ones1F[0:1, :], lnB[0:1, 512:1024],
                                     start=True, stop=True)
                wb2 = [(s_dve, l7)]
                if slot_evac[ur] > 0:
                    wb2.append((s_dve, slot_evac[ur]))
                if slot_evac[un] > 0:
                    wb2.append((s_dve, slot_evac[un]))
                b2n = p.emit("tensor", mkbc2, waits=wb2, inc=(s_pe, 1))

                ap_hist = {}
                for m in range(C):
                    w1 = [(s_pe, b2n)] if m == 0 else []
                    p.emit("vector", lambda v, m=m, ur=ur: v.tensor_tensor(
                        xres[:, m * 512:(m + 1) * 512],
                        x1sb[:, m * 512:(m + 1) * 512], slot_ap(ur),
                        op=ALU.mult), waits=w1, inc=(s_dve, 1))
                    ap_hist[m] = p.emit("vector", lambda v, m=m, un=un:
                                        v.tensor_tensor(
                        x1sb[:, m * 512:(m + 1) * 512],
                        xres[:, m * 512:(m + 1) * 512], slot_ap(un),
                        op=ALU.add), inc=(s_dve, 1))
                xres_used = ap_hist[C - 1]
                slot_evac[ur] = ap_hist[C - 1]
                slot_evac[un] = ap_hist[C - 1]

                def stx1(sy, b=b, sblk=sblk):
                    dst = slab[b][:, sblk * 512:sblk * 512 + 512].rearrange(
                        "(c p) n -> p c n", p=128)
                    return sy.dma_start(out=dst, in_=x1sb[:, :].rearrange(
                        "p (c n) -> p c n", c=C))
                p.emit("sync", stx1, waits=[(s_dve, ap_hist[C - 1])],
                       inc=(s_st, 16))

        p.emit("gpsimd", lambda g, b=b: g.collective_compute(
            "AllGather", ALU.bypass, replica_groups=[list(range(NCORES))],
            ins=[slab[b]], outs=[gath[b]]),
            waits=[(s_st, p.sems["s_st"].n)], inc=(s_cc, 1))

    aes.close()
    return (ctx["p"].sems["s_st"].n)



def weight_maps(inputs):
    return {
        "wfT": prechunk(np.asarray(inputs["L0_ffn_wf"], np.float32).T),
        "ufT": prechunk(np.asarray(inputs["L0_ffn_uf"], np.float32).T),
        "wiT": prechunk(np.asarray(inputs["L0_ffn_wi"], np.float32).T),
        "uiT": prechunk(np.asarray(inputs["L0_ffn_ui"], np.float32).T),
        "wcT": prechunk(np.asarray(inputs["L0_ffn_wc"], np.float32).T),
        "ucT": prechunk(np.asarray(inputs["L0_ffn_uc"], np.float32).T),
        "woT": prechunk(np.asarray(inputs["L0_ffn_wo"], np.float32).T),
        "uoT": prechunk(np.asarray(inputs["L0_ffn_uo"], np.float32).T),
        "bf": pack_bias(inputs["L0_ffn_bf"]), "bi": pack_bias(inputs["L0_ffn_bi"]),
        "bc": pack_bias(inputs["L0_ffn_bc"]), "bo": pack_bias(inputs["L0_ffn_bo"]),
        "inc0": pack_bias(inputs["L0_ffn_in_c"]),
    }


def assemble_out(results):
    full = np.empty((B, S, DM), np.float32)
    for k in range(NCORES):
        ok = np.asarray(results[k]["out"]).reshape(DM, B, TOK)
        full[:, k * TOK:(k + 1) * TOK, :] = ok.transpose(1, 2, 0)
    return full



def kernel(**inputs):
    inputs = {k: np.asarray(v, np.float32) for k, v in inputs.items()}
    base = weight_maps(inputs)
    base["Wq"] = prechunk(inputs["L0_attn_Wq"])
    base["Wk"] = prechunk(inputs["L0_attn_Wk"])
    base["Wv"] = prechunk(inputs["L0_attn_Wv"])
    base["Wpost"] = prechunk(inputs["L0_attn_Wpost"])
    X = inputs["L0_attn_X"]
    in_maps = []
    for k in range(NCORES):
        m = dict(base)
        m["xT"] = np.ascontiguousarray(
            X[[k, 8 + k, 16 + k, 24 + k]].transpose(2, 0, 1).reshape(DM, BLOC * S))
        in_maps.append(m)
    nc = build_program(mode="full")
    res = run_bass_kernel_spmd(nc, in_maps, core_ids=list(range(NCORES)))
    return assemble_out(res.results)


# version 05
# speedup vs baseline: 1.0587x; 1.0587x over previous
import contextlib
import numpy as np
import concourse.bass as bass
import concourse.mybir as mybir
from concourse.bass_utils import run_bass_kernel_spmd

F32 = mybir.dt.float32
BF16 = mybir.dt.bfloat16
AF = mybir.ActivationFunctionType
ALU = mybir.AluOpType

NCORES = 8
DM = 1024
C = DM // 128
B = 32
S = 2048
TOK = S // NCORES
BLOC = B // NCORES
H = 16
DK = 64
LNEPS = 1e-5


class Sem:
    def __init__(self, name):
        self.name = name
        self.n = 0
        self.h = None

    def inc(self, amt=1):
        self.n += amt
        return self.n


class Prog:

    def __init__(self):
        self.ops = {e: [] for e in ("tensor", "scalar", "vector", "gpsimd", "sync")}
        self.sems = {}

    def sem(self, name):
        if name not in self.sems:
            self.sems[name] = Sem(name)
        return self.sems[name]

    def emit(self, engine, fn, waits=(), inc=None):
        n = None
        if inc is not None:
            n = inc[0].inc(inc[1])
        self.ops[engine].append((tuple(waits), fn, inc))
        return n

    def build_engine(self, eng, name):
        for waits, fn, inc in self.ops[name]:
            for s, v in waits:
                if v > 0:
                    eng.wait_ge(s.h, v)
            inst = fn(eng)
            if inc is not None:
                assert inst is not None, f"inc on None inst in {name}"
                inst.then_inc(inc[0].h, inc[1])



def prechunk(w2):
    return np.ascontiguousarray(
        w2.reshape(C, 128, C, 128).transpose(1, 0, 2, 3).reshape(128, C * C * 128)
    )


def pack_bias(b):
    return np.ascontiguousarray(np.asarray(b, np.float32).reshape(C, 128).T)


def wslice(wsb, kc, mc):
    o = (kc * C + mc) * 128
    return wsb[:, o:o + 128]



def build_program(mode="full"):
    nc = bass.Bass("TRN2", target_bir_lowering=False)
    p = Prog()

    gath_kind = "ExternalInput" if mode == "bc" else "Internal"
    gath_kw = {} if mode == "bc" else {"addr_space": "Shared"}
    gath = [nc.dram_tensor(f"gath{j}", [NCORES * DM, S], F32, kind=gath_kind,
                           **gath_kw).ap()
            for j in range(BLOC)]
    out_h = nc.dram_tensor("out", [DM, B * TOK], F32, kind="ExternalOutput").ap()
    vsp = nc.dram_tensor("vsp", [DM, B * TOK], F32).ap()
    stats_d = nc.dram_tensor("stats_d", [B, 2 * TOK], F32).ap()
    stats_o = nc.dram_tensor("stats_o", [B, 2 * TOK], F32).ap()

    wnames_bc = ["wfT", "ufT", "wiT", "uiT", "wcT", "ucT", "woT", "uoT"]
    whbm = {n: nc.dram_tensor(n, [128, C * C * 128], F32, kind="ExternalInput").ap()
            for n in wnames_bc}
    bnames = ["bf", "bi", "bc", "bo", "inc0"]
    bhbm = {n: nc.dram_tensor(n, [128, C], F32, kind="ExternalInput").ap()
            for n in bnames}

    if mode == "full":
        xT = nc.dram_tensor("xT", [DM, BLOC * S], F32, kind="ExternalInput").ap()
        for n in ["Wq", "Wk", "Wv", "Wpost"]:
            whbm[n] = nc.dram_tensor(n, [128, C * C * 128], F32,
                                     kind="ExternalInput").ap()
        slab = [nc.dram_tensor(f"slab{j}", [DM, S], F32).ap() for j in range(BLOC)]
    else:
        slab = None
    wbf16_d = {n: nc.dram_tensor(n + "_bf", [128, C * C * 128], BF16).ap()
               for n in whbm}
    xbf_d = (nc.dram_tensor("xT_bf", [DM, BLOC * S], BF16).ap()
             if mode == "full" else None)

    es = contextlib.ExitStack()
    with es:
        T = lambda *a, **k: es.enter_context(nc.sbuf_tensor(*a, **k))
        PS = lambda *a, **k: es.enter_context(nc.psum_tensor(*a, **k))

        ones1 = T("ones1", [1, 128], BF16)
        ones1F = T("ones1F", [1, 128], F32)
        ones128 = T("ones128", [128, 1], BF16)
        epst = T("epst", [128, 1], F32)
        bias_sb = {n: T(f"sb_{n}", [128, C], F32) for n in bnames}

        s_ld = p.sem("s_ld")
        s_lf = p.sem("s_lf")
        s_st = p.sem("s_st")
        s_pe = p.sem("s_pe")
        s_act = p.sem("s_act")
        s_dve = p.sem("s_dve")
        s_cc = p.sem("s_cc")

        dyn = {}

        def ginit(g):
            dyn["g_off"] = g.snap(g.partition_id() * TOK)
            return None
        p.emit("gpsimd", ginit)

        def syinit(sy):
            dyn["s_off"] = sy.snap(sy.partition_id() * TOK)
            return None
        p.emit("sync", syinit)

        p.emit("gpsimd", lambda g: g.memset(ones1[:, :], 1.0), inc=(s_ld, 1))
        p.emit("gpsimd", lambda g: g.memset(ones1F[:, :], 1.0), inc=(s_ld, 1))
        p.emit("gpsimd", lambda g: g.memset(ones128[:, :], 1.0), inc=(s_ld, 1))
        p.emit("gpsimd", lambda g: g.memset(epst[:, :], LNEPS), inc=(s_ld, 1))
        consts_ready = (s_ld, s_ld.n)

        for n in bnames:
            p.emit("sync", lambda sy, n=n: sy.dma_start(
                out=bias_sb[n][:, :], in_=bhbm[n]), inc=(s_lf, 16))
        bias_ready = (s_lf, s_lf.n)

        wcast = {}
        for n in whbm:
            wcast[n] = p.emit("gpsimd", lambda g, n=n: g.dma_start(
                out=wbf16_d[n], in_=whbm[n]), inc=(s_ld, 16))
        if mode == "full":
            xcast = p.emit("gpsimd", lambda g: g.dma_start(
                out=xbf_d, in_=xT), inc=(s_ld, 16))


        a_store_end = 0
        if mode == "full":
            actx = dict(nc=nc, p=p, es=es, T=T, PS=PS, xT=xT, whbm=whbm,
                        wbf16_d=wbf16_d, xbf_d=xbf_d, wcast=wcast, xcast=xcast,
                        slab=slab, gath=gath, ones1=ones1, ones128=ones128,
                        ones1F=ones1F,
                        epst=epst, s_ld=s_ld, s_lf=s_lf, s_st=s_st, s_pe=s_pe,
                        s_act=s_act, s_dve=s_dve, s_cc=s_cc)
            a_store_end = build_phase_a(actx)


        w_sb = {n: T(f"w_{n}", [128, C * C * 128], BF16)
                for n in ["wfT", "ufT", "wiT", "uiT", "wcT", "ucT"]}
        w_sb["woT"] = w_sb["wfT"]
        w_sb["uoT"] = w_sb["ufT"]
        c_sb = T("c_sb", [128, C * TOK], F32)
        cbf = T("cbf", [128, C * TOK], BF16)
        xt_sb = [T(f"xt{i}", [128, C * TOK], BF16) for i in range(2)]
        gts = [T(f"gts{i}", [128, 3 * TOK], F32) for i in range(2)]
        tmp1 = T("tmp1", [128, TOK], F32)
        tmp2 = T("tmp2", [128, TOK], F32)
        uoc = T("uoc", [128, C * TOK], F32)
        x1f = [T(f"x1f{i}", [128, C * TOK], F32) for i in range(2)]
        x1b = [T(f"x1b{i}", [128, C * TOK], BF16) for i in range(2)]
        hv = T("hv", [128, C * TOK], F32)
        vb = T("vb", [128, C * TOK], BF16)
        sqb = T("sqb", [128, C * TOK], BF16)
        sst = [T(f"sst{i}", [1, 2 * TOK], F32) for i in range(2)]
        stT = T("stT", [B, 2 * TOK], F32)
        stX = T("stX", [B, TOK], F32)
        stY = T("stY", [B, TOK], F32)
        stO = T("stO", [B, 2 * TOK], F32)
        stin = [T(f"stin{i}", [1, 2 * TOK], F32) for i in range(2)]
        outsb = [T(f"outsb{i}", [128, C * TOK], F32) for i in range(2)]

        psG = [PS(f"psG{i}", [128, 512], F32) for i in range(4)]
        psS = PS("psS", [1, 512], F32)
        psB = PS("psB", [128, 512], F32)

        wA = [(s_st, a_store_end)] if mode == "full" else []
        for n in ["wfT", "ufT", "wiT", "uiT", "wcT", "ucT"]:
            p.emit("sync", lambda sy, n=n: sy.dma_start(
                out=w_sb[n][:, :], in_=wbf16_d[n]),
                waits=wA + [(s_ld, wcast[n])], inc=(s_lf, 16))
        wB_ready = (s_lf, s_lf.n)

        wc0 = [bias_ready] + ([(s_st, a_store_end)] if mode == "full" else [])
        p.emit("vector", lambda v: v.memset(c_sb[:, :], 0.0),
               waits=wc0, inc=(s_dve, 1))
        for c in range(C):
            p.emit("vector", lambda v, c=c: v.tensor_scalar_add(
                c_sb[:, c * TOK:(c + 1) * TOK], c_sb[:, c * TOK:(c + 1) * TOK],
                bias_sb["inc0"][:, c:c + 1]), inc=(s_dve, 1))

        GATES = [("wfT", "ufT", "bf", AF.Sigmoid), ("wiT", "uiT", "bi", AF.Sigmoid),
                 ("wcT", "ucT", "bc", AF.Tanh)]
        pe_group = {}
        act_of_group = {}
        xt_used = {}
        cupd = {}
        cupd[(-1, C - 1)] = s_dve.n
        cupd[(-1, C - 2)] = s_dve.n
        xt_loaded = {}
        ggrp = 0

        for t in range(B):
            j, r = t // NCORES, t % NCORES
            w = [(s_pe, xt_used[t - 2])] if t - 2 in xt_used else []
            if mode == "full":
                w = w + [(s_cc, j + 1), (s_st, a_store_end)]

            def ldxt(g, j=j, r=r, t=t):
                src = gath[j][r * DM:(r + 1) * DM, bass.ds(dyn["g_off"], TOK)]
                src = src.rearrange("(c p) n -> p c n", p=128)
                return g.dma_start(out=xt_sb[t % 2][:, :].rearrange(
                    "p (c n) -> p c n", c=C), in_=src)
            xt_loaded[t] = p.emit("gpsimd", ldxt, waits=w, inc=(s_ld, 16))

            wv = [(s_dve, cupd[(t - 1, C - 1)])]
            if t > 0:
                wv.append((s_pe, pe_group[(t - 1, C - 1, 2)]))
            cbf_n = p.emit("vector", lambda v: v.tensor_copy(cbf[:, :], c_sb[:, :]),
                           waits=wv, inc=(s_dve, 1))

            for m in range(C):
                for gi, (wn, un, bn, af) in enumerate(GATES):
                    bank = psG[ggrp % 4]
                    wmm = []
                    if m == 0 and gi == 0:
                        wmm = [(s_ld, xt_loaded[t]), (s_lf, wB_ready[1]),
                               (s_dve, cbf_n)]
                    if ggrp - 4 >= 0 and (ggrp - 4) in act_of_group:
                        wmm.append((s_act, act_of_group[ggrp - 4]))

                    def mk_mms(te, wn=wn, un=un, m=m, t=t, bank=bank):
                        inst = None
                        for k in range(C):
                            inst = te.matmul(
                                bank[:, 0:TOK], wslice(w_sb[wn], k, m),
                                xt_sb[t % 2][:, k * TOK:(k + 1) * TOK],
                                start=(k == 0), stop=False)
                        for k in range(C):
                            inst = te.matmul(
                                bank[:, 0:TOK], wslice(w_sb[un], k, m),
                                cbf[:, k * TOK:(k + 1) * TOK],
                                start=False, stop=(k == C - 1))
                        return inst
                    pe_group[(t, m, gi)] = p.emit("tensor", mk_mms, waits=wmm,
                                                  inc=(s_pe, 1))

                    wact = [(s_pe, pe_group[(t, m, gi)])]
                    key = (t, m - 2) if m >= 2 else (t - 1, C - 2 + m)
                    if key in cupd:
                        wact.append((s_dve, cupd[key]))

                    def mk_act(sc, af=af, bn=bn, m=m, gi=gi, bank=bank):
                        return sc.activation(
                            gts[m % 2][:, gi * TOK:(gi + 1) * TOK], bank[:, 0:TOK],
                            af, bias=bias_sb[bn][:, m:m + 1], scale=1.0)
                    act_of_group[ggrp] = p.emit("scalar", mk_act, waits=wact,
                                                inc=(s_act, 1))
                    ggrp += 1

                if m in (2, 5):
                    p.emit("tensor", lambda te: te.matmul(
                        psB[:, 0:TOK], wslice(w_sb["wfT"], 0, 0),
                        w_sb["ufT"][:, 0:TOK], start=True, stop=True),
                        waits=[(s_act, s_act.n)], inc=(s_pe, 1))
                wcu = [(s_act, s_act.n)]
                cs_lo, cs_hi = m * TOK, (m + 1) * TOK
                g0 = gts[m % 2]
                p.emit("vector", lambda v, g0=g0, lo=cs_lo, hi=cs_hi:
                       v.tensor_tensor(tmp1[:, :], g0[:, 0:TOK], c_sb[:, lo:hi],
                                       op=ALU.mult), waits=wcu, inc=(s_dve, 1))
                p.emit("vector", lambda v, g0=g0: v.tensor_tensor(
                    tmp2[:, :], g0[:, TOK:2 * TOK], g0[:, 2 * TOK:3 * TOK],
                    op=ALU.mult), inc=(s_dve, 1))
                cupd[(t, m)] = p.emit("vector", lambda v, lo=cs_lo, hi=cs_hi:
                                      v.tensor_tensor(c_sb[:, lo:hi], tmp1[:, :],
                                                      tmp2[:, :], op=ALU.add),
                                      inc=(s_dve, 1))
            xt_used[t] = pe_group[(t, C - 1, 2)]

        for n in ["woT", "uoT"]:
            p.emit("sync", lambda sy, n=n: sy.dma_start(
                out=w_sb[n][:, :], in_=wbf16_d[n]),
                waits=[(s_pe, pe_group[(B - 1, C - 1, 2)]),
                       (s_ld, wcast[n])], inc=(s_lf, 16))
        wC_ready = (s_lf, s_lf.n)

        cbf_fin = p.emit("vector", lambda v: v.tensor_copy(cbf[:, :], c_sb[:, :]),
                         waits=[(s_dve, cupd[(B - 1, C - 1)]),
                                (s_pe, pe_group[(B - 1, C - 1, 2)])],
                         inc=(s_dve, 1))

        uoc_done = {}
        for m in range(C):
            bank = psG[m % 2]

            def mk_uoc(te, m=m, bank=bank):
                inst = None
                for k in range(C):
                    inst = te.matmul(bank[:, 0:TOK], wslice(w_sb["uoT"], k, m),
                                     cbf[:, k * TOK:(k + 1) * TOK],
                                     start=(k == 0), stop=(k == C - 1))
                return inst
            wmm = [(s_dve, cbf_fin), (s_lf, wC_ready[1])]
            if m >= 2:
                wmm.append((s_dve, uoc_done[m - 2]))
            gn = p.emit("tensor", mk_uoc, waits=wmm, inc=(s_pe, 1))
            uoc_done[m] = p.emit("vector", lambda v, m=m, bank=bank:
                                 v.tensor_scalar_mul(
                                     uoc[:, m * TOK:(m + 1) * TOK],
                                     bank[:, 0:TOK], 1.0),
                                 waits=[(s_pe, gn)], inc=(s_dve, 1))

        x1_loaded = {}
        x1b_loaded = {}
        wo_done = {}
        a1_hist = {}
        a2_hist = {}
        vadds_last = {}
        spill_done = {}
        e2_hist = {}
        st_stored = {}
        for t in range(B):
            j, r = t // NCORES, t % NCORES
            wld = [(s_dve, vadds_last[t - 2])] if t - 2 in vadds_last else []

            def ldf(sy, j=j, r=r, t=t):
                src = gath[j][r * DM:(r + 1) * DM, bass.ds(dyn["s_off"], TOK)]
                src = src.rearrange("(c p) n -> p c n", p=128)
                return sy.dma_start(out=x1f[t % 2][:, :].rearrange(
                    "p (c n) -> p c n", c=C), in_=src)
            x1_loaded[t] = p.emit("sync", ldf, waits=wld, inc=(s_lf, 16))

            wldb = [(s_pe, wo_done[t - 2])] if t - 2 in wo_done else []

            def ldb(g, j=j, r=r, t=t):
                src = gath[j][r * DM:(r + 1) * DM, bass.ds(dyn["g_off"], TOK)]
                src = src.rearrange("(c p) n -> p c n", p=128)
                return g.dma_start(out=x1b[t % 2][:, :].rearrange(
                    "p (c n) -> p c n", c=C), in_=src)
            x1b_loaded[t] = p.emit("gpsimd", ldb, waits=wldb, inc=(s_ld, 16))

            for m in range(C):
                bank = psG[2 + (m % 2)]
                wmm = [(s_ld, x1b_loaded[t])]
                key = (t, m - 2) if m >= 2 else (t - 1, C - 2 + m)
                if key in a1_hist:
                    wmm.append((s_dve, a1_hist[key]))
                if t == 0 and m < 2:
                    wmm.append((s_dve, cbf_fin))

                def mk_wo(te, m=m, t=t, bank=bank):
                    inst = None
                    for k in range(C):
                        inst = te.matmul(bank[:, 0:TOK], wslice(w_sb["woT"], k, m),
                                         x1b[t % 2][:, k * TOK:(k + 1) * TOK],
                                         start=(k == 0), stop=(k == C - 1))
                    return inst
                gn = p.emit("tensor", mk_wo, waits=wmm, inc=(s_pe, 1))
                wo_done[t] = gn

                tmpX = [tmp1, tmp2][m % 2]
                wa1 = [(s_pe, gn)]
                keyp = (t, m - 2) if m >= 2 else (t - 1, C - 2 + m)
                if keyp in a2_hist:
                    wa1.append((s_act, a2_hist[keyp]))
                a1_hist[(t, m)] = p.emit(
                    "vector", lambda v, m=m, bank=bank, tmpX=tmpX:
                    v.tensor_tensor(tmpX[:, :], bank[:, 0:TOK],
                                    uoc[:, m * TOK:(m + 1) * TOK], op=ALU.add),
                    waits=wa1, inc=(s_dve, 1))

                wa2 = [(s_dve, a1_hist[(t, m)])]
                if m == 0 and (t - 1) in spill_done:
                    wa2.append((s_st, spill_done[t - 1]))
                a2_hist[(t, m)] = p.emit(
                    "scalar", lambda sc, m=m, tmpX=tmpX: sc.activation(
                        hv[:, m * TOK:(m + 1) * TOK], tmpX[:, :], AF.Tanh,
                        bias=bias_sb["bo"][:, m:m + 1], scale=1.0),
                    waits=wa2, inc=(s_act, 1))

            for m in range(C):
                wv = [(s_act, a2_hist[(t, m)])]
                if m == 0:
                    wv.append((s_lf, x1_loaded[t]))
                n = p.emit("vector", lambda v, m=m, t=t: v.tensor_tensor(
                    hv[:, m * TOK:(m + 1) * TOK], hv[:, m * TOK:(m + 1) * TOK],
                    x1f[t % 2][:, m * TOK:(m + 1) * TOK], op=ALU.add),
                    waits=wv, inc=(s_dve, 1))
            vadds_last[t] = n

            cp1 = p.emit("vector", lambda v: v.tensor_copy(vb[:, :], hv[:, :]),
                         inc=(s_dve, 1))
            cp2 = p.emit("vector", lambda v: v.tensor_tensor(
                sqb[:, :], vb[:, :], vb[:, :], op=ALU.mult), inc=(s_dve, 1))

            def mk_sums(te):
                inst = None
                for c in range(C):
                    inst = te.matmul(psS[:, 0:TOK], ones128[:, :],
                                     vb[:, c * TOK:(c + 1) * TOK],
                                     start=(c == 0), stop=(c == C - 1))
                for c in range(C):
                    inst = te.matmul(psS[:, TOK:2 * TOK], ones128[:, :],
                                     sqb[:, c * TOK:(c + 1) * TOK],
                                     start=(c == 0), stop=(c == C - 1))
                return inst
            wsum = [(s_dve, cp2), (s_ld, consts_ready[1])]
            if t - 1 in e2_hist:
                wsum.append((s_dve, e2_hist[t - 1]))
            sums_n = p.emit("tensor", mk_sums, waits=wsum, inc=(s_pe, 1))

            we1 = [(s_pe, sums_n)]
            if t - 2 in st_stored:
                we1.append((s_st, st_stored[t - 2]))
            p.emit("vector", lambda v, t=t: v.tensor_scalar_mul(
                sst[t % 2][:, 0:TOK], psS[:, 0:TOK], 1.0 / DM),
                waits=we1, inc=(s_dve, 1))
            e2_hist[t] = p.emit("vector", lambda v, t=t: v.tensor_scalar_mul(
                sst[t % 2][:, TOK:2 * TOK], psS[:, TOK:2 * TOK], 1.0 / DM),
                inc=(s_dve, 1))
            st_stored[t] = p.emit("sync", lambda sy, t=t: sy.dma_start(
                out=stats_d[t:t + 1, :], in_=sst[t % 2][:, :]),
                waits=[(s_dve, e2_hist[t])], inc=(s_st, 16))

            def spv(sy, t=t):
                dst = vsp[:, t * TOK:(t + 1) * TOK].rearrange(
                    "(c p) n -> p c n", p=128)
                return sy.dma_start(out=dst, in_=hv[:, :].rearrange(
                    "p (c n) -> p c n", c=C))
            spill_done[t] = p.emit("sync", spv, waits=[(s_dve, vadds_last[t])],
                                   inc=(s_st, 16))

        st_ld = p.emit("sync", lambda sy: sy.dma_start(
            out=stT[:, :], in_=stats_d),
            waits=[(s_st, st_stored[B - 1])], inc=(s_lf, 16))
        rb1 = p.emit("vector", lambda v: v.tensor_tensor(
            stX[:, :], stT[:, 0:TOK], stT[:, 0:TOK], op=ALU.mult),
            waits=[(s_lf, st_ld)], inc=(s_dve, 1))
        rb2 = p.emit("vector", lambda v: v.tensor_tensor(
            stX[:, :], stT[:, TOK:2 * TOK], stX[:, :], op=ALU.subtract),
            inc=(s_dve, 1))
        rb3 = p.emit("scalar", lambda sc: sc.activation(
            stY[:, :], stX[:, :], AF.Ln, bias=epst[0:B, :], scale=1.0),
            waits=[(s_dve, rb2)], inc=(s_act, 1))
        rb4 = p.emit("scalar", lambda sc: sc.activation(
            stX[:, :], stY[:, :], AF.Exp, bias=0.0, scale=-0.5), inc=(s_act, 1))
        rb5 = p.emit("vector", lambda v: v.tensor_copy(
            stO[:, 0:TOK], stX[:, :]), waits=[(s_act, rb4)], inc=(s_dve, 1))
        rb6 = p.emit("vector", lambda v: v.tensor_tensor(
            stY[:, :], stT[:, 0:TOK], stX[:, :], op=ALU.mult), inc=(s_dve, 1))
        rb7 = p.emit("vector", lambda v: v.tensor_scalar(
            stO[:, TOK:2 * TOK], stY[:, :], -1.0, None, op0=ALU.mult),
            inc=(s_dve, 1))
        stO_st = p.emit("sync", lambda sy: sy.dma_start(
            out=stats_o, in_=stO[:, :]), waits=[(s_dve, rb7)], inc=(s_st, 16))

        out_stored = {}
        appl_done = {}
        bc_hist = {}
        vld = {}
        for t in range(B):
            wldv = ([(s_dve, appl_done[t - 2])] if t - 2 in appl_done
                    else [(s_dve, vadds_last[B - 2 + t])])

            def ldv(g, t=t):
                src = vsp[:, t * TOK:(t + 1) * TOK].rearrange(
                    "(c p) n -> p c n", p=128)
                return g.dma_start(out=x1f[t % 2][:, :].rearrange(
                    "p (c n) -> p c n", c=C), in_=src)
            wldv.append((s_st, spill_done[t]))
            vld[t] = p.emit("gpsimd", ldv, waits=wldv, inc=(s_ld, 16))

            wsi = [(s_st, stO_st)]
            if t - 2 in bc_hist:
                wsi.append((s_pe, bc_hist[t - 2]))
            sld = p.emit("sync", lambda sy, t=t: sy.dma_start(
                out=stin[t % 2][:, :], in_=stats_o[t:t + 1, :]),
                waits=wsi, inc=(s_lf, 16))

            def mk_bc(te, t=t):
                te.matmul(psB[:, 0:TOK], ones1F[:, :],
                          stin[t % 2][:, 0:TOK], start=True, stop=True)
                return te.matmul(psB[:, TOK:2 * TOK], ones1F[:, :],
                                 stin[t % 2][:, TOK:2 * TOK],
                                 start=True, stop=True)
            wbc = [(s_lf, sld)]
            if t - 1 in appl_done:
                wbc.append((s_dve, appl_done[t - 1]))
            bc_n = p.emit("tensor", mk_bc, waits=wbc, inc=(s_pe, 1))
            bc_hist[t] = bc_n

            for m in range(C):
                w1 = [(s_pe, bc_n), (s_ld, vld[t])] if m == 0 else []
                if m == 0 and t - 2 in out_stored:
                    w1.append((s_st, out_stored[t - 2]))
                p.emit("vector", lambda v, m=m, t=t: v.tensor_tensor(
                    outsb[t % 2][:, m * TOK:(m + 1) * TOK],
                    x1f[t % 2][:, m * TOK:(m + 1) * TOK], psB[:, 0:TOK],
                    op=ALU.mult), waits=w1, inc=(s_dve, 1))
                appl_done[t] = p.emit("vector", lambda v, m=m, t=t:
                                      v.tensor_tensor(
                    outsb[t % 2][:, m * TOK:(m + 1) * TOK],
                    outsb[t % 2][:, m * TOK:(m + 1) * TOK], psB[:, TOK:2 * TOK],
                    op=ALU.add), inc=(s_dve, 1))

            def stout(sy, t=t):
                dst = out_h[:, t * TOK:(t + 1) * TOK].rearrange(
                    "(c p) n -> p c n", p=128)
                return sy.dma_start(out=dst, in_=outsb[t % 2][:, :].rearrange(
                    "p (c n) -> p c n", c=C))
            out_stored[t] = p.emit("sync", stout, waits=[(s_dve, appl_done[t])],
                                   inc=(s_st, 16))

        p.emit("sync", lambda sy: None, waits=[(s_st, s_st.n)])

        for s in p.sems.values():
            s.h = es.enter_context(nc.semaphore(s.name))
        with nc.Block() as block:
            @block.tensor
            def _(te):
                p.build_engine(te, "tensor")

            @block.scalar
            def _(sc):
                p.build_engine(sc, "scalar")

            @block.vector
            def _(v):
                p.build_engine(v, "vector")

            @block.gpsimd
            def _(g):
                p.build_engine(g, "gpsimd")

            @block.sync
            def _(sy):
                p.build_engine(sy, "sync")

    return nc


def build_phase_a(ctx):
    nc, p, es = ctx["nc"], ctx["p"], ctx["es"]
    T, PS = ctx["T"], ctx["PS"]
    xT, whbm, slab, gath = ctx["xT"], ctx["whbm"], ctx["slab"], ctx["gath"]
    ones1, ones128, epst = ctx["ones1"], ctx["ones128"], ctx["epst"]
    ones1F = ctx["ones1F"]
    s_ld, s_lf, s_st = ctx["s_ld"], ctx["s_lf"], ctx["s_st"]
    s_pe, s_act, s_dve, s_cc = ctx["s_pe"], ctx["s_act"], ctx["s_dve"], ctx["s_cc"]

    aes = contextlib.ExitStack()
    ctx["aes"] = aes
    TA = lambda *a, **k: aes.enter_context(nc.sbuf_tensor(*a, **k))
    PA = lambda *a, **k: aes.enter_context(nc.psum_tensor(*a, **k))

    arena1 = TA("arena1", [128, C * S], BF16)
    xbf = arena1[:, :]
    aT = arena1
    _war = TA("warena", [128, C * C * 128], BF16)
    warena = [_war, _war]
    qT = TA("qT", [128, C * S], BF16)
    kT = TA("kT", [128, C * S], BF16)
    vP = TA("vP", [128, 16 * (H * 65)], BF16)
    exp_sb = [TA(f"exp{i}", [128, 1024], BF16) for i in range(3)]
    vtmpf = TA("vtmpf", [128, 512], F32)
    rb_row = TA("rb_row", [1, 1024], BF16)
    lnzrow = TA("lnzrow", [1, 1024], F32)
    rbB = TA("rbB", [64, 1024], BF16)
    x1sb = TA("x1sb", [128, C * 512], F32)
    xres = TA("xres", [128, C * 512], F32)
    xpb = arena1[:, 8192:8192 + 4096]
    sqA = arena1[:, 8192 + 4096:16384]
    lnS = TA("lnS", [1, 512], F32)
    lnS2 = TA("lnS2", [1, 512], F32)
    lnT = TA("lnT", [1, 512], F32)
    lnU = TA("lnU", [1, 512], F32)
    lnB = TA("lnB", [1, 1024], F32)
    neg3 = TA("neg3", [128, 1], F32)
    identF = TA("identF", [128, 128], F32)

    psSC = [PA(f"psSC{i}", [128, 1024], F32) for i in range(2)]
    psU = [PA(f"psU{i}", [128, 1024], F32) for i in range(2)]

    SLOT = [(psSC[0], 0), (psSC[0], 512), (psSC[1], 0), (psSC[1], 512)]
    slot_evac = [0, 0, 0, 0]
    slot_mm = [0, 0, 0, 0]

    def slot_ap(u, p0=0, p1=128, w=512):
        t_, off = SLOT[u % 4]
        return t_[p0:p1, off:off + w]

    p.emit("gpsimd", lambda g: g.memset(neg3[:, :], -3.0), inc=(s_ld, 1))
    p.emit("gpsimd", lambda g: g.memset(vP[:, :], 1.0), inc=(s_ld, 1))
    p.emit("gpsimd", lambda g: g.memset(identF[:, :], 0.0))
    p.emit("gpsimd", lambda g: g.affine_select(
        out=identF[:, :], in_=identF[:, :], compare_op=ALU.not_equal,
        fill=1.0, base=0, pattern=[[-1, 128]], channel_multiplier=1),
        inc=(s_ld, 1))
    aconsts = (s_ld, s_ld.n)

    PW = [("Wq", qT), ("Wk", kT), ("Wv", None), ("Wpost", None)]
    su = 0
    xbf_loaded = {}
    wload = {}
    wload_mm = {}
    aT_free = {-1: 0}
    smn_last = {}
    store_cnt = 0
    exp_hist = {}
    av_hist = {}
    tt_hist = {}
    gelu_hist = {}

    for b in range(BLOC):
        wx = [(s_pe, smn_last[b - 1])] if b - 1 in smn_last else []

        def ldx_half(sy, b=b, half=0):
            src = ctx["xbf_d"][half * 512:half * 512 + 512,
                               b * S:(b + 1) * S].rearrange(
                "(c p) n -> p c n", p=128)
            dst = arena1[:, half * 4 * S:(half + 1) * 4 * S].rearrange(
                "p (c n) -> p c n", c=4)
            return sy.dma_start(out=dst, in_=src)
        wx = wx + [(s_ld, ctx["xcast"])]
        xbf_lo = p.emit("sync", lambda sy, b=b: ldx_half(sy, b, 0),
                        waits=wx, inc=(s_lf, 16))
        xbf_hi = p.emit("sync", lambda sy, b=b: ldx_half(sy, b, 1),
                        inc=(s_lf, 16))
        xbf_loaded[b] = xbf_hi

        def emit_wload(pi, b=b):
            wwl = []
            prev = (b, pi - 1) if pi >= 1 else (b - 1, 3)
            if prev in wload_mm:
                wwl = [(s_pe, wload_mm[prev])]
            wn = PW[pi][0]
            wload[(b, pi)] = p.emit("sync", lambda sy, wn=wn, pi=pi: sy.dma_start(
                out=warena[pi % 2][:, :], in_=ctx["wbf16_d"][wn]),
                waits=wwl + [(s_ld, ctx["wcast"][wn])], inc=(s_lf, 16))
        emit_wload(0)

        for pi in range(3):
            wn, dstT = PW[pi][0], PW[pi][1]
            for m in range(C):
                wmm = [(s_lf, max(wload[(b, pi)], xbf_lo))]
                for u in range(4):
                    if slot_evac[u] > 0:
                        wmm.append((s_dve, slot_evac[u]))

                def mk_p(te, pi=pi, m=m, b=b, klo=0, khi=4):
                    inst = None
                    for k in range(klo, khi):
                        lhs = wslice(warena[pi % 2], k, m)
                        for sblk in range(4):
                            inst = te.matmul(
                                slot_ap(sblk), lhs,
                                xbf[:, k * S + sblk * 512:k * S + sblk * 512 + 512],
                                start=(k == 0), stop=(k == C - 1))
                    return inst
                p.emit("tensor", mk_p, waits=wmm, inc=(s_pe, 1))
                gn = p.emit("tensor",
                            lambda te, pi=pi, m=m, b=b: mk_p(te, pi, m, b, 4, 8),
                            waits=[(s_lf, xbf_hi)], inc=(s_pe, 1))
                wload_mm[(b, pi)] = gn

                if pi < 2:
                    for sblk in range(4):
                        def ev(v, dstT=dstT, m=m, sblk=sblk):
                            return v.tensor_scalar_mul(
                                dstT[:, m * S + sblk * 512:m * S + sblk * 512 + 512],
                                slot_ap(sblk), 1.0)
                        slot_evac[sblk] = p.emit("vector", ev, waits=[(s_pe, gn)],
                                                 inc=(s_dve, 1))
                else:
                    for sblk in range(4):
                        def evv(v, sblk=sblk):
                            return v.tensor_scalar_mul(vtmpf[:, :],
                                                       slot_ap(sblk), 1.0)
                        wev = [(s_pe, gn)]
                        if (b, m, sblk) != (0, 0, 0):
                            wev.append((s_pe, trn))
                        evn = p.emit("vector", evv, waits=wev, inc=(s_dve, 1))
                        slot_evac[sblk] = evn
                        for ti in range(4):
                            tc = sblk * 4 + ti

                            def mktr(te, ti=ti):
                                return te.transpose(
                                    psU[0][0:128, 0:128],
                                    vtmpf[:, ti * 128:(ti + 1) * 128],
                                    identF[:, :])
                            wtr = [(s_dve, evn), (s_ld, aconsts[1])]
                            if (b, m, sblk, ti) != (0, 0, 0, 0):
                                wtr.append((s_dve, vp_ev))
                            trn = p.emit("tensor", mktr, waits=wtr, inc=(s_pe, 1))

                            def evp(v, tc=tc, m=m):
                                base = tc * H * 65 + (2 * m) * 65
                                dst = vP[:, base:base + 130].rearrange(
                                    "p (h e) -> p h e", h=2)[:, :, 0:64]
                                srcv = psU[0][0:128, 0:128].rearrange(
                                    "p (h e) -> p h e", h=2)
                                return v.tensor_copy(dst, srcv)
                            vp_ev = p.emit("vector", evp, waits=[(s_pe, trn)],
                                           inc=(s_dve, 1))
            emit_wload(pi + 1)
        vp_done = vp_ev

        for sh in range(2):
            expz_hist = {}
            bc_hist = {}

            def emit_scores(h, tc):
                bank = psSC[(tc + h) % 2]
                wsc = []
                kexp = (b, sh, h, tc - 2) if tc >= 2 else \
                       (b, sh, h - 1, 15 - tc) if h >= 1 else \
                       (b, sh - 1, 15, 15 - tc) if sh >= 1 else \
                       (b - 1, 1, 15, 15 - tc)
                if kexp in exp_hist:
                    wsc.append((s_act, exp_hist[kexp]))
                if h == 0 and tc == 0:
                    wsc.append((s_dve, vp_done))
                    for u in range(4):
                        wsc.append((s_dve, slot_evac[u]))

                def mksc(te, h=h, tc=tc, bank=bank):
                    inst = None
                    if tc % 2 == 0:
                        te.matmul(bank[:, 0:512],
                                  kT[:, (h // 2) * S + tc * 128:
                                     (h // 2) * S + tc * 128 + 128],
                                  qT[:, (h // 2) * S + sh * 1024:
                                     (h // 2) * S + sh * 1024 + 512],
                                  start=True, stop=True)
                    lhs = kT[(h % 2) * 64:(h % 2) * 64 + 64,
                             (h // 2) * S + tc * 128:(h // 2) * S + tc * 128 + 128]
                    for si in range(2):
                        rhs = qT[(h % 2) * 64:(h % 2) * 64 + 64,
                                 (h // 2) * S + sh * 1024 + si * 512:
                                 (h // 2) * S + sh * 1024 + si * 512 + 512]
                        inst = te.matmul(bank[:, si * 512:si * 512 + 512],
                                         lhs, rhs, start=True, stop=True)
                    return inst
                return p.emit("tensor", mksc, waits=wsc, inc=(s_pe, 1))

            def emit_exp(h, tc, scn):
                bank = psSC[(tc + h) % 2]
                wex = [(s_pe, scn)]
                tcp = 15 - ((15 - tc) % 3)
                kav = (b, sh, h, tc - 3) if tc >= 3 else \
                      (b, sh, h - 1, tcp) if h >= 1 else \
                      (b, sh - 1, 15, tcp) if sh >= 1 else \
                      (b - 1, 1, 15, tcp)
                if kav in av_hist:
                    wex.append((s_pe, av_hist[kav]))

                def mkex(sc, tc=tc, bank=bank):
                    return sc.activation(exp_sb[tc % 3][:, :], bank[:, :],
                                         AF.Exp, bias=neg3[:, :], scale=0.125)
                exp_hist[(b, sh, h, tc)] = p.emit("scalar", mkex, waits=wex,
                                                  inc=(s_act, 1))

            def emit_av(h, tc):
                wav = [(s_act, exp_hist[(b, sh, h, tc)])]
                if tc == 0:
                    ktt = (b, sh, h - 2) if h >= 2 else \
                          (b, sh - 1, 14 + h) if sh >= 1 else \
                          (b - 1, 1, 14 + h)
                    if ktt in tt_hist:
                        wav.append((s_dve, tt_hist[ktt]))

                def mkav(te, h=h, tc=tc):
                    inst = None
                    lhs = vP[:, tc * H * 65 + h * 65:tc * H * 65 + h * 65 + 65]
                    for si in range(2):
                        inst = te.matmul(
                            psU[h % 2][0:65, si * 512:si * 512 + 512],
                            lhs, exp_sb[tc % 3][:, si * 512:si * 512 + 512],
                            start=(tc == 0), stop=(tc == 15))
                    return inst
                av_hist[(b, sh, h, tc)] = p.emit("tensor", mkav, waits=wav,
                                                 inc=(s_pe, 1))

            def emit_tail_a(h):
                ln_n = p.emit("scalar", lambda sc, h=h: sc.activation(
                    lnzrow[:, :], psU[h % 2][64:65, :], AF.Ln),
                    waits=[(s_pe, av_hist[(b, sh, h, 15)])], inc=(s_act, 1))

                def mkexpz(sc, h=h):
                    with nc.allow_low_precision(
                            reason="softmax 1/Z stored bf16; product is "
                                   "rounded to bf16 anyway; diluted by residual"):
                        return sc.activation(rb_row[:, :], lnzrow[:, :],
                                             AF.Exp, scale=-1.0)
                wz = []
                if h - 2 in bc_hist:
                    wz.append((s_pe, bc_hist[h - 2]))
                expz_hist[h] = p.emit("scalar", mkexpz, waits=wz, inc=(s_act, 1))

            def emit_tail_b(h):
                def mkbc(te, h=h):
                    inst = None
                    for si in range(2):
                        inst = te.matmul(
                            psU[h % 2][64:128, si * 512:si * 512 + 512],
                            ones1[0:1, 0:64],
                            rb_row[:, si * 512:si * 512 + 512],
                            start=True, stop=True)
                    return inst
                bc_hist[h] = p.emit("tensor", mkbc, waits=[(s_act, expz_hist[h])],
                                    inc=(s_pe, 1))
                ebn = p.emit("vector", lambda v, h=h: v.tensor_scalar_mul(
                    rbB[:, :], psU[h % 2][64:128, :], 1.0),
                    waits=[(s_pe, bc_hist[h])], inc=(s_dve, 1))

                def mktt(v, h=h):
                    dst = arena1[(h % 2) * 64:(h % 2) * 64 + 64,
                                 (h // 2) * 1024:(h // 2) * 1024 + 1024]
                    return v.tensor_tensor(dst, psU[h % 2][0:64, :], rbB[:, :],
                                           op=ALU.mult)
                tt_hist[(b, sh, h)] = p.emit("vector", mktt, inc=(s_dve, 1))

            for h in range(16):
                for tc in range(16):
                    scn = emit_scores(h, tc)
                    emit_exp(h, tc, scn)
                    if tc >= 1:
                        emit_av(h, tc - 1)
                    if h >= 1 and tc == 2:
                        emit_tail_a(h - 1)
                    if h >= 1 and tc == 8:
                        emit_tail_b(h - 1)
                emit_av(h, 15)
            emit_tail_a(15)
            emit_tail_b(15)

            def mkgl(sc):
                return sc.activation(arena1[:, 0:C * 1024], arena1[:, 0:C * 1024],
                                     AF.Gelu)
            gelu_hist[(b, sh)] = p.emit("scalar", mkgl,
                                        waits=[(s_dve, tt_hist[(b, sh, 15)])],
                                        inc=(s_act, 1))

            for sb2 in range(2):
                sblk = sh * 2 + sb2

                def ldxr(sy, b=b, sblk=sblk):
                    src = xT[:, b * S + sblk * 512:b * S + sblk * 512 + 512]
                    src = src.rearrange("(c p) n -> p c n", p=128)
                    return sy.dma_start(out=xres[:, :].rearrange(
                        "p (c n) -> p c n", c=C), in_=src)
                wxr = [(s_dve, xres_used)] if b + sblk > 0 else []
                xrn = p.emit("sync", ldxr, waits=wxr, inc=(s_lf, 16))

                stt_hist = {}
                for m in range(C):
                    u = su % 4
                    wmm = [(s_act, gelu_hist[(b, sh)])]
                    if slot_evac[u] > 0:
                        wmm.append((s_dve, slot_evac[u]))

                    def mkwp(te, m=m, sb2=sb2, u=u):
                        inst = None
                        for k in range(C):
                            inst = te.matmul(
                                slot_ap(u), wslice(warena[1], k, m),
                                arena1[:, k * 1024 + sb2 * 512:
                                       k * 1024 + sb2 * 512 + 512],
                                start=(k == 0), stop=(k == C - 1))
                        return inst
                    gn = p.emit("tensor", mkwp, waits=wmm, inc=(s_pe, 1))
                    slot_mm[u] = gn
                    aT_free[b] = gn
                    wload_mm[(b, 3)] = gn

                    def mkstt(v, m=m, u=u):
                        return v.scalar_tensor_tensor(
                            x1sb[:, m * 512:(m + 1) * 512], xres[:, m * 512:(m + 1) * 512],
                            2.0, slot_ap(u), op0=ALU.mult, op1=ALU.add)
                    ws = [(s_pe, gn)]
                    if m == 0:
                        ws.append((s_lf, xrn))
                    slot_evac[u] = p.emit("vector", mkstt, waits=ws, inc=(s_dve, 1))
                    stt_hist[m] = slot_evac[u]
                    su += 1

                cpx = p.emit("vector", lambda v: v.tensor_copy(
                    xpb, x1sb[:, :]), waits=[(s_dve, stt_hist[C - 1])],
                    inc=(s_dve, 1))
                cps = p.emit("vector", lambda v: v.tensor_tensor(
                    sqA, xpb, xpb, op=ALU.mult), inc=(s_dve, 1))

                us_, uq = su % 4, (su + 1) % 4
                su += 2

                def mksums(te, us_=us_, uq=uq):
                    inst = None
                    for k in range(C):
                        inst = te.matmul(slot_ap(us_, 0, 1), ones128[:, :],
                                         xpb[:, k * 512:(k + 1) * 512],
                                         start=(k == 0), stop=(k == C - 1))
                    for k in range(C):
                        inst = te.matmul(slot_ap(uq, 0, 1), ones128[:, :],
                                         sqA[:, k * 512:(k + 1) * 512],
                                         start=(k == 0), stop=(k == C - 1))
                    return inst
                wsm = [(s_dve, cps)]
                if slot_evac[us_] > 0:
                    wsm.append((s_dve, slot_evac[us_]))
                if slot_evac[uq] > 0:
                    wsm.append((s_dve, slot_evac[uq]))
                smn = p.emit("tensor", mksums, waits=wsm, inc=(s_pe, 1))
                smn_last[b] = smn

                e1 = p.emit("vector", lambda v, us_=us_: v.tensor_scalar_mul(
                    lnS[:, :], slot_ap(us_, 0, 1), 1.0 / DM),
                    waits=[(s_pe, smn)], inc=(s_dve, 1))
                e2 = p.emit("vector", lambda v, uq=uq: v.tensor_scalar_mul(
                    lnS2[:, :], slot_ap(uq, 0, 1), 1.0 / DM), inc=(s_dve, 1))
                slot_evac[us_] = e1
                slot_evac[uq] = e2
                l1 = p.emit("vector", lambda v: v.tensor_tensor(
                    lnT[:, :], lnS[:, :], lnS[:, :], op=ALU.mult), inc=(s_dve, 1))
                l2 = p.emit("vector", lambda v: v.tensor_tensor(
                    lnT[:, :], lnS2[:, :], lnT[:, :], op=ALU.subtract),
                    inc=(s_dve, 1))
                l3 = p.emit("scalar", lambda sc: sc.activation(
                    lnU[:, :], lnT[:, :], AF.Ln, bias=epst[0:1, :], scale=1.0),
                    waits=[(s_dve, l2)], inc=(s_act, 1))
                l4 = p.emit("scalar", lambda sc: sc.activation(
                    lnT[:, :], lnU[:, :], AF.Exp, bias=0.0, scale=-0.5),
                    inc=(s_act, 1))
                l5 = p.emit("vector", lambda v: v.tensor_copy(
                    lnB[:, 0:512], lnT[:, :]), waits=[(s_act, l4)], inc=(s_dve, 1))
                l6 = p.emit("vector", lambda v: v.tensor_tensor(
                    lnU[:, :], lnS[:, :], lnT[:, :], op=ALU.mult), inc=(s_dve, 1))
                l7 = p.emit("vector", lambda v: v.tensor_scalar(
                    lnB[:, 512:1024], lnU[:, :], -1.0, None, op0=ALU.mult),
                    inc=(s_dve, 1))

                for wsem, wval in [(s_dve, e1), (s_dve, l2), (s_act, l3),
                                   (s_dve, l7)]:
                    def mkdum(te):
                        return te.matmul(psU[0][0:128, 0:512], kT[:, 0:128],
                                         qT[:, 0:512], start=True, stop=True)
                    p.emit("tensor", mkdum,
                           waits=[(wsem, wval),
                                  (s_dve, tt_hist[(b, sh, 14)])], inc=(s_pe, 1))

                ur, un = su % 4, (su + 1) % 4
                su += 2

                def mkbc2(te, ur=ur, un=un):
                    te.matmul(slot_ap(ur), ones1F[0:1, :], lnB[0:1, 0:512],
                              start=True, stop=True)
                    return te.matmul(slot_ap(un), ones1F[0:1, :], lnB[0:1, 512:1024],
                                     start=True, stop=True)
                wb2 = [(s_dve, l7)]
                if slot_evac[ur] > 0:
                    wb2.append((s_dve, slot_evac[ur]))
                if slot_evac[un] > 0:
                    wb2.append((s_dve, slot_evac[un]))
                b2n = p.emit("tensor", mkbc2, waits=wb2, inc=(s_pe, 1))

                ap_hist = {}
                for m in range(C):
                    w1 = [(s_pe, b2n)] if m == 0 else []
                    p.emit("vector", lambda v, m=m, ur=ur: v.tensor_tensor(
                        xres[:, m * 512:(m + 1) * 512],
                        x1sb[:, m * 512:(m + 1) * 512], slot_ap(ur),
                        op=ALU.mult), waits=w1, inc=(s_dve, 1))
                    ap_hist[m] = p.emit("vector", lambda v, m=m, un=un:
                                        v.tensor_tensor(
                        x1sb[:, m * 512:(m + 1) * 512],
                        xres[:, m * 512:(m + 1) * 512], slot_ap(un),
                        op=ALU.add), inc=(s_dve, 1))
                xres_used = ap_hist[C - 1]
                slot_evac[ur] = ap_hist[C - 1]
                slot_evac[un] = ap_hist[C - 1]

                def stx1(sy, b=b, sblk=sblk):
                    dst = slab[b][:, sblk * 512:sblk * 512 + 512].rearrange(
                        "(c p) n -> p c n", p=128)
                    return sy.dma_start(out=dst, in_=x1sb[:, :].rearrange(
                        "p (c n) -> p c n", c=C))
                p.emit("sync", stx1, waits=[(s_dve, ap_hist[C - 1])],
                       inc=(s_st, 16))

        p.emit("gpsimd", lambda g, b=b: g.collective_compute(
            "AllGather", ALU.bypass, replica_groups=[list(range(NCORES))],
            ins=[slab[b]], outs=[gath[b]]),
            waits=[(s_st, p.sems["s_st"].n)], inc=(s_cc, 1))

    aes.close()
    return (ctx["p"].sems["s_st"].n)



def weight_maps(inputs):
    return {
        "wfT": prechunk(np.asarray(inputs["L0_ffn_wf"], np.float32).T),
        "ufT": prechunk(np.asarray(inputs["L0_ffn_uf"], np.float32).T),
        "wiT": prechunk(np.asarray(inputs["L0_ffn_wi"], np.float32).T),
        "uiT": prechunk(np.asarray(inputs["L0_ffn_ui"], np.float32).T),
        "wcT": prechunk(np.asarray(inputs["L0_ffn_wc"], np.float32).T),
        "ucT": prechunk(np.asarray(inputs["L0_ffn_uc"], np.float32).T),
        "woT": prechunk(np.asarray(inputs["L0_ffn_wo"], np.float32).T),
        "uoT": prechunk(np.asarray(inputs["L0_ffn_uo"], np.float32).T),
        "bf": pack_bias(inputs["L0_ffn_bf"]), "bi": pack_bias(inputs["L0_ffn_bi"]),
        "bc": pack_bias(inputs["L0_ffn_bc"]), "bo": pack_bias(inputs["L0_ffn_bo"]),
        "inc0": pack_bias(inputs["L0_ffn_in_c"]),
    }


def assemble_out(results):
    full = np.empty((B, S, DM), np.float32)
    for k in range(NCORES):
        ok = np.asarray(results[k]["out"]).reshape(DM, B, TOK)
        full[:, k * TOK:(k + 1) * TOK, :] = ok.transpose(1, 2, 0)
    return full



def kernel(**inputs):
    inputs = {k: np.asarray(v, np.float32) for k, v in inputs.items()}
    base = weight_maps(inputs)
    base["Wq"] = prechunk(inputs["L0_attn_Wq"])
    base["Wk"] = prechunk(inputs["L0_attn_Wk"])
    base["Wv"] = prechunk(inputs["L0_attn_Wv"])
    base["Wpost"] = prechunk(inputs["L0_attn_Wpost"])
    X = inputs["L0_attn_X"]
    in_maps = []
    for k in range(NCORES):
        m = dict(base)
        m["xT"] = np.ascontiguousarray(
            X[[k, 8 + k, 16 + k, 24 + k]].transpose(2, 0, 1).reshape(DM, BLOC * S))
        in_maps.append(m)
    nc = build_program(mode="full")
    res = run_bass_kernel_spmd(nc, in_maps, core_ids=list(range(NCORES)))
    return assemble_out(res.results)


# version 06
# speedup vs baseline: 1.0985x; 1.0375x over previous
import contextlib
import numpy as np
import concourse.bass as bass
import concourse.mybir as mybir
from concourse.bass_utils import run_bass_kernel_spmd

F32 = mybir.dt.float32
BF16 = mybir.dt.bfloat16
AF = mybir.ActivationFunctionType
ALU = mybir.AluOpType

NCORES = 8
DM = 1024
C = DM // 128
B = 32
S = 2048
TOK = S // NCORES
BLOC = B // NCORES
H = 16
DK = 64
LNEPS = 1e-5


class Sem:
    def __init__(self, name):
        self.name = name
        self.n = 0
        self.h = None

    def inc(self, amt=1):
        self.n += amt
        return self.n


class Prog:

    def __init__(self):
        self.ops = {e: [] for e in ("tensor", "scalar", "vector", "gpsimd", "sync")}
        self.sems = {}

    def sem(self, name):
        if name not in self.sems:
            self.sems[name] = Sem(name)
        return self.sems[name]

    def emit(self, engine, fn, waits=(), inc=None):
        n = None
        if inc is not None:
            n = inc[0].inc(inc[1])
        self.ops[engine].append((tuple(waits), fn, inc))
        return n

    def build_engine(self, eng, name):
        for waits, fn, inc in self.ops[name]:
            for s, v in waits:
                if v > 0:
                    eng.wait_ge(s.h, v)
            inst = fn(eng)
            if inc is not None:
                assert inst is not None, f"inc on None inst in {name}"
                inst.then_inc(inc[0].h, inc[1])



def prechunk(w2):
    return np.ascontiguousarray(
        w2.reshape(C, 128, C, 128).transpose(1, 0, 2, 3).reshape(128, C * C * 128)
    )


def pack_bias(b):
    return np.ascontiguousarray(np.asarray(b, np.float32).reshape(C, 128).T)


def wslice(wsb, kc, mc):
    o = (kc * C + mc) * 128
    return wsb[:, o:o + 128]



def build_program(mode="full"):
    nc = bass.Bass("TRN2", target_bir_lowering=False)
    p = Prog()

    gath_kind = "ExternalInput" if mode == "bc" else "Internal"
    gath_kw = {} if mode == "bc" else {"addr_space": "Shared"}
    gath = [nc.dram_tensor(f"gath{j}", [NCORES * DM, S], F32, kind=gath_kind,
                           **gath_kw).ap()
            for j in range(BLOC)]
    out_h = nc.dram_tensor("out", [DM, B * TOK], F32, kind="ExternalOutput").ap()
    vsp = nc.dram_tensor("vsp", [DM, B * TOK], F32).ap()
    stats_d = nc.dram_tensor("stats_d", [B, 2 * TOK], F32).ap()
    stats_o = nc.dram_tensor("stats_o", [B, 2 * TOK], F32).ap()

    wnames_bc = ["wfT", "ufT", "wiT", "uiT", "wcT", "ucT", "woT", "uoT"]
    whbm = {n: nc.dram_tensor(n, [128, C * C * 128], F32, kind="ExternalInput").ap()
            for n in wnames_bc}
    bnames = ["bf", "bi", "bc", "bo", "inc0"]
    bhbm = {n: nc.dram_tensor(n, [128, C], F32, kind="ExternalInput").ap()
            for n in bnames}

    if mode == "full":
        xT = nc.dram_tensor("xT", [DM, BLOC * S], F32, kind="ExternalInput").ap()
        for n in ["Wq", "Wk", "Wv", "Wpost"]:
            whbm[n] = nc.dram_tensor(n, [128, C * C * 128], F32,
                                     kind="ExternalInput").ap()
        slab = [nc.dram_tensor(f"slab{j}", [DM, S], F32).ap() for j in range(BLOC)]
    else:
        slab = None
    wbf16_d = {n: nc.dram_tensor(n + "_bf", [128, C * C * 128], BF16).ap()
               for n in whbm}
    xbf_d = (nc.dram_tensor("xT_bf", [DM, BLOC * S], BF16).ap()
             if mode == "full" else None)

    es = contextlib.ExitStack()
    with es:
        T = lambda *a, **k: es.enter_context(nc.sbuf_tensor(*a, **k))
        PS = lambda *a, **k: es.enter_context(nc.psum_tensor(*a, **k))

        ones1 = T("ones1", [1, 128], BF16)
        ones1F = T("ones1F", [1, 128], F32)
        ones128 = T("ones128", [128, 1], BF16)
        epst = T("epst", [128, 1], F32)
        bias_sb = {n: T(f"sb_{n}", [128, C], F32) for n in bnames}

        s_ld = p.sem("s_ld")
        s_lf = p.sem("s_lf")
        s_st = p.sem("s_st")
        s_pe = p.sem("s_pe")
        s_act = p.sem("s_act")
        s_dve = p.sem("s_dve")
        s_cc = p.sem("s_cc")

        dyn = {}

        def ginit(g):
            dyn["g_off"] = g.snap(g.partition_id() * TOK)
            return None
        p.emit("gpsimd", ginit)

        def syinit(sy):
            dyn["s_off"] = sy.snap(sy.partition_id() * TOK)
            return None
        p.emit("sync", syinit)

        p.emit("gpsimd", lambda g: g.memset(ones1[:, :], 1.0), inc=(s_ld, 1))
        p.emit("gpsimd", lambda g: g.memset(ones1F[:, :], 1.0), inc=(s_ld, 1))
        p.emit("gpsimd", lambda g: g.memset(ones128[:, :], 1.0), inc=(s_ld, 1))
        p.emit("gpsimd", lambda g: g.memset(epst[:, :], LNEPS), inc=(s_ld, 1))
        consts_ready = (s_ld, s_ld.n)

        for n in bnames:
            p.emit("sync", lambda sy, n=n: sy.dma_start(
                out=bias_sb[n][:, :], in_=bhbm[n]), inc=(s_lf, 16))
        bias_ready = (s_lf, s_lf.n)

        wcast = {}
        for n in whbm:
            wcast[n] = p.emit("gpsimd", lambda g, n=n: g.dma_start(
                out=wbf16_d[n], in_=whbm[n]), inc=(s_ld, 16))
        if mode == "full":
            xcast = p.emit("gpsimd", lambda g: g.dma_start(
                out=xbf_d, in_=xT), inc=(s_ld, 16))


        a_store_end = 0
        if mode == "full":
            actx = dict(nc=nc, p=p, es=es, T=T, PS=PS, xT=xT, whbm=whbm,
                        wbf16_d=wbf16_d, xbf_d=xbf_d, wcast=wcast, xcast=xcast,
                        slab=slab, gath=gath, ones1=ones1, ones128=ones128,
                        ones1F=ones1F,
                        epst=epst, s_ld=s_ld, s_lf=s_lf, s_st=s_st, s_pe=s_pe,
                        s_act=s_act, s_dve=s_dve, s_cc=s_cc)
            a_store_end = build_phase_a(actx)


        w_sb = {n: T(f"w_{n}", [128, C * C * 128], BF16)
                for n in ["wfT", "ufT", "wiT", "uiT", "wcT", "ucT"]}
        w_sb["woT"] = w_sb["wfT"]
        w_sb["uoT"] = w_sb["ufT"]
        c_sb = T("c_sb", [128, C * TOK], F32)
        cbf = T("cbf", [128, C * TOK], BF16)
        xt_sb = [T(f"xt{i}", [128, C * TOK], BF16) for i in range(2)]
        gts = [T(f"gts{i}", [128, 3 * TOK], F32) for i in range(2)]
        tmp1 = T("tmp1", [128, TOK], F32)
        tmp2 = T("tmp2", [128, TOK], F32)
        uoc = T("uoc", [128, C * TOK], F32)
        x1f = [T(f"x1f{i}", [128, C * TOK], F32) for i in range(2)]
        x1b = [T(f"x1b{i}", [128, C * TOK], BF16) for i in range(2)]
        hv = T("hv", [128, C * TOK], F32)
        vb = T("vb", [128, C * TOK], BF16)
        sqb = T("sqb", [128, C * TOK], BF16)
        sst = [T(f"sst{i}", [1, 2 * TOK], F32) for i in range(2)]
        stT = T("stT", [B, 2 * TOK], F32)
        stX = T("stX", [B, TOK], F32)
        stY = T("stY", [B, TOK], F32)
        stO = T("stO", [B, 2 * TOK], F32)
        stin = [T(f"stin{i}", [1, 2 * TOK], F32) for i in range(2)]
        outsb = [T(f"outsb{i}", [128, C * TOK], F32) for i in range(2)]

        psG = [PS(f"psG{i}", [128, 512], F32) for i in range(4)]
        psS = PS("psS", [1, 512], F32)
        psB = PS("psB", [128, 512], F32)

        wA = [(s_st, a_store_end)] if mode == "full" else []
        for n in ["wfT", "ufT", "wiT", "uiT", "wcT", "ucT"]:
            p.emit("sync", lambda sy, n=n: sy.dma_start(
                out=w_sb[n][:, :], in_=wbf16_d[n]),
                waits=wA + [(s_ld, wcast[n])], inc=(s_lf, 16))
        wB_ready = (s_lf, s_lf.n)

        wc0 = [bias_ready] + ([(s_st, a_store_end)] if mode == "full" else [])
        p.emit("vector", lambda v: v.memset(c_sb[:, :], 0.0),
               waits=wc0, inc=(s_dve, 1))
        for c in range(C):
            p.emit("vector", lambda v, c=c: v.tensor_scalar_add(
                c_sb[:, c * TOK:(c + 1) * TOK], c_sb[:, c * TOK:(c + 1) * TOK],
                bias_sb["inc0"][:, c:c + 1]), inc=(s_dve, 1))

        GATES = [("wfT", "ufT", "bf", AF.Sigmoid), ("wiT", "uiT", "bi", AF.Sigmoid),
                 ("wcT", "ucT", "bc", AF.Tanh)]
        pe_group = {}
        act_of_group = {}
        xt_used = {}
        cupd = {}
        cupd[(-1, C - 1)] = s_dve.n
        cupd[(-1, C - 2)] = s_dve.n
        xt_loaded = {}
        ggrp = 0

        for t in range(B):
            j, r = t // NCORES, t % NCORES
            w = [(s_pe, xt_used[t - 2])] if t - 2 in xt_used else []
            if mode == "full":
                w = w + [(s_cc, j + 1), (s_st, a_store_end)]

            def ldxt(g, j=j, r=r, t=t):
                src = gath[j][r * DM:(r + 1) * DM, bass.ds(dyn["g_off"], TOK)]
                src = src.rearrange("(c p) n -> p c n", p=128)
                return g.dma_start(out=xt_sb[t % 2][:, :].rearrange(
                    "p (c n) -> p c n", c=C), in_=src)
            xt_loaded[t] = p.emit("gpsimd", ldxt, waits=w, inc=(s_ld, 16))

            wv = [(s_dve, cupd[(t - 1, C - 1)])]
            if t > 0:
                wv.append((s_pe, pe_group[(t - 1, C - 1, 2)]))
            cbf_n = p.emit("vector", lambda v: v.tensor_copy(cbf[:, :], c_sb[:, :]),
                           waits=wv, inc=(s_dve, 1))

            for m in range(C):
                for gi, (wn, un, bn, af) in enumerate(GATES):
                    bank = psG[ggrp % 4]
                    wmm = []
                    if m == 0 and gi == 0:
                        wmm = [(s_ld, xt_loaded[t]), (s_lf, wB_ready[1]),
                               (s_dve, cbf_n)]
                    if ggrp - 4 >= 0 and (ggrp - 4) in act_of_group:
                        wmm.append((s_act, act_of_group[ggrp - 4]))

                    def mk_mms(te, wn=wn, un=un, m=m, t=t, bank=bank):
                        inst = None
                        for k in range(C):
                            inst = te.matmul(
                                bank[:, 0:TOK], wslice(w_sb[wn], k, m),
                                xt_sb[t % 2][:, k * TOK:(k + 1) * TOK],
                                start=(k == 0), stop=False)
                        for k in range(C):
                            inst = te.matmul(
                                bank[:, 0:TOK], wslice(w_sb[un], k, m),
                                cbf[:, k * TOK:(k + 1) * TOK],
                                start=False, stop=(k == C - 1))
                        return inst
                    pe_group[(t, m, gi)] = p.emit("tensor", mk_mms, waits=wmm,
                                                  inc=(s_pe, 1))

                    wact = [(s_pe, pe_group[(t, m, gi)])]
                    key = (t, m - 2) if m >= 2 else (t - 1, C - 2 + m)
                    if key in cupd:
                        wact.append((s_dve, cupd[key]))

                    def mk_act(sc, af=af, bn=bn, m=m, gi=gi, bank=bank):
                        return sc.activation(
                            gts[m % 2][:, gi * TOK:(gi + 1) * TOK], bank[:, 0:TOK],
                            af, bias=bias_sb[bn][:, m:m + 1], scale=1.0)
                    act_of_group[ggrp] = p.emit("scalar", mk_act, waits=wact,
                                                inc=(s_act, 1))
                    ggrp += 1

                wcu = [(s_act, s_act.n)]
                cs_lo, cs_hi = m * TOK, (m + 1) * TOK
                g0 = gts[m % 2]
                p.emit("vector", lambda v, g0=g0, lo=cs_lo, hi=cs_hi:
                       v.tensor_tensor(tmp1[:, :], g0[:, 0:TOK], c_sb[:, lo:hi],
                                       op=ALU.mult), waits=wcu, inc=(s_dve, 1))
                p.emit("vector", lambda v, g0=g0: v.tensor_tensor(
                    tmp2[:, :], g0[:, TOK:2 * TOK], g0[:, 2 * TOK:3 * TOK],
                    op=ALU.mult), inc=(s_dve, 1))
                cupd[(t, m)] = p.emit("vector", lambda v, lo=cs_lo, hi=cs_hi:
                                      v.tensor_tensor(c_sb[:, lo:hi], tmp1[:, :],
                                                      tmp2[:, :], op=ALU.add),
                                      inc=(s_dve, 1))
            xt_used[t] = pe_group[(t, C - 1, 2)]

        for n in ["woT", "uoT"]:
            p.emit("sync", lambda sy, n=n: sy.dma_start(
                out=w_sb[n][:, :], in_=wbf16_d[n]),
                waits=[(s_pe, pe_group[(B - 1, C - 1, 2)]),
                       (s_ld, wcast[n])], inc=(s_lf, 16))
        wC_ready = (s_lf, s_lf.n)

        cbf_fin = p.emit("vector", lambda v: v.tensor_copy(cbf[:, :], c_sb[:, :]),
                         waits=[(s_dve, cupd[(B - 1, C - 1)]),
                                (s_pe, pe_group[(B - 1, C - 1, 2)])],
                         inc=(s_dve, 1))

        uoc_done = {}
        for m in range(C):
            bank = psG[m % 2]

            def mk_uoc(te, m=m, bank=bank):
                inst = None
                for k in range(C):
                    inst = te.matmul(bank[:, 0:TOK], wslice(w_sb["uoT"], k, m),
                                     cbf[:, k * TOK:(k + 1) * TOK],
                                     start=(k == 0), stop=(k == C - 1))
                return inst
            wmm = [(s_dve, cbf_fin), (s_lf, wC_ready[1])]
            if m >= 2:
                wmm.append((s_dve, uoc_done[m - 2]))
            gn = p.emit("tensor", mk_uoc, waits=wmm, inc=(s_pe, 1))
            uoc_done[m] = p.emit("vector", lambda v, m=m, bank=bank:
                                 v.tensor_scalar_mul(
                                     uoc[:, m * TOK:(m + 1) * TOK],
                                     bank[:, 0:TOK], 1.0),
                                 waits=[(s_pe, gn)], inc=(s_dve, 1))

        x1_loaded = {}
        x1b_loaded = {}
        wo_done = {}
        a1_hist = {}
        a2_hist = {}
        vadds_last = {}
        spill_done = {}
        e2_hist = {}
        st_stored = {}
        for t in range(B):
            j, r = t // NCORES, t % NCORES
            wld = [(s_dve, vadds_last[t - 2])] if t - 2 in vadds_last else []

            def ldf(sy, j=j, r=r, t=t):
                src = gath[j][r * DM:(r + 1) * DM, bass.ds(dyn["s_off"], TOK)]
                src = src.rearrange("(c p) n -> p c n", p=128)
                return sy.dma_start(out=x1f[t % 2][:, :].rearrange(
                    "p (c n) -> p c n", c=C), in_=src)
            x1_loaded[t] = p.emit("sync", ldf, waits=wld, inc=(s_lf, 16))

            wldb = [(s_pe, wo_done[t - 2])] if t - 2 in wo_done else []

            def ldb(g, j=j, r=r, t=t):
                src = gath[j][r * DM:(r + 1) * DM, bass.ds(dyn["g_off"], TOK)]
                src = src.rearrange("(c p) n -> p c n", p=128)
                return g.dma_start(out=x1b[t % 2][:, :].rearrange(
                    "p (c n) -> p c n", c=C), in_=src)
            x1b_loaded[t] = p.emit("gpsimd", ldb, waits=wldb, inc=(s_ld, 16))

            for m in range(C):
                bank = psG[2 + (m % 2)]
                wmm = [(s_ld, x1b_loaded[t])]
                key = (t, m - 2) if m >= 2 else (t - 1, C - 2 + m)
                if key in a1_hist:
                    wmm.append((s_dve, a1_hist[key]))
                if t == 0 and m < 2:
                    wmm.append((s_dve, cbf_fin))

                def mk_wo(te, m=m, t=t, bank=bank):
                    inst = None
                    for k in range(C):
                        inst = te.matmul(bank[:, 0:TOK], wslice(w_sb["woT"], k, m),
                                         x1b[t % 2][:, k * TOK:(k + 1) * TOK],
                                         start=(k == 0), stop=(k == C - 1))
                    return inst
                gn = p.emit("tensor", mk_wo, waits=wmm, inc=(s_pe, 1))
                wo_done[t] = gn

                tmpX = [tmp1, tmp2][m % 2]
                wa1 = [(s_pe, gn)]
                keyp = (t, m - 2) if m >= 2 else (t - 1, C - 2 + m)
                if keyp in a2_hist:
                    wa1.append((s_act, a2_hist[keyp]))
                a1_hist[(t, m)] = p.emit(
                    "vector", lambda v, m=m, bank=bank, tmpX=tmpX:
                    v.tensor_tensor(tmpX[:, :], bank[:, 0:TOK],
                                    uoc[:, m * TOK:(m + 1) * TOK], op=ALU.add),
                    waits=wa1, inc=(s_dve, 1))

                wa2 = [(s_dve, a1_hist[(t, m)])]
                if m == 0 and (t - 1) in spill_done:
                    wa2.append((s_st, spill_done[t - 1]))
                a2_hist[(t, m)] = p.emit(
                    "scalar", lambda sc, m=m, tmpX=tmpX: sc.activation(
                        hv[:, m * TOK:(m + 1) * TOK], tmpX[:, :], AF.Tanh,
                        bias=bias_sb["bo"][:, m:m + 1], scale=1.0),
                    waits=wa2, inc=(s_act, 1))

            for m in range(C):
                wv = [(s_act, a2_hist[(t, m)])]
                if m == 0:
                    wv.append((s_lf, x1_loaded[t]))
                n = p.emit("vector", lambda v, m=m, t=t: v.tensor_tensor(
                    hv[:, m * TOK:(m + 1) * TOK], hv[:, m * TOK:(m + 1) * TOK],
                    x1f[t % 2][:, m * TOK:(m + 1) * TOK], op=ALU.add),
                    waits=wv, inc=(s_dve, 1))
            vadds_last[t] = n

            cp1 = p.emit("vector", lambda v: v.tensor_copy(vb[:, :], hv[:, :]),
                         inc=(s_dve, 1))
            cp2 = p.emit("vector", lambda v: v.tensor_tensor(
                sqb[:, :], vb[:, :], vb[:, :], op=ALU.mult), inc=(s_dve, 1))

            def mk_sums(te):
                inst = None
                for c in range(C):
                    inst = te.matmul(psS[:, 0:TOK], ones128[:, :],
                                     vb[:, c * TOK:(c + 1) * TOK],
                                     start=(c == 0), stop=(c == C - 1))
                for c in range(C):
                    inst = te.matmul(psS[:, TOK:2 * TOK], ones128[:, :],
                                     sqb[:, c * TOK:(c + 1) * TOK],
                                     start=(c == 0), stop=(c == C - 1))
                return inst
            wsum = [(s_dve, cp2), (s_ld, consts_ready[1])]
            if t - 1 in e2_hist:
                wsum.append((s_dve, e2_hist[t - 1]))
            sums_n = p.emit("tensor", mk_sums, waits=wsum, inc=(s_pe, 1))

            we1 = [(s_pe, sums_n)]
            if t - 2 in st_stored:
                we1.append((s_st, st_stored[t - 2]))
            p.emit("vector", lambda v, t=t: v.tensor_scalar_mul(
                sst[t % 2][:, 0:TOK], psS[:, 0:TOK], 1.0 / DM),
                waits=we1, inc=(s_dve, 1))
            e2_hist[t] = p.emit("vector", lambda v, t=t: v.tensor_scalar_mul(
                sst[t % 2][:, TOK:2 * TOK], psS[:, TOK:2 * TOK], 1.0 / DM),
                inc=(s_dve, 1))
            st_stored[t] = p.emit("sync", lambda sy, t=t: sy.dma_start(
                out=stats_d[t:t + 1, :], in_=sst[t % 2][:, :]),
                waits=[(s_dve, e2_hist[t])], inc=(s_st, 16))

            def spv(sy, t=t):
                dst = vsp[:, t * TOK:(t + 1) * TOK].rearrange(
                    "(c p) n -> p c n", p=128)
                return sy.dma_start(out=dst, in_=hv[:, :].rearrange(
                    "p (c n) -> p c n", c=C))
            spill_done[t] = p.emit("sync", spv, waits=[(s_dve, vadds_last[t])],
                                   inc=(s_st, 16))

        st_ld = p.emit("sync", lambda sy: sy.dma_start(
            out=stT[:, :], in_=stats_d),
            waits=[(s_st, st_stored[B - 1])], inc=(s_lf, 16))
        rb1 = p.emit("vector", lambda v: v.tensor_tensor(
            stX[:, :], stT[:, 0:TOK], stT[:, 0:TOK], op=ALU.mult),
            waits=[(s_lf, st_ld)], inc=(s_dve, 1))
        rb2 = p.emit("vector", lambda v: v.tensor_tensor(
            stX[:, :], stT[:, TOK:2 * TOK], stX[:, :], op=ALU.subtract),
            inc=(s_dve, 1))
        rb3 = p.emit("scalar", lambda sc: sc.activation(
            stY[:, :], stX[:, :], AF.Ln, bias=epst[0:B, :], scale=1.0),
            waits=[(s_dve, rb2)], inc=(s_act, 1))
        rb4 = p.emit("scalar", lambda sc: sc.activation(
            stX[:, :], stY[:, :], AF.Exp, bias=0.0, scale=-0.5), inc=(s_act, 1))
        rb5 = p.emit("vector", lambda v: v.tensor_copy(
            stO[:, 0:TOK], stX[:, :]), waits=[(s_act, rb4)], inc=(s_dve, 1))
        rb6 = p.emit("vector", lambda v: v.tensor_tensor(
            stY[:, :], stT[:, 0:TOK], stX[:, :], op=ALU.mult), inc=(s_dve, 1))
        rb7 = p.emit("vector", lambda v: v.tensor_scalar(
            stO[:, TOK:2 * TOK], stY[:, :], -1.0, None, op0=ALU.mult),
            inc=(s_dve, 1))
        stO_st = p.emit("sync", lambda sy: sy.dma_start(
            out=stats_o, in_=stO[:, :]), waits=[(s_dve, rb7)], inc=(s_st, 16))

        out_stored = {}
        appl_done = {}
        bc_hist = {}
        vld = {}
        for t in range(B):
            wldv = ([(s_dve, appl_done[t - 2])] if t - 2 in appl_done
                    else [(s_dve, vadds_last[B - 2 + t])])

            def ldv(g, t=t):
                src = vsp[:, t * TOK:(t + 1) * TOK].rearrange(
                    "(c p) n -> p c n", p=128)
                return g.dma_start(out=x1f[t % 2][:, :].rearrange(
                    "p (c n) -> p c n", c=C), in_=src)
            wldv.append((s_st, spill_done[t]))
            vld[t] = p.emit("gpsimd", ldv, waits=wldv, inc=(s_ld, 16))

            wsi = [(s_st, stO_st)]
            if t - 2 in bc_hist:
                wsi.append((s_pe, bc_hist[t - 2]))
            sld = p.emit("sync", lambda sy, t=t: sy.dma_start(
                out=stin[t % 2][:, :], in_=stats_o[t:t + 1, :]),
                waits=wsi, inc=(s_lf, 16))

            def mk_bc(te, t=t):
                te.matmul(psB[:, 0:TOK], ones1F[:, :],
                          stin[t % 2][:, 0:TOK], start=True, stop=True)
                return te.matmul(psB[:, TOK:2 * TOK], ones1F[:, :],
                                 stin[t % 2][:, TOK:2 * TOK],
                                 start=True, stop=True)
            wbc = [(s_lf, sld)]
            if t - 1 in appl_done:
                wbc.append((s_dve, appl_done[t - 1]))
            bc_n = p.emit("tensor", mk_bc, waits=wbc, inc=(s_pe, 1))
            bc_hist[t] = bc_n

            for m in range(C):
                w1 = [(s_pe, bc_n), (s_ld, vld[t])] if m == 0 else []
                if m == 0 and t - 2 in out_stored:
                    w1.append((s_st, out_stored[t - 2]))
                p.emit("vector", lambda v, m=m, t=t: v.tensor_tensor(
                    outsb[t % 2][:, m * TOK:(m + 1) * TOK],
                    x1f[t % 2][:, m * TOK:(m + 1) * TOK], psB[:, 0:TOK],
                    op=ALU.mult), waits=w1, inc=(s_dve, 1))
                appl_done[t] = p.emit("vector", lambda v, m=m, t=t:
                                      v.tensor_tensor(
                    outsb[t % 2][:, m * TOK:(m + 1) * TOK],
                    outsb[t % 2][:, m * TOK:(m + 1) * TOK], psB[:, TOK:2 * TOK],
                    op=ALU.add), inc=(s_dve, 1))

            def stout(sy, t=t):
                dst = out_h[:, t * TOK:(t + 1) * TOK].rearrange(
                    "(c p) n -> p c n", p=128)
                return sy.dma_start(out=dst, in_=outsb[t % 2][:, :].rearrange(
                    "p (c n) -> p c n", c=C))
            out_stored[t] = p.emit("sync", stout, waits=[(s_dve, appl_done[t])],
                                   inc=(s_st, 16))

        p.emit("sync", lambda sy: None, waits=[(s_st, s_st.n)])

        for s in p.sems.values():
            s.h = es.enter_context(nc.semaphore(s.name))
        with nc.Block() as block:
            @block.tensor
            def _(te):
                p.build_engine(te, "tensor")

            @block.scalar
            def _(sc):
                p.build_engine(sc, "scalar")

            @block.vector
            def _(v):
                p.build_engine(v, "vector")

            @block.gpsimd
            def _(g):
                p.build_engine(g, "gpsimd")

            @block.sync
            def _(sy):
                p.build_engine(sy, "sync")

    return nc


def build_phase_a(ctx):
    nc, p, es = ctx["nc"], ctx["p"], ctx["es"]
    T, PS = ctx["T"], ctx["PS"]
    xT, whbm, slab, gath = ctx["xT"], ctx["whbm"], ctx["slab"], ctx["gath"]
    ones1, ones128, epst = ctx["ones1"], ctx["ones128"], ctx["epst"]
    ones1F = ctx["ones1F"]
    s_ld, s_lf, s_st = ctx["s_ld"], ctx["s_lf"], ctx["s_st"]
    s_pe, s_act, s_dve, s_cc = ctx["s_pe"], ctx["s_act"], ctx["s_dve"], ctx["s_cc"]

    aes = contextlib.ExitStack()
    ctx["aes"] = aes
    TA = lambda *a, **k: aes.enter_context(nc.sbuf_tensor(*a, **k))
    PA = lambda *a, **k: aes.enter_context(nc.psum_tensor(*a, **k))

    arena1 = TA("arena1", [128, C * S], BF16)
    xbf = arena1[:, :]
    aT = arena1
    _war = TA("warena", [128, C * C * 128], BF16)
    warena = [_war, _war]
    qT = TA("qT", [128, C * S], BF16)
    kT = TA("kT", [128, C * S], BF16)
    vP = TA("vP", [128, 16 * (H * 65)], BF16)
    exp_sb = [TA(f"exp{i}", [128, 1024], BF16) for i in range(3)]
    vtmpf = TA("vtmpf", [128, 512], F32)
    rb_row = TA("rb_row", [1, 1024], BF16)
    lnzrow = TA("lnzrow", [1, 1024], F32)
    rbB = TA("rbB", [64, 1024], BF16)
    x1sb = TA("x1sb", [128, C * 512], F32)
    xres = TA("xres", [128, C * 512], F32)
    xpb = arena1[:, 8192:8192 + 4096]
    sqA = arena1[:, 8192 + 4096:16384]
    lnS = TA("lnS", [1, 512], F32)
    lnS2 = TA("lnS2", [1, 512], F32)
    lnT = TA("lnT", [1, 512], F32)
    lnU = TA("lnU", [1, 512], F32)
    lnB = TA("lnB", [1, 1024], F32)
    neg3 = TA("neg3", [128, 1], F32)
    identF = TA("identF", [128, 128], F32)

    psSC = [PA(f"psSC{i}", [128, 1024], F32) for i in range(2)]
    psU = [PA(f"psU{i}", [128, 1024], F32) for i in range(2)]

    SLOT = [(psSC[0], 0), (psSC[0], 512), (psSC[1], 0), (psSC[1], 512)]
    slot_evac = [0, 0, 0, 0]
    slot_mm = [0, 0, 0, 0]

    def slot_ap(u, p0=0, p1=128, w=512):
        t_, off = SLOT[u % 4]
        return t_[p0:p1, off:off + w]

    p.emit("gpsimd", lambda g: g.memset(neg3[:, :], -3.0), inc=(s_ld, 1))
    p.emit("gpsimd", lambda g: g.memset(vP[:, :], 1.0), inc=(s_ld, 1))
    p.emit("gpsimd", lambda g: g.memset(identF[:, :], 0.0))
    p.emit("gpsimd", lambda g: g.affine_select(
        out=identF[:, :], in_=identF[:, :], compare_op=ALU.not_equal,
        fill=1.0, base=0, pattern=[[-1, 128]], channel_multiplier=1),
        inc=(s_ld, 1))
    aconsts = (s_ld, s_ld.n)

    PW = [("Wq", qT), ("Wk", kT), ("Wv", None), ("Wpost", None)]
    su = 0
    xbf_loaded = {}
    wload = {}
    wload_mm = {}
    aT_free = {-1: 0}
    smn_last = {}
    store_cnt = 0
    exp_hist = {}
    av_hist = {}
    tt_hist = {}
    gelu_hist = {}

    for b in range(BLOC):
        wx = [(s_pe, smn_last[b - 1])] if b - 1 in smn_last else []

        def ldx_half(sy, b=b, half=0):
            src = ctx["xbf_d"][half * 512:half * 512 + 512,
                               b * S:(b + 1) * S].rearrange(
                "(c p) n -> p c n", p=128)
            dst = arena1[:, half * 4 * S:(half + 1) * 4 * S].rearrange(
                "p (c n) -> p c n", c=4)
            return sy.dma_start(out=dst, in_=src)
        wx = wx + [(s_ld, ctx["xcast"])]
        xbf_lo = p.emit("sync", lambda sy, b=b: ldx_half(sy, b, 0),
                        waits=wx, inc=(s_lf, 16))
        xbf_hi = p.emit("sync", lambda sy, b=b: ldx_half(sy, b, 1),
                        inc=(s_lf, 16))
        xbf_loaded[b] = xbf_hi

        def emit_wload(pi, b=b):
            wwl = []
            prev = (b, pi - 1) if pi >= 1 else (b - 1, 3)
            if prev in wload_mm:
                wwl = [(s_pe, wload_mm[prev])]
            wn = PW[pi][0]
            wload[(b, pi)] = p.emit("sync", lambda sy, wn=wn, pi=pi: sy.dma_start(
                out=warena[pi % 2][:, :], in_=ctx["wbf16_d"][wn]),
                waits=wwl + [(s_ld, ctx["wcast"][wn])], inc=(s_lf, 16))
        emit_wload(0)

        for pi in range(3):
            wn, dstT = PW[pi][0], PW[pi][1]
            for m in range(C):
                wmm = [(s_lf, max(wload[(b, pi)], xbf_lo))]
                for u in range(4):
                    if slot_evac[u] > 0:
                        wmm.append((s_dve, slot_evac[u]))

                def mk_p(te, pi=pi, m=m, b=b, klo=0, khi=4):
                    inst = None
                    for k in range(klo, khi):
                        lhs = wslice(warena[pi % 2], k, m)
                        for sblk in range(4):
                            inst = te.matmul(
                                slot_ap(sblk), lhs,
                                xbf[:, k * S + sblk * 512:k * S + sblk * 512 + 512],
                                start=(k == 0), stop=(k == C - 1))
                    return inst
                p.emit("tensor", mk_p, waits=wmm, inc=(s_pe, 1))
                gn = p.emit("tensor",
                            lambda te, pi=pi, m=m, b=b: mk_p(te, pi, m, b, 4, 8),
                            waits=[(s_lf, xbf_hi)], inc=(s_pe, 1))
                wload_mm[(b, pi)] = gn

                if pi < 2:
                    for sblk in range(4):
                        def ev(v, dstT=dstT, m=m, sblk=sblk):
                            return v.tensor_scalar_mul(
                                dstT[:, m * S + sblk * 512:m * S + sblk * 512 + 512],
                                slot_ap(sblk), 1.0)
                        slot_evac[sblk] = p.emit("vector", ev, waits=[(s_pe, gn)],
                                                 inc=(s_dve, 1))
                else:
                    for sblk in range(4):
                        def evv(v, sblk=sblk):
                            return v.tensor_scalar_mul(vtmpf[:, :],
                                                       slot_ap(sblk), 1.0)
                        wev = [(s_pe, gn)]
                        if (b, m, sblk) != (0, 0, 0):
                            wev.append((s_pe, trn))
                        evn = p.emit("vector", evv, waits=wev, inc=(s_dve, 1))
                        slot_evac[sblk] = evn
                        for ti in range(4):
                            tc = sblk * 4 + ti

                            def mktr(te, ti=ti):
                                return te.transpose(
                                    psU[0][0:128, 0:128],
                                    vtmpf[:, ti * 128:(ti + 1) * 128],
                                    identF[:, :])
                            wtr = [(s_dve, evn), (s_ld, aconsts[1])]
                            if (b, m, sblk, ti) != (0, 0, 0, 0):
                                wtr.append((s_dve, vp_ev))
                            trn = p.emit("tensor", mktr, waits=wtr, inc=(s_pe, 1))

                            def evp(v, tc=tc, m=m):
                                base = tc * H * 65 + (2 * m) * 65
                                dst = vP[:, base:base + 130].rearrange(
                                    "p (h e) -> p h e", h=2)[:, :, 0:64]
                                srcv = psU[0][0:128, 0:128].rearrange(
                                    "p (h e) -> p h e", h=2)
                                return v.tensor_copy(dst, srcv)
                            vp_ev = p.emit("vector", evp, waits=[(s_pe, trn)],
                                           inc=(s_dve, 1))
            emit_wload(pi + 1)
        vp_done = vp_ev

        expz_hist = {}
        bc_hist = {}
        for sh in range(2):

            def emit_scores(sq, hp, tc):
                bank = psSC[tc % 2]
                wsc = []
                kexp = (b, sq, hp, tc - 2) if tc >= 2 else \
                       (b, sq, hp - 1, 14 + tc) if hp >= 1 else \
                       (b, sq - 1, 7, 14 + tc) if sq >= 1 else \
                       (b - 1, 3, 7, 14 + tc)
                if kexp in exp_hist:
                    wsc.append((s_act, exp_hist[kexp]))
                if hp == 0 and tc == 0 and sq % 2 == 0:
                    for u in range(4):
                        wsc.append((s_dve, slot_evac[u]))
                    if sq == 0:
                        wsc.append((s_dve, vp_done))

                def mksc(te, sq=sq, hp=hp, tc=tc, bank=bank):
                    te.matmul(bank[:, 0:512],
                              kT[0:64, hp * S + tc * 128:hp * S + tc * 128 + 128],
                              qT[0:64, hp * S + sq * 512:hp * S + sq * 512 + 512],
                              start=True, stop=True)
                    return te.matmul(
                        bank[:, 512:1024],
                        kT[64:128, hp * S + tc * 128:hp * S + tc * 128 + 128],
                        qT[64:128, hp * S + sq * 512:hp * S + sq * 512 + 512],
                        start=True, stop=True)
                return p.emit("tensor", mksc, waits=wsc, inc=(s_pe, 1))

            def emit_exp(sq, hp, tc, scn):
                bank = psSC[tc % 2]
                wex = [(s_pe, scn)]
                tcp = 15 - ((15 - tc) % 3)
                kav = (b, sq, hp, tc - 3) if tc >= 3 else \
                      (b, sq, hp - 1, tcp) if hp >= 1 else \
                      (b, sq - 1, 7, tcp) if sq >= 1 else \
                      (b - 1, 3, 7, tcp)
                if kav in av_hist:
                    wex.append((s_pe, av_hist[kav]))

                def mkex(sc, tc=tc, bank=bank):
                    return sc.activation(exp_sb[tc % 3][:, :], bank[:, :],
                                         AF.Exp, bias=neg3[:, :], scale=0.125)
                exp_hist[(b, sq, hp, tc)] = p.emit("scalar", mkex, waits=wex,
                                                   inc=(s_act, 1))

            def emit_av(sq, hp, tc):
                wav = [(s_act, exp_hist[(b, sq, hp, tc)])]
                if tc == 0:
                    ktt = (b, sq, hp - 2) if hp >= 2 else \
                          (b, sq - 1, 6 + hp) if sq >= 1 else \
                          (b - 1, 3, 6 + hp)
                    if ktt in tt_hist:
                        wav.append((s_dve, tt_hist[ktt]))

                def mkav(te, hp=hp, tc=tc):
                    te.matmul(psU[hp % 2][0:65, 0:512],
                              vP[:, tc * H * 65 + 2 * hp * 65:
                                 tc * H * 65 + 2 * hp * 65 + 65],
                              exp_sb[tc % 3][:, 0:512],
                              start=(tc == 0), stop=(tc == 15))
                    return te.matmul(
                        psU[hp % 2][0:65, 512:1024],
                        vP[:, tc * H * 65 + (2 * hp + 1) * 65:
                           tc * H * 65 + (2 * hp + 1) * 65 + 65],
                        exp_sb[tc % 3][:, 512:1024],
                        start=(tc == 0), stop=(tc == 15))
                av_hist[(b, sq, hp, tc)] = p.emit("tensor", mkav, waits=wav,
                                                  inc=(s_pe, 1))

            def emit_tail_a(sq, hp):
                ln_n = p.emit("scalar", lambda sc, hp=hp: sc.activation(
                    lnzrow[:, :], psU[hp % 2][64:65, :], AF.Ln),
                    waits=[(s_pe, av_hist[(b, sq, hp, 15)])], inc=(s_act, 1))

                def mkexpz(sc):
                    with nc.allow_low_precision(
                            reason="softmax 1/Z stored bf16; product is "
                                   "rounded to bf16 anyway; diluted by residual"):
                        return sc.activation(rb_row[:, :], lnzrow[:, :],
                                             AF.Exp, scale=-1.0)
                wz = []
                kbc = (sq, hp - 2) if hp >= 2 else (sq - 1, 6 + hp)
                if kbc in bc_hist:
                    wz.append((s_pe, bc_hist[kbc]))
                expz_hist[(sq, hp)] = p.emit("scalar", mkexpz, waits=wz,
                                             inc=(s_act, 1))

            def emit_tail_b(sq, hp, sq2):
                def mkbc(te, hp=hp):
                    inst = None
                    for si in range(2):
                        inst = te.matmul(
                            psU[hp % 2][64:128, si * 512:si * 512 + 512],
                            ones1[0:1, 0:64],
                            rb_row[:, si * 512:si * 512 + 512],
                            start=True, stop=True)
                    return inst
                bc_hist[(sq, hp)] = p.emit(
                    "tensor", mkbc, waits=[(s_act, expz_hist[(sq, hp)])],
                    inc=(s_pe, 1))
                ebn = p.emit("vector", lambda v, hp=hp: v.tensor_scalar_mul(
                    rbB[:, :], psU[hp % 2][64:128, :], 1.0),
                    waits=[(s_pe, bc_hist[(sq, hp)])], inc=(s_dve, 1))
                p.emit("vector", lambda v, hp=hp, sq2=sq2: v.tensor_tensor(
                    arena1[0:64, hp * 1024 + sq2 * 512:hp * 1024 + sq2 * 512 + 512],
                    psU[hp % 2][0:64, 0:512], rbB[:, 0:512], op=ALU.mult),
                    inc=(s_dve, 1))
                tt_hist[(b, sq, hp)] = p.emit(
                    "vector", lambda v, hp=hp, sq2=sq2: v.tensor_tensor(
                        arena1[64:128, hp * 1024 + sq2 * 512:
                               hp * 1024 + sq2 * 512 + 512],
                        psU[hp % 2][0:64, 512:1024], rbB[:, 512:1024],
                        op=ALU.mult), inc=(s_dve, 1))

            for sq2 in range(2):
                sq = sh * 2 + sq2
                for hp in range(8):
                    for tc in range(16):
                        scn = emit_scores(sq, hp, tc)
                        emit_exp(sq, hp, tc, scn)
                        if tc >= 1:
                            emit_av(sq, hp, tc - 1)
                        kprev = (sq, hp - 1) if hp >= 1 else (sq - 1, 7)
                        if kprev in av_hist or (b, kprev[0], kprev[1], 15) in av_hist:
                            pass
                        if tc == 2 and (b, kprev[0], kprev[1], 15) in av_hist:
                            if kprev not in expz_hist:
                                emit_tail_a(kprev[0], kprev[1])
                        if tc == 8 and kprev in expz_hist and kprev not in bc_hist:
                            emit_tail_b(kprev[0], kprev[1], kprev[0] % 2)
                    emit_av(sq, hp, 15)
            emit_tail_a(2 * sh + 1, 7)
            emit_tail_b(2 * sh + 1, 7, 1)

            def mkgl(sc):
                return sc.activation(arena1[:, 0:C * 1024], arena1[:, 0:C * 1024],
                                     AF.Gelu)
            gelu_hist[(b, sh)] = p.emit("scalar", mkgl,
                                        waits=[(s_dve, tt_hist[(b, 2 * sh + 1, 7)])],
                                        inc=(s_act, 1))

            for sb2 in range(2):
                sblk = sh * 2 + sb2

                def ldxr(sy, b=b, sblk=sblk):
                    src = xT[:, b * S + sblk * 512:b * S + sblk * 512 + 512]
                    src = src.rearrange("(c p) n -> p c n", p=128)
                    return sy.dma_start(out=xres[:, :].rearrange(
                        "p (c n) -> p c n", c=C), in_=src)
                wxr = [(s_dve, xres_used)] if b + sblk > 0 else []
                xrn = p.emit("sync", ldxr, waits=wxr, inc=(s_lf, 16))

                stt_hist = {}
                for m in range(C):
                    u = su % 4
                    wmm = [(s_act, gelu_hist[(b, sh)])]
                    if slot_evac[u] > 0:
                        wmm.append((s_dve, slot_evac[u]))

                    def mkwp(te, m=m, sb2=sb2, u=u):
                        inst = None
                        for k in range(C):
                            inst = te.matmul(
                                slot_ap(u), wslice(warena[1], k, m),
                                arena1[:, k * 1024 + sb2 * 512:
                                       k * 1024 + sb2 * 512 + 512],
                                start=(k == 0), stop=(k == C - 1))
                        return inst
                    gn = p.emit("tensor", mkwp, waits=wmm, inc=(s_pe, 1))
                    slot_mm[u] = gn
                    aT_free[b] = gn
                    wload_mm[(b, 3)] = gn

                    def mkstt(v, m=m, u=u):
                        return v.scalar_tensor_tensor(
                            x1sb[:, m * 512:(m + 1) * 512], xres[:, m * 512:(m + 1) * 512],
                            2.0, slot_ap(u), op0=ALU.mult, op1=ALU.add)
                    ws = [(s_pe, gn)]
                    if m == 0:
                        ws.append((s_lf, xrn))
                    slot_evac[u] = p.emit("vector", mkstt, waits=ws, inc=(s_dve, 1))
                    stt_hist[m] = slot_evac[u]
                    su += 1

                cpx = p.emit("vector", lambda v: v.tensor_copy(
                    xpb, x1sb[:, :]), waits=[(s_dve, stt_hist[C - 1])],
                    inc=(s_dve, 1))
                cps = p.emit("vector", lambda v: v.tensor_tensor(
                    sqA, xpb, xpb, op=ALU.mult), inc=(s_dve, 1))

                us_, uq = su % 4, (su + 1) % 4
                su += 2

                def mksums(te, us_=us_, uq=uq):
                    inst = None
                    for k in range(C):
                        inst = te.matmul(slot_ap(us_, 0, 1), ones128[:, :],
                                         xpb[:, k * 512:(k + 1) * 512],
                                         start=(k == 0), stop=(k == C - 1))
                    for k in range(C):
                        inst = te.matmul(slot_ap(uq, 0, 1), ones128[:, :],
                                         sqA[:, k * 512:(k + 1) * 512],
                                         start=(k == 0), stop=(k == C - 1))
                    return inst
                wsm = [(s_dve, cps)]
                if slot_evac[us_] > 0:
                    wsm.append((s_dve, slot_evac[us_]))
                if slot_evac[uq] > 0:
                    wsm.append((s_dve, slot_evac[uq]))
                smn = p.emit("tensor", mksums, waits=wsm, inc=(s_pe, 1))
                smn_last[b] = smn

                e1 = p.emit("vector", lambda v, us_=us_: v.tensor_scalar_mul(
                    lnS[:, :], slot_ap(us_, 0, 1), 1.0 / DM),
                    waits=[(s_pe, smn)], inc=(s_dve, 1))
                e2 = p.emit("vector", lambda v, uq=uq: v.tensor_scalar_mul(
                    lnS2[:, :], slot_ap(uq, 0, 1), 1.0 / DM), inc=(s_dve, 1))
                slot_evac[us_] = e1
                slot_evac[uq] = e2
                l1 = p.emit("vector", lambda v: v.tensor_tensor(
                    lnT[:, :], lnS[:, :], lnS[:, :], op=ALU.mult), inc=(s_dve, 1))
                l2 = p.emit("vector", lambda v: v.tensor_tensor(
                    lnT[:, :], lnS2[:, :], lnT[:, :], op=ALU.subtract),
                    inc=(s_dve, 1))
                l3 = p.emit("scalar", lambda sc: sc.activation(
                    lnU[:, :], lnT[:, :], AF.Ln, bias=epst[0:1, :], scale=1.0),
                    waits=[(s_dve, l2)], inc=(s_act, 1))
                l4 = p.emit("scalar", lambda sc: sc.activation(
                    lnT[:, :], lnU[:, :], AF.Exp, bias=0.0, scale=-0.5),
                    inc=(s_act, 1))
                l5 = p.emit("vector", lambda v: v.tensor_copy(
                    lnB[:, 0:512], lnT[:, :]), waits=[(s_act, l4)], inc=(s_dve, 1))
                l6 = p.emit("vector", lambda v: v.tensor_tensor(
                    lnU[:, :], lnS[:, :], lnT[:, :], op=ALU.mult), inc=(s_dve, 1))
                l7 = p.emit("vector", lambda v: v.tensor_scalar(
                    lnB[:, 512:1024], lnU[:, :], -1.0, None, op0=ALU.mult),
                    inc=(s_dve, 1))

                for wsem, wval in [(s_dve, e1), (s_dve, l2), (s_act, l3),
                                   (s_dve, l7)]:
                    def mkdum(te):
                        return te.matmul(psU[0][0:128, 0:512], kT[:, 0:128],
                                         qT[:, 0:512], start=True, stop=True)
                    p.emit("tensor", mkdum,
                           waits=[(wsem, wval),
                                  (s_dve, tt_hist[(b, 2 * sh + 1, 6)])],
                           inc=(s_pe, 1))

                ur, un = su % 4, (su + 1) % 4
                su += 2

                def mkbc2(te, ur=ur, un=un):
                    te.matmul(slot_ap(ur), ones1F[0:1, :], lnB[0:1, 0:512],
                              start=True, stop=True)
                    return te.matmul(slot_ap(un), ones1F[0:1, :], lnB[0:1, 512:1024],
                                     start=True, stop=True)
                wb2 = [(s_dve, l7)]
                if slot_evac[ur] > 0:
                    wb2.append((s_dve, slot_evac[ur]))
                if slot_evac[un] > 0:
                    wb2.append((s_dve, slot_evac[un]))
                b2n = p.emit("tensor", mkbc2, waits=wb2, inc=(s_pe, 1))

                ap_hist = {}
                for m in range(C):
                    w1 = [(s_pe, b2n)] if m == 0 else []
                    p.emit("vector", lambda v, m=m, ur=ur: v.tensor_tensor(
                        xres[:, m * 512:(m + 1) * 512],
                        x1sb[:, m * 512:(m + 1) * 512], slot_ap(ur),
                        op=ALU.mult), waits=w1, inc=(s_dve, 1))
                    ap_hist[m] = p.emit("vector", lambda v, m=m, un=un:
                                        v.tensor_tensor(
                        x1sb[:, m * 512:(m + 1) * 512],
                        xres[:, m * 512:(m + 1) * 512], slot_ap(un),
                        op=ALU.add), inc=(s_dve, 1))
                xres_used = ap_hist[C - 1]
                slot_evac[ur] = ap_hist[C - 1]
                slot_evac[un] = ap_hist[C - 1]

                def stx1(sy, b=b, sblk=sblk):
                    dst = slab[b][:, sblk * 512:sblk * 512 + 512].rearrange(
                        "(c p) n -> p c n", p=128)
                    return sy.dma_start(out=dst, in_=x1sb[:, :].rearrange(
                        "p (c n) -> p c n", c=C))
                p.emit("sync", stx1, waits=[(s_dve, ap_hist[C - 1])],
                       inc=(s_st, 16))

        p.emit("gpsimd", lambda g, b=b: g.collective_compute(
            "AllGather", ALU.bypass, replica_groups=[list(range(NCORES))],
            ins=[slab[b]], outs=[gath[b]]),
            waits=[(s_st, p.sems["s_st"].n)], inc=(s_cc, 1))

    aes.close()
    return (ctx["p"].sems["s_st"].n)



def weight_maps(inputs):
    return {
        "wfT": prechunk(np.asarray(inputs["L0_ffn_wf"], np.float32).T),
        "ufT": prechunk(np.asarray(inputs["L0_ffn_uf"], np.float32).T),
        "wiT": prechunk(np.asarray(inputs["L0_ffn_wi"], np.float32).T),
        "uiT": prechunk(np.asarray(inputs["L0_ffn_ui"], np.float32).T),
        "wcT": prechunk(np.asarray(inputs["L0_ffn_wc"], np.float32).T),
        "ucT": prechunk(np.asarray(inputs["L0_ffn_uc"], np.float32).T),
        "woT": prechunk(np.asarray(inputs["L0_ffn_wo"], np.float32).T),
        "uoT": prechunk(np.asarray(inputs["L0_ffn_uo"], np.float32).T),
        "bf": pack_bias(inputs["L0_ffn_bf"]), "bi": pack_bias(inputs["L0_ffn_bi"]),
        "bc": pack_bias(inputs["L0_ffn_bc"]), "bo": pack_bias(inputs["L0_ffn_bo"]),
        "inc0": pack_bias(inputs["L0_ffn_in_c"]),
    }


def assemble_out(results):
    full = np.empty((B, S, DM), np.float32)
    for k in range(NCORES):
        ok = np.asarray(results[k]["out"]).reshape(DM, B, TOK)
        full[:, k * TOK:(k + 1) * TOK, :] = ok.transpose(1, 2, 0)
    return full



def kernel(**inputs):
    inputs = {k: np.asarray(v, np.float32) for k, v in inputs.items()}
    base = weight_maps(inputs)
    base["Wq"] = prechunk(inputs["L0_attn_Wq"])
    base["Wk"] = prechunk(inputs["L0_attn_Wk"])
    base["Wv"] = prechunk(inputs["L0_attn_Wv"])
    base["Wpost"] = prechunk(inputs["L0_attn_Wpost"])
    X = inputs["L0_attn_X"]
    in_maps = []
    for k in range(NCORES):
        m = dict(base)
        m["xT"] = np.ascontiguousarray(
            X[[k, 8 + k, 16 + k, 24 + k]].transpose(2, 0, 1).reshape(DM, BLOC * S))
        in_maps.append(m)
    nc = build_program(mode="full")
    res = run_bass_kernel_spmd(nc, in_maps, core_ids=list(range(NCORES)))
    return assemble_out(res.results)


# version 07
# speedup vs baseline: 1.1590x; 1.0551x over previous
import contextlib
import numpy as np
import concourse.bass as bass
import concourse.mybir as mybir
from concourse.bass_utils import run_bass_kernel_spmd

F32 = mybir.dt.float32
BF16 = mybir.dt.bfloat16
AF = mybir.ActivationFunctionType
ALU = mybir.AluOpType

NCORES = 8
DM = 1024
C = DM // 128
B = 32
S = 2048
TOK = S // NCORES
BLOC = B // NCORES
H = 16
DK = 64
LNEPS = 1e-5


class Sem:
    def __init__(self, name):
        self.name = name
        self.n = 0
        self.h = None

    def inc(self, amt=1):
        self.n += amt
        return self.n


class Prog:

    def __init__(self):
        self.ops = {e: [] for e in ("tensor", "scalar", "vector", "gpsimd", "sync")}
        self.sems = {}

    def sem(self, name):
        if name not in self.sems:
            self.sems[name] = Sem(name)
        return self.sems[name]

    def emit(self, engine, fn, waits=(), inc=None):
        n = None
        if inc is not None:
            n = inc[0].inc(inc[1])
        self.ops[engine].append((tuple(waits), fn, inc))
        return n

    def build_engine(self, eng, name):
        for waits, fn, inc in self.ops[name]:
            for s, v in waits:
                if v > 0:
                    eng.wait_ge(s.h, v)
            inst = fn(eng)
            if inc is not None:
                assert inst is not None, f"inc on None inst in {name}"
                inst.then_inc(inc[0].h, inc[1])



def prechunk(w2):
    return np.ascontiguousarray(
        w2.reshape(C, 128, C, 128).transpose(1, 0, 2, 3).reshape(128, C * C * 128)
    )


def pack_bias(b):
    return np.ascontiguousarray(np.asarray(b, np.float32).reshape(C, 128).T)


def wslice(wsb, kc, mc):
    o = (kc * C + mc) * 128
    return wsb[:, o:o + 128]



def build_program(mode="full"):
    nc = bass.Bass("TRN2", target_bir_lowering=False)
    p = Prog()

    gath_kind = "ExternalInput" if mode == "bc" else "Internal"
    gath_kw = {} if mode == "bc" else {"addr_space": "Shared"}
    gath = [nc.dram_tensor(f"gath{j}", [NCORES * DM, S], F32, kind=gath_kind,
                           **gath_kw).ap()
            for j in range(BLOC)]
    out_h = nc.dram_tensor("out", [DM, B * TOK], F32, kind="ExternalOutput").ap()
    vsp = nc.dram_tensor("vsp", [DM, B * TOK], F32).ap()
    stats_d = nc.dram_tensor("stats_d", [B, 2 * TOK], F32).ap()
    stats_o = nc.dram_tensor("stats_o", [B, 2 * TOK], F32).ap()

    wnames_bc = ["wfT", "ufT", "wiT", "uiT", "wcT", "ucT", "woT", "uoT"]
    whbm = {n: nc.dram_tensor(n, [128, C * C * 128], F32, kind="ExternalInput").ap()
            for n in wnames_bc}
    bnames = ["bf", "bi", "bc", "bo", "inc0"]
    bhbm = {n: nc.dram_tensor(n, [128, C], F32, kind="ExternalInput").ap()
            for n in bnames}

    if mode == "full":
        xT = nc.dram_tensor("xT", [DM, BLOC * S], F32, kind="ExternalInput").ap()
        for n in ["Wq", "Wk", "Wv", "Wpost"]:
            whbm[n] = nc.dram_tensor(n, [128, C * C * 128], F32,
                                     kind="ExternalInput").ap()
        slab = [nc.dram_tensor(f"slab{j}", [DM, S], F32).ap() for j in range(BLOC)]
    else:
        slab = None
    wbf16_d = {n: nc.dram_tensor(n + "_bf", [128, C * C * 128], BF16).ap()
               for n in whbm}
    xbf_d = (nc.dram_tensor("xT_bf", [DM, BLOC * S], BF16).ap()
             if mode == "full" else None)

    es = contextlib.ExitStack()
    with es:
        T = lambda *a, **k: es.enter_context(nc.sbuf_tensor(*a, **k))
        PS = lambda *a, **k: es.enter_context(nc.psum_tensor(*a, **k))

        ones1 = T("ones1", [1, 128], BF16)
        ones1F = T("ones1F", [1, 128], F32)
        ones128 = T("ones128", [128, 1], BF16)
        epst = T("epst", [128, 1], F32)
        bias_sb = {n: T(f"sb_{n}", [128, C], F32) for n in bnames}

        s_ld = p.sem("s_ld")
        s_lf = p.sem("s_lf")
        s_st = p.sem("s_st")
        s_pe = p.sem("s_pe")
        s_act = p.sem("s_act")
        s_dve = p.sem("s_dve")
        s_cc = p.sem("s_cc")

        dyn = {}

        def ginit(g):
            dyn["g_off"] = g.snap(g.partition_id() * TOK)
            return None
        p.emit("gpsimd", ginit)

        def syinit(sy):
            dyn["s_off"] = sy.snap(sy.partition_id() * TOK)
            return None
        p.emit("sync", syinit)

        p.emit("gpsimd", lambda g: g.memset(ones1[:, :], 1.0), inc=(s_ld, 1))
        p.emit("gpsimd", lambda g: g.memset(ones1F[:, :], 1.0), inc=(s_ld, 1))
        p.emit("gpsimd", lambda g: g.memset(ones128[:, :], 1.0), inc=(s_ld, 1))
        p.emit("gpsimd", lambda g: g.memset(epst[:, :], LNEPS), inc=(s_ld, 1))
        consts_ready = (s_ld, s_ld.n)

        for n in bnames:
            p.emit("sync", lambda sy, n=n: sy.dma_start(
                out=bias_sb[n][:, :], in_=bhbm[n]), inc=(s_lf, 16))
        bias_ready = (s_lf, s_lf.n)

        wcast = {}
        worder = ([n for n in ("Wq", "Wk", "Wv", "Wpost") if n in whbm] +
                  [n for n in whbm if n not in ("Wq", "Wk", "Wv", "Wpost")])
        for n in worder:
            wcast[n] = p.emit("gpsimd", lambda g, n=n: g.dma_start(
                out=wbf16_d[n], in_=whbm[n]), inc=(s_ld, 16))
        if mode == "full":
            xcast = {}
            for bb_ in range(BLOC):
                xcast[bb_] = p.emit("gpsimd", lambda g, bb_=bb_: g.dma_start(
                    out=xbf_d[:, bb_ * S:(bb_ + 1) * S],
                    in_=xT[:, bb_ * S:(bb_ + 1) * S]), inc=(s_ld, 16))


        a_store_end = 0
        if mode == "full":
            actx = dict(nc=nc, p=p, es=es, T=T, PS=PS, xT=xT, whbm=whbm,
                        wbf16_d=wbf16_d, xbf_d=xbf_d, wcast=wcast, xcast=xcast,
                        slab=slab, gath=gath, ones1=ones1, ones128=ones128,
                        ones1F=ones1F,
                        epst=epst, s_ld=s_ld, s_lf=s_lf, s_st=s_st, s_pe=s_pe,
                        s_act=s_act, s_dve=s_dve, s_cc=s_cc)
            a_store_end = build_phase_a(actx)


        w_sb = {n: T(f"w_{n}", [128, C * C * 128], BF16)
                for n in ["wfT", "ufT", "wiT", "uiT", "wcT", "ucT"]}
        w_sb["woT"] = w_sb["wfT"]
        w_sb["uoT"] = w_sb["ufT"]
        c_sb = T("c_sb", [128, C * TOK], F32)
        cbf = T("cbf", [128, C * TOK], BF16)
        xt_sb = [T(f"xt{i}", [128, C * TOK], BF16) for i in range(2)]
        gts = [T(f"gts{i}", [128, 3 * TOK], F32) for i in range(2)]
        tmp1 = T("tmp1", [128, TOK], F32)
        tmp2 = T("tmp2", [128, TOK], F32)
        uoc = T("uoc", [128, C * TOK], F32)
        x1f = [T(f"x1f{i}", [128, C * TOK], F32) for i in range(2)]
        x1b = [T(f"x1b{i}", [128, C * TOK], BF16) for i in range(2)]
        hv = T("hv", [128, C * TOK], F32)
        vb = T("vb", [128, C * TOK], BF16)
        sqb = T("sqb", [128, C * TOK], BF16)
        sst = [T(f"sst{i}", [1, 2 * TOK], F32) for i in range(2)]
        stT = T("stT", [B, 2 * TOK], F32)
        stX = T("stX", [B, TOK], F32)
        stY = T("stY", [B, TOK], F32)
        stO = T("stO", [B, 2 * TOK], F32)
        stin = [T(f"stin{i}", [1, 2 * TOK], F32) for i in range(2)]
        outsb = [T(f"outsb{i}", [128, C * TOK], F32) for i in range(2)]

        psG = [PS(f"psG{i}", [128, 512], F32) for i in range(4)]
        psS = PS("psS", [1, 512], F32)
        psB = PS("psB", [128, 512], F32)

        wA = [(s_st, a_store_end)] if mode == "full" else []
        for n in ["wfT", "ufT", "wiT", "uiT", "wcT", "ucT"]:
            p.emit("sync", lambda sy, n=n: sy.dma_start(
                out=w_sb[n][:, :], in_=wbf16_d[n]),
                waits=wA + [(s_ld, wcast[n])], inc=(s_lf, 16))
        wB_ready = (s_lf, s_lf.n)

        wc0 = [bias_ready] + ([(s_st, a_store_end)] if mode == "full" else [])
        p.emit("vector", lambda v: v.memset(c_sb[:, :], 0.0),
               waits=wc0, inc=(s_dve, 1))
        for c in range(C):
            p.emit("vector", lambda v, c=c: v.tensor_scalar_add(
                c_sb[:, c * TOK:(c + 1) * TOK], c_sb[:, c * TOK:(c + 1) * TOK],
                bias_sb["inc0"][:, c:c + 1]), inc=(s_dve, 1))

        GATES = [("wfT", "ufT", "bf", AF.Sigmoid), ("wiT", "uiT", "bi", AF.Sigmoid),
                 ("wcT", "ucT", "bc", AF.Tanh)]
        pe_group = {}
        act_of_group = {}
        xt_used = {}
        cupd = {}
        cupd[(-1, C - 1)] = s_dve.n
        cupd[(-1, C - 2)] = s_dve.n
        xt_loaded = {}
        ggrp = 0

        for t in range(B):
            j, r = t // NCORES, t % NCORES
            w = [(s_pe, xt_used[t - 2])] if t - 2 in xt_used else []
            if mode == "full":
                w = w + [(s_cc, j + 1), (s_st, a_store_end)]

            def ldxt(g, j=j, r=r, t=t):
                src = gath[j][r * DM:(r + 1) * DM, bass.ds(dyn["g_off"], TOK)]
                src = src.rearrange("(c p) n -> p c n", p=128)
                return g.dma_start(out=xt_sb[t % 2][:, :].rearrange(
                    "p (c n) -> p c n", c=C), in_=src)
            xt_loaded[t] = p.emit("gpsimd", ldxt, waits=w, inc=(s_ld, 16))

            wv = [(s_dve, cupd[(t - 1, C - 1)])]
            if t > 0:
                wv.append((s_pe, pe_group[(t - 1, C - 1, 2)]))
            cbf_n = p.emit("vector", lambda v: v.tensor_copy(cbf[:, :], c_sb[:, :]),
                           waits=wv, inc=(s_dve, 1))

            for m in range(C):
                for gi, (wn, un, bn, af) in enumerate(GATES):
                    bank = psG[ggrp % 4]
                    wmm = []
                    if m == 0 and gi == 0:
                        wmm = [(s_ld, xt_loaded[t]), (s_lf, wB_ready[1]),
                               (s_dve, cbf_n)]
                    if ggrp - 4 >= 0 and (ggrp - 4) in act_of_group:
                        wmm.append((s_act, act_of_group[ggrp - 4]))

                    def mk_mms(te, wn=wn, un=un, m=m, t=t, bank=bank):
                        inst = None
                        for k in range(C):
                            inst = te.matmul(
                                bank[:, 0:TOK], wslice(w_sb[wn], k, m),
                                xt_sb[t % 2][:, k * TOK:(k + 1) * TOK],
                                start=(k == 0), stop=False)
                        for k in range(C):
                            inst = te.matmul(
                                bank[:, 0:TOK], wslice(w_sb[un], k, m),
                                cbf[:, k * TOK:(k + 1) * TOK],
                                start=False, stop=(k == C - 1))
                        return inst
                    pe_group[(t, m, gi)] = p.emit("tensor", mk_mms, waits=wmm,
                                                  inc=(s_pe, 1))

                    wact = [(s_pe, pe_group[(t, m, gi)])]
                    key = (t, m - 2) if m >= 2 else (t - 1, C - 2 + m)
                    if key in cupd:
                        wact.append((s_dve, cupd[key]))

                    def mk_act(sc, af=af, bn=bn, m=m, gi=gi, bank=bank):
                        return sc.activation(
                            gts[m % 2][:, gi * TOK:(gi + 1) * TOK], bank[:, 0:TOK],
                            af, bias=bias_sb[bn][:, m:m + 1], scale=1.0)
                    act_of_group[ggrp] = p.emit("scalar", mk_act, waits=wact,
                                                inc=(s_act, 1))
                    ggrp += 1

                wcu = [(s_act, s_act.n)]
                cs_lo, cs_hi = m * TOK, (m + 1) * TOK
                g0 = gts[m % 2]
                p.emit("vector", lambda v, g0=g0, lo=cs_lo, hi=cs_hi:
                       v.tensor_tensor(tmp1[:, :], g0[:, 0:TOK], c_sb[:, lo:hi],
                                       op=ALU.mult), waits=wcu, inc=(s_dve, 1))
                p.emit("vector", lambda v, g0=g0: v.tensor_tensor(
                    tmp2[:, :], g0[:, TOK:2 * TOK], g0[:, 2 * TOK:3 * TOK],
                    op=ALU.mult), inc=(s_dve, 1))
                cupd[(t, m)] = p.emit("vector", lambda v, lo=cs_lo, hi=cs_hi:
                                      v.tensor_tensor(c_sb[:, lo:hi], tmp1[:, :],
                                                      tmp2[:, :], op=ALU.add),
                                      inc=(s_dve, 1))
            xt_used[t] = pe_group[(t, C - 1, 2)]

        for n in ["woT", "uoT"]:
            p.emit("sync", lambda sy, n=n: sy.dma_start(
                out=w_sb[n][:, :], in_=wbf16_d[n]),
                waits=[(s_pe, pe_group[(B - 1, C - 1, 2)]),
                       (s_ld, wcast[n])], inc=(s_lf, 16))
        wC_ready = (s_lf, s_lf.n)

        cbf_fin = p.emit("vector", lambda v: v.tensor_copy(cbf[:, :], c_sb[:, :]),
                         waits=[(s_dve, cupd[(B - 1, C - 1)]),
                                (s_pe, pe_group[(B - 1, C - 1, 2)])],
                         inc=(s_dve, 1))

        uoc_done = {}
        for m in range(C):
            bank = psG[m % 2]

            def mk_uoc(te, m=m, bank=bank):
                inst = None
                for k in range(C):
                    inst = te.matmul(bank[:, 0:TOK], wslice(w_sb["uoT"], k, m),
                                     cbf[:, k * TOK:(k + 1) * TOK],
                                     start=(k == 0), stop=(k == C - 1))
                return inst
            wmm = [(s_dve, cbf_fin), (s_lf, wC_ready[1])]
            if m >= 2:
                wmm.append((s_dve, uoc_done[m - 2]))
            gn = p.emit("tensor", mk_uoc, waits=wmm, inc=(s_pe, 1))
            uoc_done[m] = p.emit("vector", lambda v, m=m, bank=bank:
                                 v.tensor_scalar_mul(
                                     uoc[:, m * TOK:(m + 1) * TOK],
                                     bank[:, 0:TOK], 1.0),
                                 waits=[(s_pe, gn)], inc=(s_dve, 1))

        x1_loaded = {}
        x1b_loaded = {}
        wo_done = {}
        a1_hist = {}
        a2_hist = {}
        vadds_last = {}
        spill_done = {}
        e2_hist = {}
        st_stored = {}
        for t in range(B):
            j, r = t // NCORES, t % NCORES
            wld = [(s_dve, vadds_last[t - 2])] if t - 2 in vadds_last else []

            def ldf(sy, j=j, r=r, t=t):
                src = gath[j][r * DM:(r + 1) * DM, bass.ds(dyn["s_off"], TOK)]
                src = src.rearrange("(c p) n -> p c n", p=128)
                return sy.dma_start(out=x1f[t % 2][:, :].rearrange(
                    "p (c n) -> p c n", c=C), in_=src)
            x1_loaded[t] = p.emit("sync", ldf, waits=wld, inc=(s_lf, 16))

            wldb = [(s_pe, wo_done[t - 2])] if t - 2 in wo_done else []

            def ldb(g, j=j, r=r, t=t):
                src = gath[j][r * DM:(r + 1) * DM, bass.ds(dyn["g_off"], TOK)]
                src = src.rearrange("(c p) n -> p c n", p=128)
                return g.dma_start(out=x1b[t % 2][:, :].rearrange(
                    "p (c n) -> p c n", c=C), in_=src)
            x1b_loaded[t] = p.emit("gpsimd", ldb, waits=wldb, inc=(s_ld, 16))

            for m in range(C):
                bank = psG[2 + (m % 2)]
                wmm = [(s_ld, x1b_loaded[t])]
                key = (t, m - 2) if m >= 2 else (t - 1, C - 2 + m)
                if key in a1_hist:
                    wmm.append((s_dve, a1_hist[key]))
                if t == 0 and m < 2:
                    wmm.append((s_dve, cbf_fin))

                def mk_wo(te, m=m, t=t, bank=bank):
                    inst = None
                    for k in range(C):
                        inst = te.matmul(bank[:, 0:TOK], wslice(w_sb["woT"], k, m),
                                         x1b[t % 2][:, k * TOK:(k + 1) * TOK],
                                         start=(k == 0), stop=(k == C - 1))
                    return inst
                gn = p.emit("tensor", mk_wo, waits=wmm, inc=(s_pe, 1))
                wo_done[t] = gn

                tmpX = [tmp1, tmp2][m % 2]
                wa1 = [(s_pe, gn)]
                keyp = (t, m - 2) if m >= 2 else (t - 1, C - 2 + m)
                if keyp in a2_hist:
                    wa1.append((s_act, a2_hist[keyp]))
                a1_hist[(t, m)] = p.emit(
                    "vector", lambda v, m=m, bank=bank, tmpX=tmpX:
                    v.tensor_tensor(tmpX[:, :], bank[:, 0:TOK],
                                    uoc[:, m * TOK:(m + 1) * TOK], op=ALU.add),
                    waits=wa1, inc=(s_dve, 1))

                wa2 = [(s_dve, a1_hist[(t, m)])]
                if m == 0 and (t - 1) in spill_done:
                    wa2.append((s_st, spill_done[t - 1]))
                a2_hist[(t, m)] = p.emit(
                    "scalar", lambda sc, m=m, tmpX=tmpX: sc.activation(
                        hv[:, m * TOK:(m + 1) * TOK], tmpX[:, :], AF.Tanh,
                        bias=bias_sb["bo"][:, m:m + 1], scale=1.0),
                    waits=wa2, inc=(s_act, 1))

            for m in range(C):
                wv = [(s_act, a2_hist[(t, m)])]
                if m == 0:
                    wv.append((s_lf, x1_loaded[t]))
                n = p.emit("vector", lambda v, m=m, t=t: v.tensor_tensor(
                    hv[:, m * TOK:(m + 1) * TOK], hv[:, m * TOK:(m + 1) * TOK],
                    x1f[t % 2][:, m * TOK:(m + 1) * TOK], op=ALU.add),
                    waits=wv, inc=(s_dve, 1))
            vadds_last[t] = n

            cp1 = p.emit("vector", lambda v: v.tensor_copy(vb[:, :], hv[:, :]),
                         inc=(s_dve, 1))
            cp2 = p.emit("vector", lambda v: v.tensor_tensor(
                sqb[:, :], vb[:, :], vb[:, :], op=ALU.mult), inc=(s_dve, 1))

            def mk_sums(te):
                inst = None
                for c in range(C):
                    inst = te.matmul(psS[:, 0:TOK], ones128[:, :],
                                     vb[:, c * TOK:(c + 1) * TOK],
                                     start=(c == 0), stop=(c == C - 1))
                for c in range(C):
                    inst = te.matmul(psS[:, TOK:2 * TOK], ones128[:, :],
                                     sqb[:, c * TOK:(c + 1) * TOK],
                                     start=(c == 0), stop=(c == C - 1))
                return inst
            wsum = [(s_dve, cp2), (s_ld, consts_ready[1])]
            if t - 1 in e2_hist:
                wsum.append((s_dve, e2_hist[t - 1]))
            sums_n = p.emit("tensor", mk_sums, waits=wsum, inc=(s_pe, 1))

            we1 = [(s_pe, sums_n)]
            if t - 2 in st_stored:
                we1.append((s_st, st_stored[t - 2]))
            p.emit("vector", lambda v, t=t: v.tensor_scalar_mul(
                sst[t % 2][:, 0:TOK], psS[:, 0:TOK], 1.0 / DM),
                waits=we1, inc=(s_dve, 1))
            e2_hist[t] = p.emit("vector", lambda v, t=t: v.tensor_scalar_mul(
                sst[t % 2][:, TOK:2 * TOK], psS[:, TOK:2 * TOK], 1.0 / DM),
                inc=(s_dve, 1))
            st_stored[t] = p.emit("sync", lambda sy, t=t: sy.dma_start(
                out=stats_d[t:t + 1, :], in_=sst[t % 2][:, :]),
                waits=[(s_dve, e2_hist[t])], inc=(s_st, 16))

            def spv(sy, t=t):
                dst = vsp[:, t * TOK:(t + 1) * TOK].rearrange(
                    "(c p) n -> p c n", p=128)
                return sy.dma_start(out=dst, in_=hv[:, :].rearrange(
                    "p (c n) -> p c n", c=C))
            spill_done[t] = p.emit("sync", spv, waits=[(s_dve, vadds_last[t])],
                                   inc=(s_st, 16))

        st_ld = p.emit("sync", lambda sy: sy.dma_start(
            out=stT[:, :], in_=stats_d),
            waits=[(s_st, st_stored[B - 1])], inc=(s_lf, 16))
        rb1 = p.emit("vector", lambda v: v.tensor_tensor(
            stX[:, :], stT[:, 0:TOK], stT[:, 0:TOK], op=ALU.mult),
            waits=[(s_lf, st_ld)], inc=(s_dve, 1))
        rb2 = p.emit("vector", lambda v: v.tensor_tensor(
            stX[:, :], stT[:, TOK:2 * TOK], stX[:, :], op=ALU.subtract),
            inc=(s_dve, 1))
        rb3 = p.emit("scalar", lambda sc: sc.activation(
            stY[:, :], stX[:, :], AF.Ln, bias=epst[0:B, :], scale=1.0),
            waits=[(s_dve, rb2)], inc=(s_act, 1))
        rb4 = p.emit("scalar", lambda sc: sc.activation(
            stX[:, :], stY[:, :], AF.Exp, bias=0.0, scale=-0.5), inc=(s_act, 1))
        rb5 = p.emit("vector", lambda v: v.tensor_copy(
            stO[:, 0:TOK], stX[:, :]), waits=[(s_act, rb4)], inc=(s_dve, 1))
        rb6 = p.emit("vector", lambda v: v.tensor_tensor(
            stY[:, :], stT[:, 0:TOK], stX[:, :], op=ALU.mult), inc=(s_dve, 1))
        rb7 = p.emit("vector", lambda v: v.tensor_scalar(
            stO[:, TOK:2 * TOK], stY[:, :], -1.0, None, op0=ALU.mult),
            inc=(s_dve, 1))
        stO_st = p.emit("sync", lambda sy: sy.dma_start(
            out=stats_o, in_=stO[:, :]), waits=[(s_dve, rb7)], inc=(s_st, 16))

        out_stored = {}
        appl_done = {}
        bc_hist = {}
        vld = {}
        for t in range(B):
            wldv = ([(s_dve, appl_done[t - 2])] if t - 2 in appl_done
                    else [(s_dve, vadds_last[B - 2 + t])])

            def ldv(g, t=t):
                src = vsp[:, t * TOK:(t + 1) * TOK].rearrange(
                    "(c p) n -> p c n", p=128)
                return g.dma_start(out=x1f[t % 2][:, :].rearrange(
                    "p (c n) -> p c n", c=C), in_=src)
            wldv.append((s_st, spill_done[t]))
            vld[t] = p.emit("gpsimd", ldv, waits=wldv, inc=(s_ld, 16))

            wsi = [(s_st, stO_st)]
            if t - 2 in bc_hist:
                wsi.append((s_pe, bc_hist[t - 2]))
            sld = p.emit("sync", lambda sy, t=t: sy.dma_start(
                out=stin[t % 2][:, :], in_=stats_o[t:t + 1, :]),
                waits=wsi, inc=(s_lf, 16))

            def mk_bc(te, t=t):
                te.matmul(psB[:, 0:TOK], ones1F[:, :],
                          stin[t % 2][:, 0:TOK], start=True, stop=True)
                return te.matmul(psB[:, TOK:2 * TOK], ones1F[:, :],
                                 stin[t % 2][:, TOK:2 * TOK],
                                 start=True, stop=True)
            wbc = [(s_lf, sld)]
            if t - 1 in appl_done:
                wbc.append((s_dve, appl_done[t - 1]))
            bc_n = p.emit("tensor", mk_bc, waits=wbc, inc=(s_pe, 1))
            bc_hist[t] = bc_n

            for m in range(C):
                w1 = [(s_pe, bc_n), (s_ld, vld[t])] if m == 0 else []
                if m == 0 and t - 2 in out_stored:
                    w1.append((s_st, out_stored[t - 2]))
                p.emit("vector", lambda v, m=m, t=t: v.tensor_tensor(
                    outsb[t % 2][:, m * TOK:(m + 1) * TOK],
                    x1f[t % 2][:, m * TOK:(m + 1) * TOK], psB[:, 0:TOK],
                    op=ALU.mult), waits=w1, inc=(s_dve, 1))
                appl_done[t] = p.emit("vector", lambda v, m=m, t=t:
                                      v.tensor_tensor(
                    outsb[t % 2][:, m * TOK:(m + 1) * TOK],
                    outsb[t % 2][:, m * TOK:(m + 1) * TOK], psB[:, TOK:2 * TOK],
                    op=ALU.add), inc=(s_dve, 1))

            def stout(sy, t=t):
                dst = out_h[:, t * TOK:(t + 1) * TOK].rearrange(
                    "(c p) n -> p c n", p=128)
                return sy.dma_start(out=dst, in_=outsb[t % 2][:, :].rearrange(
                    "p (c n) -> p c n", c=C))
            out_stored[t] = p.emit("sync", stout, waits=[(s_dve, appl_done[t])],
                                   inc=(s_st, 16))

        p.emit("sync", lambda sy: None, waits=[(s_st, s_st.n)])

        for s in p.sems.values():
            s.h = es.enter_context(nc.semaphore(s.name))
        with nc.Block() as block:
            @block.tensor
            def _(te):
                p.build_engine(te, "tensor")

            @block.scalar
            def _(sc):
                p.build_engine(sc, "scalar")

            @block.vector
            def _(v):
                p.build_engine(v, "vector")

            @block.gpsimd
            def _(g):
                p.build_engine(g, "gpsimd")

            @block.sync
            def _(sy):
                p.build_engine(sy, "sync")

    return nc


def build_phase_a(ctx):
    nc, p, es = ctx["nc"], ctx["p"], ctx["es"]
    T, PS = ctx["T"], ctx["PS"]
    xT, whbm, slab, gath = ctx["xT"], ctx["whbm"], ctx["slab"], ctx["gath"]
    ones1, ones128, epst = ctx["ones1"], ctx["ones128"], ctx["epst"]
    ones1F = ctx["ones1F"]
    s_ld, s_lf, s_st = ctx["s_ld"], ctx["s_lf"], ctx["s_st"]
    s_pe, s_act, s_dve, s_cc = ctx["s_pe"], ctx["s_act"], ctx["s_dve"], ctx["s_cc"]

    aes = contextlib.ExitStack()
    ctx["aes"] = aes
    TA = lambda *a, **k: aes.enter_context(nc.sbuf_tensor(*a, **k))
    PA = lambda *a, **k: aes.enter_context(nc.psum_tensor(*a, **k))

    arena1 = TA("arena1", [128, C * S], BF16)
    xbf = arena1[:, :]
    aT = arena1
    _war = TA("warena", [128, C * C * 128], BF16)
    warena = [_war, _war]
    qT = TA("qT", [128, C * S], BF16)
    kT = TA("kT", [128, C * S], BF16)
    vP = TA("vP", [128, 16 * (H * 65)], BF16)
    exp_sb = [TA(f"exp{i}", [128, 1024], BF16) for i in range(3)]
    vtmpf = TA("vtmpf", [128, 512], F32)
    rb_row = TA("rb_row", [1, 1024], BF16)
    lnzrow = TA("lnzrow", [1, 1024], F32)
    rbB = TA("rbB", [64, 1024], BF16)
    x1sb = TA("x1sb", [128, C * 512], F32)
    xres = TA("xres", [128, C * 512], F32)
    xpb = arena1[:, 8192:8192 + 4096]
    sqA = arena1[:, 8192 + 4096:16384]
    lnS = TA("lnS", [1, 512], F32)
    lnS2 = TA("lnS2", [1, 512], F32)
    lnT = TA("lnT", [1, 512], F32)
    lnU = TA("lnU", [1, 512], F32)
    lnB = TA("lnB", [1, 1024], F32)
    neg3 = TA("neg3", [128, 1], F32)
    identF = TA("identF", [128, 128], F32)

    psSC = [PA(f"psSC{i}", [128, 1024], F32) for i in range(2)]
    psU = [PA(f"psU{i}", [128, 1024], F32) for i in range(2)]

    SLOT = [(psSC[0], 0), (psSC[0], 512), (psSC[1], 0), (psSC[1], 512)]
    slot_evac = [0, 0, 0, 0]
    slot_mm = [0, 0, 0, 0]

    def slot_ap(u, p0=0, p1=128, w=512):
        t_, off = SLOT[u % 4]
        return t_[p0:p1, off:off + w]

    p.emit("gpsimd", lambda g: g.memset(neg3[:, :], -3.0), inc=(s_ld, 1))
    p.emit("gpsimd", lambda g: g.memset(vP[:, :], 1.0), inc=(s_ld, 1))
    p.emit("gpsimd", lambda g: g.memset(identF[:, :], 0.0))
    p.emit("gpsimd", lambda g: g.affine_select(
        out=identF[:, :], in_=identF[:, :], compare_op=ALU.not_equal,
        fill=1.0, base=0, pattern=[[-1, 128]], channel_multiplier=1),
        inc=(s_ld, 1))
    aconsts = (s_ld, s_ld.n)

    PW = [("Wq", qT), ("Wk", kT), ("Wv", None), ("Wpost", None)]
    su = 0
    xbf_loaded = {}
    wload = {}
    wload_mm = {}
    aT_free = {-1: 0}
    smn_last = {}
    store_cnt = 0
    exp_hist = {}
    av_hist = {}
    tt_hist = {}
    gelu_hist = {}

    for b in range(BLOC):
        wx = [(s_pe, smn_last[b - 1])] if b - 1 in smn_last else []

        def ldx_half(sy, b=b, half=0):
            src = ctx["xbf_d"][half * 512:half * 512 + 512,
                               b * S:(b + 1) * S].rearrange(
                "(c p) n -> p c n", p=128)
            dst = arena1[:, half * 4 * S:(half + 1) * 4 * S].rearrange(
                "p (c n) -> p c n", c=4)
            return sy.dma_start(out=dst, in_=src)
        wx = wx + [(s_ld, ctx["xcast"][b])]
        xbf_lo = p.emit("sync", lambda sy, b=b: ldx_half(sy, b, 0),
                        waits=wx, inc=(s_lf, 16))
        xbf_hi = p.emit("sync", lambda sy, b=b: ldx_half(sy, b, 1),
                        inc=(s_lf, 16))
        xbf_loaded[b] = xbf_hi

        def emit_wload(pi, b=b):
            wwl = []
            prev = (b, pi - 1) if pi >= 1 else (b - 1, 3)
            if prev in wload_mm:
                wwl = [(s_pe, wload_mm[prev])]
            wn = PW[pi][0]
            wload[(b, pi)] = p.emit("sync", lambda sy, wn=wn, pi=pi: sy.dma_start(
                out=warena[pi % 2][:, :], in_=ctx["wbf16_d"][wn]),
                waits=wwl + [(s_ld, ctx["wcast"][wn])], inc=(s_lf, 16))
        emit_wload(0)

        for pi in range(3):
            wn, dstT = PW[pi][0], PW[pi][1]
            for m in range(C):
                wmm = [(s_lf, max(wload[(b, pi)], xbf_lo))]
                for u in range(4):
                    if slot_evac[u] > 0:
                        wmm.append((s_dve, slot_evac[u]))

                def mk_p(te, pi=pi, m=m, b=b, klo=0, khi=4):
                    inst = None
                    for k in range(klo, khi):
                        lhs = wslice(warena[pi % 2], k, m)
                        for sblk in range(4):
                            inst = te.matmul(
                                slot_ap(sblk), lhs,
                                xbf[:, k * S + sblk * 512:k * S + sblk * 512 + 512],
                                start=(k == 0), stop=(k == C - 1))
                    return inst
                p.emit("tensor", mk_p, waits=wmm, inc=(s_pe, 1))
                gn = p.emit("tensor",
                            lambda te, pi=pi, m=m, b=b: mk_p(te, pi, m, b, 4, 8),
                            waits=[(s_lf, xbf_hi)], inc=(s_pe, 1))
                wload_mm[(b, pi)] = gn

                if pi < 2:
                    for sblk in range(4):
                        def ev(v, dstT=dstT, m=m, sblk=sblk):
                            return v.tensor_scalar_mul(
                                dstT[:, m * S + sblk * 512:m * S + sblk * 512 + 512],
                                slot_ap(sblk), 1.0)
                        slot_evac[sblk] = p.emit("vector", ev, waits=[(s_pe, gn)],
                                                 inc=(s_dve, 1))
                else:
                    for sblk in range(4):
                        def evv(v, sblk=sblk):
                            return v.tensor_scalar_mul(vtmpf[:, :],
                                                       slot_ap(sblk), 1.0)
                        wev = [(s_pe, gn)]
                        if (b, m, sblk) != (0, 0, 0):
                            wev.append((s_pe, trn))
                        evn = p.emit("vector", evv, waits=wev, inc=(s_dve, 1))
                        slot_evac[sblk] = evn
                        for ti in range(4):
                            tc = sblk * 4 + ti

                            def mktr(te, ti=ti):
                                return te.transpose(
                                    psU[0][0:128, 0:128],
                                    vtmpf[:, ti * 128:(ti + 1) * 128],
                                    identF[:, :])
                            wtr = [(s_dve, evn), (s_ld, aconsts[1])]
                            if (b, m, sblk, ti) != (0, 0, 0, 0):
                                wtr.append((s_dve, vp_ev))
                            trn = p.emit("tensor", mktr, waits=wtr, inc=(s_pe, 1))

                            def evp(v, tc=tc, m=m):
                                base = tc * H * 65 + (2 * m) * 65
                                dst = vP[:, base:base + 130].rearrange(
                                    "p (h e) -> p h e", h=2)[:, :, 0:64]
                                srcv = psU[0][0:128, 0:128].rearrange(
                                    "p (h e) -> p h e", h=2)
                                return v.tensor_copy(dst, srcv)
                            vp_ev = p.emit("vector", evp, waits=[(s_pe, trn)],
                                           inc=(s_dve, 1))
            emit_wload(pi + 1)
        vp_done = vp_ev

        expz_hist = {}
        bc_hist = {}
        for sh in range(2):

            def emit_scores(sq, hp, tc):
                bank = psSC[tc % 2]
                wsc = []
                kexp = (b, sq, hp, tc - 2) if tc >= 2 else \
                       (b, sq, hp - 1, 14 + tc) if hp >= 1 else \
                       (b, sq - 1, 7, 14 + tc) if sq >= 1 else \
                       (b - 1, 3, 7, 14 + tc)
                if kexp in exp_hist:
                    wsc.append((s_act, exp_hist[kexp]))
                if hp == 0 and tc == 0 and sq % 2 == 0:
                    for u in range(4):
                        wsc.append((s_dve, slot_evac[u]))
                    if sq == 0:
                        wsc.append((s_dve, vp_done))

                def mksc(te, sq=sq, hp=hp, tc=tc, bank=bank):
                    te.matmul(bank[:, 0:512],
                              kT[0:64, hp * S + tc * 128:hp * S + tc * 128 + 128],
                              qT[0:64, hp * S + sq * 512:hp * S + sq * 512 + 512],
                              start=True, stop=True)
                    return te.matmul(
                        bank[:, 512:1024],
                        kT[64:128, hp * S + tc * 128:hp * S + tc * 128 + 128],
                        qT[64:128, hp * S + sq * 512:hp * S + sq * 512 + 512],
                        start=True, stop=True)
                return p.emit("tensor", mksc, waits=wsc, inc=(s_pe, 1))

            def emit_exp(sq, hp, tc, scn):
                bank = psSC[tc % 2]
                wex = [(s_pe, scn)]
                tcp = 15 - ((15 - tc) % 3)
                kav = (b, sq, hp, tc - 3) if tc >= 3 else \
                      (b, sq, hp - 1, tcp) if hp >= 1 else \
                      (b, sq - 1, 7, tcp) if sq >= 1 else \
                      (b - 1, 3, 7, tcp)
                if kav in av_hist:
                    wex.append((s_pe, av_hist[kav]))

                def mkex(sc, tc=tc, bank=bank):
                    return sc.activation(exp_sb[tc % 3][:, :], bank[:, :],
                                         AF.Exp, bias=neg3[:, :], scale=0.125)
                exp_hist[(b, sq, hp, tc)] = p.emit("scalar", mkex, waits=wex,
                                                   inc=(s_act, 1))

            def emit_av(sq, hp, tc):
                wav = [(s_act, exp_hist[(b, sq, hp, tc)])]
                if tc == 0:
                    ktt = (b, sq, hp - 2) if hp >= 2 else \
                          (b, sq - 1, 6 + hp) if sq >= 1 else \
                          (b - 1, 3, 6 + hp)
                    if ktt in tt_hist:
                        wav.append((s_dve, tt_hist[ktt]))

                def mkav(te, hp=hp, tc=tc):
                    te.matmul(psU[hp % 2][0:65, 0:512],
                              vP[:, tc * H * 65 + 2 * hp * 65:
                                 tc * H * 65 + 2 * hp * 65 + 65],
                              exp_sb[tc % 3][:, 0:512],
                              start=(tc == 0), stop=(tc == 15))
                    return te.matmul(
                        psU[hp % 2][0:65, 512:1024],
                        vP[:, tc * H * 65 + (2 * hp + 1) * 65:
                           tc * H * 65 + (2 * hp + 1) * 65 + 65],
                        exp_sb[tc % 3][:, 512:1024],
                        start=(tc == 0), stop=(tc == 15))
                av_hist[(b, sq, hp, tc)] = p.emit("tensor", mkav, waits=wav,
                                                  inc=(s_pe, 1))

            def emit_tail_a(sq, hp):
                ln_n = p.emit("scalar", lambda sc, hp=hp: sc.activation(
                    lnzrow[:, :], psU[hp % 2][64:65, :], AF.Ln),
                    waits=[(s_pe, av_hist[(b, sq, hp, 15)])], inc=(s_act, 1))

                def mkexpz(sc):
                    with nc.allow_low_precision(
                            reason="softmax 1/Z stored bf16; product is "
                                   "rounded to bf16 anyway; diluted by residual"):
                        return sc.activation(rb_row[:, :], lnzrow[:, :],
                                             AF.Exp, scale=-1.0)
                wz = []
                kbc = (sq, hp - 2) if hp >= 2 else (sq - 1, 6 + hp)
                if kbc in bc_hist:
                    wz.append((s_pe, bc_hist[kbc]))
                expz_hist[(sq, hp)] = p.emit("scalar", mkexpz, waits=wz,
                                             inc=(s_act, 1))

            def emit_tail_b(sq, hp, sq2):
                def mkbc(te, hp=hp):
                    inst = None
                    for si in range(2):
                        inst = te.matmul(
                            psU[hp % 2][64:128, si * 512:si * 512 + 512],
                            ones1[0:1, 0:64],
                            rb_row[:, si * 512:si * 512 + 512],
                            start=True, stop=True)
                    return inst
                bc_hist[(sq, hp)] = p.emit(
                    "tensor", mkbc, waits=[(s_act, expz_hist[(sq, hp)])],
                    inc=(s_pe, 1))
                ebn = p.emit("vector", lambda v, hp=hp: v.tensor_scalar_mul(
                    rbB[:, :], psU[hp % 2][64:128, :], 1.0),
                    waits=[(s_pe, bc_hist[(sq, hp)])], inc=(s_dve, 1))
                p.emit("vector", lambda v, hp=hp, sq2=sq2: v.tensor_tensor(
                    arena1[0:64, hp * 1024 + sq2 * 512:hp * 1024 + sq2 * 512 + 512],
                    psU[hp % 2][0:64, 0:512], rbB[:, 0:512], op=ALU.mult),
                    inc=(s_dve, 1))
                tt_hist[(b, sq, hp)] = p.emit(
                    "vector", lambda v, hp=hp, sq2=sq2: v.tensor_tensor(
                        arena1[64:128, hp * 1024 + sq2 * 512:
                               hp * 1024 + sq2 * 512 + 512],
                        psU[hp % 2][0:64, 512:1024], rbB[:, 512:1024],
                        op=ALU.mult), inc=(s_dve, 1))

            for sq2 in range(2):
                sq = sh * 2 + sq2
                for hp in range(8):
                    for tc in range(16):
                        scn = emit_scores(sq, hp, tc)
                        emit_exp(sq, hp, tc, scn)
                        if tc >= 1:
                            emit_av(sq, hp, tc - 1)
                        kprev = (sq, hp - 1) if hp >= 1 else (sq - 1, 7)
                        if kprev in av_hist or (b, kprev[0], kprev[1], 15) in av_hist:
                            pass
                        if tc == 2 and (b, kprev[0], kprev[1], 15) in av_hist:
                            if kprev not in expz_hist:
                                emit_tail_a(kprev[0], kprev[1])
                        if tc == 8 and kprev in expz_hist and kprev not in bc_hist:
                            emit_tail_b(kprev[0], kprev[1], kprev[0] % 2)
                    emit_av(sq, hp, 15)
            emit_tail_a(2 * sh + 1, 7)
            emit_tail_b(2 * sh + 1, 7, 1)

            def mkgl(sc):
                return sc.activation(arena1[:, 0:C * 1024], arena1[:, 0:C * 1024],
                                     AF.Gelu)
            gelu_hist[(b, sh)] = p.emit("scalar", mkgl,
                                        waits=[(s_dve, tt_hist[(b, 2 * sh + 1, 7)])],
                                        inc=(s_act, 1))

            for sb2 in range(2):
                sblk = sh * 2 + sb2

                def ldxr(sy, b=b, sblk=sblk):
                    src = xT[:, b * S + sblk * 512:b * S + sblk * 512 + 512]
                    src = src.rearrange("(c p) n -> p c n", p=128)
                    return sy.dma_start(out=xres[:, :].rearrange(
                        "p (c n) -> p c n", c=C), in_=src)
                wxr = [(s_dve, xres_used)] if b + sblk > 0 else []
                xrn = p.emit("sync", ldxr, waits=wxr, inc=(s_lf, 16))

                stt_hist = {}
                for m in range(C):
                    u = su % 4
                    wmm = [(s_act, gelu_hist[(b, sh)])]
                    if slot_evac[u] > 0:
                        wmm.append((s_dve, slot_evac[u]))

                    def mkwp(te, m=m, sb2=sb2, u=u):
                        inst = None
                        for k in range(C):
                            inst = te.matmul(
                                slot_ap(u), wslice(warena[1], k, m),
                                arena1[:, k * 1024 + sb2 * 512:
                                       k * 1024 + sb2 * 512 + 512],
                                start=(k == 0), stop=(k == C - 1))
                        return inst
                    gn = p.emit("tensor", mkwp, waits=wmm, inc=(s_pe, 1))
                    slot_mm[u] = gn
                    aT_free[b] = gn
                    wload_mm[(b, 3)] = gn

                    def mkstt(v, m=m, u=u):
                        return v.scalar_tensor_tensor(
                            x1sb[:, m * 512:(m + 1) * 512], xres[:, m * 512:(m + 1) * 512],
                            2.0, slot_ap(u), op0=ALU.mult, op1=ALU.add)
                    ws = [(s_pe, gn)]
                    if m == 0:
                        ws.append((s_lf, xrn))
                    slot_evac[u] = p.emit("vector", mkstt, waits=ws, inc=(s_dve, 1))
                    stt_hist[m] = slot_evac[u]
                    su += 1

                cpx = p.emit("vector", lambda v: v.tensor_copy(
                    xpb, x1sb[:, :]), waits=[(s_dve, stt_hist[C - 1])],
                    inc=(s_dve, 1))
                cps = p.emit("vector", lambda v: v.tensor_tensor(
                    sqA, xpb, xpb, op=ALU.mult), inc=(s_dve, 1))

                us_, uq = su % 4, (su + 1) % 4
                su += 2

                def mksums(te, us_=us_, uq=uq):
                    inst = None
                    for k in range(C):
                        inst = te.matmul(slot_ap(us_, 0, 1), ones128[:, :],
                                         xpb[:, k * 512:(k + 1) * 512],
                                         start=(k == 0), stop=(k == C - 1))
                    for k in range(C):
                        inst = te.matmul(slot_ap(uq, 0, 1), ones128[:, :],
                                         sqA[:, k * 512:(k + 1) * 512],
                                         start=(k == 0), stop=(k == C - 1))
                    return inst
                wsm = [(s_dve, cps)]
                if slot_evac[us_] > 0:
                    wsm.append((s_dve, slot_evac[us_]))
                if slot_evac[uq] > 0:
                    wsm.append((s_dve, slot_evac[uq]))
                smn = p.emit("tensor", mksums, waits=wsm, inc=(s_pe, 1))
                smn_last[b] = smn

                e1 = p.emit("vector", lambda v, us_=us_: v.tensor_scalar_mul(
                    lnS[:, :], slot_ap(us_, 0, 1), 1.0 / DM),
                    waits=[(s_pe, smn)], inc=(s_dve, 1))
                e2 = p.emit("vector", lambda v, uq=uq: v.tensor_scalar_mul(
                    lnS2[:, :], slot_ap(uq, 0, 1), 1.0 / DM), inc=(s_dve, 1))
                slot_evac[us_] = e1
                slot_evac[uq] = e2
                l1 = p.emit("vector", lambda v: v.tensor_tensor(
                    lnT[:, :], lnS[:, :], lnS[:, :], op=ALU.mult), inc=(s_dve, 1))
                l2 = p.emit("vector", lambda v: v.tensor_tensor(
                    lnT[:, :], lnS2[:, :], lnT[:, :], op=ALU.subtract),
                    inc=(s_dve, 1))
                l3 = p.emit("scalar", lambda sc: sc.activation(
                    lnU[:, :], lnT[:, :], AF.Ln, bias=epst[0:1, :], scale=1.0),
                    waits=[(s_dve, l2)], inc=(s_act, 1))
                l4 = p.emit("scalar", lambda sc: sc.activation(
                    lnT[:, :], lnU[:, :], AF.Exp, bias=0.0, scale=-0.5),
                    inc=(s_act, 1))
                l5 = p.emit("vector", lambda v: v.tensor_copy(
                    lnB[:, 0:512], lnT[:, :]), waits=[(s_act, l4)], inc=(s_dve, 1))
                l6 = p.emit("vector", lambda v: v.tensor_tensor(
                    lnU[:, :], lnS[:, :], lnT[:, :], op=ALU.mult), inc=(s_dve, 1))
                l7 = p.emit("vector", lambda v: v.tensor_scalar(
                    lnB[:, 512:1024], lnU[:, :], -1.0, None, op0=ALU.mult),
                    inc=(s_dve, 1))

                for wsem, wval in [(s_dve, e1), (s_dve, l2), (s_act, l3),
                                   (s_dve, l7)]:
                    def mkdum(te):
                        return te.matmul(psU[0][0:128, 0:512], kT[:, 0:128],
                                         qT[:, 0:512], start=True, stop=True)
                    p.emit("tensor", mkdum,
                           waits=[(wsem, wval),
                                  (s_dve, tt_hist[(b, 2 * sh + 1, 6)])],
                           inc=(s_pe, 1))

                ur, un = su % 4, (su + 1) % 4
                su += 2

                def mkbc2(te, ur=ur, un=un):
                    te.matmul(slot_ap(ur), ones1F[0:1, :], lnB[0:1, 0:512],
                              start=True, stop=True)
                    return te.matmul(slot_ap(un), ones1F[0:1, :], lnB[0:1, 512:1024],
                                     start=True, stop=True)
                wb2 = [(s_dve, l7)]
                if slot_evac[ur] > 0:
                    wb2.append((s_dve, slot_evac[ur]))
                if slot_evac[un] > 0:
                    wb2.append((s_dve, slot_evac[un]))
                b2n = p.emit("tensor", mkbc2, waits=wb2, inc=(s_pe, 1))

                ap_hist = {}
                for m in range(C):
                    w1 = [(s_pe, b2n)] if m == 0 else []
                    p.emit("vector", lambda v, m=m, ur=ur: v.tensor_tensor(
                        xres[:, m * 512:(m + 1) * 512],
                        x1sb[:, m * 512:(m + 1) * 512], slot_ap(ur),
                        op=ALU.mult), waits=w1, inc=(s_dve, 1))
                    ap_hist[m] = p.emit("vector", lambda v, m=m, un=un:
                                        v.tensor_tensor(
                        x1sb[:, m * 512:(m + 1) * 512],
                        xres[:, m * 512:(m + 1) * 512], slot_ap(un),
                        op=ALU.add), inc=(s_dve, 1))
                xres_used = ap_hist[C - 1]
                slot_evac[ur] = ap_hist[C - 1]
                slot_evac[un] = ap_hist[C - 1]

                def stx1(sy, b=b, sblk=sblk):
                    dst = slab[b][:, sblk * 512:sblk * 512 + 512].rearrange(
                        "(c p) n -> p c n", p=128)
                    return sy.dma_start(out=dst, in_=x1sb[:, :].rearrange(
                        "p (c n) -> p c n", c=C))
                p.emit("sync", stx1, waits=[(s_dve, ap_hist[C - 1])],
                       inc=(s_st, 16))

        p.emit("gpsimd", lambda g, b=b: g.collective_compute(
            "AllGather", ALU.bypass, replica_groups=[list(range(NCORES))],
            ins=[slab[b]], outs=[gath[b]]),
            waits=[(s_st, p.sems["s_st"].n)], inc=(s_cc, 1))

    aes.close()
    return (ctx["p"].sems["s_st"].n)



def weight_maps(inputs):
    return {
        "wfT": prechunk(np.asarray(inputs["L0_ffn_wf"], np.float32).T),
        "ufT": prechunk(np.asarray(inputs["L0_ffn_uf"], np.float32).T),
        "wiT": prechunk(np.asarray(inputs["L0_ffn_wi"], np.float32).T),
        "uiT": prechunk(np.asarray(inputs["L0_ffn_ui"], np.float32).T),
        "wcT": prechunk(np.asarray(inputs["L0_ffn_wc"], np.float32).T),
        "ucT": prechunk(np.asarray(inputs["L0_ffn_uc"], np.float32).T),
        "woT": prechunk(np.asarray(inputs["L0_ffn_wo"], np.float32).T),
        "uoT": prechunk(np.asarray(inputs["L0_ffn_uo"], np.float32).T),
        "bf": pack_bias(inputs["L0_ffn_bf"]), "bi": pack_bias(inputs["L0_ffn_bi"]),
        "bc": pack_bias(inputs["L0_ffn_bc"]), "bo": pack_bias(inputs["L0_ffn_bo"]),
        "inc0": pack_bias(inputs["L0_ffn_in_c"]),
    }


def assemble_out(results):
    full = np.empty((B, S, DM), np.float32)
    for k in range(NCORES):
        ok = np.asarray(results[k]["out"]).reshape(DM, B, TOK)
        full[:, k * TOK:(k + 1) * TOK, :] = ok.transpose(1, 2, 0)
    return full


def kernel(**inputs):
    inputs = {k: np.asarray(v, np.float32) for k, v in inputs.items()}
    base = weight_maps(inputs)
    base["Wq"] = prechunk(inputs["L0_attn_Wq"])
    base["Wk"] = prechunk(inputs["L0_attn_Wk"])
    base["Wv"] = prechunk(inputs["L0_attn_Wv"])
    base["Wpost"] = prechunk(inputs["L0_attn_Wpost"])
    X = inputs["L0_attn_X"]
    in_maps = []
    for k in range(NCORES):
        m = dict(base)
        m["xT"] = np.ascontiguousarray(
            X[[k, 8 + k, 16 + k, 24 + k]].transpose(2, 0, 1).reshape(DM, BLOC * S))
        in_maps.append(m)
    nc = build_program(mode="full")
    res = run_bass_kernel_spmd(nc, in_maps, core_ids=list(range(NCORES)))
    return assemble_out(res.results)


# version 08
# speedup vs baseline: 1.1911x; 1.0277x over previous
import contextlib
import numpy as np
import concourse.bass as bass
import concourse.mybir as mybir
from concourse.bass_utils import run_bass_kernel_spmd

F32 = mybir.dt.float32
BF16 = mybir.dt.bfloat16
AF = mybir.ActivationFunctionType
ALU = mybir.AluOpType

NCORES = 8
DM = 1024
C = DM // 128
B = 32
S = 2048
TOK = S // NCORES
BLOC = B // NCORES
H = 16
DK = 64
LNEPS = 1e-5


class Sem:
    def __init__(self, name):
        self.name = name
        self.n = 0
        self.h = None

    def inc(self, amt=1):
        self.n += amt
        return self.n


class Prog:

    def __init__(self):
        self.ops = {e: [] for e in ("tensor", "scalar", "vector", "gpsimd", "sync")}
        self.sems = {}

    def sem(self, name):
        if name not in self.sems:
            self.sems[name] = Sem(name)
        return self.sems[name]

    def emit(self, engine, fn, waits=(), inc=None):
        n = None
        if inc is not None:
            n = inc[0].inc(inc[1])
        self.ops[engine].append((tuple(waits), fn, inc))
        return n

    def build_engine(self, eng, name):
        for waits, fn, inc in self.ops[name]:
            for s, v in waits:
                if v > 0:
                    eng.wait_ge(s.h, v)
            inst = fn(eng)
            if inc is not None:
                assert inst is not None, f"inc on None inst in {name}"
                inst.then_inc(inc[0].h, inc[1])



def prechunk(w2):
    return np.ascontiguousarray(
        w2.reshape(C, 128, C, 128).transpose(1, 0, 2, 3).reshape(128, C * C * 128)
    )


def pack_bias(b):
    return np.ascontiguousarray(np.asarray(b, np.float32).reshape(C, 128).T)


def wslice(wsb, kc, mc):
    o = (kc * C + mc) * 128
    return wsb[:, o:o + 128]



def build_program(mode="full"):
    nc = bass.Bass("TRN2", target_bir_lowering=False)
    p = Prog()

    gath_kind = "ExternalInput" if mode == "bc" else "Internal"
    gath_kw = {} if mode == "bc" else {"addr_space": "Shared"}
    gath = [nc.dram_tensor(f"gath{j}", [NCORES * DM, S], BF16, kind=gath_kind,
                           **gath_kw).ap()
            for j in range(BLOC)]
    out_h = nc.dram_tensor("out", [DM, B * TOK], F32, kind="ExternalOutput").ap()
    vsp = nc.dram_tensor("vsp", [DM, B * TOK], F32).ap()
    stats_d = nc.dram_tensor("stats_d", [B, 2 * TOK], F32).ap()
    stats_o = nc.dram_tensor("stats_o", [B, 2 * TOK], F32).ap()

    wnames_bc = ["wfT", "ufT", "wiT", "uiT", "wcT", "ucT", "woT", "uoT"]
    whbm = {n: nc.dram_tensor(n, [128, C * C * 128], F32, kind="ExternalInput").ap()
            for n in wnames_bc}
    bnames = ["bf", "bi", "bc", "bo", "inc0"]
    bhbm = {n: nc.dram_tensor(n, [128, C], F32, kind="ExternalInput").ap()
            for n in bnames}

    if mode == "full":
        xT = nc.dram_tensor("xT", [DM, BLOC * S], F32, kind="ExternalInput").ap()
        for n in ["Wq", "Wk", "Wv", "Wpost"]:
            whbm[n] = nc.dram_tensor(n, [128, C * C * 128], F32,
                                     kind="ExternalInput").ap()
        slab = [nc.dram_tensor(f"slab{j}", [DM, S], BF16).ap() for j in range(BLOC)]
    else:
        slab = None
    wbf16_d = {n: nc.dram_tensor(n + "_bf", [128, C * C * 128], BF16).ap()
               for n in whbm}
    xbf_d = (nc.dram_tensor("xT_bf", [DM, BLOC * S], BF16).ap()
             if mode == "full" else None)

    es = contextlib.ExitStack()
    with es:
        T = lambda *a, **k: es.enter_context(nc.sbuf_tensor(*a, **k))
        PS = lambda *a, **k: es.enter_context(nc.psum_tensor(*a, **k))

        ones1 = T("ones1", [1, 128], BF16)
        ones1F = T("ones1F", [1, 128], F32)
        ones128 = T("ones128", [128, 1], BF16)
        epst = T("epst", [128, 1], F32)
        bias_sb = {n: T(f"sb_{n}", [128, C], F32) for n in bnames}

        s_ld = p.sem("s_ld")
        s_lf = p.sem("s_lf")
        s_st = p.sem("s_st")
        s_pe = p.sem("s_pe")
        s_act = p.sem("s_act")
        s_dve = p.sem("s_dve")
        s_cc = p.sem("s_cc")

        dyn = {}

        def ginit(g):
            dyn["g_off"] = g.snap(g.partition_id() * TOK)
            return None
        p.emit("gpsimd", ginit)

        def syinit(sy):
            dyn["s_off"] = sy.snap(sy.partition_id() * TOK)
            return None
        p.emit("sync", syinit)

        p.emit("gpsimd", lambda g: g.memset(ones1[:, :], 1.0), inc=(s_ld, 1))
        p.emit("gpsimd", lambda g: g.memset(ones1F[:, :], 1.0), inc=(s_ld, 1))
        p.emit("gpsimd", lambda g: g.memset(ones128[:, :], 1.0), inc=(s_ld, 1))
        p.emit("gpsimd", lambda g: g.memset(epst[:, :], LNEPS), inc=(s_ld, 1))
        consts_ready = (s_ld, s_ld.n)

        for n in bnames:
            p.emit("sync", lambda sy, n=n: sy.dma_start(
                out=bias_sb[n][:, :], in_=bhbm[n]), inc=(s_lf, 16))
        bias_ready = (s_lf, s_lf.n)

        wcast = {}
        worder = ([n for n in ("Wq", "Wk", "Wv", "Wpost") if n in whbm] +
                  [n for n in whbm if n not in ("Wq", "Wk", "Wv", "Wpost")])
        for n in worder:
            wcast[n] = p.emit("gpsimd", lambda g, n=n: g.dma_start(
                out=wbf16_d[n], in_=whbm[n]), inc=(s_ld, 16))
        if mode == "full":
            xcast = {}
            for bb_ in range(BLOC):
                xcast[bb_] = p.emit("gpsimd", lambda g, bb_=bb_: g.dma_start(
                    out=xbf_d[:, bb_ * S:(bb_ + 1) * S],
                    in_=xT[:, bb_ * S:(bb_ + 1) * S]), inc=(s_ld, 16))


        a_store_end = 0
        if mode == "full":
            actx = dict(nc=nc, p=p, es=es, T=T, PS=PS, xT=xT, whbm=whbm,
                        wbf16_d=wbf16_d, xbf_d=xbf_d, wcast=wcast, xcast=xcast,
                        slab=slab, gath=gath, ones1=ones1, ones128=ones128,
                        ones1F=ones1F,
                        epst=epst, s_ld=s_ld, s_lf=s_lf, s_st=s_st, s_pe=s_pe,
                        s_act=s_act, s_dve=s_dve, s_cc=s_cc)
            a_store_end = build_phase_a(actx)


        w_sb = {n: T(f"w_{n}", [128, C * C * 128], BF16)
                for n in ["wfT", "ufT", "wiT", "uiT", "wcT", "ucT"]}
        w_sb["woT"] = w_sb["wfT"]
        w_sb["uoT"] = w_sb["ufT"]
        c_sb = T("c_sb", [128, C * TOK], F32)
        cbf = T("cbf", [128, C * TOK], BF16)
        xt_sb = [T(f"xt{i}", [128, C * TOK], BF16) for i in range(2)]
        gts = [T(f"gts{i}", [128, 3 * TOK], F32) for i in range(2)]
        tmp1 = T("tmp1", [128, TOK], F32)
        tmp2 = T("tmp2", [128, TOK], F32)
        uoc = T("uoc", [128, C * TOK], F32)
        x1f = [T(f"x1f{i}", [128, C * TOK], F32) for i in range(2)]
        x1b = [T(f"x1b{i}", [128, C * TOK], BF16) for i in range(2)]
        hv = T("hv", [128, C * TOK], F32)
        vb = T("vb", [128, C * TOK], BF16)
        sqb = T("sqb", [128, C * TOK], BF16)
        sst = [T(f"sst{i}", [1, 2 * TOK], F32) for i in range(2)]
        stT = T("stT", [B, 2 * TOK], F32)
        stX = T("stX", [B, TOK], F32)
        stY = T("stY", [B, TOK], F32)
        stO = T("stO", [B, 2 * TOK], F32)
        stin = [T(f"stin{i}", [1, 2 * TOK], F32) for i in range(2)]
        outsb = [T(f"outsb{i}", [128, C * TOK], F32) for i in range(2)]

        psG = [PS(f"psG{i}", [128, 512], F32) for i in range(4)]
        psS = PS("psS", [1, 512], F32)
        psB = PS("psB", [128, 512], F32)

        wA = [(s_ld, a_store_end)] if mode == "full" else []
        for n in ["wfT", "ufT", "wiT", "uiT", "wcT", "ucT"]:
            p.emit("sync", lambda sy, n=n: sy.dma_start(
                out=w_sb[n][:, :], in_=wbf16_d[n]),
                waits=wA + [(s_ld, wcast[n])], inc=(s_lf, 16))
        wB_ready = (s_lf, s_lf.n)

        wc0 = [bias_ready] + ([(s_ld, a_store_end)] if mode == "full" else [])
        p.emit("vector", lambda v: v.memset(c_sb[:, :], 0.0),
               waits=wc0, inc=(s_dve, 1))
        for c in range(C):
            p.emit("vector", lambda v, c=c: v.tensor_scalar_add(
                c_sb[:, c * TOK:(c + 1) * TOK], c_sb[:, c * TOK:(c + 1) * TOK],
                bias_sb["inc0"][:, c:c + 1]), inc=(s_dve, 1))

        GATES = [("wfT", "ufT", "bf", AF.Sigmoid), ("wiT", "uiT", "bi", AF.Sigmoid),
                 ("wcT", "ucT", "bc", AF.Tanh)]
        pe_group = {}
        act_of_group = {}
        xt_used = {}
        cupd = {}
        cupd[(-1, C - 1)] = s_dve.n
        cupd[(-1, C - 2)] = s_dve.n
        xt_loaded = {}
        ggrp = 0

        for t in range(B):
            j, r = t // NCORES, t % NCORES
            w = [(s_pe, xt_used[t - 2])] if t - 2 in xt_used else []
            if mode == "full":
                w = w + [(s_cc, j + 1), (s_ld, a_store_end)]

            def ldxt(g, j=j, r=r, t=t):
                src = gath[j][r * DM:(r + 1) * DM, bass.ds(dyn["g_off"], TOK)]
                src = src.rearrange("(c p) n -> p c n", p=128)
                return g.dma_start(out=xt_sb[t % 2][:, :].rearrange(
                    "p (c n) -> p c n", c=C), in_=src)
            xt_loaded[t] = p.emit("gpsimd", ldxt, waits=w, inc=(s_ld, 16))

            wv = [(s_dve, cupd[(t - 1, C - 1)])]
            if t > 0:
                wv.append((s_pe, pe_group[(t - 1, C - 1, 2)]))
            cbf_n = p.emit("vector", lambda v: v.tensor_copy(cbf[:, :], c_sb[:, :]),
                           waits=wv, inc=(s_dve, 1))

            for m in range(C):
                for gi, (wn, un, bn, af) in enumerate(GATES):
                    bank = psG[ggrp % 4]
                    wmm = []
                    if m == 0 and gi == 0:
                        wmm = [(s_ld, xt_loaded[t]), (s_lf, wB_ready[1]),
                               (s_dve, cbf_n)]
                    if ggrp - 4 >= 0 and (ggrp - 4) in act_of_group:
                        wmm.append((s_act, act_of_group[ggrp - 4]))

                    def mk_mms(te, wn=wn, un=un, m=m, t=t, bank=bank):
                        inst = None
                        for k in range(C):
                            inst = te.matmul(
                                bank[:, 0:TOK], wslice(w_sb[wn], k, m),
                                xt_sb[t % 2][:, k * TOK:(k + 1) * TOK],
                                start=(k == 0), stop=False)
                        for k in range(C):
                            inst = te.matmul(
                                bank[:, 0:TOK], wslice(w_sb[un], k, m),
                                cbf[:, k * TOK:(k + 1) * TOK],
                                start=False, stop=(k == C - 1))
                        return inst
                    pe_group[(t, m, gi)] = p.emit("tensor", mk_mms, waits=wmm,
                                                  inc=(s_pe, 1))

                    wact = [(s_pe, pe_group[(t, m, gi)])]
                    key = (t, m - 2) if m >= 2 else (t - 1, C - 2 + m)
                    if key in cupd:
                        wact.append((s_dve, cupd[key]))

                    def mk_act(sc, af=af, bn=bn, m=m, gi=gi, bank=bank):
                        return sc.activation(
                            gts[m % 2][:, gi * TOK:(gi + 1) * TOK], bank[:, 0:TOK],
                            af, bias=bias_sb[bn][:, m:m + 1], scale=1.0)
                    act_of_group[ggrp] = p.emit("scalar", mk_act, waits=wact,
                                                inc=(s_act, 1))
                    ggrp += 1

                wcu = [(s_act, s_act.n)]
                cs_lo, cs_hi = m * TOK, (m + 1) * TOK
                g0 = gts[m % 2]
                p.emit("vector", lambda v, g0=g0, lo=cs_lo, hi=cs_hi:
                       v.tensor_tensor(tmp1[:, :], g0[:, 0:TOK], c_sb[:, lo:hi],
                                       op=ALU.mult), waits=wcu, inc=(s_dve, 1))
                p.emit("vector", lambda v, g0=g0: v.tensor_tensor(
                    tmp2[:, :], g0[:, TOK:2 * TOK], g0[:, 2 * TOK:3 * TOK],
                    op=ALU.mult), inc=(s_dve, 1))
                cupd[(t, m)] = p.emit("vector", lambda v, lo=cs_lo, hi=cs_hi:
                                      v.tensor_tensor(c_sb[:, lo:hi], tmp1[:, :],
                                                      tmp2[:, :], op=ALU.add),
                                      inc=(s_dve, 1))
            xt_used[t] = pe_group[(t, C - 1, 2)]

        for n in ["woT", "uoT"]:
            p.emit("sync", lambda sy, n=n: sy.dma_start(
                out=w_sb[n][:, :], in_=wbf16_d[n]),
                waits=[(s_pe, pe_group[(B - 1, C - 1, 2)]),
                       (s_ld, wcast[n])], inc=(s_lf, 16))
        wC_ready = (s_lf, s_lf.n)

        cbf_fin = p.emit("vector", lambda v: v.tensor_copy(cbf[:, :], c_sb[:, :]),
                         waits=[(s_dve, cupd[(B - 1, C - 1)]),
                                (s_pe, pe_group[(B - 1, C - 1, 2)])],
                         inc=(s_dve, 1))

        uoc_done = {}
        for m in range(C):
            bank = psG[m % 2]

            def mk_uoc(te, m=m, bank=bank):
                inst = None
                for k in range(C):
                    inst = te.matmul(bank[:, 0:TOK], wslice(w_sb["uoT"], k, m),
                                     cbf[:, k * TOK:(k + 1) * TOK],
                                     start=(k == 0), stop=(k == C - 1))
                return inst
            wmm = [(s_dve, cbf_fin), (s_lf, wC_ready[1])]
            if m >= 2:
                wmm.append((s_dve, uoc_done[m - 2]))
            gn = p.emit("tensor", mk_uoc, waits=wmm, inc=(s_pe, 1))
            uoc_done[m] = p.emit("vector", lambda v, m=m, bank=bank:
                                 v.tensor_scalar_mul(
                                     uoc[:, m * TOK:(m + 1) * TOK],
                                     bank[:, 0:TOK], 1.0),
                                 waits=[(s_pe, gn)], inc=(s_dve, 1))

        x1_loaded = {}
        x1b_loaded = {}
        wo_done = {}
        a1_hist = {}
        a2_hist = {}
        vadds_last = {}
        spill_done = {}
        e2_hist = {}
        st_stored = {}
        for t in range(B):
            j, r = t // NCORES, t % NCORES
            wldb = [(s_pe, wo_done[t - 2])] if t - 2 in wo_done else []
            if t - 2 in vadds_last:
                wldb.append((s_dve, vadds_last[t - 2]))

            def ldb(g, j=j, r=r, t=t):
                src = gath[j][r * DM:(r + 1) * DM, bass.ds(dyn["g_off"], TOK)]
                src = src.rearrange("(c p) n -> p c n", p=128)
                return g.dma_start(out=x1b[t % 2][:, :].rearrange(
                    "p (c n) -> p c n", c=C), in_=src)
            x1b_loaded[t] = p.emit("gpsimd", ldb, waits=wldb, inc=(s_ld, 16))

            for m in range(C):
                bank = psG[2 + (m % 2)]
                wmm = [(s_ld, x1b_loaded[t])]
                key = (t, m - 2) if m >= 2 else (t - 1, C - 2 + m)
                if key in a1_hist:
                    wmm.append((s_dve, a1_hist[key]))
                if t == 0 and m < 2:
                    wmm.append((s_dve, cbf_fin))

                def mk_wo(te, m=m, t=t, bank=bank):
                    inst = None
                    for k in range(C):
                        inst = te.matmul(bank[:, 0:TOK], wslice(w_sb["woT"], k, m),
                                         x1b[t % 2][:, k * TOK:(k + 1) * TOK],
                                         start=(k == 0), stop=(k == C - 1))
                    return inst
                gn = p.emit("tensor", mk_wo, waits=wmm, inc=(s_pe, 1))
                wo_done[t] = gn

                tmpX = [tmp1, tmp2][m % 2]
                wa1 = [(s_pe, gn)]
                keyp = (t, m - 2) if m >= 2 else (t - 1, C - 2 + m)
                if keyp in a2_hist:
                    wa1.append((s_act, a2_hist[keyp]))
                a1_hist[(t, m)] = p.emit(
                    "vector", lambda v, m=m, bank=bank, tmpX=tmpX:
                    v.tensor_tensor(tmpX[:, :], bank[:, 0:TOK],
                                    uoc[:, m * TOK:(m + 1) * TOK], op=ALU.add),
                    waits=wa1, inc=(s_dve, 1))

                wa2 = [(s_dve, a1_hist[(t, m)])]
                if m == 0 and (t - 1) in spill_done:
                    wa2.append((s_st, spill_done[t - 1]))
                a2_hist[(t, m)] = p.emit(
                    "scalar", lambda sc, m=m, tmpX=tmpX: sc.activation(
                        hv[:, m * TOK:(m + 1) * TOK], tmpX[:, :], AF.Tanh,
                        bias=bias_sb["bo"][:, m:m + 1], scale=1.0),
                    waits=wa2, inc=(s_act, 1))

            for m in range(C):
                wv = [(s_act, a2_hist[(t, m)])]
                n = p.emit("vector", lambda v, m=m, t=t: v.tensor_tensor(
                    hv[:, m * TOK:(m + 1) * TOK], hv[:, m * TOK:(m + 1) * TOK],
                    x1b[t % 2][:, m * TOK:(m + 1) * TOK], op=ALU.add),
                    waits=wv, inc=(s_dve, 1))
            vadds_last[t] = n

            cp1 = p.emit("vector", lambda v: v.tensor_copy(vb[:, :], hv[:, :]),
                         inc=(s_dve, 1))
            cp2 = p.emit("vector", lambda v: v.tensor_tensor(
                sqb[:, :], vb[:, :], vb[:, :], op=ALU.mult), inc=(s_dve, 1))

            def mk_sums(te):
                inst = None
                for c in range(C):
                    inst = te.matmul(psS[:, 0:TOK], ones128[:, :],
                                     vb[:, c * TOK:(c + 1) * TOK],
                                     start=(c == 0), stop=(c == C - 1))
                for c in range(C):
                    inst = te.matmul(psS[:, TOK:2 * TOK], ones128[:, :],
                                     sqb[:, c * TOK:(c + 1) * TOK],
                                     start=(c == 0), stop=(c == C - 1))
                return inst
            wsum = [(s_dve, cp2), (s_ld, consts_ready[1])]
            if t - 1 in e2_hist:
                wsum.append((s_dve, e2_hist[t - 1]))
            sums_n = p.emit("tensor", mk_sums, waits=wsum, inc=(s_pe, 1))

            we1 = [(s_pe, sums_n)]
            if t - 2 in st_stored:
                we1.append((s_st, st_stored[t - 2]))
            p.emit("vector", lambda v, t=t: v.tensor_scalar_mul(
                sst[t % 2][:, 0:TOK], psS[:, 0:TOK], 1.0 / DM),
                waits=we1, inc=(s_dve, 1))
            e2_hist[t] = p.emit("vector", lambda v, t=t: v.tensor_scalar_mul(
                sst[t % 2][:, TOK:2 * TOK], psS[:, TOK:2 * TOK], 1.0 / DM),
                inc=(s_dve, 1))
            st_stored[t] = p.emit("sync", lambda sy, t=t: sy.dma_start(
                out=stats_d[t:t + 1, :], in_=sst[t % 2][:, :]),
                waits=[(s_dve, e2_hist[t])], inc=(s_st, 16))

            def spv(sy, t=t):
                dst = vsp[:, t * TOK:(t + 1) * TOK].rearrange(
                    "(c p) n -> p c n", p=128)
                return sy.dma_start(out=dst, in_=hv[:, :].rearrange(
                    "p (c n) -> p c n", c=C))
            spill_done[t] = p.emit("sync", spv, waits=[(s_dve, vadds_last[t])],
                                   inc=(s_st, 16))

        st_ld = p.emit("sync", lambda sy: sy.dma_start(
            out=stT[:, :], in_=stats_d),
            waits=[(s_st, st_stored[B - 1])], inc=(s_lf, 16))
        rb1 = p.emit("vector", lambda v: v.tensor_tensor(
            stX[:, :], stT[:, 0:TOK], stT[:, 0:TOK], op=ALU.mult),
            waits=[(s_lf, st_ld)], inc=(s_dve, 1))
        rb2 = p.emit("vector", lambda v: v.tensor_tensor(
            stX[:, :], stT[:, TOK:2 * TOK], stX[:, :], op=ALU.subtract),
            inc=(s_dve, 1))
        rb3 = p.emit("scalar", lambda sc: sc.activation(
            stY[:, :], stX[:, :], AF.Ln, bias=epst[0:B, :], scale=1.0),
            waits=[(s_dve, rb2)], inc=(s_act, 1))
        rb4 = p.emit("scalar", lambda sc: sc.activation(
            stX[:, :], stY[:, :], AF.Exp, bias=0.0, scale=-0.5), inc=(s_act, 1))
        rb5 = p.emit("vector", lambda v: v.tensor_copy(
            stO[:, 0:TOK], stX[:, :]), waits=[(s_act, rb4)], inc=(s_dve, 1))
        rb6 = p.emit("vector", lambda v: v.tensor_tensor(
            stY[:, :], stT[:, 0:TOK], stX[:, :], op=ALU.mult), inc=(s_dve, 1))
        rb7 = p.emit("vector", lambda v: v.tensor_scalar(
            stO[:, TOK:2 * TOK], stY[:, :], -1.0, None, op0=ALU.mult),
            inc=(s_dve, 1))
        stO_st = p.emit("sync", lambda sy: sy.dma_start(
            out=stats_o, in_=stO[:, :]), waits=[(s_dve, rb7)], inc=(s_st, 16))

        out_stored = {}
        appl_done = {}
        bc_hist = {}
        vld = {}
        for t in range(B):
            wldv = ([(s_dve, appl_done[t - 2])] if t - 2 in appl_done
                    else [(s_dve, vadds_last[B - 2 + t])])

            def ldv(g, t=t):
                src = vsp[:, t * TOK:(t + 1) * TOK].rearrange(
                    "(c p) n -> p c n", p=128)
                return g.dma_start(out=x1f[t % 2][:, :].rearrange(
                    "p (c n) -> p c n", c=C), in_=src)
            wldv.append((s_st, spill_done[t]))
            vld[t] = p.emit("gpsimd", ldv, waits=wldv, inc=(s_ld, 16))

            wsi = [(s_st, stO_st)]
            if t - 2 in bc_hist:
                wsi.append((s_pe, bc_hist[t - 2]))
            sld = p.emit("sync", lambda sy, t=t: sy.dma_start(
                out=stin[t % 2][:, :], in_=stats_o[t:t + 1, :]),
                waits=wsi, inc=(s_lf, 16))

            def mk_bc(te, t=t):
                te.matmul(psB[:, 0:TOK], ones1F[:, :],
                          stin[t % 2][:, 0:TOK], start=True, stop=True)
                return te.matmul(psB[:, TOK:2 * TOK], ones1F[:, :],
                                 stin[t % 2][:, TOK:2 * TOK],
                                 start=True, stop=True)
            wbc = [(s_lf, sld)]
            if t - 1 in appl_done:
                wbc.append((s_dve, appl_done[t - 1]))
            bc_n = p.emit("tensor", mk_bc, waits=wbc, inc=(s_pe, 1))
            bc_hist[t] = bc_n

            for m in range(C):
                w1 = [(s_pe, bc_n), (s_ld, vld[t])] if m == 0 else []
                if m == 0 and t - 2 in out_stored:
                    w1.append((s_st, out_stored[t - 2]))
                p.emit("vector", lambda v, m=m, t=t: v.tensor_tensor(
                    outsb[t % 2][:, m * TOK:(m + 1) * TOK],
                    x1f[t % 2][:, m * TOK:(m + 1) * TOK], psB[:, 0:TOK],
                    op=ALU.mult), waits=w1, inc=(s_dve, 1))
                appl_done[t] = p.emit("vector", lambda v, m=m, t=t:
                                      v.tensor_tensor(
                    outsb[t % 2][:, m * TOK:(m + 1) * TOK],
                    outsb[t % 2][:, m * TOK:(m + 1) * TOK], psB[:, TOK:2 * TOK],
                    op=ALU.add), inc=(s_dve, 1))

            def stout(sy, t=t):
                dst = out_h[:, t * TOK:(t + 1) * TOK].rearrange(
                    "(c p) n -> p c n", p=128)
                return sy.dma_start(out=dst, in_=outsb[t % 2][:, :].rearrange(
                    "p (c n) -> p c n", c=C))
            out_stored[t] = p.emit("sync", stout, waits=[(s_dve, appl_done[t])],
                                   inc=(s_st, 16))

        p.emit("sync", lambda sy: None, waits=[(s_st, s_st.n)])

        for s in p.sems.values():
            s.h = es.enter_context(nc.semaphore(s.name))
        with nc.Block() as block:
            @block.tensor
            def _(te):
                p.build_engine(te, "tensor")

            @block.scalar
            def _(sc):
                p.build_engine(sc, "scalar")

            @block.vector
            def _(v):
                p.build_engine(v, "vector")

            @block.gpsimd
            def _(g):
                p.build_engine(g, "gpsimd")

            @block.sync
            def _(sy):
                p.build_engine(sy, "sync")

    return nc


def build_phase_a(ctx):
    nc, p, es = ctx["nc"], ctx["p"], ctx["es"]
    T, PS = ctx["T"], ctx["PS"]
    xT, whbm, slab, gath = ctx["xT"], ctx["whbm"], ctx["slab"], ctx["gath"]
    ones1, ones128, epst = ctx["ones1"], ctx["ones128"], ctx["epst"]
    ones1F = ctx["ones1F"]
    s_ld, s_lf, s_st = ctx["s_ld"], ctx["s_lf"], ctx["s_st"]
    s_pe, s_act, s_dve, s_cc = ctx["s_pe"], ctx["s_act"], ctx["s_dve"], ctx["s_cc"]

    aes = contextlib.ExitStack()
    ctx["aes"] = aes
    TA = lambda *a, **k: aes.enter_context(nc.sbuf_tensor(*a, **k))
    PA = lambda *a, **k: aes.enter_context(nc.psum_tensor(*a, **k))

    arena1 = TA("arena1", [128, C * S], BF16)
    xbf = arena1[:, :]
    aT = arena1
    _war = TA("warena", [128, C * C * 128], BF16)
    warena = [_war, _war]
    qT = TA("qT", [128, C * S], BF16)
    kT = TA("kT", [128, C * S], BF16)
    vP = TA("vP", [128, 16 * (H * 65)], BF16)
    exp_sb = [TA(f"exp{i}", [128, 1024], BF16) for i in range(3)]
    vtmpf = TA("vtmpf", [128, 512], F32)
    rb_row = TA("rb_row", [1, 1024], BF16)
    lnzrow = TA("lnzrow", [1, 1024], F32)
    rbB = TA("rbB", [64, 1024], BF16)
    x1sb = TA("x1sb", [128, C * 512], F32)
    xres = TA("xres", [128, C * 512], F32)
    xpb = arena1[:, 8192:8192 + 4096]
    sqA = arena1[:, 8192 + 4096:16384]
    lnS = TA("lnS", [1, 512], F32)
    lnS2 = TA("lnS2", [1, 512], F32)
    lnT = TA("lnT", [1, 512], F32)
    lnU = TA("lnU", [1, 512], F32)
    lnB = TA("lnB", [1, 1024], F32)
    neg3 = TA("neg3", [128, 1], F32)
    identF = TA("identF", [128, 128], F32)

    psSC = [PA(f"psSC{i}", [128, 1024], F32) for i in range(2)]
    psU = [PA(f"psU{i}", [128, 1024], F32) for i in range(2)]

    SLOT = [(psSC[0], 0), (psSC[0], 512), (psSC[1], 0), (psSC[1], 512)]
    slot_evac = [0, 0, 0, 0]
    slot_mm = [0, 0, 0, 0]

    def slot_ap(u, p0=0, p1=128, w=512):
        t_, off = SLOT[u % 4]
        return t_[p0:p1, off:off + w]

    p.emit("gpsimd", lambda g: g.memset(neg3[:, :], -3.0), inc=(s_ld, 1))
    p.emit("gpsimd", lambda g: g.memset(vP[:, :], 1.0), inc=(s_ld, 1))
    p.emit("gpsimd", lambda g: g.memset(identF[:, :], 0.0))
    p.emit("gpsimd", lambda g: g.affine_select(
        out=identF[:, :], in_=identF[:, :], compare_op=ALU.not_equal,
        fill=1.0, base=0, pattern=[[-1, 128]], channel_multiplier=1),
        inc=(s_ld, 1))
    aconsts = (s_ld, s_ld.n)

    PW = [("Wq", qT), ("Wk", kT), ("Wv", None), ("Wpost", None)]
    su = 0
    xbf_loaded = {}
    wload = {}
    wload_mm = {}
    aT_free = {-1: 0}
    smn_last = {}
    store_cnt = 0
    store_hist = []
    exp_hist = {}
    av_hist = {}
    tt_hist = {}
    gelu_hist = {}

    for b in range(BLOC):
        wx = [(s_pe, smn_last[b - 1])] if b - 1 in smn_last else []

        def ldx_half(sy, b=b, half=0):
            src = ctx["xbf_d"][half * 512:half * 512 + 512,
                               b * S:(b + 1) * S].rearrange(
                "(c p) n -> p c n", p=128)
            dst = arena1[:, half * 4 * S:(half + 1) * 4 * S].rearrange(
                "p (c n) -> p c n", c=4)
            return sy.dma_start(out=dst, in_=src)
        wx = wx + [(s_ld, ctx["xcast"][b])]
        xbf_lo = p.emit("sync", lambda sy, b=b: ldx_half(sy, b, 0),
                        waits=wx, inc=(s_lf, 16))
        xbf_hi = p.emit("sync", lambda sy, b=b: ldx_half(sy, b, 1),
                        inc=(s_lf, 16))
        xbf_loaded[b] = xbf_hi

        def emit_wload(pi, b=b):
            wwl = []
            prev = (b, pi - 1) if pi >= 1 else (b - 1, 3)
            if prev in wload_mm:
                wwl = [(s_pe, wload_mm[prev])]
            wn = PW[pi][0]
            wload[(b, pi)] = p.emit("sync", lambda sy, wn=wn, pi=pi: sy.dma_start(
                out=warena[pi % 2][:, :], in_=ctx["wbf16_d"][wn]),
                waits=wwl + [(s_ld, ctx["wcast"][wn])], inc=(s_lf, 16))
        emit_wload(0)

        for pi in range(3):
            wn, dstT = PW[pi][0], PW[pi][1]
            for m in range(C):
                wmm = [(s_lf, max(wload[(b, pi)], xbf_lo))]
                for u in range(4):
                    if slot_evac[u] > 0:
                        wmm.append((s_dve, slot_evac[u]))

                def mk_p(te, pi=pi, m=m, b=b, klo=0, khi=4):
                    inst = None
                    for k in range(klo, khi):
                        lhs = wslice(warena[pi % 2], k, m)
                        for sblk in range(4):
                            inst = te.matmul(
                                slot_ap(sblk), lhs,
                                xbf[:, k * S + sblk * 512:k * S + sblk * 512 + 512],
                                start=(k == 0), stop=(k == C - 1))
                    return inst
                p.emit("tensor", mk_p, waits=wmm, inc=(s_pe, 1))
                gn = p.emit("tensor",
                            lambda te, pi=pi, m=m, b=b: mk_p(te, pi, m, b, 4, 8),
                            waits=[(s_lf, xbf_hi)], inc=(s_pe, 1))
                wload_mm[(b, pi)] = gn

                if pi < 2:
                    for sblk in range(4):
                        def ev(v, dstT=dstT, m=m, sblk=sblk):
                            return v.tensor_scalar_mul(
                                dstT[:, m * S + sblk * 512:m * S + sblk * 512 + 512],
                                slot_ap(sblk), 1.0)
                        slot_evac[sblk] = p.emit("vector", ev, waits=[(s_pe, gn)],
                                                 inc=(s_dve, 1))
                else:
                    for sblk in range(4):
                        def evv(v, sblk=sblk):
                            return v.tensor_scalar_mul(vtmpf[:, :],
                                                       slot_ap(sblk), 1.0)
                        wev = [(s_pe, gn)]
                        if (b, m, sblk) != (0, 0, 0):
                            wev.append((s_pe, trn))
                        evn = p.emit("vector", evv, waits=wev, inc=(s_dve, 1))
                        slot_evac[sblk] = evn
                        for ti in range(4):
                            tc = sblk * 4 + ti

                            def mktr(te, ti=ti):
                                return te.transpose(
                                    psU[0][0:128, 0:128],
                                    vtmpf[:, ti * 128:(ti + 1) * 128],
                                    identF[:, :])
                            wtr = [(s_dve, evn), (s_ld, aconsts[1])]
                            if (b, m, sblk, ti) != (0, 0, 0, 0):
                                wtr.append((s_dve, vp_ev))
                            trn = p.emit("tensor", mktr, waits=wtr, inc=(s_pe, 1))

                            def evp(v, tc=tc, m=m):
                                base = tc * H * 65 + (2 * m) * 65
                                dst = vP[:, base:base + 130].rearrange(
                                    "p (h e) -> p h e", h=2)[:, :, 0:64]
                                srcv = psU[0][0:128, 0:128].rearrange(
                                    "p (h e) -> p h e", h=2)
                                return v.tensor_copy(dst, srcv)
                            vp_ev = p.emit("vector", evp, waits=[(s_pe, trn)],
                                           inc=(s_dve, 1))
            emit_wload(pi + 1)
        vp_done = vp_ev

        expz_hist = {}
        bc_hist = {}
        for sh in range(2):

            def emit_scores(sq, hp, tc):
                bank = psSC[tc % 2]
                wsc = []
                kexp = (b, sq, hp, tc - 2) if tc >= 2 else \
                       (b, sq, hp - 1, 14 + tc) if hp >= 1 else \
                       (b, sq - 1, 7, 14 + tc) if sq >= 1 else \
                       (b - 1, 3, 7, 14 + tc)
                if kexp in exp_hist:
                    wsc.append((s_act, exp_hist[kexp]))
                if hp == 0 and tc == 0 and sq % 2 == 0:
                    for u in range(4):
                        wsc.append((s_dve, slot_evac[u]))
                    if sq == 0:
                        wsc.append((s_dve, vp_done))

                def mksc(te, sq=sq, hp=hp, tc=tc, bank=bank):
                    te.matmul(bank[:, 0:512],
                              kT[0:64, hp * S + tc * 128:hp * S + tc * 128 + 128],
                              qT[0:64, hp * S + sq * 512:hp * S + sq * 512 + 512],
                              start=True, stop=True)
                    return te.matmul(
                        bank[:, 512:1024],
                        kT[64:128, hp * S + tc * 128:hp * S + tc * 128 + 128],
                        qT[64:128, hp * S + sq * 512:hp * S + sq * 512 + 512],
                        start=True, stop=True)
                return p.emit("tensor", mksc, waits=wsc, inc=(s_pe, 1))

            def emit_exp(sq, hp, tc, scn):
                bank = psSC[tc % 2]
                wex = [(s_pe, scn)]
                tcp = 15 - ((15 - tc) % 3)
                kav = (b, sq, hp, tc - 3) if tc >= 3 else \
                      (b, sq, hp - 1, tcp) if hp >= 1 else \
                      (b, sq - 1, 7, tcp) if sq >= 1 else \
                      (b - 1, 3, 7, tcp)
                if kav in av_hist:
                    wex.append((s_pe, av_hist[kav]))

                def mkex(sc, tc=tc, bank=bank):
                    return sc.activation(exp_sb[tc % 3][:, :], bank[:, :],
                                         AF.Exp, bias=neg3[:, :], scale=0.125)
                exp_hist[(b, sq, hp, tc)] = p.emit("scalar", mkex, waits=wex,
                                                   inc=(s_act, 1))

            def emit_av(sq, hp, tc):
                wav = [(s_act, exp_hist[(b, sq, hp, tc)])]
                if tc == 0:
                    ktt = (b, sq, hp - 2) if hp >= 2 else \
                          (b, sq - 1, 6 + hp) if sq >= 1 else \
                          (b - 1, 3, 6 + hp)
                    if ktt in tt_hist:
                        wav.append((s_dve, tt_hist[ktt]))

                def mkav(te, hp=hp, tc=tc):
                    te.matmul(psU[hp % 2][0:65, 0:512],
                              vP[:, tc * H * 65 + 2 * hp * 65:
                                 tc * H * 65 + 2 * hp * 65 + 65],
                              exp_sb[tc % 3][:, 0:512],
                              start=(tc == 0), stop=(tc == 15))
                    return te.matmul(
                        psU[hp % 2][0:65, 512:1024],
                        vP[:, tc * H * 65 + (2 * hp + 1) * 65:
                           tc * H * 65 + (2 * hp + 1) * 65 + 65],
                        exp_sb[tc % 3][:, 512:1024],
                        start=(tc == 0), stop=(tc == 15))
                av_hist[(b, sq, hp, tc)] = p.emit("tensor", mkav, waits=wav,
                                                  inc=(s_pe, 1))

            def emit_tail_a(sq, hp):
                ln_n = p.emit("scalar", lambda sc, hp=hp: sc.activation(
                    lnzrow[:, :], psU[hp % 2][64:65, :], AF.Ln),
                    waits=[(s_pe, av_hist[(b, sq, hp, 15)])], inc=(s_act, 1))

                def mkexpz(sc):
                    with nc.allow_low_precision(
                            reason="softmax 1/Z stored bf16; product is "
                                   "rounded to bf16 anyway; diluted by residual"):
                        return sc.activation(rb_row[:, :], lnzrow[:, :],
                                             AF.Exp, scale=-1.0)
                wz = []
                kbc = (sq, hp - 2) if hp >= 2 else (sq - 1, 6 + hp)
                if kbc in bc_hist:
                    wz.append((s_pe, bc_hist[kbc]))
                expz_hist[(sq, hp)] = p.emit("scalar", mkexpz, waits=wz,
                                             inc=(s_act, 1))

            def emit_tail_b(sq, hp, sq2):
                def mkbc(te, hp=hp):
                    inst = None
                    for si in range(2):
                        inst = te.matmul(
                            psU[hp % 2][64:128, si * 512:si * 512 + 512],
                            ones1[0:1, 0:64],
                            rb_row[:, si * 512:si * 512 + 512],
                            start=True, stop=True)
                    return inst
                bc_hist[(sq, hp)] = p.emit(
                    "tensor", mkbc, waits=[(s_act, expz_hist[(sq, hp)])],
                    inc=(s_pe, 1))
                ebn = p.emit("vector", lambda v, hp=hp: v.tensor_scalar_mul(
                    rbB[:, :], psU[hp % 2][64:128, :], 1.0),
                    waits=[(s_pe, bc_hist[(sq, hp)])], inc=(s_dve, 1))
                p.emit("vector", lambda v, hp=hp, sq2=sq2: v.tensor_tensor(
                    arena1[0:64, hp * 1024 + sq2 * 512:hp * 1024 + sq2 * 512 + 512],
                    psU[hp % 2][0:64, 0:512], rbB[:, 0:512], op=ALU.mult),
                    inc=(s_dve, 1))
                tt_hist[(b, sq, hp)] = p.emit(
                    "vector", lambda v, hp=hp, sq2=sq2: v.tensor_tensor(
                        arena1[64:128, hp * 1024 + sq2 * 512:
                               hp * 1024 + sq2 * 512 + 512],
                        psU[hp % 2][0:64, 512:1024], rbB[:, 512:1024],
                        op=ALU.mult), inc=(s_dve, 1))

            for sq2 in range(2):
                sq = sh * 2 + sq2
                for hp in range(8):
                    for tc in range(16):
                        scn = emit_scores(sq, hp, tc)
                        emit_exp(sq, hp, tc, scn)
                        if tc >= 1:
                            emit_av(sq, hp, tc - 1)
                        kprev = (sq, hp - 1) if hp >= 1 else (sq - 1, 7)
                        if kprev in av_hist or (b, kprev[0], kprev[1], 15) in av_hist:
                            pass
                        if tc == 2 and (b, kprev[0], kprev[1], 15) in av_hist:
                            if kprev not in expz_hist:
                                emit_tail_a(kprev[0], kprev[1])
                        if tc == 8 and kprev in expz_hist and kprev not in bc_hist:
                            emit_tail_b(kprev[0], kprev[1], kprev[0] % 2)
                    emit_av(sq, hp, 15)
            emit_tail_a(2 * sh + 1, 7)
            emit_tail_b(2 * sh + 1, 7, 1)

            def mkgl(sc):
                return sc.activation(arena1[:, 0:C * 1024], arena1[:, 0:C * 1024],
                                     AF.Gelu)
            gelu_hist[(b, sh)] = p.emit("scalar", mkgl,
                                        waits=[(s_dve, tt_hist[(b, 2 * sh + 1, 7)])],
                                        inc=(s_act, 1))

            for sb2 in range(2):
                sblk = sh * 2 + sb2

                def ldxr(sy, b=b, sblk=sblk):
                    src = xT[:, b * S + sblk * 512:b * S + sblk * 512 + 512]
                    src = src.rearrange("(c p) n -> p c n", p=128)
                    return sy.dma_start(out=xres[:, :].rearrange(
                        "p (c n) -> p c n", c=C), in_=src)
                wxr = [(s_dve, xres_used)] if b + sblk > 0 else []
                xrn = p.emit("sync", ldxr, waits=wxr, inc=(s_lf, 16))

                stt_hist = {}
                for m in range(C):
                    u = su % 4
                    wmm = [(s_act, gelu_hist[(b, sh)])]
                    if slot_evac[u] > 0:
                        wmm.append((s_dve, slot_evac[u]))

                    def mkwp(te, m=m, sb2=sb2, u=u):
                        inst = None
                        for k in range(C):
                            inst = te.matmul(
                                slot_ap(u), wslice(warena[1], k, m),
                                arena1[:, k * 1024 + sb2 * 512:
                                       k * 1024 + sb2 * 512 + 512],
                                start=(k == 0), stop=(k == C - 1))
                        return inst
                    gn = p.emit("tensor", mkwp, waits=wmm, inc=(s_pe, 1))
                    slot_mm[u] = gn
                    aT_free[b] = gn
                    wload_mm[(b, 3)] = gn

                    def mkstt(v, m=m, u=u):
                        return v.scalar_tensor_tensor(
                            x1sb[:, m * 512:(m + 1) * 512], xres[:, m * 512:(m + 1) * 512],
                            2.0, slot_ap(u), op0=ALU.mult, op1=ALU.add)
                    ws = [(s_pe, gn)]
                    if m == 0:
                        ws.append((s_lf, xrn))
                        if store_hist:
                            ws.append((s_ld, store_hist[-1]))
                    slot_evac[u] = p.emit("vector", mkstt, waits=ws, inc=(s_dve, 1))
                    stt_hist[m] = slot_evac[u]
                    su += 1

                cpx = p.emit("vector", lambda v: v.tensor_copy(
                    xpb, x1sb[:, :]), waits=[(s_dve, stt_hist[C - 1])],
                    inc=(s_dve, 1))
                cps = p.emit("vector", lambda v: v.tensor_tensor(
                    sqA, xpb, xpb, op=ALU.mult), inc=(s_dve, 1))

                us_, uq = su % 4, (su + 1) % 4
                su += 2

                def mksums(te, us_=us_, uq=uq):
                    inst = None
                    for k in range(C):
                        inst = te.matmul(slot_ap(us_, 0, 1), ones128[:, :],
                                         xpb[:, k * 512:(k + 1) * 512],
                                         start=(k == 0), stop=(k == C - 1))
                    for k in range(C):
                        inst = te.matmul(slot_ap(uq, 0, 1), ones128[:, :],
                                         sqA[:, k * 512:(k + 1) * 512],
                                         start=(k == 0), stop=(k == C - 1))
                    return inst
                wsm = [(s_dve, cps)]
                if slot_evac[us_] > 0:
                    wsm.append((s_dve, slot_evac[us_]))
                if slot_evac[uq] > 0:
                    wsm.append((s_dve, slot_evac[uq]))
                smn = p.emit("tensor", mksums, waits=wsm, inc=(s_pe, 1))
                smn_last[b] = smn

                e1 = p.emit("vector", lambda v, us_=us_: v.tensor_scalar_mul(
                    lnS[:, :], slot_ap(us_, 0, 1), 1.0 / DM),
                    waits=[(s_pe, smn)], inc=(s_dve, 1))
                e2 = p.emit("vector", lambda v, uq=uq: v.tensor_scalar_mul(
                    lnS2[:, :], slot_ap(uq, 0, 1), 1.0 / DM), inc=(s_dve, 1))
                slot_evac[us_] = e1
                slot_evac[uq] = e2
                l1 = p.emit("vector", lambda v: v.tensor_tensor(
                    lnT[:, :], lnS[:, :], lnS[:, :], op=ALU.mult), inc=(s_dve, 1))
                l2 = p.emit("vector", lambda v: v.tensor_tensor(
                    lnT[:, :], lnS2[:, :], lnT[:, :], op=ALU.subtract),
                    inc=(s_dve, 1))
                l3 = p.emit("scalar", lambda sc: sc.activation(
                    lnU[:, :], lnT[:, :], AF.Ln, bias=epst[0:1, :], scale=1.0),
                    waits=[(s_dve, l2)], inc=(s_act, 1))
                l4 = p.emit("scalar", lambda sc: sc.activation(
                    lnT[:, :], lnU[:, :], AF.Exp, bias=0.0, scale=-0.5),
                    inc=(s_act, 1))
                l5 = p.emit("vector", lambda v: v.tensor_copy(
                    lnB[:, 0:512], lnT[:, :]), waits=[(s_act, l4)], inc=(s_dve, 1))
                l6 = p.emit("vector", lambda v: v.tensor_tensor(
                    lnU[:, :], lnS[:, :], lnT[:, :], op=ALU.mult), inc=(s_dve, 1))
                l7 = p.emit("vector", lambda v: v.tensor_scalar(
                    lnB[:, 512:1024], lnU[:, :], -1.0, None, op0=ALU.mult),
                    inc=(s_dve, 1))

                for wsem, wval in [(s_dve, e1), (s_dve, l2), (s_act, l3),
                                   (s_dve, l7)]:
                    def mkdum(te):
                        return te.matmul(psU[0][0:128, 0:512], kT[:, 0:128],
                                         qT[:, 0:512], start=True, stop=True)
                    p.emit("tensor", mkdum,
                           waits=[(wsem, wval),
                                  (s_dve, tt_hist[(b, 2 * sh + 1, 6)])],
                           inc=(s_pe, 1))

                ur, un = su % 4, (su + 1) % 4
                su += 2

                def mkbc2(te, ur=ur, un=un):
                    te.matmul(slot_ap(ur), ones1F[0:1, :], lnB[0:1, 0:512],
                              start=True, stop=True)
                    return te.matmul(slot_ap(un), ones1F[0:1, :], lnB[0:1, 512:1024],
                                     start=True, stop=True)
                wb2 = [(s_dve, l7)]
                if slot_evac[ur] > 0:
                    wb2.append((s_dve, slot_evac[ur]))
                if slot_evac[un] > 0:
                    wb2.append((s_dve, slot_evac[un]))
                b2n = p.emit("tensor", mkbc2, waits=wb2, inc=(s_pe, 1))

                ap_hist = {}
                for m in range(C):
                    w1 = [(s_pe, b2n)] if m == 0 else []
                    p.emit("vector", lambda v, m=m, ur=ur: v.tensor_tensor(
                        xres[:, m * 512:(m + 1) * 512],
                        x1sb[:, m * 512:(m + 1) * 512], slot_ap(ur),
                        op=ALU.mult), waits=w1, inc=(s_dve, 1))
                    ap_hist[m] = p.emit("vector", lambda v, m=m, un=un:
                                        v.tensor_tensor(
                        x1sb[:, m * 512:(m + 1) * 512],
                        xres[:, m * 512:(m + 1) * 512], slot_ap(un),
                        op=ALU.add), inc=(s_dve, 1))
                xres_used = ap_hist[C - 1]
                slot_evac[ur] = ap_hist[C - 1]
                slot_evac[un] = ap_hist[C - 1]

                def stx1(g, b=b, sblk=sblk):
                    dst = slab[b][:, sblk * 512:sblk * 512 + 512].rearrange(
                        "(c p) n -> p c n", p=128)
                    return g.dma_start(out=dst, in_=x1sb[:, :].rearrange(
                        "p (c n) -> p c n", c=C))
                store_hist.append(p.emit(
                    "gpsimd", stx1, waits=[(s_dve, ap_hist[C - 1])],
                    inc=(s_ld, 16)))

        p.emit("gpsimd", lambda g, b=b: g.collective_compute(
            "AllGather", ALU.bypass, replica_groups=[list(range(NCORES))],
            ins=[slab[b]], outs=[gath[b]]),
            waits=[(s_ld, p.sems["s_ld"].n)], inc=(s_cc, 1))

    aes.close()
    return (ctx["p"].sems["s_ld"].n)



def weight_maps(inputs):
    return {
        "wfT": prechunk(np.asarray(inputs["L0_ffn_wf"], np.float32).T),
        "ufT": prechunk(np.asarray(inputs["L0_ffn_uf"], np.float32).T),
        "wiT": prechunk(np.asarray(inputs["L0_ffn_wi"], np.float32).T),
        "uiT": prechunk(np.asarray(inputs["L0_ffn_ui"], np.float32).T),
        "wcT": prechunk(np.asarray(inputs["L0_ffn_wc"], np.float32).T),
        "ucT": prechunk(np.asarray(inputs["L0_ffn_uc"], np.float32).T),
        "woT": prechunk(np.asarray(inputs["L0_ffn_wo"], np.float32).T),
        "uoT": prechunk(np.asarray(inputs["L0_ffn_uo"], np.float32).T),
        "bf": pack_bias(inputs["L0_ffn_bf"]), "bi": pack_bias(inputs["L0_ffn_bi"]),
        "bc": pack_bias(inputs["L0_ffn_bc"]), "bo": pack_bias(inputs["L0_ffn_bo"]),
        "inc0": pack_bias(inputs["L0_ffn_in_c"]),
    }


def assemble_out(results):
    full = np.empty((B, S, DM), np.float32)
    for k in range(NCORES):
        ok = np.asarray(results[k]["out"]).reshape(DM, B, TOK)
        full[:, k * TOK:(k + 1) * TOK, :] = ok.transpose(1, 2, 0)
    return full


def kernel(**inputs):
    inputs = {k: np.asarray(v, np.float32) for k, v in inputs.items()}
    base = weight_maps(inputs)
    base["Wq"] = prechunk(inputs["L0_attn_Wq"])
    base["Wk"] = prechunk(inputs["L0_attn_Wk"])
    base["Wv"] = prechunk(inputs["L0_attn_Wv"])
    base["Wpost"] = prechunk(inputs["L0_attn_Wpost"])
    X = inputs["L0_attn_X"]
    in_maps = []
    for k in range(NCORES):
        m = dict(base)
        m["xT"] = np.ascontiguousarray(
            X[[k, 8 + k, 16 + k, 24 + k]].transpose(2, 0, 1).reshape(DM, BLOC * S))
        in_maps.append(m)
    nc = build_program(mode="full")
    res = run_bass_kernel_spmd(nc, in_maps, core_ids=list(range(NCORES)))
    return assemble_out(res.results)


# version 09
# speedup vs baseline: 1.2077x; 1.0140x over previous
import contextlib
import numpy as np
import concourse.bass as bass
import concourse.mybir as mybir
from concourse.bass_utils import run_bass_kernel_spmd

F32 = mybir.dt.float32
BF16 = mybir.dt.bfloat16
AF = mybir.ActivationFunctionType
ALU = mybir.AluOpType

NCORES = 8
DM = 1024
C = DM // 128
B = 32
S = 2048
TOK = S // NCORES
BLOC = B // NCORES
H = 16
DK = 64
LNEPS = 1e-5


class Sem:
    def __init__(self, name):
        self.name = name
        self.n = 0
        self.h = None

    def inc(self, amt=1):
        self.n += amt
        return self.n


class Prog:

    def __init__(self):
        self.ops = {e: [] for e in ("tensor", "scalar", "vector", "gpsimd", "sync")}
        self.sems = {}

    def sem(self, name):
        if name not in self.sems:
            self.sems[name] = Sem(name)
        return self.sems[name]

    def emit(self, engine, fn, waits=(), inc=None):
        n = None
        if inc is not None:
            n = inc[0].inc(inc[1])
        self.ops[engine].append((tuple(waits), fn, inc))
        return n

    def build_engine(self, eng, name):
        for waits, fn, inc in self.ops[name]:
            for s, v in waits:
                if v > 0:
                    eng.wait_ge(s.h, v)
            inst = fn(eng)
            if inc is not None:
                assert inst is not None, f"inc on None inst in {name}"
                inst.then_inc(inc[0].h, inc[1])



def prechunk(w2):
    return np.ascontiguousarray(
        w2.reshape(C, 128, C, 128).transpose(1, 0, 2, 3).reshape(128, C * C * 128)
    )


def pack_bias(b):
    return np.ascontiguousarray(np.asarray(b, np.float32).reshape(C, 128).T)


def wslice(wsb, kc, mc):
    o = (kc * C + mc) * 128
    return wsb[:, o:o + 128]



def build_program(mode="full"):
    nc = bass.Bass("TRN2", target_bir_lowering=False)
    p = Prog()

    gath_kind = "ExternalInput" if mode == "bc" else "Internal"
    gath_kw = {} if mode == "bc" else {"addr_space": "Shared"}
    gath = [nc.dram_tensor(f"gath{j}", [NCORES * DM, S], BF16, kind=gath_kind,
                           **gath_kw).ap()
            for j in range(BLOC)]
    out_h = nc.dram_tensor("out", [DM, B * TOK], F32, kind="ExternalOutput").ap()
    vsp = nc.dram_tensor("vsp", [DM, B * TOK], F32).ap()
    stats_d = nc.dram_tensor("stats_d", [B, 2 * TOK], F32).ap()
    stats_o = nc.dram_tensor("stats_o", [B, 2 * TOK], F32).ap()

    wnames_bc = ["wfT", "ufT", "wiT", "uiT", "wcT", "ucT", "woT", "uoT"]
    whbm = {n: nc.dram_tensor(n, [128, C * C * 128], F32, kind="ExternalInput").ap()
            for n in wnames_bc}
    bnames = ["bf", "bi", "bc", "bo", "inc0"]
    bhbm = {n: nc.dram_tensor(n, [128, C], F32, kind="ExternalInput").ap()
            for n in bnames}

    if mode == "full":
        xT = nc.dram_tensor("xT", [DM, BLOC * S], F32, kind="ExternalInput").ap()
        for n in ["Wq", "Wk", "Wv", "Wpost"]:
            whbm[n] = nc.dram_tensor(n, [128, C * C * 128], F32,
                                     kind="ExternalInput").ap()
        slab = [nc.dram_tensor(f"slab{j}", [DM, S], BF16).ap() for j in range(BLOC)]
    else:
        slab = None
    wbf16_d = {n: nc.dram_tensor(n + "_bf", [128, C * C * 128], BF16).ap()
               for n in whbm}
    xbf_d = (nc.dram_tensor("xT_bf", [DM, BLOC * S], BF16).ap()
             if mode == "full" else None)

    es = contextlib.ExitStack()
    with es:
        T = lambda *a, **k: es.enter_context(nc.sbuf_tensor(*a, **k))
        PS = lambda *a, **k: es.enter_context(nc.psum_tensor(*a, **k))

        ones1 = T("ones1", [1, 128], BF16)
        ones1F = T("ones1F", [1, 128], F32)
        ones128 = T("ones128", [128, 1], BF16)
        epst = T("epst", [128, 1], F32)
        bias_sb = {n: T(f"sb_{n}", [128, C], F32) for n in bnames}

        s_ld = p.sem("s_ld")
        s_lf = p.sem("s_lf")
        s_st = p.sem("s_st")
        s_pe = p.sem("s_pe")
        s_act = p.sem("s_act")
        s_dve = p.sem("s_dve")
        s_cc = p.sem("s_cc")

        dyn = {}

        def ginit(g):
            dyn["g_off"] = g.snap(g.partition_id() * TOK)
            return None
        p.emit("gpsimd", ginit)

        def syinit(sy):
            dyn["s_off"] = sy.snap(sy.partition_id() * TOK)
            return None
        p.emit("sync", syinit)

        p.emit("gpsimd", lambda g: g.memset(ones1[:, :], 1.0), inc=(s_ld, 1))
        p.emit("gpsimd", lambda g: g.memset(ones1F[:, :], 1.0), inc=(s_ld, 1))
        p.emit("gpsimd", lambda g: g.memset(ones128[:, :], 1.0), inc=(s_ld, 1))
        p.emit("gpsimd", lambda g: g.memset(epst[:, :], LNEPS), inc=(s_ld, 1))
        consts_ready = (s_ld, s_ld.n)

        for n in bnames:
            p.emit("sync", lambda sy, n=n: sy.dma_start(
                out=bias_sb[n][:, :], in_=bhbm[n]), inc=(s_lf, 16))
        bias_ready = (s_lf, s_lf.n)

        wcast = {}
        xcast = {}

        def emit_wcast(n):
            wcast[n] = p.emit("gpsimd", lambda g, n=n: g.dma_start(
                out=wbf16_d[n], in_=whbm[n]), inc=(s_ld, 16))

        def emit_xcast(bb_):
            xcast[bb_] = p.emit("gpsimd", lambda g, bb_=bb_: g.dma_start(
                out=xbf_d[:, bb_ * S:(bb_ + 1) * S],
                in_=xT[:, bb_ * S:(bb_ + 1) * S]), inc=(s_ld, 16))

        if mode == "full":
            emit_wcast("Wq")
            emit_xcast(0)
            for n in ("Wk", "Wv", "Wpost"):
                emit_wcast(n)
            for bb_ in range(1, BLOC):
                emit_xcast(bb_)
        for n in whbm:
            if n not in wcast:
                emit_wcast(n)


        a_store_end = 0
        if mode == "full":
            actx = dict(nc=nc, p=p, es=es, T=T, PS=PS, xT=xT, whbm=whbm,
                        wbf16_d=wbf16_d, xbf_d=xbf_d, wcast=wcast, xcast=xcast,
                        slab=slab, gath=gath, ones1=ones1, ones128=ones128,
                        ones1F=ones1F,
                        epst=epst, s_ld=s_ld, s_lf=s_lf, s_st=s_st, s_pe=s_pe,
                        s_act=s_act, s_dve=s_dve, s_cc=s_cc)
            a_store_end = build_phase_a(actx)


        w_sb = {n: T(f"w_{n}", [128, C * C * 128], BF16)
                for n in ["wfT", "ufT", "wiT", "uiT", "wcT", "ucT"]}
        w_sb["woT"] = w_sb["wfT"]
        w_sb["uoT"] = w_sb["ufT"]
        c_sb = T("c_sb", [128, C * TOK], F32)
        cbf = T("cbf", [128, C * TOK], BF16)
        xt_sb = [T(f"xt{i}", [128, C * TOK], BF16) for i in range(2)]
        gts = [T(f"gts{i}", [128, 3 * TOK], F32) for i in range(2)]
        tmp1 = T("tmp1", [128, TOK], F32)
        tmp2 = T("tmp2", [128, TOK], F32)
        uoc = T("uoc", [128, C * TOK], F32)
        x1f = [T(f"x1f{i}", [128, C * TOK], F32) for i in range(2)]
        x1b = [T(f"x1b{i}", [128, C * TOK], BF16) for i in range(2)]
        hv = T("hv", [128, C * TOK], F32)
        vb = T("vb", [128, C * TOK], BF16)
        sqb = T("sqb", [128, C * TOK], BF16)
        sst = [T(f"sst{i}", [1, 2 * TOK], F32) for i in range(2)]
        stT = T("stT", [B, 2 * TOK], F32)
        stX = T("stX", [B, TOK], F32)
        stY = T("stY", [B, TOK], F32)
        stO = T("stO", [B, 2 * TOK], F32)
        stin = [T(f"stin{i}", [1, 2 * TOK], F32) for i in range(2)]
        outsb = [T(f"outsb{i}", [128, C * TOK], F32) for i in range(2)]

        psG = [PS(f"psG{i}", [128, 512], F32) for i in range(4)]
        psS = PS("psS", [1, 512], F32)
        psB = PS("psB", [128, 512], F32)

        wA = [(s_ld, a_store_end)] if mode == "full" else []
        for n in ["wfT", "ufT", "wiT", "uiT", "wcT", "ucT"]:
            p.emit("sync", lambda sy, n=n: sy.dma_start(
                out=w_sb[n][:, :], in_=wbf16_d[n]),
                waits=wA + [(s_ld, wcast[n])], inc=(s_lf, 16))
        wB_ready = (s_lf, s_lf.n)

        wc0 = [bias_ready] + ([(s_ld, a_store_end)] if mode == "full" else [])
        p.emit("vector", lambda v: v.memset(c_sb[:, :], 0.0),
               waits=wc0, inc=(s_dve, 1))
        for c in range(C):
            p.emit("vector", lambda v, c=c: v.tensor_scalar_add(
                c_sb[:, c * TOK:(c + 1) * TOK], c_sb[:, c * TOK:(c + 1) * TOK],
                bias_sb["inc0"][:, c:c + 1]), inc=(s_dve, 1))

        GATES = [("wfT", "ufT", "bf", AF.Sigmoid), ("wiT", "uiT", "bi", AF.Sigmoid),
                 ("wcT", "ucT", "bc", AF.Tanh)]
        pe_group = {}
        act_of_group = {}
        xt_used = {}
        cupd = {}
        cupd[(-1, C - 1)] = s_dve.n
        cupd[(-1, C - 2)] = s_dve.n
        xt_loaded = {}
        ggrp = 0

        for t in range(B):
            j, r = t // NCORES, t % NCORES
            w = [(s_pe, xt_used[t - 2])] if t - 2 in xt_used else []
            if mode == "full":
                w = w + [(s_cc, j + 1), (s_ld, a_store_end)]

            def ldxt(g, j=j, r=r, t=t):
                src = gath[j][r * DM:(r + 1) * DM, bass.ds(dyn["g_off"], TOK)]
                src = src.rearrange("(c p) n -> p c n", p=128)
                return g.dma_start(out=xt_sb[t % 2][:, :].rearrange(
                    "p (c n) -> p c n", c=C), in_=src)
            xt_loaded[t] = p.emit("gpsimd", ldxt, waits=w, inc=(s_ld, 16))

            wv = [(s_dve, cupd[(t - 1, C - 1)])]
            if t > 0:
                wv.append((s_pe, pe_group[(t - 1, C - 1, 2)]))
            cbf_n = p.emit("vector", lambda v: v.tensor_copy(cbf[:, :], c_sb[:, :]),
                           waits=wv, inc=(s_dve, 1))

            for m in range(C):
                for gi, (wn, un, bn, af) in enumerate(GATES):
                    bank = psG[ggrp % 4]
                    wmm = []
                    if m == 0 and gi == 0:
                        wmm = [(s_ld, xt_loaded[t]), (s_lf, wB_ready[1]),
                               (s_dve, cbf_n)]
                    if ggrp - 4 >= 0 and (ggrp - 4) in act_of_group:
                        wmm.append((s_act, act_of_group[ggrp - 4]))

                    def mk_mms(te, wn=wn, un=un, m=m, t=t, bank=bank):
                        inst = None
                        for k in range(C):
                            inst = te.matmul(
                                bank[:, 0:TOK], wslice(w_sb[wn], k, m),
                                xt_sb[t % 2][:, k * TOK:(k + 1) * TOK],
                                start=(k == 0), stop=False)
                        for k in range(C):
                            inst = te.matmul(
                                bank[:, 0:TOK], wslice(w_sb[un], k, m),
                                cbf[:, k * TOK:(k + 1) * TOK],
                                start=False, stop=(k == C - 1))
                        return inst
                    pe_group[(t, m, gi)] = p.emit("tensor", mk_mms, waits=wmm,
                                                  inc=(s_pe, 1))

                    wact = [(s_pe, pe_group[(t, m, gi)])]
                    key = (t, m - 2) if m >= 2 else (t - 1, C - 2 + m)
                    if key in cupd:
                        wact.append((s_dve, cupd[key]))

                    def mk_act(sc, af=af, bn=bn, m=m, gi=gi, bank=bank):
                        return sc.activation(
                            gts[m % 2][:, gi * TOK:(gi + 1) * TOK], bank[:, 0:TOK],
                            af, bias=bias_sb[bn][:, m:m + 1], scale=1.0)
                    act_of_group[ggrp] = p.emit("scalar", mk_act, waits=wact,
                                                inc=(s_act, 1))
                    ggrp += 1

                wcu = [(s_act, s_act.n)]
                cs_lo, cs_hi = m * TOK, (m + 1) * TOK
                g0 = gts[m % 2]
                p.emit("vector", lambda v, g0=g0, lo=cs_lo, hi=cs_hi:
                       v.tensor_tensor(tmp1[:, :], g0[:, 0:TOK], c_sb[:, lo:hi],
                                       op=ALU.mult), waits=wcu, inc=(s_dve, 1))
                p.emit("vector", lambda v, g0=g0: v.tensor_tensor(
                    tmp2[:, :], g0[:, TOK:2 * TOK], g0[:, 2 * TOK:3 * TOK],
                    op=ALU.mult), inc=(s_dve, 1))
                cupd[(t, m)] = p.emit("vector", lambda v, lo=cs_lo, hi=cs_hi:
                                      v.tensor_tensor(c_sb[:, lo:hi], tmp1[:, :],
                                                      tmp2[:, :], op=ALU.add),
                                      inc=(s_dve, 1))
            xt_used[t] = pe_group[(t, C - 1, 2)]

        for n in ["woT", "uoT"]:
            p.emit("sync", lambda sy, n=n: sy.dma_start(
                out=w_sb[n][:, :], in_=wbf16_d[n]),
                waits=[(s_pe, pe_group[(B - 1, C - 1, 2)]),
                       (s_ld, wcast[n])], inc=(s_lf, 16))
        wC_ready = (s_lf, s_lf.n)

        cbf_fin = p.emit("vector", lambda v: v.tensor_copy(cbf[:, :], c_sb[:, :]),
                         waits=[(s_dve, cupd[(B - 1, C - 1)]),
                                (s_pe, pe_group[(B - 1, C - 1, 2)])],
                         inc=(s_dve, 1))

        uoc_done = {}
        for m in range(C):
            bank = psG[m % 2]

            def mk_uoc(te, m=m, bank=bank):
                inst = None
                for k in range(C):
                    inst = te.matmul(bank[:, 0:TOK], wslice(w_sb["uoT"], k, m),
                                     cbf[:, k * TOK:(k + 1) * TOK],
                                     start=(k == 0), stop=(k == C - 1))
                return inst
            wmm = [(s_dve, cbf_fin), (s_lf, wC_ready[1])]
            if m >= 2:
                wmm.append((s_dve, uoc_done[m - 2]))
            gn = p.emit("tensor", mk_uoc, waits=wmm, inc=(s_pe, 1))
            uoc_done[m] = p.emit("vector", lambda v, m=m, bank=bank:
                                 v.tensor_scalar_mul(
                                     uoc[:, m * TOK:(m + 1) * TOK],
                                     bank[:, 0:TOK], 1.0),
                                 waits=[(s_pe, gn)], inc=(s_dve, 1))

        x1_loaded = {}
        x1b_loaded = {}
        wo_done = {}
        a1_hist = {}
        a2_hist = {}
        vadds_last = {}
        spill_done = {}
        e2_hist = {}
        st_stored = {}
        for t in range(B):
            j, r = t // NCORES, t % NCORES
            wldb = [(s_pe, wo_done[t - 2])] if t - 2 in wo_done else []
            if t - 2 in vadds_last:
                wldb.append((s_dve, vadds_last[t - 2]))

            def ldb(g, j=j, r=r, t=t):
                src = gath[j][r * DM:(r + 1) * DM, bass.ds(dyn["g_off"], TOK)]
                src = src.rearrange("(c p) n -> p c n", p=128)
                return g.dma_start(out=x1b[t % 2][:, :].rearrange(
                    "p (c n) -> p c n", c=C), in_=src)
            x1b_loaded[t] = p.emit("gpsimd", ldb, waits=wldb, inc=(s_ld, 16))

            for m in range(C):
                bank = psG[2 + (m % 2)]
                wmm = [(s_ld, x1b_loaded[t])]
                key = (t, m - 2) if m >= 2 else (t - 1, C - 2 + m)
                if key in a1_hist:
                    wmm.append((s_dve, a1_hist[key]))
                if t == 0 and m < 2:
                    wmm.append((s_dve, cbf_fin))

                def mk_wo(te, m=m, t=t, bank=bank):
                    inst = None
                    for k in range(C):
                        inst = te.matmul(bank[:, 0:TOK], wslice(w_sb["woT"], k, m),
                                         x1b[t % 2][:, k * TOK:(k + 1) * TOK],
                                         start=(k == 0), stop=(k == C - 1))
                    return inst
                gn = p.emit("tensor", mk_wo, waits=wmm, inc=(s_pe, 1))
                wo_done[t] = gn

                tmpX = [tmp1, tmp2][m % 2]
                wa1 = [(s_pe, gn)]
                keyp = (t, m - 2) if m >= 2 else (t - 1, C - 2 + m)
                if keyp in a2_hist:
                    wa1.append((s_act, a2_hist[keyp]))
                a1_hist[(t, m)] = p.emit(
                    "vector", lambda v, m=m, bank=bank, tmpX=tmpX:
                    v.tensor_tensor(tmpX[:, :], bank[:, 0:TOK],
                                    uoc[:, m * TOK:(m + 1) * TOK], op=ALU.add),
                    waits=wa1, inc=(s_dve, 1))

                wa2 = [(s_dve, a1_hist[(t, m)])]
                if m == 0 and (t - 1) in spill_done:
                    wa2.append((s_st, spill_done[t - 1]))
                a2_hist[(t, m)] = p.emit(
                    "scalar", lambda sc, m=m, tmpX=tmpX: sc.activation(
                        hv[:, m * TOK:(m + 1) * TOK], tmpX[:, :], AF.Tanh,
                        bias=bias_sb["bo"][:, m:m + 1], scale=1.0),
                    waits=wa2, inc=(s_act, 1))

            for m in range(C):
                wv = [(s_act, a2_hist[(t, m)])]
                n = p.emit("vector", lambda v, m=m, t=t: v.tensor_tensor(
                    hv[:, m * TOK:(m + 1) * TOK], hv[:, m * TOK:(m + 1) * TOK],
                    x1b[t % 2][:, m * TOK:(m + 1) * TOK], op=ALU.add),
                    waits=wv, inc=(s_dve, 1))
            vadds_last[t] = n

            cp1 = p.emit("vector", lambda v: v.tensor_copy(vb[:, :], hv[:, :]),
                         inc=(s_dve, 1))
            cp2 = p.emit("vector", lambda v: v.tensor_tensor(
                sqb[:, :], vb[:, :], vb[:, :], op=ALU.mult), inc=(s_dve, 1))

            def mk_sums(te):
                inst = None
                for c in range(C):
                    inst = te.matmul(psS[:, 0:TOK], ones128[:, :],
                                     vb[:, c * TOK:(c + 1) * TOK],
                                     start=(c == 0), stop=(c == C - 1))
                for c in range(C):
                    inst = te.matmul(psS[:, TOK:2 * TOK], ones128[:, :],
                                     sqb[:, c * TOK:(c + 1) * TOK],
                                     start=(c == 0), stop=(c == C - 1))
                return inst
            wsum = [(s_dve, cp2), (s_ld, consts_ready[1])]
            if t - 1 in e2_hist:
                wsum.append((s_dve, e2_hist[t - 1]))
            sums_n = p.emit("tensor", mk_sums, waits=wsum, inc=(s_pe, 1))

            we1 = [(s_pe, sums_n)]
            if t - 2 in st_stored:
                we1.append((s_st, st_stored[t - 2]))
            p.emit("vector", lambda v, t=t: v.tensor_scalar_mul(
                sst[t % 2][:, 0:TOK], psS[:, 0:TOK], 1.0 / DM),
                waits=we1, inc=(s_dve, 1))
            e2_hist[t] = p.emit("vector", lambda v, t=t: v.tensor_scalar_mul(
                sst[t % 2][:, TOK:2 * TOK], psS[:, TOK:2 * TOK], 1.0 / DM),
                inc=(s_dve, 1))
            st_stored[t] = p.emit("sync", lambda sy, t=t: sy.dma_start(
                out=stats_d[t:t + 1, :], in_=sst[t % 2][:, :]),
                waits=[(s_dve, e2_hist[t])], inc=(s_st, 16))

            def spv(sy, t=t):
                dst = vsp[:, t * TOK:(t + 1) * TOK].rearrange(
                    "(c p) n -> p c n", p=128)
                return sy.dma_start(out=dst, in_=hv[:, :].rearrange(
                    "p (c n) -> p c n", c=C))
            spill_done[t] = p.emit("sync", spv, waits=[(s_dve, vadds_last[t])],
                                   inc=(s_st, 16))

        st_ld = p.emit("sync", lambda sy: sy.dma_start(
            out=stT[:, :], in_=stats_d),
            waits=[(s_st, st_stored[B - 1])], inc=(s_lf, 16))
        rb1 = p.emit("vector", lambda v: v.tensor_tensor(
            stX[:, :], stT[:, 0:TOK], stT[:, 0:TOK], op=ALU.mult),
            waits=[(s_lf, st_ld)], inc=(s_dve, 1))
        rb2 = p.emit("vector", lambda v: v.tensor_tensor(
            stX[:, :], stT[:, TOK:2 * TOK], stX[:, :], op=ALU.subtract),
            inc=(s_dve, 1))
        rb3 = p.emit("scalar", lambda sc: sc.activation(
            stY[:, :], stX[:, :], AF.Ln, bias=epst[0:B, :], scale=1.0),
            waits=[(s_dve, rb2)], inc=(s_act, 1))
        rb4 = p.emit("scalar", lambda sc: sc.activation(
            stX[:, :], stY[:, :], AF.Exp, bias=0.0, scale=-0.5), inc=(s_act, 1))
        rb5 = p.emit("vector", lambda v: v.tensor_copy(
            stO[:, 0:TOK], stX[:, :]), waits=[(s_act, rb4)], inc=(s_dve, 1))
        rb6 = p.emit("vector", lambda v: v.tensor_tensor(
            stY[:, :], stT[:, 0:TOK], stX[:, :], op=ALU.mult), inc=(s_dve, 1))
        rb7 = p.emit("vector", lambda v: v.tensor_scalar(
            stO[:, TOK:2 * TOK], stY[:, :], -1.0, None, op0=ALU.mult),
            inc=(s_dve, 1))
        stO_st = p.emit("sync", lambda sy: sy.dma_start(
            out=stats_o, in_=stO[:, :]), waits=[(s_dve, rb7)], inc=(s_st, 16))

        out_stored = {}
        appl_done = {}
        bc_hist = {}
        vld = {}
        for t in range(B):
            wldv = ([(s_dve, appl_done[t - 2])] if t - 2 in appl_done
                    else [(s_dve, vadds_last[B - 2 + t])])

            def ldv(g, t=t):
                src = vsp[:, t * TOK:(t + 1) * TOK].rearrange(
                    "(c p) n -> p c n", p=128)
                return g.dma_start(out=x1f[t % 2][:, :].rearrange(
                    "p (c n) -> p c n", c=C), in_=src)
            wldv.append((s_st, spill_done[t]))
            vld[t] = p.emit("gpsimd", ldv, waits=wldv, inc=(s_ld, 16))

            wsi = [(s_st, stO_st)]
            if t - 2 in bc_hist:
                wsi.append((s_pe, bc_hist[t - 2]))
            sld = p.emit("sync", lambda sy, t=t: sy.dma_start(
                out=stin[t % 2][:, :], in_=stats_o[t:t + 1, :]),
                waits=wsi, inc=(s_lf, 16))

            def mk_bc(te, t=t):
                te.matmul(psB[:, 0:TOK], ones1F[:, :],
                          stin[t % 2][:, 0:TOK], start=True, stop=True)
                return te.matmul(psB[:, TOK:2 * TOK], ones1F[:, :],
                                 stin[t % 2][:, TOK:2 * TOK],
                                 start=True, stop=True)
            wbc = [(s_lf, sld)]
            if t - 1 in appl_done:
                wbc.append((s_dve, appl_done[t - 1]))
            bc_n = p.emit("tensor", mk_bc, waits=wbc, inc=(s_pe, 1))
            bc_hist[t] = bc_n

            for m in range(C):
                w1 = [(s_pe, bc_n), (s_ld, vld[t])] if m == 0 else []
                if m == 0 and t - 2 in out_stored:
                    w1.append((s_st, out_stored[t - 2]))
                p.emit("vector", lambda v, m=m, t=t: v.tensor_tensor(
                    outsb[t % 2][:, m * TOK:(m + 1) * TOK],
                    x1f[t % 2][:, m * TOK:(m + 1) * TOK], psB[:, 0:TOK],
                    op=ALU.mult), waits=w1, inc=(s_dve, 1))
                appl_done[t] = p.emit("vector", lambda v, m=m, t=t:
                                      v.tensor_tensor(
                    outsb[t % 2][:, m * TOK:(m + 1) * TOK],
                    outsb[t % 2][:, m * TOK:(m + 1) * TOK], psB[:, TOK:2 * TOK],
                    op=ALU.add), inc=(s_dve, 1))

            def stout(sy, t=t):
                dst = out_h[:, t * TOK:(t + 1) * TOK].rearrange(
                    "(c p) n -> p c n", p=128)
                return sy.dma_start(out=dst, in_=outsb[t % 2][:, :].rearrange(
                    "p (c n) -> p c n", c=C))
            out_stored[t] = p.emit("sync", stout, waits=[(s_dve, appl_done[t])],
                                   inc=(s_st, 16))

        p.emit("sync", lambda sy: None, waits=[(s_st, s_st.n)])

        for s in p.sems.values():
            s.h = es.enter_context(nc.semaphore(s.name))
        with nc.Block() as block:
            @block.tensor
            def _(te):
                p.build_engine(te, "tensor")

            @block.scalar
            def _(sc):
                p.build_engine(sc, "scalar")

            @block.vector
            def _(v):
                p.build_engine(v, "vector")

            @block.gpsimd
            def _(g):
                p.build_engine(g, "gpsimd")

            @block.sync
            def _(sy):
                p.build_engine(sy, "sync")

    return nc


def build_phase_a(ctx):
    nc, p, es = ctx["nc"], ctx["p"], ctx["es"]
    T, PS = ctx["T"], ctx["PS"]
    xT, whbm, slab, gath = ctx["xT"], ctx["whbm"], ctx["slab"], ctx["gath"]
    ones1, ones128, epst = ctx["ones1"], ctx["ones128"], ctx["epst"]
    ones1F = ctx["ones1F"]
    s_ld, s_lf, s_st = ctx["s_ld"], ctx["s_lf"], ctx["s_st"]
    s_pe, s_act, s_dve, s_cc = ctx["s_pe"], ctx["s_act"], ctx["s_dve"], ctx["s_cc"]

    aes = contextlib.ExitStack()
    ctx["aes"] = aes
    TA = lambda *a, **k: aes.enter_context(nc.sbuf_tensor(*a, **k))
    PA = lambda *a, **k: aes.enter_context(nc.psum_tensor(*a, **k))

    arena1 = TA("arena1", [128, C * S], BF16)
    xbf = arena1[:, :]
    aT = arena1
    _war = TA("warena", [128, C * C * 128], BF16)
    warena = [_war, _war]
    qT = TA("qT", [128, C * S], BF16)
    kT = TA("kT", [128, C * S], BF16)
    vP = TA("vP", [128, 16 * (H * 65)], BF16)
    exp_sb = [TA(f"exp{i}", [128, 1024], BF16) for i in range(3)]
    vtmpf = TA("vtmpf", [128, 512], F32)
    rb_row = TA("rb_row", [1, 1024], BF16)
    lnzrow = TA("lnzrow", [1, 1024], F32)
    rbB = TA("rbB", [64, 1024], BF16)
    x1sb = TA("x1sb", [128, C * 512], F32)
    xres = TA("xres", [128, C * 512], F32)
    xpb = arena1[:, 8192:8192 + 4096]
    sqA = arena1[:, 8192 + 4096:16384]
    lnS = TA("lnS", [1, 512], F32)
    lnS2 = TA("lnS2", [1, 512], F32)
    lnT = TA("lnT", [1, 512], F32)
    lnU = TA("lnU", [1, 512], F32)
    lnB = TA("lnB", [1, 1024], F32)
    neg3 = TA("neg3", [128, 1], F32)
    identF = TA("identF", [128, 128], F32)

    psSC = [PA(f"psSC{i}", [128, 1024], F32) for i in range(2)]
    psU = [PA(f"psU{i}", [128, 1024], F32) for i in range(2)]

    SLOT = [(psSC[0], 0), (psSC[0], 512), (psSC[1], 0), (psSC[1], 512)]
    slot_evac = [0, 0, 0, 0]
    slot_mm = [0, 0, 0, 0]

    def slot_ap(u, p0=0, p1=128, w=512):
        t_, off = SLOT[u % 4]
        return t_[p0:p1, off:off + w]

    p.emit("gpsimd", lambda g: g.memset(neg3[:, :], -3.0), inc=(s_ld, 1))
    p.emit("gpsimd", lambda g: g.memset(vP[:, :], 1.0), inc=(s_ld, 1))
    p.emit("gpsimd", lambda g: g.memset(identF[:, :], 0.0))
    p.emit("gpsimd", lambda g: g.affine_select(
        out=identF[:, :], in_=identF[:, :], compare_op=ALU.not_equal,
        fill=1.0, base=0, pattern=[[-1, 128]], channel_multiplier=1),
        inc=(s_ld, 1))
    aconsts = (s_ld, s_ld.n)

    PW = [("Wq", qT), ("Wk", kT), ("Wv", None), ("Wpost", None)]
    su = 0
    xbf_loaded = {}
    wload = {}
    wload_mm = {}
    aT_free = {-1: 0}
    smn_last = {}
    store_cnt = 0
    store_hist = []
    exp_hist = {}
    av_hist = {}
    tt_hist = {}
    gelu_hist = {}

    for b in range(BLOC):
        wx = [(s_pe, smn_last[b - 1])] if b - 1 in smn_last else []

        def ldx_half(sy, b=b, half=0):
            src = ctx["xbf_d"][half * 512:half * 512 + 512,
                               b * S:(b + 1) * S].rearrange(
                "(c p) n -> p c n", p=128)
            dst = arena1[:, half * 4 * S:(half + 1) * 4 * S].rearrange(
                "p (c n) -> p c n", c=4)
            return sy.dma_start(out=dst, in_=src)
        wx = wx + [(s_ld, ctx["xcast"][b])]
        xbf_lo = p.emit("sync", lambda sy, b=b: ldx_half(sy, b, 0),
                        waits=wx, inc=(s_lf, 16))
        xbf_hi = p.emit("sync", lambda sy, b=b: ldx_half(sy, b, 1),
                        inc=(s_lf, 16))
        xbf_loaded[b] = xbf_hi

        def emit_wload(pi, b=b):
            wwl = []
            prev = (b, pi - 1) if pi >= 1 else (b - 1, 3)
            if prev in wload_mm:
                wwl = [(s_pe, wload_mm[prev])]
            wn = PW[pi][0]
            wload[(b, pi)] = p.emit("sync", lambda sy, wn=wn, pi=pi: sy.dma_start(
                out=warena[pi % 2][:, :], in_=ctx["wbf16_d"][wn]),
                waits=wwl + [(s_ld, ctx["wcast"][wn])], inc=(s_lf, 16))
        emit_wload(0)

        for pi in range(3):
            wn, dstT = PW[pi][0], PW[pi][1]
            for m in range(C):
                wmm = [(s_lf, max(wload[(b, pi)], xbf_lo))]
                for u in range(4):
                    if slot_evac[u] > 0:
                        wmm.append((s_dve, slot_evac[u]))

                def mk_p(te, pi=pi, m=m, b=b, klo=0, khi=4):
                    inst = None
                    for k in range(klo, khi):
                        lhs = wslice(warena[pi % 2], k, m)
                        for sblk in range(4):
                            inst = te.matmul(
                                slot_ap(sblk), lhs,
                                xbf[:, k * S + sblk * 512:k * S + sblk * 512 + 512],
                                start=(k == 0), stop=(k == C - 1))
                    return inst
                p.emit("tensor", mk_p, waits=wmm, inc=(s_pe, 1))
                gn = p.emit("tensor",
                            lambda te, pi=pi, m=m, b=b: mk_p(te, pi, m, b, 4, 8),
                            waits=[(s_lf, xbf_hi)], inc=(s_pe, 1))
                wload_mm[(b, pi)] = gn

                if pi < 2:
                    for sblk in range(4):
                        def ev(v, dstT=dstT, m=m, sblk=sblk):
                            return v.tensor_scalar_mul(
                                dstT[:, m * S + sblk * 512:m * S + sblk * 512 + 512],
                                slot_ap(sblk), 1.0)
                        slot_evac[sblk] = p.emit("vector", ev, waits=[(s_pe, gn)],
                                                 inc=(s_dve, 1))
                else:
                    for sblk in range(4):
                        def evv(v, sblk=sblk):
                            return v.tensor_scalar_mul(vtmpf[:, :],
                                                       slot_ap(sblk), 1.0)
                        wev = [(s_pe, gn)]
                        if (b, m, sblk) != (0, 0, 0):
                            wev.append((s_pe, trn))
                        evn = p.emit("vector", evv, waits=wev, inc=(s_dve, 1))
                        slot_evac[sblk] = evn
                        for ti in range(4):
                            tc = sblk * 4 + ti

                            def mktr(te, ti=ti):
                                return te.transpose(
                                    psU[0][0:128, 0:128],
                                    vtmpf[:, ti * 128:(ti + 1) * 128],
                                    identF[:, :])
                            wtr = [(s_dve, evn), (s_ld, aconsts[1])]
                            if (b, m, sblk, ti) != (0, 0, 0, 0):
                                wtr.append((s_dve, vp_ev))
                            trn = p.emit("tensor", mktr, waits=wtr, inc=(s_pe, 1))

                            def evp(v, tc=tc, m=m):
                                base = tc * H * 65 + (2 * m) * 65
                                dst = vP[:, base:base + 130].rearrange(
                                    "p (h e) -> p h e", h=2)[:, :, 0:64]
                                srcv = psU[0][0:128, 0:128].rearrange(
                                    "p (h e) -> p h e", h=2)
                                return v.tensor_copy(dst, srcv)
                            vp_ev = p.emit("vector", evp, waits=[(s_pe, trn)],
                                           inc=(s_dve, 1))
            emit_wload(pi + 1)
        vp_done = vp_ev

        expz_hist = {}
        bc_hist = {}
        for sh in range(2):

            def emit_scores(sq, hp, tc):
                bank = psSC[tc % 2]
                wsc = []
                kexp = (b, sq, hp, tc - 2) if tc >= 2 else \
                       (b, sq, hp - 1, 14 + tc) if hp >= 1 else \
                       (b, sq - 1, 7, 14 + tc) if sq >= 1 else \
                       (b - 1, 3, 7, 14 + tc)
                if kexp in exp_hist:
                    wsc.append((s_act, exp_hist[kexp]))
                if hp == 0 and tc == 0 and sq % 2 == 0:
                    for u in range(4):
                        wsc.append((s_dve, slot_evac[u]))
                    if sq == 0:
                        wsc.append((s_dve, vp_done))

                def mksc(te, sq=sq, hp=hp, tc=tc, bank=bank):
                    te.matmul(bank[:, 0:512],
                              kT[0:64, hp * S + tc * 128:hp * S + tc * 128 + 128],
                              qT[0:64, hp * S + sq * 512:hp * S + sq * 512 + 512],
                              start=True, stop=True)
                    return te.matmul(
                        bank[:, 512:1024],
                        kT[64:128, hp * S + tc * 128:hp * S + tc * 128 + 128],
                        qT[64:128, hp * S + sq * 512:hp * S + sq * 512 + 512],
                        start=True, stop=True)
                return p.emit("tensor", mksc, waits=wsc, inc=(s_pe, 1))

            def emit_exp(sq, hp, tc, scn):
                bank = psSC[tc % 2]
                wex = [(s_pe, scn)]
                tcp = 15 - ((15 - tc) % 3)
                kav = (b, sq, hp, tc - 3) if tc >= 3 else \
                      (b, sq, hp - 1, tcp) if hp >= 1 else \
                      (b, sq - 1, 7, tcp) if sq >= 1 else \
                      (b - 1, 3, 7, tcp)
                if kav in av_hist:
                    wex.append((s_pe, av_hist[kav]))

                def mkex(sc, tc=tc, bank=bank):
                    return sc.activation(exp_sb[tc % 3][:, :], bank[:, :],
                                         AF.Exp, bias=neg3[:, :], scale=0.125)
                exp_hist[(b, sq, hp, tc)] = p.emit("scalar", mkex, waits=wex,
                                                   inc=(s_act, 1))

            def emit_av(sq, hp, tc):
                wav = [(s_act, exp_hist[(b, sq, hp, tc)])]
                if tc == 0:
                    ktt = (b, sq, hp - 2) if hp >= 2 else \
                          (b, sq - 1, 6 + hp) if sq >= 1 else \
                          (b - 1, 3, 6 + hp)
                    if ktt in tt_hist:
                        wav.append((s_dve, tt_hist[ktt]))

                def mkav(te, hp=hp, tc=tc):
                    te.matmul(psU[hp % 2][0:65, 0:512],
                              vP[:, tc * H * 65 + 2 * hp * 65:
                                 tc * H * 65 + 2 * hp * 65 + 65],
                              exp_sb[tc % 3][:, 0:512],
                              start=(tc == 0), stop=(tc == 15))
                    return te.matmul(
                        psU[hp % 2][0:65, 512:1024],
                        vP[:, tc * H * 65 + (2 * hp + 1) * 65:
                           tc * H * 65 + (2 * hp + 1) * 65 + 65],
                        exp_sb[tc % 3][:, 512:1024],
                        start=(tc == 0), stop=(tc == 15))
                av_hist[(b, sq, hp, tc)] = p.emit("tensor", mkav, waits=wav,
                                                  inc=(s_pe, 1))

            def emit_tail_a(sq, hp):
                ln_n = p.emit("scalar", lambda sc, hp=hp: sc.activation(
                    lnzrow[:, :], psU[hp % 2][64:65, :], AF.Ln),
                    waits=[(s_pe, av_hist[(b, sq, hp, 15)])], inc=(s_act, 1))

                def mkexpz(sc):
                    with nc.allow_low_precision(
                            reason="softmax 1/Z stored bf16; product is "
                                   "rounded to bf16 anyway; diluted by residual"):
                        return sc.activation(rb_row[:, :], lnzrow[:, :],
                                             AF.Exp, scale=-1.0)
                wz = []
                kbc = (sq, hp - 2) if hp >= 2 else (sq - 1, 6 + hp)
                if kbc in bc_hist:
                    wz.append((s_pe, bc_hist[kbc]))
                expz_hist[(sq, hp)] = p.emit("scalar", mkexpz, waits=wz,
                                             inc=(s_act, 1))

            def emit_tail_b(sq, hp, sq2):
                def mkbc(te, hp=hp):
                    inst = None
                    for si in range(2):
                        inst = te.matmul(
                            psU[hp % 2][64:128, si * 512:si * 512 + 512],
                            ones1[0:1, 0:64],
                            rb_row[:, si * 512:si * 512 + 512],
                            start=True, stop=True)
                    return inst
                bc_hist[(sq, hp)] = p.emit(
                    "tensor", mkbc, waits=[(s_act, expz_hist[(sq, hp)])],
                    inc=(s_pe, 1))
                ebn = p.emit("vector", lambda v, hp=hp: v.tensor_scalar_mul(
                    rbB[:, :], psU[hp % 2][64:128, :], 1.0),
                    waits=[(s_pe, bc_hist[(sq, hp)])], inc=(s_dve, 1))
                p.emit("vector", lambda v, hp=hp, sq2=sq2: v.tensor_tensor(
                    arena1[0:64, hp * 1024 + sq2 * 512:hp * 1024 + sq2 * 512 + 512],
                    psU[hp % 2][0:64, 0:512], rbB[:, 0:512], op=ALU.mult),
                    inc=(s_dve, 1))
                tt_hist[(b, sq, hp)] = p.emit(
                    "vector", lambda v, hp=hp, sq2=sq2: v.tensor_tensor(
                        arena1[64:128, hp * 1024 + sq2 * 512:
                               hp * 1024 + sq2 * 512 + 512],
                        psU[hp % 2][0:64, 512:1024], rbB[:, 512:1024],
                        op=ALU.mult), inc=(s_dve, 1))

            for sq2 in range(2):
                sq = sh * 2 + sq2
                for hp in range(8):
                    for tc in range(16):
                        scn = emit_scores(sq, hp, tc)
                        emit_exp(sq, hp, tc, scn)
                        if tc >= 1:
                            emit_av(sq, hp, tc - 1)
                        kprev = (sq, hp - 1) if hp >= 1 else (sq - 1, 7)
                        if kprev in av_hist or (b, kprev[0], kprev[1], 15) in av_hist:
                            pass
                        if tc == 2 and (b, kprev[0], kprev[1], 15) in av_hist:
                            if kprev not in expz_hist:
                                emit_tail_a(kprev[0], kprev[1])
                        if tc == 8 and kprev in expz_hist and kprev not in bc_hist:
                            emit_tail_b(kprev[0], kprev[1], kprev[0] % 2)
                    emit_av(sq, hp, 15)
            emit_tail_a(2 * sh + 1, 7)
            emit_tail_b(2 * sh + 1, 7, 1)

            def mkgl(sc):
                return sc.activation(arena1[:, 0:C * 1024], arena1[:, 0:C * 1024],
                                     AF.Gelu)
            gelu_hist[(b, sh)] = p.emit("scalar", mkgl,
                                        waits=[(s_dve, tt_hist[(b, 2 * sh + 1, 7)])],
                                        inc=(s_act, 1))

            for sb2 in range(2):
                sblk = sh * 2 + sb2

                def ldxr(sy, b=b, sblk=sblk):
                    src = xT[:, b * S + sblk * 512:b * S + sblk * 512 + 512]
                    src = src.rearrange("(c p) n -> p c n", p=128)
                    return sy.dma_start(out=xres[:, :].rearrange(
                        "p (c n) -> p c n", c=C), in_=src)
                wxr = [(s_dve, xres_used)] if b + sblk > 0 else []
                xrn = p.emit("sync", ldxr, waits=wxr, inc=(s_lf, 16))

                stt_hist = {}
                for m in range(C):
                    u = su % 4
                    wmm = [(s_act, gelu_hist[(b, sh)])]
                    if slot_evac[u] > 0:
                        wmm.append((s_dve, slot_evac[u]))

                    def mkwp(te, m=m, sb2=sb2, u=u):
                        inst = None
                        for k in range(C):
                            inst = te.matmul(
                                slot_ap(u), wslice(warena[1], k, m),
                                arena1[:, k * 1024 + sb2 * 512:
                                       k * 1024 + sb2 * 512 + 512],
                                start=(k == 0), stop=(k == C - 1))
                        return inst
                    gn = p.emit("tensor", mkwp, waits=wmm, inc=(s_pe, 1))
                    slot_mm[u] = gn
                    aT_free[b] = gn
                    wload_mm[(b, 3)] = gn

                    def mkstt(v, m=m, u=u):
                        return v.scalar_tensor_tensor(
                            x1sb[:, m * 512:(m + 1) * 512], xres[:, m * 512:(m + 1) * 512],
                            2.0, slot_ap(u), op0=ALU.mult, op1=ALU.add)
                    ws = [(s_pe, gn)]
                    if m == 0:
                        ws.append((s_lf, xrn))
                        if store_hist:
                            ws.append((s_ld, store_hist[-1]))
                    slot_evac[u] = p.emit("vector", mkstt, waits=ws, inc=(s_dve, 1))
                    stt_hist[m] = slot_evac[u]
                    su += 1

                cpx = p.emit("vector", lambda v: v.tensor_copy(
                    xpb, x1sb[:, :]), waits=[(s_dve, stt_hist[C - 1])],
                    inc=(s_dve, 1))
                cps = p.emit("vector", lambda v: v.tensor_tensor(
                    sqA, xpb, xpb, op=ALU.mult), inc=(s_dve, 1))

                us_, uq = su % 4, (su + 1) % 4
                su += 2

                def mksums(te, us_=us_, uq=uq):
                    inst = None
                    for k in range(C):
                        inst = te.matmul(slot_ap(us_, 0, 1), ones128[:, :],
                                         xpb[:, k * 512:(k + 1) * 512],
                                         start=(k == 0), stop=(k == C - 1))
                    for k in range(C):
                        inst = te.matmul(slot_ap(uq, 0, 1), ones128[:, :],
                                         sqA[:, k * 512:(k + 1) * 512],
                                         start=(k == 0), stop=(k == C - 1))
                    return inst
                wsm = [(s_dve, cps)]
                if slot_evac[us_] > 0:
                    wsm.append((s_dve, slot_evac[us_]))
                if slot_evac[uq] > 0:
                    wsm.append((s_dve, slot_evac[uq]))
                smn = p.emit("tensor", mksums, waits=wsm, inc=(s_pe, 1))
                smn_last[b] = smn

                e1 = p.emit("vector", lambda v, us_=us_: v.tensor_scalar_mul(
                    lnS[:, :], slot_ap(us_, 0, 1), 1.0 / DM),
                    waits=[(s_pe, smn)], inc=(s_dve, 1))
                e2 = p.emit("vector", lambda v, uq=uq: v.tensor_scalar_mul(
                    lnS2[:, :], slot_ap(uq, 0, 1), 1.0 / DM), inc=(s_dve, 1))
                slot_evac[us_] = e1
                slot_evac[uq] = e2
                l1 = p.emit("vector", lambda v: v.tensor_tensor(
                    lnT[:, :], lnS[:, :], lnS[:, :], op=ALU.mult), inc=(s_dve, 1))
                l2 = p.emit("vector", lambda v: v.tensor_tensor(
                    lnT[:, :], lnS2[:, :], lnT[:, :], op=ALU.subtract),
                    inc=(s_dve, 1))
                l3 = p.emit("scalar", lambda sc: sc.activation(
                    lnU[:, :], lnT[:, :], AF.Ln, bias=epst[0:1, :], scale=1.0),
                    waits=[(s_dve, l2)], inc=(s_act, 1))
                l4 = p.emit("scalar", lambda sc: sc.activation(
                    lnT[:, :], lnU[:, :], AF.Exp, bias=0.0, scale=-0.5),
                    inc=(s_act, 1))
                l5 = p.emit("vector", lambda v: v.tensor_copy(
                    lnB[:, 0:512], lnT[:, :]), waits=[(s_act, l4)], inc=(s_dve, 1))
                l6 = p.emit("vector", lambda v: v.tensor_tensor(
                    lnU[:, :], lnS[:, :], lnT[:, :], op=ALU.mult), inc=(s_dve, 1))
                l7 = p.emit("vector", lambda v: v.tensor_scalar(
                    lnB[:, 512:1024], lnU[:, :], -1.0, None, op0=ALU.mult),
                    inc=(s_dve, 1))

                for wsem, wval in [(s_dve, e1), (s_dve, l2), (s_act, l3),
                                   (s_dve, l7)]:
                    def mkdum(te):
                        return te.matmul(psU[0][0:128, 0:512], kT[:, 0:128],
                                         qT[:, 0:512], start=True, stop=True)
                    p.emit("tensor", mkdum,
                           waits=[(wsem, wval),
                                  (s_dve, tt_hist[(b, 2 * sh + 1, 6)])],
                           inc=(s_pe, 1))

                ur, un = su % 4, (su + 1) % 4
                su += 2

                def mkbc2(te, ur=ur, un=un):
                    te.matmul(slot_ap(ur), ones1F[0:1, :], lnB[0:1, 0:512],
                              start=True, stop=True)
                    return te.matmul(slot_ap(un), ones1F[0:1, :], lnB[0:1, 512:1024],
                                     start=True, stop=True)
                wb2 = [(s_dve, l7)]
                if slot_evac[ur] > 0:
                    wb2.append((s_dve, slot_evac[ur]))
                if slot_evac[un] > 0:
                    wb2.append((s_dve, slot_evac[un]))
                b2n = p.emit("tensor", mkbc2, waits=wb2, inc=(s_pe, 1))

                ap_hist = {}
                for m in range(C):
                    w1 = [(s_pe, b2n)] if m == 0 else []
                    p.emit("vector", lambda v, m=m, ur=ur: v.tensor_tensor(
                        xres[:, m * 512:(m + 1) * 512],
                        x1sb[:, m * 512:(m + 1) * 512], slot_ap(ur),
                        op=ALU.mult), waits=w1, inc=(s_dve, 1))
                    ap_hist[m] = p.emit("vector", lambda v, m=m, un=un:
                                        v.tensor_tensor(
                        x1sb[:, m * 512:(m + 1) * 512],
                        xres[:, m * 512:(m + 1) * 512], slot_ap(un),
                        op=ALU.add), inc=(s_dve, 1))
                xres_used = ap_hist[C - 1]
                slot_evac[ur] = ap_hist[C - 1]
                slot_evac[un] = ap_hist[C - 1]

                def stx1(g, b=b, sblk=sblk):
                    dst = slab[b][:, sblk * 512:sblk * 512 + 512].rearrange(
                        "(c p) n -> p c n", p=128)
                    return g.dma_start(out=dst, in_=x1sb[:, :].rearrange(
                        "p (c n) -> p c n", c=C))
                store_hist.append(p.emit(
                    "gpsimd", stx1, waits=[(s_dve, ap_hist[C - 1])],
                    inc=(s_ld, 16)))

        p.emit("gpsimd", lambda g, b=b: g.collective_compute(
            "AllGather", ALU.bypass, replica_groups=[list(range(NCORES))],
            ins=[slab[b]], outs=[gath[b]]),
            waits=[(s_ld, p.sems["s_ld"].n)], inc=(s_cc, 1))

    aes.close()
    return (ctx["p"].sems["s_ld"].n)



def weight_maps(inputs):
    return {
        "wfT": prechunk(np.asarray(inputs["L0_ffn_wf"], np.float32).T),
        "ufT": prechunk(np.asarray(inputs["L0_ffn_uf"], np.float32).T),
        "wiT": prechunk(np.asarray(inputs["L0_ffn_wi"], np.float32).T),
        "uiT": prechunk(np.asarray(inputs["L0_ffn_ui"], np.float32).T),
        "wcT": prechunk(np.asarray(inputs["L0_ffn_wc"], np.float32).T),
        "ucT": prechunk(np.asarray(inputs["L0_ffn_uc"], np.float32).T),
        "woT": prechunk(np.asarray(inputs["L0_ffn_wo"], np.float32).T),
        "uoT": prechunk(np.asarray(inputs["L0_ffn_uo"], np.float32).T),
        "bf": pack_bias(inputs["L0_ffn_bf"]), "bi": pack_bias(inputs["L0_ffn_bi"]),
        "bc": pack_bias(inputs["L0_ffn_bc"]), "bo": pack_bias(inputs["L0_ffn_bo"]),
        "inc0": pack_bias(inputs["L0_ffn_in_c"]),
    }


def assemble_out(results):
    full = np.empty((B, S, DM), np.float32)
    for k in range(NCORES):
        ok = np.asarray(results[k]["out"]).reshape(DM, B, TOK)
        full[:, k * TOK:(k + 1) * TOK, :] = ok.transpose(1, 2, 0)
    return full


def kernel(**inputs):
    inputs = {k: np.asarray(v, np.float32) for k, v in inputs.items()}
    base = weight_maps(inputs)
    base["Wq"] = prechunk(inputs["L0_attn_Wq"])
    base["Wk"] = prechunk(inputs["L0_attn_Wk"])
    base["Wv"] = prechunk(inputs["L0_attn_Wv"])
    base["Wpost"] = prechunk(inputs["L0_attn_Wpost"])
    X = inputs["L0_attn_X"]
    in_maps = []
    for k in range(NCORES):
        m = dict(base)
        m["xT"] = np.ascontiguousarray(
            X[[k, 8 + k, 16 + k, 24 + k]].transpose(2, 0, 1).reshape(DM, BLOC * S))
        in_maps.append(m)
    nc = build_program(mode="full")
    res = run_bass_kernel_spmd(nc, in_maps, core_ids=list(range(NCORES)))
    return assemble_out(res.results)


# version 10
# speedup vs baseline: 1.2123x; 1.0037x over previous
import contextlib
import numpy as np
import concourse.bass as bass
import concourse.mybir as mybir
from concourse.bass_utils import run_bass_kernel_spmd

F32 = mybir.dt.float32
BF16 = mybir.dt.bfloat16
AF = mybir.ActivationFunctionType
ALU = mybir.AluOpType

NCORES = 8
DM = 1024
C = DM // 128
B = 32
S = 2048
TOK = S // NCORES
BLOC = B // NCORES
H = 16
DK = 64
LNEPS = 1e-5


class Sem:
    def __init__(self, name):
        self.name = name
        self.n = 0
        self.h = None

    def inc(self, amt=1):
        self.n += amt
        return self.n


class Prog:

    def __init__(self):
        self.ops = {e: [] for e in ("tensor", "scalar", "vector", "gpsimd", "sync")}
        self.sems = {}

    def sem(self, name):
        if name not in self.sems:
            self.sems[name] = Sem(name)
        return self.sems[name]

    def emit(self, engine, fn, waits=(), inc=None):
        n = None
        if inc is not None:
            n = inc[0].inc(inc[1])
        self.ops[engine].append((tuple(waits), fn, inc))
        return n

    def build_engine(self, eng, name):
        for waits, fn, inc in self.ops[name]:
            for s, v in waits:
                if v > 0:
                    eng.wait_ge(s.h, v)
            inst = fn(eng)
            if inc is not None:
                assert inst is not None, f"inc on None inst in {name}"
                inst.then_inc(inc[0].h, inc[1])



def prechunk(w2):
    return np.ascontiguousarray(
        w2.reshape(C, 128, C, 128).transpose(1, 0, 2, 3).reshape(128, C * C * 128)
    )


def pack_bias(b):
    return np.ascontiguousarray(np.asarray(b, np.float32).reshape(C, 128).T)


def wslice(wsb, kc, mc):
    o = (kc * C + mc) * 128
    return wsb[:, o:o + 128]



def build_program(mode="full"):
    nc = bass.Bass("TRN2", target_bir_lowering=False)
    p = Prog()

    gath_kind = "ExternalInput" if mode == "bc" else "Internal"
    gath_kw = {} if mode == "bc" else {"addr_space": "Shared"}
    gath = [nc.dram_tensor(f"gath{j}", [NCORES * DM, S], BF16, kind=gath_kind,
                           **gath_kw).ap()
            for j in range(BLOC)]
    out_h = nc.dram_tensor("out", [DM, B * TOK], F32, kind="ExternalOutput").ap()
    vsp = nc.dram_tensor("vsp", [DM, B * TOK], F32).ap()
    stats_d = nc.dram_tensor("stats_d", [B, 2 * TOK], F32).ap()
    stats_o = nc.dram_tensor("stats_o", [B, 2 * TOK], F32).ap()

    wnames_bc = ["wfT", "ufT", "wiT", "uiT", "wcT", "ucT", "woT", "uoT"]
    whbm = {n: nc.dram_tensor(n, [128, C * C * 128], F32, kind="ExternalInput").ap()
            for n in wnames_bc}
    bnames = ["bf", "bi", "bc", "bo", "inc0"]
    bhbm = {n: nc.dram_tensor(n, [128, C], F32, kind="ExternalInput").ap()
            for n in bnames}

    if mode == "full":
        xT = nc.dram_tensor("xT", [DM, BLOC * S], F32, kind="ExternalInput").ap()
        for n in ["Wq", "Wk", "Wv", "Wpost"]:
            whbm[n] = nc.dram_tensor(n, [128, C * C * 128], F32,
                                     kind="ExternalInput").ap()
        slab = [nc.dram_tensor(f"slab{j}", [DM, S], BF16).ap() for j in range(BLOC)]
    else:
        slab = None
    wbf16_d = {n: nc.dram_tensor(n + "_bf", [128, C * C * 128], BF16).ap()
               for n in whbm}
    xbf_d = (nc.dram_tensor("xT_bf", [DM, BLOC * S], BF16).ap()
             if mode == "full" else None)

    es = contextlib.ExitStack()
    with es:
        T = lambda *a, **k: es.enter_context(nc.sbuf_tensor(*a, **k))
        PS = lambda *a, **k: es.enter_context(nc.psum_tensor(*a, **k))

        ones1 = T("ones1", [1, 128], BF16)
        ones1F = T("ones1F", [1, 128], F32)
        ones128 = T("ones128", [128, 1], BF16)
        epst = T("epst", [128, 1], F32)
        bias_sb = {n: T(f"sb_{n}", [128, C], F32) for n in bnames}

        s_ld = p.sem("s_ld")
        s_lf = p.sem("s_lf")
        s_st = p.sem("s_st")
        s_pe = p.sem("s_pe")
        s_act = p.sem("s_act")
        s_dve = p.sem("s_dve")
        s_cc = p.sem("s_cc")

        dyn = {}

        def ginit(g):
            dyn["g_off"] = g.snap(g.partition_id() * TOK)
            return None
        p.emit("gpsimd", ginit)

        def syinit(sy):
            dyn["s_off"] = sy.snap(sy.partition_id() * TOK)
            return None
        p.emit("sync", syinit)

        p.emit("gpsimd", lambda g: g.memset(ones1[:, :], 1.0), inc=(s_ld, 1))
        p.emit("gpsimd", lambda g: g.memset(ones1F[:, :], 1.0), inc=(s_ld, 1))
        p.emit("gpsimd", lambda g: g.memset(ones128[:, :], 1.0), inc=(s_ld, 1))
        p.emit("gpsimd", lambda g: g.memset(epst[:, :], LNEPS), inc=(s_ld, 1))
        consts_ready = (s_ld, s_ld.n)

        for n in bnames:
            p.emit("sync", lambda sy, n=n: sy.dma_start(
                out=bias_sb[n][:, :], in_=bhbm[n]), inc=(s_lf, 16))
        bias_ready = (s_lf, s_lf.n)

        wcast = {}
        xcast = {}

        def emit_wcast(n):
            wcast[n] = p.emit("gpsimd", lambda g, n=n: g.dma_start(
                out=wbf16_d[n], in_=whbm[n]), inc=(s_ld, 16))

        def emit_xcast(bb_):
            xcast[bb_] = p.emit("gpsimd", lambda g, bb_=bb_: g.dma_start(
                out=xbf_d[:, bb_ * S:(bb_ + 1) * S],
                in_=xT[:, bb_ * S:(bb_ + 1) * S]), inc=(s_ld, 16))

        if mode == "full":
            emit_wcast("Wq")
            emit_xcast(0)
            for n in ("Wk", "Wv", "Wpost"):
                emit_wcast(n)
            for bb_ in range(1, BLOC):
                emit_xcast(bb_)
        for n in whbm:
            if n not in wcast:
                emit_wcast(n)


        a_store_end = 0
        if mode == "full":
            actx = dict(nc=nc, p=p, es=es, T=T, PS=PS, xT=xT, whbm=whbm,
                        wbf16_d=wbf16_d, xbf_d=xbf_d, wcast=wcast, xcast=xcast,
                        slab=slab, gath=gath, ones1=ones1, ones128=ones128,
                        ones1F=ones1F,
                        epst=epst, s_ld=s_ld, s_lf=s_lf, s_st=s_st, s_pe=s_pe,
                        s_act=s_act, s_dve=s_dve, s_cc=s_cc)
            a_store_end = build_phase_a(actx)


        w_sb = {n: T(f"w_{n}", [128, C * C * 128], BF16)
                for n in ["wfT", "ufT", "wiT", "uiT", "wcT", "ucT"]}
        w_sb["woT"] = w_sb["wfT"]
        w_sb["uoT"] = w_sb["ufT"]
        c_sb = T("c_sb", [128, C * TOK], F32)
        cbf = T("cbf", [128, C * TOK], BF16)
        xt_sb = [T(f"xt{i}", [128, C * TOK], BF16) for i in range(2)]
        gts = [T(f"gts{i}", [128, 3 * TOK], F32) for i in range(2)]
        tmp1 = T("tmp1", [128, TOK], F32)
        tmp2 = T("tmp2", [128, TOK], F32)
        uoc = T("uoc", [128, C * TOK], F32)
        x1f = [T(f"x1f{i}", [128, C * TOK], F32) for i in range(2)]
        x1b = [T(f"x1b{i}", [128, C * TOK], BF16) for i in range(2)]
        hv = T("hv", [128, C * TOK], F32)
        vb = T("vb", [128, C * TOK], BF16)
        sqb = T("sqb", [128, C * TOK], BF16)
        sst = [T(f"sst{i}", [1, 2 * TOK], F32) for i in range(2)]
        stT = T("stT", [B, 2 * TOK], F32)
        stX = T("stX", [B, TOK], F32)
        stY = T("stY", [B, TOK], F32)
        stO = T("stO", [B, 2 * TOK], F32)
        stin = [T(f"stin{i}", [1, 2 * TOK], F32) for i in range(2)]
        outsb = [T(f"outsb{i}", [128, C * TOK], F32) for i in range(2)]

        psG = [PS(f"psG{i}", [128, 512], F32) for i in range(4)]
        psS = PS("psS", [1, 512], F32)
        psB = PS("psB", [128, 512], F32)

        wA = [(s_ld, a_store_end)] if mode == "full" else []
        for n in ["wfT", "ufT", "wiT", "uiT", "wcT", "ucT"]:
            p.emit("sync", lambda sy, n=n: sy.dma_start(
                out=w_sb[n][:, :], in_=wbf16_d[n]),
                waits=wA + [(s_ld, wcast[n])], inc=(s_lf, 16))
        wB_ready = (s_lf, s_lf.n)

        wc0 = [bias_ready] + ([(s_ld, a_store_end)] if mode == "full" else [])
        p.emit("vector", lambda v: v.memset(c_sb[:, :], 0.0),
               waits=wc0, inc=(s_dve, 1))
        for c in range(C):
            p.emit("vector", lambda v, c=c: v.tensor_scalar_add(
                c_sb[:, c * TOK:(c + 1) * TOK], c_sb[:, c * TOK:(c + 1) * TOK],
                bias_sb["inc0"][:, c:c + 1]), inc=(s_dve, 1))

        GATES = [("wfT", "ufT", "bf", AF.Sigmoid), ("wiT", "uiT", "bi", AF.Sigmoid),
                 ("wcT", "ucT", "bc", AF.Tanh)]
        pe_group = {}
        act_of_group = {}
        xt_used = {}
        cupd = {}
        cupd[(-1, C - 1)] = s_dve.n
        cupd[(-1, C - 2)] = s_dve.n
        xt_loaded = {}
        ggrp = 0

        for t in range(B):
            j, r = t // NCORES, t % NCORES
            w = [(s_pe, xt_used[t - 2])] if t - 2 in xt_used else []
            if mode == "full":
                w = w + [(s_cc, j + 1), (s_ld, a_store_end)]

            def ldxt(g, j=j, r=r, t=t):
                src = gath[j][r * DM:(r + 1) * DM, bass.ds(dyn["g_off"], TOK)]
                src = src.rearrange("(c p) n -> p c n", p=128)
                return g.dma_start(out=xt_sb[t % 2][:, :].rearrange(
                    "p (c n) -> p c n", c=C), in_=src)
            xt_loaded[t] = p.emit("gpsimd", ldxt, waits=w, inc=(s_ld, 16))

            wv = [(s_dve, cupd[(t - 1, C - 1)])]
            if t > 0:
                wv.append((s_pe, pe_group[(t - 1, C - 1, 2)]))
            cbf_n = p.emit("vector", lambda v: v.tensor_copy(cbf[:, :], c_sb[:, :]),
                           waits=wv, inc=(s_dve, 1))

            for m in range(C):
                for gi, (wn, un, bn, af) in enumerate(GATES):
                    bank = psG[ggrp % 4]
                    wmm = []
                    if m == 0 and gi == 0:
                        wmm = [(s_ld, xt_loaded[t]), (s_lf, wB_ready[1]),
                               (s_dve, cbf_n)]
                    if ggrp - 4 >= 0 and (ggrp - 4) in act_of_group:
                        wmm.append((s_act, act_of_group[ggrp - 4]))

                    def mk_mms(te, wn=wn, un=un, m=m, t=t, bank=bank):
                        inst = None
                        for k in range(C):
                            inst = te.matmul(
                                bank[:, 0:TOK], wslice(w_sb[wn], k, m),
                                xt_sb[t % 2][:, k * TOK:(k + 1) * TOK],
                                start=(k == 0), stop=False)
                        for k in range(C):
                            inst = te.matmul(
                                bank[:, 0:TOK], wslice(w_sb[un], k, m),
                                cbf[:, k * TOK:(k + 1) * TOK],
                                start=False, stop=(k == C - 1))
                        return inst
                    pe_group[(t, m, gi)] = p.emit("tensor", mk_mms, waits=wmm,
                                                  inc=(s_pe, 1))

                    wact = [(s_pe, pe_group[(t, m, gi)])]
                    key = (t, m - 2) if m >= 2 else (t - 1, C - 2 + m)
                    if key in cupd:
                        wact.append((s_dve, cupd[key]))

                    def mk_act(sc, af=af, bn=bn, m=m, gi=gi, bank=bank):
                        return sc.activation(
                            gts[m % 2][:, gi * TOK:(gi + 1) * TOK], bank[:, 0:TOK],
                            af, bias=bias_sb[bn][:, m:m + 1], scale=1.0)
                    act_of_group[ggrp] = p.emit("scalar", mk_act, waits=wact,
                                                inc=(s_act, 1))
                    ggrp += 1

                wcu = [(s_act, s_act.n)]
                cs_lo, cs_hi = m * TOK, (m + 1) * TOK
                g0 = gts[m % 2]
                p.emit("vector", lambda v, g0=g0, lo=cs_lo, hi=cs_hi:
                       v.tensor_tensor(tmp1[:, :], g0[:, 0:TOK], c_sb[:, lo:hi],
                                       op=ALU.mult), waits=wcu, inc=(s_dve, 1))
                p.emit("vector", lambda v, g0=g0: v.tensor_tensor(
                    tmp2[:, :], g0[:, TOK:2 * TOK], g0[:, 2 * TOK:3 * TOK],
                    op=ALU.mult), inc=(s_dve, 1))
                cupd[(t, m)] = p.emit("vector", lambda v, lo=cs_lo, hi=cs_hi:
                                      v.tensor_tensor(c_sb[:, lo:hi], tmp1[:, :],
                                                      tmp2[:, :], op=ALU.add),
                                      inc=(s_dve, 1))
            xt_used[t] = pe_group[(t, C - 1, 2)]

        for n in ["woT", "uoT"]:
            p.emit("sync", lambda sy, n=n: sy.dma_start(
                out=w_sb[n][:, :], in_=wbf16_d[n]),
                waits=[(s_pe, pe_group[(B - 1, C - 1, 2)]),
                       (s_ld, wcast[n])], inc=(s_lf, 16))
        wC_ready = (s_lf, s_lf.n)

        cbf_fin = p.emit("vector", lambda v: v.tensor_copy(cbf[:, :], c_sb[:, :]),
                         waits=[(s_dve, cupd[(B - 1, C - 1)]),
                                (s_pe, pe_group[(B - 1, C - 1, 2)])],
                         inc=(s_dve, 1))

        uoc_done = {}
        for m in range(C):
            bank = psG[m % 2]

            def mk_uoc(te, m=m, bank=bank):
                inst = None
                for k in range(C):
                    inst = te.matmul(bank[:, 0:TOK], wslice(w_sb["uoT"], k, m),
                                     cbf[:, k * TOK:(k + 1) * TOK],
                                     start=(k == 0), stop=(k == C - 1))
                return inst
            wmm = [(s_dve, cbf_fin), (s_lf, wC_ready[1])]
            if m >= 2:
                wmm.append((s_dve, uoc_done[m - 2]))
            gn = p.emit("tensor", mk_uoc, waits=wmm, inc=(s_pe, 1))
            uoc_done[m] = p.emit("vector", lambda v, m=m, bank=bank:
                                 v.tensor_scalar_mul(
                                     uoc[:, m * TOK:(m + 1) * TOK],
                                     bank[:, 0:TOK], 1.0),
                                 waits=[(s_pe, gn)], inc=(s_dve, 1))

        x1_loaded = {}
        x1b_loaded = {}
        wo_done = {}
        a1_hist = {}
        a2_hist = {}
        vadds_last = {}
        spill_done = {}
        e2_hist = {}
        st_stored = {}
        for t in range(B):
            j, r = t // NCORES, t % NCORES
            wldb = [(s_pe, wo_done[t - 2])] if t - 2 in wo_done else []
            if t - 2 in vadds_last:
                wldb.append((s_dve, vadds_last[t - 2]))

            def ldb(g, j=j, r=r, t=t):
                src = gath[j][r * DM:(r + 1) * DM, bass.ds(dyn["g_off"], TOK)]
                src = src.rearrange("(c p) n -> p c n", p=128)
                return g.dma_start(out=x1b[t % 2][:, :].rearrange(
                    "p (c n) -> p c n", c=C), in_=src)
            x1b_loaded[t] = p.emit("gpsimd", ldb, waits=wldb, inc=(s_ld, 16))

            for m in range(C):
                bank = psG[2 + (m % 2)]
                wmm = [(s_ld, x1b_loaded[t])]
                key = (t, m - 2) if m >= 2 else (t - 1, C - 2 + m)
                if key in a1_hist:
                    wmm.append((s_dve, a1_hist[key]))
                if t == 0 and m < 2:
                    wmm.append((s_dve, cbf_fin))

                def mk_wo(te, m=m, t=t, bank=bank):
                    inst = None
                    for k in range(C):
                        inst = te.matmul(bank[:, 0:TOK], wslice(w_sb["woT"], k, m),
                                         x1b[t % 2][:, k * TOK:(k + 1) * TOK],
                                         start=(k == 0), stop=(k == C - 1))
                    return inst
                gn = p.emit("tensor", mk_wo, waits=wmm, inc=(s_pe, 1))
                wo_done[t] = gn

                tmpX = [tmp1, tmp2][m % 2]
                wa1 = [(s_pe, gn)]
                keyp = (t, m - 2) if m >= 2 else (t - 1, C - 2 + m)
                if keyp in a2_hist:
                    wa1.append((s_act, a2_hist[keyp]))
                a1_hist[(t, m)] = p.emit(
                    "vector", lambda v, m=m, bank=bank, tmpX=tmpX:
                    v.tensor_tensor(tmpX[:, :], bank[:, 0:TOK],
                                    uoc[:, m * TOK:(m + 1) * TOK], op=ALU.add),
                    waits=wa1, inc=(s_dve, 1))

                wa2 = [(s_dve, a1_hist[(t, m)])]
                if m == 0 and (t - 1) in spill_done:
                    wa2.append((s_st, spill_done[t - 1]))
                a2_hist[(t, m)] = p.emit(
                    "scalar", lambda sc, m=m, tmpX=tmpX: sc.activation(
                        hv[:, m * TOK:(m + 1) * TOK], tmpX[:, :], AF.Tanh,
                        bias=bias_sb["bo"][:, m:m + 1], scale=1.0),
                    waits=wa2, inc=(s_act, 1))

            for m in range(C):
                wv = [(s_act, a2_hist[(t, m)])]
                n = p.emit("vector", lambda v, m=m, t=t: v.tensor_tensor(
                    hv[:, m * TOK:(m + 1) * TOK], hv[:, m * TOK:(m + 1) * TOK],
                    x1b[t % 2][:, m * TOK:(m + 1) * TOK], op=ALU.add),
                    waits=wv, inc=(s_dve, 1))
            vadds_last[t] = n

            cp1 = p.emit("vector", lambda v: v.tensor_copy(vb[:, :], hv[:, :]),
                         inc=(s_dve, 1))
            cp2 = p.emit("vector", lambda v: v.tensor_tensor(
                sqb[:, :], vb[:, :], vb[:, :], op=ALU.mult), inc=(s_dve, 1))

            def mk_sums(te):
                inst = None
                for c in range(C):
                    inst = te.matmul(psS[:, 0:TOK], ones128[:, :],
                                     vb[:, c * TOK:(c + 1) * TOK],
                                     start=(c == 0), stop=(c == C - 1))
                for c in range(C):
                    inst = te.matmul(psS[:, TOK:2 * TOK], ones128[:, :],
                                     sqb[:, c * TOK:(c + 1) * TOK],
                                     start=(c == 0), stop=(c == C - 1))
                return inst
            wsum = [(s_dve, cp2), (s_ld, consts_ready[1])]
            if t - 1 in e2_hist:
                wsum.append((s_dve, e2_hist[t - 1]))
            sums_n = p.emit("tensor", mk_sums, waits=wsum, inc=(s_pe, 1))

            we1 = [(s_pe, sums_n)]
            if t - 2 in st_stored:
                we1.append((s_st, st_stored[t - 2]))
            p.emit("vector", lambda v, t=t: v.tensor_scalar_mul(
                sst[t % 2][:, 0:TOK], psS[:, 0:TOK], 1.0 / DM),
                waits=we1, inc=(s_dve, 1))
            e2_hist[t] = p.emit("vector", lambda v, t=t: v.tensor_scalar_mul(
                sst[t % 2][:, TOK:2 * TOK], psS[:, TOK:2 * TOK], 1.0 / DM),
                inc=(s_dve, 1))
            st_stored[t] = p.emit("sync", lambda sy, t=t: sy.dma_start(
                out=stats_d[t:t + 1, :], in_=sst[t % 2][:, :]),
                waits=[(s_dve, e2_hist[t])], inc=(s_st, 16))

            def spv(sy, t=t):
                dst = vsp[:, t * TOK:(t + 1) * TOK].rearrange(
                    "(c p) n -> p c n", p=128)
                return sy.dma_start(out=dst, in_=hv[:, :].rearrange(
                    "p (c n) -> p c n", c=C))
            spill_done[t] = p.emit("sync", spv, waits=[(s_dve, vadds_last[t])],
                                   inc=(s_st, 16))

        st_ld = p.emit("sync", lambda sy: sy.dma_start(
            out=stT[:, :], in_=stats_d),
            waits=[(s_st, st_stored[B - 1])], inc=(s_lf, 16))
        rb1 = p.emit("vector", lambda v: v.tensor_tensor(
            stX[:, :], stT[:, 0:TOK], stT[:, 0:TOK], op=ALU.mult),
            waits=[(s_lf, st_ld)], inc=(s_dve, 1))
        rb2 = p.emit("vector", lambda v: v.tensor_tensor(
            stX[:, :], stT[:, TOK:2 * TOK], stX[:, :], op=ALU.subtract),
            inc=(s_dve, 1))
        rb3 = p.emit("scalar", lambda sc: sc.activation(
            stY[:, :], stX[:, :], AF.Ln, bias=epst[0:B, :], scale=1.0),
            waits=[(s_dve, rb2)], inc=(s_act, 1))
        rb4 = p.emit("scalar", lambda sc: sc.activation(
            stX[:, :], stY[:, :], AF.Exp, bias=0.0, scale=-0.5), inc=(s_act, 1))
        rb5 = p.emit("vector", lambda v: v.tensor_copy(
            stO[:, 0:TOK], stX[:, :]), waits=[(s_act, rb4)], inc=(s_dve, 1))
        rb6 = p.emit("vector", lambda v: v.tensor_tensor(
            stY[:, :], stT[:, 0:TOK], stX[:, :], op=ALU.mult), inc=(s_dve, 1))
        rb7 = p.emit("vector", lambda v: v.tensor_scalar(
            stO[:, TOK:2 * TOK], stY[:, :], -1.0, None, op0=ALU.mult),
            inc=(s_dve, 1))
        stO_st = p.emit("sync", lambda sy: sy.dma_start(
            out=stats_o, in_=stO[:, :]), waits=[(s_dve, rb7)], inc=(s_st, 16))

        out_stored = {}
        appl_done = {}
        bc_hist = {}
        vld = {}
        for t in range(B):
            wldv = ([(s_dve, appl_done[t - 2])] if t - 2 in appl_done
                    else [(s_dve, vadds_last[B - 2 + t])])

            def ldv(g, t=t):
                src = vsp[:, t * TOK:(t + 1) * TOK].rearrange(
                    "(c p) n -> p c n", p=128)
                return g.dma_start(out=x1f[t % 2][:, :].rearrange(
                    "p (c n) -> p c n", c=C), in_=src)
            wldv.append((s_st, spill_done[t]))
            vld[t] = p.emit("gpsimd", ldv, waits=wldv, inc=(s_ld, 16))

            wsi = [(s_st, stO_st)]
            if t - 2 in bc_hist:
                wsi.append((s_pe, bc_hist[t - 2]))
            sld = p.emit("sync", lambda sy, t=t: sy.dma_start(
                out=stin[t % 2][:, :], in_=stats_o[t:t + 1, :]),
                waits=wsi, inc=(s_lf, 16))

            def mk_bc(te, t=t):
                te.matmul(psB[:, 0:TOK], ones1F[:, :],
                          stin[t % 2][:, 0:TOK], start=True, stop=True)
                return te.matmul(psB[:, TOK:2 * TOK], ones1F[:, :],
                                 stin[t % 2][:, TOK:2 * TOK],
                                 start=True, stop=True)
            wbc = [(s_lf, sld)]
            if t - 1 in appl_done:
                wbc.append((s_dve, appl_done[t - 1]))
            bc_n = p.emit("tensor", mk_bc, waits=wbc, inc=(s_pe, 1))
            bc_hist[t] = bc_n

            for m in range(C):
                w1 = [(s_pe, bc_n), (s_ld, vld[t])] if m == 0 else []
                if m == 0 and t - 2 in out_stored:
                    w1.append((s_st, out_stored[t - 2]))
                p.emit("vector", lambda v, m=m, t=t: v.tensor_tensor(
                    outsb[t % 2][:, m * TOK:(m + 1) * TOK],
                    x1f[t % 2][:, m * TOK:(m + 1) * TOK], psB[:, 0:TOK],
                    op=ALU.mult), waits=w1, inc=(s_dve, 1))
                appl_done[t] = p.emit("vector", lambda v, m=m, t=t:
                                      v.tensor_tensor(
                    outsb[t % 2][:, m * TOK:(m + 1) * TOK],
                    outsb[t % 2][:, m * TOK:(m + 1) * TOK], psB[:, TOK:2 * TOK],
                    op=ALU.add), inc=(s_dve, 1))

            def stout(sy, t=t):
                dst = out_h[:, t * TOK:(t + 1) * TOK].rearrange(
                    "(c p) n -> p c n", p=128)
                return sy.dma_start(out=dst, in_=outsb[t % 2][:, :].rearrange(
                    "p (c n) -> p c n", c=C))
            out_stored[t] = p.emit("sync", stout, waits=[(s_dve, appl_done[t])],
                                   inc=(s_st, 16))

        p.emit("sync", lambda sy: None, waits=[(s_st, s_st.n)])

        for s in p.sems.values():
            s.h = es.enter_context(nc.semaphore(s.name))
        with nc.Block() as block:
            @block.tensor
            def _(te):
                p.build_engine(te, "tensor")

            @block.scalar
            def _(sc):
                p.build_engine(sc, "scalar")

            @block.vector
            def _(v):
                p.build_engine(v, "vector")

            @block.gpsimd
            def _(g):
                p.build_engine(g, "gpsimd")

            @block.sync
            def _(sy):
                p.build_engine(sy, "sync")

    return nc


def build_phase_a(ctx):
    nc, p, es = ctx["nc"], ctx["p"], ctx["es"]
    T, PS = ctx["T"], ctx["PS"]
    xT, whbm, slab, gath = ctx["xT"], ctx["whbm"], ctx["slab"], ctx["gath"]
    ones1, ones128, epst = ctx["ones1"], ctx["ones128"], ctx["epst"]
    ones1F = ctx["ones1F"]
    s_ld, s_lf, s_st = ctx["s_ld"], ctx["s_lf"], ctx["s_st"]
    s_pe, s_act, s_dve, s_cc = ctx["s_pe"], ctx["s_act"], ctx["s_dve"], ctx["s_cc"]

    aes = contextlib.ExitStack()
    ctx["aes"] = aes
    TA = lambda *a, **k: aes.enter_context(nc.sbuf_tensor(*a, **k))
    PA = lambda *a, **k: aes.enter_context(nc.psum_tensor(*a, **k))

    arena1 = TA("arena1", [128, C * S], BF16)
    xbf = arena1[:, :]
    aT = arena1
    _war = TA("warena", [128, C * C * 128], BF16)
    warena = [_war, _war]
    qT = TA("qT", [128, C * S], BF16)
    kT = TA("kT", [128, C * S], BF16)
    vP = TA("vP", [128, 16 * (H * 65)], BF16)
    exp_sb = [TA(f"exp{i}", [128, 1024], BF16) for i in range(6)]

    rb_row = TA("rb_row", [1, 1024], BF16)
    lnPool = TA("lnPool", [1, 1024], F32)
    lnzrow = lnPool[:, 0:1024]
    rbB = TA("rbB", [64, 1024], BF16)
    x1sb = TA("x1sb", [128, C * 512], F32)
    vtmpf = TA("vtmpf", [128, 512], F32)
    xres = TA("xres", [128, C * 512], F32)
    xpb = arena1[:, 8192:8192 + 4096]
    sqA = arena1[:, 8192 + 4096:16384]
    lnS = TA("lnS", [1, 512], F32)
    lnS2 = TA("lnS2", [1, 512], F32)
    lnT = lnPool[:, 0:512]
    lnB = lnPool[:, 512:1024]
    neg3 = TA("neg3", [128, 1], F32)
    identF = TA("identF", [128, 128], F32)

    psSC = [PA(f"psSC{i}", [128, 1024], F32) for i in range(3)]
    psUs = PA("psUs", [128, 1024], F32)
    psU = [psUs, psUs]

    SLOT = [(psSC[0], 0), (psSC[0], 512), (psSC[1], 0), (psSC[1], 512)]
    slot_evac = [0, 0, 0, 0]
    slot_mm = [0, 0, 0, 0]

    def slot_ap(u, p0=0, p1=128, w=512):
        t_, off = SLOT[u % 4]
        return t_[p0:p1, off:off + w]

    p.emit("gpsimd", lambda g: g.memset(neg3[:, :], -3.0), inc=(s_ld, 1))
    p.emit("gpsimd", lambda g: g.memset(vP[:, :], 1.0), inc=(s_ld, 1))
    p.emit("gpsimd", lambda g: g.memset(identF[:, :], 0.0))
    p.emit("gpsimd", lambda g: g.affine_select(
        out=identF[:, :], in_=identF[:, :], compare_op=ALU.not_equal,
        fill=1.0, base=0, pattern=[[-1, 128]], channel_multiplier=1),
        inc=(s_ld, 1))
    aconsts = (s_ld, s_ld.n)

    PW = [("Wq", qT), ("Wk", kT), ("Wv", None), ("Wpost", None)]
    su = 0
    xbf_loaded = {}
    wload = {}
    wload_mm = {}
    aT_free = {-1: 0}
    smn_last = {}
    store_cnt = 0
    store_hist = []
    exp_hist = {}
    av_hist = {}
    tt_hist = {}
    gelu_hist = {}

    for b in range(BLOC):
        wx = [(s_pe, smn_last[b - 1])] if b - 1 in smn_last else []

        def ldx_half(sy, b=b, half=0):
            src = ctx["xbf_d"][half * 512:half * 512 + 512,
                               b * S:(b + 1) * S].rearrange(
                "(c p) n -> p c n", p=128)
            dst = arena1[:, half * 4 * S:(half + 1) * 4 * S].rearrange(
                "p (c n) -> p c n", c=4)
            return sy.dma_start(out=dst, in_=src)
        wx = wx + [(s_ld, ctx["xcast"][b])]
        xbf_lo = p.emit("sync", lambda sy, b=b: ldx_half(sy, b, 0),
                        waits=wx, inc=(s_lf, 16))
        xbf_hi = p.emit("sync", lambda sy, b=b: ldx_half(sy, b, 1),
                        inc=(s_lf, 16))
        xbf_loaded[b] = xbf_hi

        def emit_wload(pi, b=b):
            wwl = []
            prev = (b, pi - 1) if pi >= 1 else (b - 1, 3)
            if prev in wload_mm:
                wwl = [(s_pe, wload_mm[prev])]
            wn = PW[pi][0]
            wload[(b, pi)] = p.emit("sync", lambda sy, wn=wn, pi=pi: sy.dma_start(
                out=warena[pi % 2][:, :], in_=ctx["wbf16_d"][wn]),
                waits=wwl + [(s_ld, ctx["wcast"][wn])], inc=(s_lf, 16))
        emit_wload(0)

        for pi in range(3):
            wn, dstT = PW[pi][0], PW[pi][1]
            for m in range(C):
                wmm = [(s_lf, max(wload[(b, pi)], xbf_lo))]
                for u in range(4):
                    if slot_evac[u] > 0:
                        wmm.append((s_dve, slot_evac[u]))

                def mk_p(te, pi=pi, m=m, b=b, klo=0, khi=4):
                    inst = None
                    for k in range(klo, khi):
                        lhs = wslice(warena[pi % 2], k, m)
                        for sblk in range(4):
                            inst = te.matmul(
                                slot_ap(sblk), lhs,
                                xbf[:, k * S + sblk * 512:k * S + sblk * 512 + 512],
                                start=(k == 0), stop=(k == C - 1))
                    return inst
                p.emit("tensor", mk_p, waits=wmm, inc=(s_pe, 1))
                gn = p.emit("tensor",
                            lambda te, pi=pi, m=m, b=b: mk_p(te, pi, m, b, 4, 8),
                            waits=[(s_lf, xbf_hi)], inc=(s_pe, 1))
                wload_mm[(b, pi)] = gn

                if pi < 2:
                    for sblk in range(4):
                        def ev(v, dstT=dstT, m=m, sblk=sblk):
                            return v.tensor_scalar_mul(
                                dstT[:, m * S + sblk * 512:m * S + sblk * 512 + 512],
                                slot_ap(sblk), 1.0)
                        slot_evac[sblk] = p.emit("vector", ev, waits=[(s_pe, gn)],
                                                 inc=(s_dve, 1))
                else:
                    for sblk in range(4):
                        def evv(v, sblk=sblk):
                            return v.tensor_scalar_mul(vtmpf[:, :],
                                                       slot_ap(sblk), 1.0)
                        wev = [(s_pe, gn)]
                        if (m, sblk) == (0, 0) and store_hist:
                            wev.append((s_ld, store_hist[-1]))
                        if (b, m, sblk) != (0, 0, 0):
                            wev.append((s_pe, trn))
                        evn = p.emit("vector", evv, waits=wev, inc=(s_dve, 1))
                        slot_evac[sblk] = evn
                        for ti in range(4):
                            tc = sblk * 4 + ti

                            def mktr(te, ti=ti):
                                return te.transpose(
                                    psU[0][0:128, 0:128],
                                    vtmpf[:, ti * 128:(ti + 1) * 128],
                                    identF[:, :])
                            wtr = [(s_dve, evn), (s_ld, aconsts[1])]
                            if (b, m, sblk, ti) != (0, 0, 0, 0):
                                wtr.append((s_dve, vp_ev))
                            trn = p.emit("tensor", mktr, waits=wtr, inc=(s_pe, 1))

                            def evp(v, tc=tc, m=m):
                                base = tc * H * 65 + (2 * m) * 65
                                dst = vP[:, base:base + 130].rearrange(
                                    "p (h e) -> p h e", h=2)[:, :, 0:64]
                                srcv = psU[0][0:128, 0:128].rearrange(
                                    "p (h e) -> p h e", h=2)
                                return v.tensor_copy(dst, srcv)
                            vp_ev = p.emit("vector", evp, waits=[(s_pe, trn)],
                                           inc=(s_dve, 1))
            emit_wload(pi + 1)
        vp_done = vp_ev

        expz_hist = {}
        bc_hist = {}
        for sh in range(2):

            def emit_scores(sq, hp, tc):
                bank = psSC[tc % 3]
                wsc = []
                tcp3 = 15 - ((15 - tc) % 3)
                kexp = (b, sq, hp, tc - 3) if tc >= 3 else \
                       (b, sq, hp - 1, tcp3) if hp >= 1 else \
                       (b, sq - 1, 7, tcp3) if sq >= 1 else \
                       (b - 1, 3, 7, tcp3)
                if kexp in exp_hist:
                    wsc.append((s_act, exp_hist[kexp]))
                if hp == 0 and tc == 0 and sq % 2 == 0:
                    for u in range(4):
                        if slot_evac[u] > 0:
                            wsc.append((s_dve, slot_evac[u]))
                    if sq == 0:
                        wsc.append((s_dve, vp_done))

                def mksc(te, sq=sq, hp=hp, tc=tc, bank=bank):
                    te.matmul(bank[:, 0:512],
                              kT[0:64, hp * S + tc * 128:hp * S + tc * 128 + 128],
                              qT[0:64, hp * S + sq * 512:hp * S + sq * 512 + 512],
                              start=True, stop=True)
                    return te.matmul(
                        bank[:, 512:1024],
                        kT[64:128, hp * S + tc * 128:hp * S + tc * 128 + 128],
                        qT[64:128, hp * S + sq * 512:hp * S + sq * 512 + 512],
                        start=True, stop=True)
                return p.emit("tensor", mksc, waits=wsc, inc=(s_pe, 1))

            def emit_exp(sq, hp, tc, scn):
                bank = psSC[tc % 3]
                wex = [(s_pe, scn)]
                tcq = tc + 12 if tc <= 3 else tc + 6
                kav = (b, sq, hp, tc - 6) if tc >= 6 else \
                      (b, sq, hp - 1, tcq) if hp >= 1 else \
                      (b, sq - 1, 7, tcq) if sq >= 1 else \
                      (b - 1, 3, 7, tcq)
                if kav in av_hist:
                    wex.append((s_pe, av_hist[kav]))

                def mkex(sc, tc=tc, bank=bank):
                    return sc.activation(exp_sb[tc % 6][:, :], bank[:, :],
                                         AF.Exp, bias=neg3[:, :], scale=0.125)
                exp_hist[(b, sq, hp, tc)] = p.emit("scalar", mkex, waits=wex,
                                                   inc=(s_act, 1))

            def emit_av(sq, hp, tc):
                wav = [(s_act, exp_hist[(b, sq, hp, tc)])]
                if tc == 0:
                    ktt = (b, sq, hp - 1) if hp >= 1 else \
                          (b, sq - 1, 7) if sq >= 1 else \
                          (b - 1, 3, 7)
                    if ktt in tt_hist:
                        wav.append((s_dve, tt_hist[ktt]))

                def mkav(te, hp=hp, tc=tc):
                    te.matmul(psUs[0:65, 0:512],
                              vP[:, tc * H * 65 + 2 * hp * 65:
                                 tc * H * 65 + 2 * hp * 65 + 65],
                              exp_sb[tc % 6][:, 0:512],
                              start=(tc == 0), stop=(tc == 15))
                    return te.matmul(
                        psUs[0:65, 512:1024],
                        vP[:, tc * H * 65 + (2 * hp + 1) * 65:
                           tc * H * 65 + (2 * hp + 1) * 65 + 65],
                        exp_sb[tc % 6][:, 512:1024],
                        start=(tc == 0), stop=(tc == 15))
                av_hist[(b, sq, hp, tc)] = p.emit("tensor", mkav, waits=wav,
                                                  inc=(s_pe, 1))

            def emit_tail_a(sq, hp):
                ln_n = p.emit("scalar", lambda sc: sc.activation(
                    lnzrow, psUs[64:65, :], AF.Ln),
                    waits=[(s_pe, av_hist[(b, sq, hp, 15)])], inc=(s_act, 1))

                def mkexpz(sc):
                    with nc.allow_low_precision(
                            reason="softmax 1/Z stored bf16; product is "
                                   "rounded to bf16 anyway; diluted by residual"):
                        return sc.activation(rb_row[:, :], lnzrow,
                                             AF.Exp, scale=-1.0)
                wz = []
                kbc = (sq, hp - 1) if hp >= 1 else (sq - 1, 7)
                if kbc in bc_hist:
                    wz.append((s_pe, bc_hist[kbc]))
                expz_hist[(sq, hp)] = p.emit("scalar", mkexpz, waits=wz,
                                             inc=(s_act, 1))

            def emit_tail_b(sq, hp, sq2):
                def mkbc(te):
                    inst = None
                    for si in range(2):
                        inst = te.matmul(
                            psUs[64:128, si * 512:si * 512 + 512],
                            ones1[0:1, 0:64],
                            rb_row[:, si * 512:si * 512 + 512],
                            start=True, stop=True)
                    return inst
                bc_hist[(sq, hp)] = p.emit(
                    "tensor", mkbc, waits=[(s_act, expz_hist[(sq, hp)])],
                    inc=(s_pe, 1))
                ebn = p.emit("vector", lambda v: v.tensor_scalar_mul(
                    rbB[:, :], psUs[64:128, :], 1.0),
                    waits=[(s_pe, bc_hist[(sq, hp)])], inc=(s_dve, 1))
                p.emit("vector", lambda v, hp=hp, sq2=sq2: v.tensor_tensor(
                    arena1[0:64, hp * 1024 + sq2 * 512:hp * 1024 + sq2 * 512 + 512],
                    psUs[0:64, 0:512], rbB[:, 0:512], op=ALU.mult),
                    inc=(s_dve, 1))
                tt_hist[(b, sq, hp)] = p.emit(
                    "vector", lambda v, hp=hp, sq2=sq2: v.tensor_tensor(
                        arena1[64:128, hp * 1024 + sq2 * 512:
                               hp * 1024 + sq2 * 512 + 512],
                        psUs[0:64, 512:1024], rbB[:, 512:1024],
                        op=ALU.mult), inc=(s_dve, 1))

            for sq2 in range(2):
                sq = sh * 2 + sq2
                for hp in range(8):
                    kprev = (sq, hp - 1) if hp >= 1 else (sq - 1, 7)
                    for tc in range(16):
                        scn = emit_scores(sq, hp, tc)
                        if tc >= 6:
                            emit_av(sq, hp, tc - 6)
                        emit_exp(sq, hp, tc, scn)
                        if tc == 2 and (b, kprev[0], kprev[1], 15) in av_hist \
                                and kprev not in expz_hist:
                            emit_tail_a(kprev[0], kprev[1])
                        if tc == 5 and kprev in expz_hist \
                                and kprev not in bc_hist:
                            emit_tail_b(kprev[0], kprev[1], kprev[0] % 2)
                    for tcl in range(10, 16):
                        emit_av(sq, hp, tcl)
            emit_tail_a(2 * sh + 1, 7)
            emit_tail_b(2 * sh + 1, 7, 1)

            def mkgl(sc):
                return sc.activation(arena1[:, 0:C * 1024], arena1[:, 0:C * 1024],
                                     AF.Gelu)
            gelu_hist[(b, sh)] = p.emit("scalar", mkgl,
                                        waits=[(s_dve, tt_hist[(b, 2 * sh + 1, 7)])],
                                        inc=(s_act, 1))

            for sb2 in range(2):
                sblk = sh * 2 + sb2

                def ldxr(sy, b=b, sblk=sblk):
                    srcx = xT[:, b * S + sblk * 512:b * S + sblk * 512 + 512]
                    srcx = srcx.rearrange("(c p) n -> p c n", p=128)
                    return sy.dma_start(out=xres[:, :].rearrange(
                        "p (c n) -> p c n", c=C), in_=srcx)
                wxr = [(s_dve, xres_used)] if b + sblk > 0 else []
                xrn = p.emit("sync", ldxr, waits=wxr, inc=(s_lf, 16))

                stt_hist = {}
                for m in range(C):
                    u = su % 4
                    wmm = [(s_act, gelu_hist[(b, sh)])]
                    if slot_evac[u] > 0:
                        wmm.append((s_dve, slot_evac[u]))

                    def mkwp(te, m=m, sb2=sb2, u=u):
                        inst = None
                        for k in range(C):
                            inst = te.matmul(
                                slot_ap(u), wslice(warena[1], k, m),
                                arena1[:, k * 1024 + sb2 * 512:
                                       k * 1024 + sb2 * 512 + 512],
                                start=(k == 0), stop=(k == C - 1))
                        return inst
                    gn = p.emit("tensor", mkwp, waits=wmm, inc=(s_pe, 1))
                    slot_mm[u] = gn
                    aT_free[b] = gn
                    wload_mm[(b, 3)] = gn

                    def mkstt(v, m=m, u=u):
                        return v.scalar_tensor_tensor(
                            x1sb[:, m * 512:(m + 1) * 512],
                            xres[:, m * 512:(m + 1) * 512],
                            2.0, slot_ap(u), op0=ALU.mult, op1=ALU.add)
                    ws = [(s_pe, gn)]
                    if m == 0:
                        ws.append((s_lf, xrn))
                        if store_hist:
                            ws.append((s_ld, store_hist[-1]))
                    slot_evac[u] = p.emit("vector", mkstt, waits=ws, inc=(s_dve, 1))
                    stt_hist[m] = slot_evac[u]
                    su += 1

                cpx = p.emit("vector", lambda v: v.tensor_copy(
                    xpb, x1sb[:, :]), waits=[(s_dve, stt_hist[C - 1])],
                    inc=(s_dve, 1))
                cps = p.emit("vector", lambda v: v.tensor_tensor(
                    sqA, xpb, xpb, op=ALU.mult), inc=(s_dve, 1))

                us_, uq = su % 4, (su + 1) % 4
                su += 2

                def mksums(te, us_=us_, uq=uq):
                    inst = None
                    for k in range(C):
                        inst = te.matmul(slot_ap(us_, 0, 1), ones128[:, :],
                                         xpb[:, k * 512:(k + 1) * 512],
                                         start=(k == 0), stop=(k == C - 1))
                    for k in range(C):
                        inst = te.matmul(slot_ap(uq, 0, 1), ones128[:, :],
                                         sqA[:, k * 512:(k + 1) * 512],
                                         start=(k == 0), stop=(k == C - 1))
                    return inst
                wsm = [(s_dve, cps)]
                if slot_evac[us_] > 0:
                    wsm.append((s_dve, slot_evac[us_]))
                if slot_evac[uq] > 0:
                    wsm.append((s_dve, slot_evac[uq]))
                smn = p.emit("tensor", mksums, waits=wsm, inc=(s_pe, 1))
                smn_last[b] = smn

                e1 = p.emit("vector", lambda v, us_=us_: v.tensor_scalar_mul(
                    lnS[:, :], slot_ap(us_, 0, 1), 1.0 / DM),
                    waits=[(s_pe, smn)], inc=(s_dve, 1))
                e2 = p.emit("vector", lambda v, uq=uq: v.tensor_scalar_mul(
                    lnS2[:, :], slot_ap(uq, 0, 1), 1.0 / DM), inc=(s_dve, 1))
                slot_evac[us_] = e1
                slot_evac[uq] = e2
                l1 = p.emit("vector", lambda v: v.tensor_tensor(
                    lnT, lnS[:, :], lnS[:, :], op=ALU.mult), inc=(s_dve, 1))
                l2 = p.emit("vector", lambda v: v.tensor_tensor(
                    lnT, lnS2[:, :], lnT, op=ALU.subtract),
                    inc=(s_dve, 1))
                l3 = p.emit("scalar", lambda sc: sc.activation(
                    lnS2[:, :], lnT, AF.Ln, bias=epst[0:1, :], scale=1.0),
                    waits=[(s_dve, l2)], inc=(s_act, 1))
                l4 = p.emit("scalar", lambda sc: sc.activation(
                    lnT, lnS2[:, :], AF.Exp, bias=0.0, scale=-0.5),
                    inc=(s_act, 1))
                l5 = p.emit("vector", lambda v: v.tensor_copy(
                    lnB, lnT), waits=[(s_act, l4)], inc=(s_dve, 1))
                l6 = p.emit("vector", lambda v: v.tensor_tensor(
                    lnS2[:, :], lnS[:, :], lnT, op=ALU.mult), inc=(s_dve, 1))
                l7 = p.emit("vector", lambda v: v.tensor_scalar(
                    lnS[:, :], lnS2[:, :], -1.0, None, op0=ALU.mult),
                    inc=(s_dve, 1))

                for wsem, wval in [(s_dve, e1), (s_dve, l2), (s_act, l3),
                                   (s_dve, l7)]:
                    def mkdum(te):
                        return te.matmul(psU[0][0:128, 0:512], kT[:, 0:128],
                                         qT[:, 0:512], start=True, stop=True)
                    p.emit("tensor", mkdum,
                           waits=[(wsem, wval),
                                  (s_dve, tt_hist[(b, 2 * sh + 1, 7)])],
                           inc=(s_pe, 1))

                ur, un = su % 4, (su + 1) % 4
                su += 2

                def mkbc2(te, ur=ur, un=un):
                    te.matmul(slot_ap(ur), ones1F[0:1, :], lnB,
                              start=True, stop=True)
                    return te.matmul(slot_ap(un), ones1F[0:1, :], lnS[0:1, :],
                                     start=True, stop=True)
                wb2 = [(s_dve, l7)]
                if slot_evac[ur] > 0:
                    wb2.append((s_dve, slot_evac[ur]))
                if slot_evac[un] > 0:
                    wb2.append((s_dve, slot_evac[un]))
                b2n = p.emit("tensor", mkbc2, waits=wb2, inc=(s_pe, 1))

                ap_hist = {}
                for m in range(C):
                    w1 = [(s_pe, b2n)] if m == 0 else []
                    p.emit("vector", lambda v, m=m, ur=ur: v.tensor_tensor(
                        xres[:, m * 512:(m + 1) * 512],
                        x1sb[:, m * 512:(m + 1) * 512], slot_ap(ur),
                        op=ALU.mult), waits=w1, inc=(s_dve, 1))
                    ap_hist[m] = p.emit("vector", lambda v, m=m, un=un:
                                        v.tensor_tensor(
                        x1sb[:, m * 512:(m + 1) * 512],
                        xres[:, m * 512:(m + 1) * 512], slot_ap(un),
                        op=ALU.add), inc=(s_dve, 1))
                xres_used = ap_hist[C - 1]
                slot_evac[ur] = ap_hist[C - 1]
                slot_evac[un] = ap_hist[C - 1]

                def stx1(g, b=b, sblk=sblk):
                    dst = slab[b][:, sblk * 512:sblk * 512 + 512].rearrange(
                        "(c p) n -> p c n", p=128)
                    return g.dma_start(out=dst, in_=x1sb[:, :].rearrange(
                        "p (c n) -> p c n", c=C))
                store_hist.append(p.emit(
                    "gpsimd", stx1, waits=[(s_dve, ap_hist[C - 1])],
                    inc=(s_ld, 16)))

        p.emit("gpsimd", lambda g, b=b: g.collective_compute(
            "AllGather", ALU.bypass, replica_groups=[list(range(NCORES))],
            ins=[slab[b]], outs=[gath[b]]),
            waits=[(s_ld, p.sems["s_ld"].n)], inc=(s_cc, 1))

    aes.close()
    return (ctx["p"].sems["s_ld"].n)



def weight_maps(inputs):
    return {
        "wfT": prechunk(np.asarray(inputs["L0_ffn_wf"], np.float32).T),
        "ufT": prechunk(np.asarray(inputs["L0_ffn_uf"], np.float32).T),
        "wiT": prechunk(np.asarray(inputs["L0_ffn_wi"], np.float32).T),
        "uiT": prechunk(np.asarray(inputs["L0_ffn_ui"], np.float32).T),
        "wcT": prechunk(np.asarray(inputs["L0_ffn_wc"], np.float32).T),
        "ucT": prechunk(np.asarray(inputs["L0_ffn_uc"], np.float32).T),
        "woT": prechunk(np.asarray(inputs["L0_ffn_wo"], np.float32).T),
        "uoT": prechunk(np.asarray(inputs["L0_ffn_uo"], np.float32).T),
        "bf": pack_bias(inputs["L0_ffn_bf"]), "bi": pack_bias(inputs["L0_ffn_bi"]),
        "bc": pack_bias(inputs["L0_ffn_bc"]), "bo": pack_bias(inputs["L0_ffn_bo"]),
        "inc0": pack_bias(inputs["L0_ffn_in_c"]),
    }


def assemble_out(results):
    full = np.empty((B, S, DM), np.float32)
    for k in range(NCORES):
        ok = np.asarray(results[k]["out"]).reshape(DM, B, TOK)
        full[:, k * TOK:(k + 1) * TOK, :] = ok.transpose(1, 2, 0)
    return full


def kernel(**inputs):
    inputs = {k: np.asarray(v, np.float32) for k, v in inputs.items()}
    base = weight_maps(inputs)
    base["Wq"] = prechunk(inputs["L0_attn_Wq"])
    base["Wk"] = prechunk(inputs["L0_attn_Wk"])
    base["Wv"] = prechunk(inputs["L0_attn_Wv"])
    base["Wpost"] = prechunk(inputs["L0_attn_Wpost"])
    X = inputs["L0_attn_X"]
    in_maps = []
    for k in range(NCORES):
        m = dict(base)
        m["xT"] = np.ascontiguousarray(
            X[[k, 8 + k, 16 + k, 24 + k]].transpose(2, 0, 1).reshape(DM, BLOC * S))
        in_maps.append(m)
    nc = build_program(mode="full")
    res = run_bass_kernel_spmd(nc, in_maps, core_ids=list(range(NCORES)))
    return assemble_out(res.results)


# version 11
# speedup vs baseline: 1.2363x; 1.0199x over previous
import contextlib
import numpy as np
import concourse.bass as bass
import concourse.mybir as mybir
from concourse.bass_utils import run_bass_kernel_spmd

F32 = mybir.dt.float32
BF16 = mybir.dt.bfloat16
AF = mybir.ActivationFunctionType
ALU = mybir.AluOpType

NCORES = 8
DM = 1024
C = DM // 128
B = 32
S = 2048
TOK = S // NCORES
BLOC = B // NCORES
H = 16
DK = 64
LNEPS = 1e-5


class Sem:
    def __init__(self, name):
        self.name = name
        self.n = 0
        self.h = None

    def inc(self, amt=1):
        self.n += amt
        return self.n


class Prog:

    def __init__(self):
        self.ops = {e: [] for e in ("tensor", "scalar", "vector", "gpsimd", "sync")}
        self.sems = {}

    def sem(self, name):
        if name not in self.sems:
            self.sems[name] = Sem(name)
        return self.sems[name]

    def emit(self, engine, fn, waits=(), inc=None):
        n = None
        if inc is not None:
            n = inc[0].inc(inc[1])
        self.ops[engine].append((tuple(waits), fn, inc))
        return n

    def build_engine(self, eng, name):
        for waits, fn, inc in self.ops[name]:
            for s, v in waits:
                if v > 0:
                    eng.wait_ge(s.h, v)
            inst = fn(eng)
            if inc is not None:
                assert inst is not None, f"inc on None inst in {name}"
                inst.then_inc(inc[0].h, inc[1])



def prechunk(w2):
    return np.ascontiguousarray(
        w2.reshape(C, 128, C, 128).transpose(1, 0, 2, 3).reshape(128, C * C * 128)
    )


def pack_bias(b):
    return np.ascontiguousarray(np.asarray(b, np.float32).reshape(C, 128).T)


def wslice(wsb, kc, mc):
    o = (kc * C + mc) * 128
    return wsb[:, o:o + 128]



def build_program(mode="full"):
    nc = bass.Bass("TRN2", target_bir_lowering=False)
    p = Prog()

    gath_kind = "ExternalInput" if mode == "bc" else "Internal"
    gath_kw = {} if mode == "bc" else {"addr_space": "Shared"}
    gath = [nc.dram_tensor(f"gath{j}", [NCORES * DM, S], BF16, kind=gath_kind,
                           **gath_kw).ap()
            for j in range(BLOC)]
    out_h = nc.dram_tensor("out", [DM, B * TOK], F32, kind="ExternalOutput").ap()
    vsp = nc.dram_tensor("vsp", [DM, B * TOK], F32).ap()
    stats_d = nc.dram_tensor("stats_d", [B, 2 * TOK], F32).ap()
    stats_o = nc.dram_tensor("stats_o", [B, 2 * TOK], F32).ap()

    wnames_bc = ["wfT", "ufT", "wiT", "uiT", "wcT", "ucT", "woT", "uoT"]
    whbm = {n: nc.dram_tensor(n, [128, C * C * 128], F32, kind="ExternalInput").ap()
            for n in wnames_bc}
    bnames = ["bf", "bi", "bc", "bo", "inc0"]
    bhbm = {n: nc.dram_tensor(n, [128, C], F32, kind="ExternalInput").ap()
            for n in bnames}

    if mode == "full":
        xT = nc.dram_tensor("xT", [DM, BLOC * S], F32, kind="ExternalInput").ap()
        for n in ["Wq", "Wk", "Wv", "Wpost"]:
            whbm[n] = nc.dram_tensor(n, [128, C * C * 128], F32,
                                     kind="ExternalInput").ap()
        slab = [nc.dram_tensor(f"slab{j}", [DM, S], BF16).ap() for j in range(BLOC)]
    else:
        slab = None
    wbf16_d = {n: nc.dram_tensor(n + "_bf", [128, C * C * 128], BF16).ap()
               for n in whbm}
    xbf_d = (nc.dram_tensor("xT_bf", [DM, BLOC * S], BF16).ap()
             if mode == "full" else None)

    es = contextlib.ExitStack()
    with es:
        T = lambda *a, **k: es.enter_context(nc.sbuf_tensor(*a, **k))
        PS = lambda *a, **k: es.enter_context(nc.psum_tensor(*a, **k))

        ones1 = T("ones1", [1, 128], BF16)
        ones1F = T("ones1F", [1, 128], F32)
        ones128 = T("ones128", [128, 1], BF16)
        epst = T("epst", [128, 1], F32)
        bias_sb = {n: T(f"sb_{n}", [128, C], F32) for n in bnames}

        s_ld = p.sem("s_ld")
        s_lf = p.sem("s_lf")
        s_st = p.sem("s_st")
        s_pe = p.sem("s_pe")
        s_act = p.sem("s_act")
        s_dve = p.sem("s_dve")
        s_cc = p.sem("s_cc")

        dyn = {}

        def ginit(g):
            dyn["g_off"] = g.snap(g.partition_id() * TOK)
            return None
        p.emit("gpsimd", ginit)

        def syinit(sy):
            dyn["s_off"] = sy.snap(sy.partition_id() * TOK)
            return None
        p.emit("sync", syinit)

        p.emit("gpsimd", lambda g: g.memset(ones1[:, :], 1.0), inc=(s_ld, 1))
        p.emit("gpsimd", lambda g: g.memset(ones1F[:, :], 1.0), inc=(s_ld, 1))
        p.emit("gpsimd", lambda g: g.memset(ones128[:, :], 1.0), inc=(s_ld, 1))
        p.emit("gpsimd", lambda g: g.memset(epst[:, :], LNEPS), inc=(s_ld, 1))
        consts_ready = (s_ld, s_ld.n)

        for n in bnames:
            p.emit("sync", lambda sy, n=n: sy.dma_start(
                out=bias_sb[n][:, :], in_=bhbm[n]), inc=(s_lf, 16))
        bias_ready = (s_lf, s_lf.n)

        wcast = {}
        xcast = {}

        def emit_wcast(n):
            wcast[n] = p.emit("gpsimd", lambda g, n=n: g.dma_start(
                out=wbf16_d[n], in_=whbm[n]), inc=(s_ld, 16))

        def emit_xcast(bb_):
            xcast[bb_] = p.emit("gpsimd", lambda g, bb_=bb_: g.dma_start(
                out=xbf_d[:, bb_ * S:(bb_ + 1) * S],
                in_=xT[:, bb_ * S:(bb_ + 1) * S]), inc=(s_ld, 16))

        if mode == "full":
            emit_wcast("Wq")
            emit_xcast(0)
            for n in ("Wk", "Wv", "Wpost"):
                emit_wcast(n)
            for bb_ in range(1, BLOC):
                emit_xcast(bb_)
        for n in whbm:
            if n not in wcast:
                emit_wcast(n)


        a_store_end = 0
        if mode == "full":
            actx = dict(nc=nc, p=p, es=es, T=T, PS=PS, xT=xT, whbm=whbm,
                        wbf16_d=wbf16_d, xbf_d=xbf_d, wcast=wcast, xcast=xcast,
                        slab=slab, gath=gath, ones1=ones1, ones128=ones128,
                        ones1F=ones1F,
                        epst=epst, s_ld=s_ld, s_lf=s_lf, s_st=s_st, s_pe=s_pe,
                        s_act=s_act, s_dve=s_dve, s_cc=s_cc)
            a_store_end = build_phase_a(actx)


        w_sb = {n: T(f"w_{n}", [128, C * C * 128], BF16)
                for n in ["wfT", "ufT", "wiT", "uiT", "wcT", "ucT"]}
        w_sb["woT"] = w_sb["wfT"]
        w_sb["uoT"] = w_sb["ufT"]
        c_sb = T("c_sb", [128, C * TOK], F32)
        cbf = T("cbf", [128, C * TOK], BF16)
        xt_sb = [T(f"xt{i}", [128, C * TOK], BF16) for i in range(2)]
        gts = [T(f"gts{i}", [128, 3 * TOK], F32) for i in range(2)]
        tmp1 = T("tmp1", [128, TOK], F32)
        tmp2 = T("tmp2", [128, TOK], F32)
        uoc = T("uoc", [128, C * TOK], F32)
        x1f = [T(f"x1f{i}", [128, C * TOK], F32) for i in range(2)]
        x1b = [T(f"x1b{i}", [128, C * TOK], BF16) for i in range(2)]
        hv = T("hv", [128, C * TOK], F32)
        vb = T("vb", [128, C * TOK], BF16)
        sqb = T("sqb", [128, C * TOK], BF16)
        sst = [T(f"sst{i}", [1, 2 * TOK], F32) for i in range(2)]
        stT = T("stT", [B, 2 * TOK], F32)
        stX = T("stX", [B, TOK], F32)
        stY = T("stY", [B, TOK], F32)
        stO = T("stO", [B, 2 * TOK], F32)
        stin = [T(f"stin{i}", [1, 2 * TOK], F32) for i in range(2)]
        outsb = [T(f"outsb{i}", [128, C * TOK], F32) for i in range(2)]

        psG = [PS(f"psG{i}", [128, 512], F32) for i in range(4)]
        psS = PS("psS", [1, 512], F32)
        psB = PS("psB", [128, 512], F32)

        wA = [(s_ld, a_store_end)] if mode == "full" else []
        for n in ["wfT", "ufT", "wiT", "uiT", "wcT", "ucT"]:
            p.emit("sync", lambda sy, n=n: sy.dma_start(
                out=w_sb[n][:, :], in_=wbf16_d[n]),
                waits=wA + [(s_ld, wcast[n])], inc=(s_lf, 16))
        wB_ready = (s_lf, s_lf.n)

        wc0 = [bias_ready] + ([(s_ld, a_store_end)] if mode == "full" else [])
        p.emit("vector", lambda v: v.memset(c_sb[:, :], 0.0),
               waits=wc0, inc=(s_dve, 1))
        for c in range(C):
            p.emit("vector", lambda v, c=c: v.tensor_scalar_add(
                c_sb[:, c * TOK:(c + 1) * TOK], c_sb[:, c * TOK:(c + 1) * TOK],
                bias_sb["inc0"][:, c:c + 1]), inc=(s_dve, 1))

        GATES = [("wfT", "ufT", "bf", AF.Sigmoid), ("wiT", "uiT", "bi", AF.Sigmoid),
                 ("wcT", "ucT", "bc", AF.Tanh)]
        pe_group = {}
        act_of_group = {}
        xt_used = {}
        cupd = {}
        cupd[(-1, C - 1)] = s_dve.n
        cupd[(-1, C - 2)] = s_dve.n
        xt_loaded = {}
        ggrp = 0

        for t in range(B):
            j, r = t // NCORES, t % NCORES
            w = [(s_pe, xt_used[t - 2])] if t - 2 in xt_used else []
            if mode == "full":
                w = w + [(s_cc, j + 1), (s_ld, a_store_end)]

            def ldxt(g, j=j, r=r, t=t):
                src = gath[j][r * DM:(r + 1) * DM, bass.ds(dyn["g_off"], TOK)]
                src = src.rearrange("(c p) n -> p c n", p=128)
                return g.dma_start(out=xt_sb[t % 2][:, :].rearrange(
                    "p (c n) -> p c n", c=C), in_=src)
            xt_loaded[t] = p.emit("gpsimd", ldxt, waits=w, inc=(s_ld, 16))

            wv = [(s_dve, cupd[(t - 1, C - 1)])]
            if t > 0:
                wv.append((s_pe, pe_group[(t - 1, C - 1, 2)]))
            cbf_n = p.emit("vector", lambda v: v.tensor_copy(cbf[:, :], c_sb[:, :]),
                           waits=wv, inc=(s_dve, 1))

            for m in range(C):
                for gi, (wn, un, bn, af) in enumerate(GATES):
                    bank = psG[ggrp % 4]
                    wmm = []
                    if m == 0 and gi == 0:
                        wmm = [(s_ld, xt_loaded[t]), (s_lf, wB_ready[1]),
                               (s_dve, cbf_n)]
                    if ggrp - 4 >= 0 and (ggrp - 4) in act_of_group:
                        wmm.append((s_act, act_of_group[ggrp - 4]))

                    def mk_mms(te, wn=wn, un=un, m=m, t=t, bank=bank):
                        inst = None
                        for k in range(C):
                            inst = te.matmul(
                                bank[:, 0:TOK], wslice(w_sb[wn], k, m),
                                xt_sb[t % 2][:, k * TOK:(k + 1) * TOK],
                                start=(k == 0), stop=False)
                        for k in range(C):
                            inst = te.matmul(
                                bank[:, 0:TOK], wslice(w_sb[un], k, m),
                                cbf[:, k * TOK:(k + 1) * TOK],
                                start=False, stop=(k == C - 1))
                        return inst
                    pe_group[(t, m, gi)] = p.emit("tensor", mk_mms, waits=wmm,
                                                  inc=(s_pe, 1))

                    wact = [(s_pe, pe_group[(t, m, gi)])]
                    key = (t, m - 2) if m >= 2 else (t - 1, C - 2 + m)
                    if key in cupd:
                        wact.append((s_dve, cupd[key]))

                    def mk_act(sc, af=af, bn=bn, m=m, gi=gi, bank=bank):
                        return sc.activation(
                            gts[m % 2][:, gi * TOK:(gi + 1) * TOK], bank[:, 0:TOK],
                            af, bias=bias_sb[bn][:, m:m + 1], scale=1.0)
                    act_of_group[ggrp] = p.emit("scalar", mk_act, waits=wact,
                                                inc=(s_act, 1))
                    ggrp += 1

                wcu = [(s_act, s_act.n)]
                cs_lo, cs_hi = m * TOK, (m + 1) * TOK
                g0 = gts[m % 2]
                p.emit("vector", lambda v, g0=g0, lo=cs_lo, hi=cs_hi:
                       v.tensor_tensor(tmp1[:, :], g0[:, 0:TOK], c_sb[:, lo:hi],
                                       op=ALU.mult), waits=wcu, inc=(s_dve, 1))
                p.emit("vector", lambda v, g0=g0: v.tensor_tensor(
                    tmp2[:, :], g0[:, TOK:2 * TOK], g0[:, 2 * TOK:3 * TOK],
                    op=ALU.mult), inc=(s_dve, 1))
                cupd[(t, m)] = p.emit("vector", lambda v, lo=cs_lo, hi=cs_hi:
                                      v.tensor_tensor(c_sb[:, lo:hi], tmp1[:, :],
                                                      tmp2[:, :], op=ALU.add),
                                      inc=(s_dve, 1))
            xt_used[t] = pe_group[(t, C - 1, 2)]

        for n in ["woT", "uoT"]:
            p.emit("sync", lambda sy, n=n: sy.dma_start(
                out=w_sb[n][:, :], in_=wbf16_d[n]),
                waits=[(s_pe, pe_group[(B - 1, C - 1, 2)]),
                       (s_ld, wcast[n])], inc=(s_lf, 16))
        wC_ready = (s_lf, s_lf.n)

        cbf_fin = p.emit("vector", lambda v: v.tensor_copy(cbf[:, :], c_sb[:, :]),
                         waits=[(s_dve, cupd[(B - 1, C - 1)]),
                                (s_pe, pe_group[(B - 1, C - 1, 2)])],
                         inc=(s_dve, 1))

        uoc_done = {}
        for m in range(C):
            bank = psG[m % 2]

            def mk_uoc(te, m=m, bank=bank):
                inst = None
                for k in range(C):
                    inst = te.matmul(bank[:, 0:TOK], wslice(w_sb["uoT"], k, m),
                                     cbf[:, k * TOK:(k + 1) * TOK],
                                     start=(k == 0), stop=(k == C - 1))
                return inst
            wmm = [(s_dve, cbf_fin), (s_lf, wC_ready[1])]
            if m >= 2:
                wmm.append((s_dve, uoc_done[m - 2]))
            gn = p.emit("tensor", mk_uoc, waits=wmm, inc=(s_pe, 1))
            uoc_done[m] = p.emit("vector", lambda v, m=m, bank=bank:
                                 v.tensor_scalar_mul(
                                     uoc[:, m * TOK:(m + 1) * TOK],
                                     bank[:, 0:TOK], 1.0),
                                 waits=[(s_pe, gn)], inc=(s_dve, 1))

        x1_loaded = {}
        x1b_loaded = {}
        wo_done = {}
        a1_hist = {}
        a2_hist = {}
        vadds_last = {}
        spill_done = {}
        e2_hist = {}
        st_stored = {}
        for t in range(B):
            j, r = t // NCORES, t % NCORES
            wldb = [(s_pe, wo_done[t - 2])] if t - 2 in wo_done else []
            if t - 2 in vadds_last:
                wldb.append((s_dve, vadds_last[t - 2]))

            def ldb(g, j=j, r=r, t=t):
                src = gath[j][r * DM:(r + 1) * DM, bass.ds(dyn["g_off"], TOK)]
                src = src.rearrange("(c p) n -> p c n", p=128)
                return g.dma_start(out=x1b[t % 2][:, :].rearrange(
                    "p (c n) -> p c n", c=C), in_=src)
            x1b_loaded[t] = p.emit("gpsimd", ldb, waits=wldb, inc=(s_ld, 16))

            for m in range(C):
                bank = psG[2 + (m % 2)]
                wmm = [(s_ld, x1b_loaded[t])]
                key = (t, m - 2) if m >= 2 else (t - 1, C - 2 + m)
                if key in a1_hist:
                    wmm.append((s_dve, a1_hist[key]))
                if t == 0 and m < 2:
                    wmm.append((s_dve, cbf_fin))

                def mk_wo(te, m=m, t=t, bank=bank):
                    inst = None
                    for k in range(C):
                        inst = te.matmul(bank[:, 0:TOK], wslice(w_sb["woT"], k, m),
                                         x1b[t % 2][:, k * TOK:(k + 1) * TOK],
                                         start=(k == 0), stop=(k == C - 1))
                    return inst
                gn = p.emit("tensor", mk_wo, waits=wmm, inc=(s_pe, 1))
                wo_done[t] = gn

                tmpX = [tmp1, tmp2][m % 2]
                wa1 = [(s_pe, gn)]
                keyp = (t, m - 2) if m >= 2 else (t - 1, C - 2 + m)
                if keyp in a2_hist:
                    wa1.append((s_act, a2_hist[keyp]))
                a1_hist[(t, m)] = p.emit(
                    "vector", lambda v, m=m, bank=bank, tmpX=tmpX:
                    v.tensor_tensor(tmpX[:, :], bank[:, 0:TOK],
                                    uoc[:, m * TOK:(m + 1) * TOK], op=ALU.add),
                    waits=wa1, inc=(s_dve, 1))

                wa2 = [(s_dve, a1_hist[(t, m)])]
                if m == 0 and (t - 1) in spill_done:
                    wa2.append((s_st, spill_done[t - 1]))
                a2_hist[(t, m)] = p.emit(
                    "scalar", lambda sc, m=m, tmpX=tmpX: sc.activation(
                        hv[:, m * TOK:(m + 1) * TOK], tmpX[:, :], AF.Tanh,
                        bias=bias_sb["bo"][:, m:m + 1], scale=1.0),
                    waits=wa2, inc=(s_act, 1))

            for m in range(C):
                wv = [(s_act, a2_hist[(t, m)])]
                n = p.emit("vector", lambda v, m=m, t=t: v.tensor_tensor(
                    hv[:, m * TOK:(m + 1) * TOK], hv[:, m * TOK:(m + 1) * TOK],
                    x1b[t % 2][:, m * TOK:(m + 1) * TOK], op=ALU.add),
                    waits=wv, inc=(s_dve, 1))
            vadds_last[t] = n

            cp1 = p.emit("vector", lambda v: v.tensor_copy(vb[:, :], hv[:, :]),
                         inc=(s_dve, 1))
            cp2 = p.emit("vector", lambda v: v.tensor_tensor(
                sqb[:, :], vb[:, :], vb[:, :], op=ALU.mult), inc=(s_dve, 1))

            def mk_sums(te):
                inst = None
                for c in range(C):
                    inst = te.matmul(psS[:, 0:TOK], ones128[:, :],
                                     vb[:, c * TOK:(c + 1) * TOK],
                                     start=(c == 0), stop=(c == C - 1))
                for c in range(C):
                    inst = te.matmul(psS[:, TOK:2 * TOK], ones128[:, :],
                                     sqb[:, c * TOK:(c + 1) * TOK],
                                     start=(c == 0), stop=(c == C - 1))
                return inst
            wsum = [(s_dve, cp2), (s_ld, consts_ready[1])]
            if t - 1 in e2_hist:
                wsum.append((s_dve, e2_hist[t - 1]))
            sums_n = p.emit("tensor", mk_sums, waits=wsum, inc=(s_pe, 1))

            we1 = [(s_pe, sums_n)]
            if t - 2 in st_stored:
                we1.append((s_st, st_stored[t - 2]))
            p.emit("vector", lambda v, t=t: v.tensor_scalar_mul(
                sst[t % 2][:, 0:TOK], psS[:, 0:TOK], 1.0 / DM),
                waits=we1, inc=(s_dve, 1))
            e2_hist[t] = p.emit("vector", lambda v, t=t: v.tensor_scalar_mul(
                sst[t % 2][:, TOK:2 * TOK], psS[:, TOK:2 * TOK], 1.0 / DM),
                inc=(s_dve, 1))
            st_stored[t] = p.emit("sync", lambda sy, t=t: sy.dma_start(
                out=stats_d[t:t + 1, :], in_=sst[t % 2][:, :]),
                waits=[(s_dve, e2_hist[t])], inc=(s_st, 16))

            def spv(sy, t=t):
                dst = vsp[:, t * TOK:(t + 1) * TOK].rearrange(
                    "(c p) n -> p c n", p=128)
                return sy.dma_start(out=dst, in_=hv[:, :].rearrange(
                    "p (c n) -> p c n", c=C))
            spill_done[t] = p.emit("sync", spv, waits=[(s_dve, vadds_last[t])],
                                   inc=(s_st, 16))

        st_ld = p.emit("sync", lambda sy: sy.dma_start(
            out=stT[:, :], in_=stats_d),
            waits=[(s_st, st_stored[B - 1])], inc=(s_lf, 16))
        rb1 = p.emit("vector", lambda v: v.tensor_tensor(
            stX[:, :], stT[:, 0:TOK], stT[:, 0:TOK], op=ALU.mult),
            waits=[(s_lf, st_ld)], inc=(s_dve, 1))
        rb2 = p.emit("vector", lambda v: v.tensor_tensor(
            stX[:, :], stT[:, TOK:2 * TOK], stX[:, :], op=ALU.subtract),
            inc=(s_dve, 1))
        rb3 = p.emit("scalar", lambda sc: sc.activation(
            stY[:, :], stX[:, :], AF.Ln, bias=epst[0:B, :], scale=1.0),
            waits=[(s_dve, rb2)], inc=(s_act, 1))
        rb4 = p.emit("scalar", lambda sc: sc.activation(
            stX[:, :], stY[:, :], AF.Exp, bias=0.0, scale=-0.5), inc=(s_act, 1))
        rb5 = p.emit("vector", lambda v: v.tensor_copy(
            stO[:, 0:TOK], stX[:, :]), waits=[(s_act, rb4)], inc=(s_dve, 1))
        rb6 = p.emit("vector", lambda v: v.tensor_tensor(
            stY[:, :], stT[:, 0:TOK], stX[:, :], op=ALU.mult), inc=(s_dve, 1))
        rb7 = p.emit("vector", lambda v: v.tensor_scalar(
            stO[:, TOK:2 * TOK], stY[:, :], -1.0, None, op0=ALU.mult),
            inc=(s_dve, 1))
        stO_st = p.emit("sync", lambda sy: sy.dma_start(
            out=stats_o, in_=stO[:, :]), waits=[(s_dve, rb7)], inc=(s_st, 16))

        out_stored = {}
        appl_done = {}
        bc_hist = {}
        vld = {}
        for t in range(B):
            wldv = ([(s_dve, appl_done[t - 2])] if t - 2 in appl_done
                    else [(s_dve, vadds_last[B - 2 + t])])

            def ldv(g, t=t):
                src = vsp[:, t * TOK:(t + 1) * TOK].rearrange(
                    "(c p) n -> p c n", p=128)
                return g.dma_start(out=x1f[t % 2][:, :].rearrange(
                    "p (c n) -> p c n", c=C), in_=src)
            wldv.append((s_st, spill_done[t]))
            vld[t] = p.emit("gpsimd", ldv, waits=wldv, inc=(s_ld, 16))

            wsi = [(s_st, stO_st)]
            if t - 2 in bc_hist:
                wsi.append((s_pe, bc_hist[t - 2]))
            sld = p.emit("sync", lambda sy, t=t: sy.dma_start(
                out=stin[t % 2][:, :], in_=stats_o[t:t + 1, :]),
                waits=wsi, inc=(s_lf, 16))

            def mk_bc(te, t=t):
                te.matmul(psB[:, 0:TOK], ones1F[:, :],
                          stin[t % 2][:, 0:TOK], start=True, stop=True)
                return te.matmul(psB[:, TOK:2 * TOK], ones1F[:, :],
                                 stin[t % 2][:, TOK:2 * TOK],
                                 start=True, stop=True)
            wbc = [(s_lf, sld)]
            if t - 1 in appl_done:
                wbc.append((s_dve, appl_done[t - 1]))
            bc_n = p.emit("tensor", mk_bc, waits=wbc, inc=(s_pe, 1))
            bc_hist[t] = bc_n

            for m in range(C):
                w1 = [(s_pe, bc_n), (s_ld, vld[t])] if m == 0 else []
                if m == 0 and t - 2 in out_stored:
                    w1.append((s_st, out_stored[t - 2]))
                p.emit("vector", lambda v, m=m, t=t: v.tensor_tensor(
                    outsb[t % 2][:, m * TOK:(m + 1) * TOK],
                    x1f[t % 2][:, m * TOK:(m + 1) * TOK], psB[:, 0:TOK],
                    op=ALU.mult), waits=w1, inc=(s_dve, 1))
                appl_done[t] = p.emit("vector", lambda v, m=m, t=t:
                                      v.tensor_tensor(
                    outsb[t % 2][:, m * TOK:(m + 1) * TOK],
                    outsb[t % 2][:, m * TOK:(m + 1) * TOK], psB[:, TOK:2 * TOK],
                    op=ALU.add), inc=(s_dve, 1))

            def stout(sy, t=t):
                dst = out_h[:, t * TOK:(t + 1) * TOK].rearrange(
                    "(c p) n -> p c n", p=128)
                return sy.dma_start(out=dst, in_=outsb[t % 2][:, :].rearrange(
                    "p (c n) -> p c n", c=C))
            out_stored[t] = p.emit("sync", stout, waits=[(s_dve, appl_done[t])],
                                   inc=(s_st, 16))

        p.emit("sync", lambda sy: None, waits=[(s_st, s_st.n)])

        for s in p.sems.values():
            s.h = es.enter_context(nc.semaphore(s.name))
        with nc.Block() as block:
            @block.tensor
            def _(te):
                p.build_engine(te, "tensor")

            @block.scalar
            def _(sc):
                p.build_engine(sc, "scalar")

            @block.vector
            def _(v):
                p.build_engine(v, "vector")

            @block.gpsimd
            def _(g):
                p.build_engine(g, "gpsimd")

            @block.sync
            def _(sy):
                p.build_engine(sy, "sync")

    return nc


def build_phase_a(ctx):
    nc, p, es = ctx["nc"], ctx["p"], ctx["es"]
    T, PS = ctx["T"], ctx["PS"]
    xT, whbm, slab, gath = ctx["xT"], ctx["whbm"], ctx["slab"], ctx["gath"]
    ones1, ones128, epst = ctx["ones1"], ctx["ones128"], ctx["epst"]
    ones1F = ctx["ones1F"]
    s_ld, s_lf, s_st = ctx["s_ld"], ctx["s_lf"], ctx["s_st"]
    s_pe, s_act, s_dve, s_cc = ctx["s_pe"], ctx["s_act"], ctx["s_dve"], ctx["s_cc"]

    aes = contextlib.ExitStack()
    ctx["aes"] = aes
    TA = lambda *a, **k: aes.enter_context(nc.sbuf_tensor(*a, **k))
    PA = lambda *a, **k: aes.enter_context(nc.psum_tensor(*a, **k))

    arena1 = TA("arena1", [128, C * S], BF16)
    xbf = arena1[:, :]
    aT = arena1
    _war = TA("warena", [128, C * C * 128], BF16)
    warena = [_war, _war]
    qT = TA("qT", [128, C * S], BF16)
    kT = TA("kT", [128, C * S], BF16)
    vP = TA("vP", [128, 16 * (H * 65)], BF16)
    exp_sb = [TA(f"exp{i}", [128, 1024], BF16) for i in range(6)]

    rb_row = TA("rb_row", [1, 1024], BF16)
    lnPool = TA("lnPool", [1, 1024], F32)
    lnzrow = lnPool[:, 0:1024]
    rbB = TA("rbB", [64, 1024], BF16)
    x1sb = TA("x1sb", [128, C * 512], F32)
    vtmpf = TA("vtmpf", [128, 512], F32)
    xres = TA("xres", [128, C * 512], F32)
    xpb = arena1[:, 8192:8192 + 4096]
    sqA = arena1[:, 8192 + 4096:16384]
    lnS = TA("lnS", [1, 512], F32)
    lnS2 = TA("lnS2", [1, 512], F32)
    lnT = lnPool[:, 0:512]
    lnB = lnPool[:, 512:1024]
    neg3 = TA("neg3", [128, 1], F32)
    identF = TA("identF", [128, 128], F32)

    psSC = [PA(f"psSC{i}", [128, 1024], F32) for i in range(3)]
    psUs = PA("psUs", [128, 1024], F32)
    psU = [psUs, psUs]

    SLOT = [(psSC[0], 0), (psSC[0], 512), (psSC[1], 0), (psSC[1], 512),
            (psSC[2], 0), (psSC[2], 512), (psUs, 0), (psUs, 512)]
    slot_evac = [0] * 8
    slot_mm = [0] * 8

    def slot_ap(u, p0=0, p1=128, w=512):
        t_, off = SLOT[u]
        return t_[p0:p1, off:off + w]

    p.emit("gpsimd", lambda g: g.memset(neg3[:, :], -3.0), inc=(s_ld, 1))
    p.emit("gpsimd", lambda g: g.memset(vP[:, :], 1.0), inc=(s_ld, 1))
    p.emit("gpsimd", lambda g: g.memset(identF[:, :], 0.0))
    p.emit("gpsimd", lambda g: g.affine_select(
        out=identF[:, :], in_=identF[:, :], compare_op=ALU.not_equal,
        fill=1.0, base=0, pattern=[[-1, 128]], channel_multiplier=1),
        inc=(s_ld, 1))
    aconsts = (s_ld, s_ld.n)

    PW = [("Wq", qT), ("Wk", kT), ("Wv", None), ("Wpost", None)]
    su = 0
    xbf_loaded = {}
    wload = {}
    wload_mm = {}
    aT_free = {-1: 0}
    smn_last = {}
    store_cnt = 0
    store_hist = []
    exp_hist = {}
    av_hist = {}
    tt_hist = {}
    gelu_hist = {}

    for b in range(BLOC):
        wx = [(s_pe, smn_last[b - 1])] if b - 1 in smn_last else []

        def ldx_half(sy, b=b, half=0):
            src = ctx["xbf_d"][half * 512:half * 512 + 512,
                               b * S:(b + 1) * S].rearrange(
                "(c p) n -> p c n", p=128)
            dst = arena1[:, half * 4 * S:(half + 1) * 4 * S].rearrange(
                "p (c n) -> p c n", c=4)
            return sy.dma_start(out=dst, in_=src)
        wx = wx + [(s_ld, ctx["xcast"][b])]
        xbf_lo = p.emit("sync", lambda sy, b=b: ldx_half(sy, b, 0),
                        waits=wx, inc=(s_lf, 16))
        xbf_hi = p.emit("sync", lambda sy, b=b: ldx_half(sy, b, 1),
                        inc=(s_lf, 16))
        xbf_loaded[b] = xbf_hi

        def emit_wload(pi, b=b):
            wwl = []
            prev = (b, pi - 1) if pi >= 1 else (b - 1, 3)
            if prev in wload_mm:
                wwl = [(s_pe, wload_mm[prev])]
            wn = PW[pi][0]
            wload[(b, pi)] = p.emit("sync", lambda sy, wn=wn, pi=pi: sy.dma_start(
                out=warena[pi % 2][:, :], in_=ctx["wbf16_d"][wn]),
                waits=wwl + [(s_ld, ctx["wcast"][wn])], inc=(s_lf, 16))
        emit_wload(0)

        for pi in range(3):
            wn, dstT = PW[pi][0], PW[pi][1]
            for m in range(C):
                sset = [0, 1, 2, 3] if (pi == 2 or m % 2 == 0) else [4, 5, 6, 7]
                wmm = [(s_lf, max(wload[(b, pi)], xbf_lo))]
                for u in sset:
                    if slot_evac[u] > 0:
                        wmm.append((s_dve, slot_evac[u]))

                def mk_p(te, pi=pi, m=m, b=b, klo=0, khi=4, sset=sset):
                    inst = None
                    for k in range(klo, khi):
                        lhs = wslice(warena[pi % 2], k, m)
                        for sblk in range(4):
                            inst = te.matmul(
                                slot_ap(sset[sblk]), lhs,
                                xbf[:, k * S + sblk * 512:k * S + sblk * 512 + 512],
                                start=(k == 0), stop=(k == C - 1))
                    return inst
                p.emit("tensor", mk_p, waits=wmm, inc=(s_pe, 1))
                gn = p.emit("tensor",
                            lambda te, pi=pi, m=m, b=b: mk_p(te, pi, m, b, 4, 8),
                            waits=[(s_lf, xbf_hi)], inc=(s_pe, 1))
                wload_mm[(b, pi)] = gn

                if pi < 2:
                    for sblk in range(4):
                        def ev(v, dstT=dstT, m=m, sblk=sblk, sset=sset):
                            return v.tensor_scalar_mul(
                                dstT[:, m * S + sblk * 512:m * S + sblk * 512 + 512],
                                slot_ap(sset[sblk]), 1.0)
                        slot_evac[sset[sblk]] = p.emit(
                            "vector", ev, waits=[(s_pe, gn)], inc=(s_dve, 1))
                else:
                    for sblk in range(4):
                        def evv(v, sblk=sblk):
                            return v.tensor_scalar_mul(vtmpf[:, :],
                                                       slot_ap(sblk), 1.0)
                        wev = [(s_pe, gn)]
                        if (m, sblk) == (0, 0) and store_hist:
                            wev.append((s_ld, store_hist[-1]))
                        if (b, m, sblk) != (0, 0, 0):
                            wev.append((s_pe, trn))
                        evn = p.emit("vector", evv, waits=wev, inc=(s_dve, 1))
                        slot_evac[sblk] = evn
                        for ti in range(4):
                            tc = sblk * 4 + ti

                            def mktr(te, ti=ti):
                                return te.transpose(
                                    psU[0][0:128, 0:128],
                                    vtmpf[:, ti * 128:(ti + 1) * 128],
                                    identF[:, :])
                            wtr = [(s_dve, evn), (s_ld, aconsts[1])]
                            if (b, m, sblk, ti) != (0, 0, 0, 0):
                                wtr.append((s_dve, vp_ev))
                            trn = p.emit("tensor", mktr, waits=wtr, inc=(s_pe, 1))

                            def evp(v, tc=tc, m=m):
                                base = tc * H * 65 + (2 * m) * 65
                                dst = vP[:, base:base + 130].rearrange(
                                    "p (h e) -> p h e", h=2)[:, :, 0:64]
                                srcv = psU[0][0:128, 0:128].rearrange(
                                    "p (h e) -> p h e", h=2)
                                return v.tensor_copy(dst, srcv)
                            vp_ev = p.emit("vector", evp, waits=[(s_pe, trn)],
                                           inc=(s_dve, 1))
            emit_wload(pi + 1)
        vp_done = vp_ev

        expz_hist = {}
        bc_hist = {}
        for sh in range(2):

            def emit_scores(sq, hp, tc):
                bank = psSC[tc % 3]
                wsc = []
                tcp3 = 15 - ((15 - tc) % 3)
                kexp = (b, sq, hp, tc - 3) if tc >= 3 else \
                       (b, sq, hp - 1, tcp3) if hp >= 1 else \
                       (b, sq - 1, 7, tcp3) if sq >= 1 else \
                       (b - 1, 3, 7, tcp3)
                if kexp in exp_hist:
                    wsc.append((s_act, exp_hist[kexp]))
                if hp == 0 and tc == 0 and sq % 2 == 0:
                    for u in (range(8) if sq == 0 else range(4)):
                        if slot_evac[u] > 0:
                            wsc.append((s_dve, slot_evac[u]))
                    if sq == 0:
                        wsc.append((s_dve, vp_done))

                def mksc(te, sq=sq, hp=hp, tc=tc, bank=bank):
                    te.matmul(bank[:, 0:512],
                              kT[0:64, hp * S + tc * 128:hp * S + tc * 128 + 128],
                              qT[0:64, hp * S + sq * 512:hp * S + sq * 512 + 512],
                              start=True, stop=True)
                    return te.matmul(
                        bank[:, 512:1024],
                        kT[64:128, hp * S + tc * 128:hp * S + tc * 128 + 128],
                        qT[64:128, hp * S + sq * 512:hp * S + sq * 512 + 512],
                        start=True, stop=True)
                return p.emit("tensor", mksc, waits=wsc, inc=(s_pe, 1))

            def emit_exp(sq, hp, tc, scn):
                bank = psSC[tc % 3]
                wex = [(s_pe, scn)]
                tcq = tc + 12 if tc <= 3 else tc + 6
                kav = (b, sq, hp, tc - 6) if tc >= 6 else \
                      (b, sq, hp - 1, tcq) if hp >= 1 else \
                      (b, sq - 1, 7, tcq) if sq >= 1 else \
                      (b - 1, 3, 7, tcq)
                if kav in av_hist:
                    wex.append((s_pe, av_hist[kav]))

                def mkex(sc, tc=tc, bank=bank):
                    return sc.activation(exp_sb[tc % 6][:, :], bank[:, :],
                                         AF.Exp, bias=neg3[:, :], scale=0.125)
                exp_hist[(b, sq, hp, tc)] = p.emit("scalar", mkex, waits=wex,
                                                   inc=(s_act, 1))

            def emit_av(sq, hp, tc):
                wav = [(s_act, exp_hist[(b, sq, hp, tc)])]
                if tc == 0:
                    ktt = (b, sq, hp - 1) if hp >= 1 else \
                          (b, sq - 1, 7) if sq >= 1 else \
                          (b - 1, 3, 7)
                    if ktt in tt_hist:
                        wav.append((s_dve, tt_hist[ktt]))

                def mkav(te, hp=hp, tc=tc):
                    te.matmul(psUs[0:65, 0:512],
                              vP[:, tc * H * 65 + 2 * hp * 65:
                                 tc * H * 65 + 2 * hp * 65 + 65],
                              exp_sb[tc % 6][:, 0:512],
                              start=(tc == 0), stop=(tc == 15))
                    return te.matmul(
                        psUs[0:65, 512:1024],
                        vP[:, tc * H * 65 + (2 * hp + 1) * 65:
                           tc * H * 65 + (2 * hp + 1) * 65 + 65],
                        exp_sb[tc % 6][:, 512:1024],
                        start=(tc == 0), stop=(tc == 15))
                av_hist[(b, sq, hp, tc)] = p.emit("tensor", mkav, waits=wav,
                                                  inc=(s_pe, 1))

            def emit_tail_a(sq, hp):
                ln_n = p.emit("scalar", lambda sc: sc.activation(
                    lnzrow, psUs[64:65, :], AF.Ln),
                    waits=[(s_pe, av_hist[(b, sq, hp, 15)])], inc=(s_act, 1))

                def mkexpz(sc):
                    with nc.allow_low_precision(
                            reason="softmax 1/Z stored bf16; product is "
                                   "rounded to bf16 anyway; diluted by residual"):
                        return sc.activation(rb_row[:, :], lnzrow,
                                             AF.Exp, scale=-1.0)
                wz = []
                kbc = (sq, hp - 1) if hp >= 1 else (sq - 1, 7)
                if kbc in bc_hist:
                    wz.append((s_pe, bc_hist[kbc]))
                expz_hist[(sq, hp)] = p.emit("scalar", mkexpz, waits=wz,
                                             inc=(s_act, 1))

            def emit_tail_b(sq, hp, sq2):
                def mkbc(te):
                    inst = None
                    for si in range(2):
                        inst = te.matmul(
                            psUs[64:128, si * 512:si * 512 + 512],
                            ones1[0:1, 0:64],
                            rb_row[:, si * 512:si * 512 + 512],
                            start=True, stop=True)
                    return inst
                bc_hist[(sq, hp)] = p.emit(
                    "tensor", mkbc, waits=[(s_act, expz_hist[(sq, hp)])],
                    inc=(s_pe, 1))
                ebn = p.emit("vector", lambda v: v.tensor_scalar_mul(
                    rbB[:, :], psUs[64:128, :], 1.0),
                    waits=[(s_pe, bc_hist[(sq, hp)])], inc=(s_dve, 1))
                p.emit("vector", lambda v, hp=hp, sq2=sq2: v.tensor_tensor(
                    arena1[0:64, hp * 1024 + sq2 * 512:hp * 1024 + sq2 * 512 + 512],
                    psUs[0:64, 0:512], rbB[:, 0:512], op=ALU.mult),
                    inc=(s_dve, 1))
                tt_hist[(b, sq, hp)] = p.emit(
                    "vector", lambda v, hp=hp, sq2=sq2: v.tensor_tensor(
                        arena1[64:128, hp * 1024 + sq2 * 512:
                               hp * 1024 + sq2 * 512 + 512],
                        psUs[0:64, 512:1024], rbB[:, 512:1024],
                        op=ALU.mult), inc=(s_dve, 1))

            for sq2 in range(2):
                sq = sh * 2 + sq2
                for hp in range(8):
                    kprev = (sq, hp - 1) if hp >= 1 else (sq - 1, 7)
                    for tc in range(16):
                        scn = emit_scores(sq, hp, tc)
                        if tc >= 6:
                            emit_av(sq, hp, tc - 6)
                        emit_exp(sq, hp, tc, scn)
                        if tc == 2 and (b, kprev[0], kprev[1], 15) in av_hist \
                                and kprev not in expz_hist:
                            emit_tail_a(kprev[0], kprev[1])
                        if tc == 5 and kprev in expz_hist \
                                and kprev not in bc_hist:
                            emit_tail_b(kprev[0], kprev[1], kprev[0] % 2)
                    for tcl in range(10, 16):
                        emit_av(sq, hp, tcl)
            emit_tail_a(2 * sh + 1, 7)
            emit_tail_b(2 * sh + 1, 7, 1)

            def mkgl(sc):
                return sc.activation(arena1[:, 0:C * 1024], arena1[:, 0:C * 1024],
                                     AF.Gelu)
            gelu_hist[(b, sh)] = p.emit("scalar", mkgl,
                                        waits=[(s_dve, tt_hist[(b, 2 * sh + 1, 7)])],
                                        inc=(s_act, 1))

            for sb2 in range(2):
                sblk = sh * 2 + sb2

                def ldxr(sy, b=b, sblk=sblk):
                    srcx = xT[:, b * S + sblk * 512:b * S + sblk * 512 + 512]
                    srcx = srcx.rearrange("(c p) n -> p c n", p=128)
                    return sy.dma_start(out=xres[:, :].rearrange(
                        "p (c n) -> p c n", c=C), in_=srcx)
                wxr = [(s_dve, xres_used)] if b + sblk > 0 else []
                xrn = p.emit("sync", ldxr, waits=wxr, inc=(s_lf, 16))

                stt_hist = {}
                for m in range(C):
                    u = su % 4
                    wmm = [(s_act, gelu_hist[(b, sh)])]
                    if slot_evac[u] > 0:
                        wmm.append((s_dve, slot_evac[u]))

                    def mkwp(te, m=m, sb2=sb2, u=u):
                        inst = None
                        for k in range(C):
                            inst = te.matmul(
                                slot_ap(u), wslice(warena[1], k, m),
                                arena1[:, k * 1024 + sb2 * 512:
                                       k * 1024 + sb2 * 512 + 512],
                                start=(k == 0), stop=(k == C - 1))
                        return inst
                    gn = p.emit("tensor", mkwp, waits=wmm, inc=(s_pe, 1))
                    slot_mm[u] = gn
                    aT_free[b] = gn
                    wload_mm[(b, 3)] = gn

                    def mkstt(v, m=m, u=u):
                        return v.scalar_tensor_tensor(
                            x1sb[:, m * 512:(m + 1) * 512],
                            xres[:, m * 512:(m + 1) * 512],
                            2.0, slot_ap(u), op0=ALU.mult, op1=ALU.add)
                    ws = [(s_pe, gn)]
                    if m == 0:
                        ws.append((s_lf, xrn))
                        if store_hist:
                            ws.append((s_ld, store_hist[-1]))
                    slot_evac[u] = p.emit("vector", mkstt, waits=ws, inc=(s_dve, 1))
                    stt_hist[m] = slot_evac[u]
                    su += 1

                cpx = p.emit("vector", lambda v: v.tensor_copy(
                    xpb, x1sb[:, :]), waits=[(s_dve, stt_hist[C - 1])],
                    inc=(s_dve, 1))
                cps = p.emit("vector", lambda v: v.tensor_tensor(
                    sqA, xpb, xpb, op=ALU.mult), inc=(s_dve, 1))

                us_, uq = su % 4, (su + 1) % 4
                su += 2

                def mksums(te, us_=us_, uq=uq):
                    inst = None
                    for k in range(C):
                        inst = te.matmul(slot_ap(us_, 0, 1), ones128[:, :],
                                         xpb[:, k * 512:(k + 1) * 512],
                                         start=(k == 0), stop=(k == C - 1))
                    for k in range(C):
                        inst = te.matmul(slot_ap(uq, 0, 1), ones128[:, :],
                                         sqA[:, k * 512:(k + 1) * 512],
                                         start=(k == 0), stop=(k == C - 1))
                    return inst
                wsm = [(s_dve, cps)]
                if slot_evac[us_] > 0:
                    wsm.append((s_dve, slot_evac[us_]))
                if slot_evac[uq] > 0:
                    wsm.append((s_dve, slot_evac[uq]))
                smn = p.emit("tensor", mksums, waits=wsm, inc=(s_pe, 1))
                smn_last[b] = smn

                e1 = p.emit("vector", lambda v, us_=us_: v.tensor_scalar_mul(
                    lnS[:, :], slot_ap(us_, 0, 1), 1.0 / DM),
                    waits=[(s_pe, smn)], inc=(s_dve, 1))
                e2 = p.emit("vector", lambda v, uq=uq: v.tensor_scalar_mul(
                    lnS2[:, :], slot_ap(uq, 0, 1), 1.0 / DM), inc=(s_dve, 1))
                slot_evac[us_] = e1
                slot_evac[uq] = e2
                l1 = p.emit("vector", lambda v: v.tensor_tensor(
                    lnT, lnS[:, :], lnS[:, :], op=ALU.mult), inc=(s_dve, 1))
                l2 = p.emit("vector", lambda v: v.tensor_tensor(
                    lnT, lnS2[:, :], lnT, op=ALU.subtract),
                    inc=(s_dve, 1))
                l3 = p.emit("scalar", lambda sc: sc.activation(
                    lnS2[:, :], lnT, AF.Ln, bias=epst[0:1, :], scale=1.0),
                    waits=[(s_dve, l2)], inc=(s_act, 1))
                l4 = p.emit("scalar", lambda sc: sc.activation(
                    lnT, lnS2[:, :], AF.Exp, bias=0.0, scale=-0.5),
                    inc=(s_act, 1))
                l5 = p.emit("vector", lambda v: v.tensor_copy(
                    lnB, lnT), waits=[(s_act, l4)], inc=(s_dve, 1))
                l6 = p.emit("vector", lambda v: v.tensor_tensor(
                    lnS2[:, :], lnS[:, :], lnT, op=ALU.mult), inc=(s_dve, 1))
                l7 = p.emit("vector", lambda v: v.tensor_scalar(
                    lnS[:, :], lnS2[:, :], -1.0, None, op0=ALU.mult),
                    inc=(s_dve, 1))

                for wsem, wval in [(s_dve, e1), (s_dve, l2), (s_act, l3),
                                   (s_dve, l7)]:
                    def mkdum(te):
                        return te.matmul(psU[0][0:128, 0:512], kT[:, 0:128],
                                         qT[:, 0:512], start=True, stop=True)
                    p.emit("tensor", mkdum,
                           waits=[(wsem, wval),
                                  (s_dve, tt_hist[(b, 2 * sh + 1, 7)])],
                           inc=(s_pe, 1))

                ur, un = su % 4, (su + 1) % 4
                su += 2

                def mkbc2(te, ur=ur, un=un):
                    te.matmul(slot_ap(ur), ones1F[0:1, :], lnB,
                              start=True, stop=True)
                    return te.matmul(slot_ap(un), ones1F[0:1, :], lnS[0:1, :],
                                     start=True, stop=True)
                wb2 = [(s_dve, l7)]
                if slot_evac[ur] > 0:
                    wb2.append((s_dve, slot_evac[ur]))
                if slot_evac[un] > 0:
                    wb2.append((s_dve, slot_evac[un]))
                b2n = p.emit("tensor", mkbc2, waits=wb2, inc=(s_pe, 1))

                ap_hist = {}
                for m in range(C):
                    w1 = [(s_pe, b2n)] if m == 0 else []
                    p.emit("vector", lambda v, m=m, ur=ur: v.tensor_tensor(
                        xres[:, m * 512:(m + 1) * 512],
                        x1sb[:, m * 512:(m + 1) * 512], slot_ap(ur),
                        op=ALU.mult), waits=w1, inc=(s_dve, 1))
                    ap_hist[m] = p.emit("vector", lambda v, m=m, un=un:
                                        v.tensor_tensor(
                        x1sb[:, m * 512:(m + 1) * 512],
                        xres[:, m * 512:(m + 1) * 512], slot_ap(un),
                        op=ALU.add), inc=(s_dve, 1))
                xres_used = ap_hist[C - 1]
                slot_evac[ur] = ap_hist[C - 1]
                slot_evac[un] = ap_hist[C - 1]

                def stx1(g, b=b, sblk=sblk):
                    dst = slab[b][:, sblk * 512:sblk * 512 + 512].rearrange(
                        "(c p) n -> p c n", p=128)
                    return g.dma_start(out=dst, in_=x1sb[:, :].rearrange(
                        "p (c n) -> p c n", c=C))
                store_hist.append(p.emit(
                    "gpsimd", stx1, waits=[(s_dve, ap_hist[C - 1])],
                    inc=(s_ld, 16)))

        p.emit("gpsimd", lambda g, b=b: g.collective_compute(
            "AllGather", ALU.bypass, replica_groups=[list(range(NCORES))],
            ins=[slab[b]], outs=[gath[b]]),
            waits=[(s_ld, p.sems["s_ld"].n)], inc=(s_cc, 1))

    aes.close()
    return (ctx["p"].sems["s_ld"].n)



def weight_maps(inputs):
    return {
        "wfT": prechunk(np.asarray(inputs["L0_ffn_wf"], np.float32).T),
        "ufT": prechunk(np.asarray(inputs["L0_ffn_uf"], np.float32).T),
        "wiT": prechunk(np.asarray(inputs["L0_ffn_wi"], np.float32).T),
        "uiT": prechunk(np.asarray(inputs["L0_ffn_ui"], np.float32).T),
        "wcT": prechunk(np.asarray(inputs["L0_ffn_wc"], np.float32).T),
        "ucT": prechunk(np.asarray(inputs["L0_ffn_uc"], np.float32).T),
        "woT": prechunk(np.asarray(inputs["L0_ffn_wo"], np.float32).T),
        "uoT": prechunk(np.asarray(inputs["L0_ffn_uo"], np.float32).T),
        "bf": pack_bias(inputs["L0_ffn_bf"]), "bi": pack_bias(inputs["L0_ffn_bi"]),
        "bc": pack_bias(inputs["L0_ffn_bc"]), "bo": pack_bias(inputs["L0_ffn_bo"]),
        "inc0": pack_bias(inputs["L0_ffn_in_c"]),
    }


def assemble_out(results):
    full = np.empty((B, S, DM), np.float32)
    for k in range(NCORES):
        ok = np.asarray(results[k]["out"]).reshape(DM, B, TOK)
        full[:, k * TOK:(k + 1) * TOK, :] = ok.transpose(1, 2, 0)
    return full
def kernel(**inputs):
    inputs = {k: np.asarray(v, np.float32) for k, v in inputs.items()}
    base = weight_maps(inputs)
    base["Wq"] = prechunk(inputs["L0_attn_Wq"])
    base["Wk"] = prechunk(inputs["L0_attn_Wk"])
    base["Wv"] = prechunk(inputs["L0_attn_Wv"])
    base["Wpost"] = prechunk(inputs["L0_attn_Wpost"])
    X = inputs["L0_attn_X"]
    in_maps = []
    for k in range(NCORES):
        m = dict(base)
        m["xT"] = np.ascontiguousarray(
            X[[k, 8 + k, 16 + k, 24 + k]].transpose(2, 0, 1).reshape(DM, BLOC * S))
        in_maps.append(m)
    nc = build_program(mode="full")
    res = run_bass_kernel_spmd(nc, in_maps, core_ids=list(range(NCORES)))
    return assemble_out(res.results)
